# version 06
# speedup vs baseline: 1.8450x; 1.0742x over previous
import math
import numpy as np
import ml_dtypes

import concourse.bass as bass
import concourse.mybir as mybir
from concourse.bass_utils import run_bass_kernel_spmd

F32 = mybir.dt.float32
BF16 = mybir.dt.bfloat16
AF = mybir.ActivationFunctionType
ALU = mybir.AluOpType
AX = mybir.AxisListType

BF = ml_dtypes.bfloat16

B, T, D = 8, 2048, 1024
NCHUNK = D // 128
CAP = 2560

ENGINES = ("pe", "act", "dve", "sp", "pool")
DMA_ENGINES = {"sp": "spq", "pool": "poolq"}


NDMACLASS = 7
RRDMACLASS = 6


class Op:
    __slots__ = ("engine", "fn", "reads", "writes", "dma", "tag", "seq", "idx",
                 "need_inc", "deps", "dmaclass", "dmapos", "same_dep")

    def __init__(self, engine, fn, reads, writes, dma, tag, seq, idx):
        self.engine = engine
        self.fn = fn
        self.reads = reads
        self.writes = writes
        self.dma = dma
        self.tag = tag
        self.seq = seq
        self.idx = idx
        self.need_inc = dma
        self.deps = []
        self.dmaclass = -1
        self.dmapos = 0
        self.same_dep = False


def _key(x):
    if isinstance(x, str):
        return x
    t = getattr(x, "tensor", x)
    return t.name


class Builder:
    def __init__(self):
        self.ops = {e: [] for e in ENGINES}
        self.seq = 0
        self.res = {}
        self.dmacount = {e: 0 for e in ENGINES}
        self.dmaposc = {}

    def _r(self, k):
        return self.res.setdefault(k, {"w": None, "r": {}})

    def emit(self, engine, fn, reads=(), writes=(), dma=False, tag="", dmaclass=None):
        assert engine in ENGINES
        idx = len(self.ops[engine])
        op = Op(engine, fn, [_key(r) for r in reads], [_key(w) for w in writes],
                dma, tag, self.seq, idx)
        self.seq += 1
        deps = {}

        def add_dep(src):
            if src is None:
                return
            (e, i, sdma) = src
            if e == engine and not sdma:
                op.same_dep = True
                return
            deps[(e, i)] = (e, i, sdma)

        for k in op.reads:
            r = self._r(k)
            add_dep(r["w"])
            if k.startswith("psb"):
                for (e, sdma), i in r["r"].items():
                    add_dep((e, i, sdma))
        for k in op.writes:
            r = self._r(k)
            add_dep(r["w"])
            for (e, sdma), i in r["r"].items():
                add_dep((e, i, sdma))
        op.deps = [d for d in deps.values() if not (d[0] == engine and d[2] == dma and d[1] == idx)]
        if dma:
            if dmaclass is None:
                n = self.dmacount[engine]
                self.dmacount[engine] = n + 1
                dmaclass = n % RRDMACLASS
            op.dmaclass = dmaclass
            pos = self.dmaposc.setdefault(engine, {}).get(dmaclass, 0) + 1
            self.dmaposc[engine][dmaclass] = pos
            op.dmapos = pos
        for k in op.reads:
            r = self._r(k)
            if k.startswith("psb"):
                r["w"] = (engine, idx, dma)
                r["r"] = {}
            cur = r["r"].get((engine, dma))
            if cur is None or cur < idx:
                r["r"][(engine, dma)] = idx
        for k in op.writes:
            r = self._r(k)
            r["w"] = (engine, idx, dma)
            r["r"] = {}
        self.ops[engine].append(op)
        return op

    def haunt(self, newkey, oldkeys):
        r = self._r(newkey)
        for ok in oldkeys:
            o = self.res.get(ok)
            if not o:
                continue
            if o["w"] is not None:
                e, i, sdma = o["w"]
                cur = r["r"].get((e, sdma))
                if cur is None or cur < i:
                    r["r"][(e, sdma)] = i
            for (e, sdma), i in o["r"].items():
                cur = r["r"].get((e, sdma))
                if cur is None or cur < i:
                    r["r"][(e, sdma)] = i

    def finalize(self):
        for e in ENGINES:
            for op in self.ops[e]:
                for (se, si, sdma) in op.deps:
                    self.ops[se][si].need_inc = True
        self.cum = {}
        for e in ENGINES:
            cc, cd = 0, 0
            cums = []
            for op in self.ops[e]:
                if op.need_inc:
                    if op.dma:
                        cd += 16
                    else:
                        cc += 1
                cums.append((cc, cd))
            self.cum[e] = cums

    def wait_value(self, se, si, sdma):
        if sdma:
            return 16 * self.ops[se][si].dmapos
        cc, cd = self.cum[se][si]
        return cc

    def wait_sem(self, se, si, sdma):
        if sdma:
            return f"q{se}{self.ops[se][si].dmaclass}"
        return se

    def stats(self):
        out = {}
        for e in ENGINES:
            n = len(self.ops[e])
            ninc = sum(1 for o in self.ops[e] if o.need_inc)
            nw = sum(len(o.deps) for o in self.ops[e])
            out[e] = (n, ninc, nw)
        return out

    def build(self, nc, sems):
        bld = self

        with nc.Block() as block:
            def run_engine(engobj, ename):
                seen = {}

                def wait(semkey, val):
                    if val <= 0 or seen.get(semkey, 0) >= val:
                        return
                    seen[semkey] = val
                    engobj.wait_ge(sems[semkey], val)

                for op in bld.ops[ename]:
                    if op.same_dep and not op.dma and ename != "pe":
                        engobj.drain()
                    if op.dma and op.dmapos > 1:
                        wait(f"q{ename}{op.dmaclass}", 16 * (op.dmapos - 1))
                    for (se, si, sdma) in op.deps:
                        wait(bld.wait_sem(se, si, sdma), bld.wait_value(se, si, sdma))
                    inst = op.fn(engobj)
                    if op.need_inc:
                        assert inst is not None, f"op {op.tag} on {ename} needs inc but fn returned None"
                        if op.dma:
                            inst.then_inc(sems[f"q{ename}{op.dmaclass}"], 16)
                        else:
                            inst.then_inc(sems[ename], 1)

            if bld.ops["sp"]:
                block.sync(lambda e: run_engine(e, "sp"))
            if bld.ops["pool"]:
                block.gpsimd(lambda e: run_engine(e, "pool"))
            if bld.ops["pe"]:
                block.tensor(lambda e: run_engine(e, "pe"))
            if bld.ops["act"]:
                block.scalar(lambda e: run_engine(e, "act"))
            if bld.ops["dve"]:
                block.vector(lambda e: run_engine(e, "dve"))


SBUF_ARENA_BASE = 16512
SBUF_ARENA_END = 229300


class Arena:

    def __init__(self, nc, builder, base=SBUF_ARENA_BASE, end=SBUF_ARENA_END):
        self.nc = nc
        self.b = builder
        self.live = {}
        self.free_l = [(base, end - base)]
        self.ghosts = []
        self.hwm = 0

    def alloc(self, name, shape, dtype, align=64, high=False):
        per_part = int(np.prod(shape[1:])) * mybir.dt.size(dtype)
        per_part = (per_part + align - 1) // align * align
        ranges = enumerate(self.free_l)
        if high:
            ranges = reversed(list(ranges))
        for i, (off, sz) in ranges:
            hoff = (off + sz - per_part) // align * align
            if sz >= per_part and (not high or hoff >= off):
                if high:
                    tail = off + sz - (hoff + per_part)
                    self.free_l[i] = (off, hoff - off)
                    if tail:
                        self.free_l.append((hoff + per_part, tail))
                        self.free_l.sort()
                    off = hoff
                else:
                    self.free_l[i] = (off + per_part, sz - per_part)
                break
        else:
            raise MemoryError(f"arena OOM for {name}: {per_part}B; free={self.free_l}")
        t = self.nc.alloc_sbuf_tensor_at(name, list(shape), dtype, offset=off)
        self.live[name] = (off, per_part, t)
        self.hwm = max(self.hwm, sum(v[1] for v in self.live.values()))
        olds = [g[2] for g in self.ghosts if g[0] < off + per_part and off < g[0] + g[1]]
        if olds:
            self.b.haunt(t.name, olds)
        return t

    def free(self, *tensors):
        for t in tensors:
            name = None
            for k, (off, sz, h) in self.live.items():
                if h is t or k == t:
                    name = k
                    break
            assert name is not None, f"free of unknown tensor {t}"
            off, sz, h = self.live.pop(name)
            self.free_l.append((off, sz))
            self.ghosts.append((off, sz, h.name))
        self.free_l.sort()
        out = []
        for off, sz in self.free_l:
            if out and out[-1][0] + out[-1][1] == off:
                out[-1] = (out[-1][0], out[-1][1] + sz)
            else:
                out.append((off, sz))
        self.free_l = [tuple(x) for x in out]




def rope_tables_l0():
    d = D
    i = np.arange(d // 2, dtype=np.float64)
    theta = np.power(10000.0, -2.0 * (i - 1.0) / d)
    t = np.arange(T, dtype=np.float64)
    ang = t[None, :] * theta[:, None]
    cos = np.cos(ang)
    sin = np.sin(ang)
    C = np.empty((d, T), np.float32)
    S = np.empty((d, T), np.float32)
    C[0::2] = cos
    C[1::2] = cos
    S[0::2] = sin
    S[1::2] = -sin
    return C, S


def rope_tables_l1():
    dqk = 64
    inv = 1.0 / (10000.0 ** (np.arange(0, dqk, 2, dtype=np.float64) / dqk))
    t = np.arange(T, dtype=np.float64)
    ang = t[None, :] * inv[:, None]
    cos = np.cos(ang).astype(np.float32)
    sin = np.sin(ang).astype(np.float32)
    C2 = np.empty((128, T), np.float32)
    S2 = np.empty((128, T), np.float32)
    for h in range(2):
        b = h * 64
        C2[b:b + 32] = cos
        C2[b + 32:b + 64] = cos
        S2[b:b + 32] = -sin
        S2[b + 32:b + 64] = sin
    return C2, S2


def pair_swap_matrix():
    P = np.zeros((128, 128), np.float32)
    for i in range(0, 128, 2):
        P[i, i + 1] = 1.0
        P[i + 1, i] = 1.0
    return P


def swap32_matrix():
    P = np.zeros((128, 128), np.float32)
    for h in range(2):
        b = h * 64
        for i in range(32):
            P[b + i, b + 32 + i] = 1.0
            P[b + 32 + i, b + i] = 1.0
    return P


def host_constants(core_id):
    c = {}
    c["c_ident_bf"] = np.eye(128, dtype=np.float32).astype(BF)
    c["c_ident_f32"] = np.eye(128, dtype=np.float32)
    c["c_ones_km"] = np.ones((128, 128), np.float32).astype(BF)
    c["c_ones_row"] = np.ones((1, 128), np.float32).astype(BF)
    c["c_ones_row_f32"] = np.ones((1, 128), np.float32)
    c["c_utri"] = np.triu(np.ones((128, 128), np.float32)).astype(BF)
    c["c_causal"] = np.tril(np.ones((128, 128), np.float32)).astype(BF)
    c["c_eidx"] = np.tile(np.arange(8, dtype=np.float32), (128, 1))
    sel = np.zeros((8, 8 * 128), np.float32)
    for e in range(8):
        sel[e, e * 128:(e + 1) * 128] = 1.0
    c["c_sel8"] = sel.astype(BF)
    pref = np.zeros((8, 128), np.float32)
    pref[:core_id, :] = 1.0
    c["c_prefmask"] = pref
    C0, S0 = rope_tables_l0()
    c["c_rope0_c"] = C0
    c["c_rope0_s"] = S0
    C2, S2 = rope_tables_l1()
    c["c_rope1_c"] = C2
    c["c_rope1_s"] = S2
    c["c_pswap"] = pair_swap_matrix().astype(BF)
    c["c_swap32"] = swap32_matrix().astype(BF)
    m = np.zeros((128, 128), np.float32)
    mi = np.zeros((128, 128), np.float32)
    for hb in range(2):
        b = hb * 64
        for i in range(64):
            for j in range(64):
                if j < i:
                    m[b + i, b + j] = 1.0
                if j <= i:
                    mi[b + i, b + j] = 1.0
    c["c_dmask_strict"] = m
    c["c_dmask_incl"] = mi
    cz = np.zeros((128, 4, 512), np.float32)
    for j in range(4):
        for kk in range(128):
            cz[kk, j, :] = (128 * j + kk <= np.arange(512)).astype(np.float32)
    c["c_caus512"] = cz.astype(BF)
    s4 = np.zeros((4, 512), np.float32)
    for h in range(4):
        s4[h, 128 * h:128 * (h + 1)] = 1.0
    c["c_sel4"] = s4
    jj = np.arange(128)
    c["c_umask_s"] = (jj[:, None] < jj[None, :]).astype(np.float32)
    c["c_umask_i"] = (jj[:, None] <= jj[None, :]).astype(np.float32)
    c["c_lmask_s"] = (jj[:, None] > jj[None, :]).astype(np.float32)
    return c



class KB:

    def __init__(self, nc):
        self.nc = nc
        self.b = Builder()
        self.ar = Arena(nc, self.b)
        self.uid = 0
        self.inputs = {}
        self.psum = []
        for i in range(8):
            t = nc.alloc_psum_tensor(f"psb{i}", [128, 512], F32)
            self.psum.append(t)
        self._pb = 0

    def nid(self, prefix):
        self.uid += 1
        return f"{prefix}_{self.uid}"

    def din(self, name, shape, dtype=F32):
        t = self.nc.dram_tensor(name, list(shape), dtype, kind="ExternalInput")
        ap = t.ap()
        self.inputs[name] = ap
        return ap

    def dout(self, name, shape, dtype=F32):
        t = self.nc.dram_tensor(name, list(shape), dtype, kind="ExternalOutput")
        return t.ap()

    def bank(self):
        t = self.psum[self._pb]
        self._pb = (self._pb + 1) % 8
        return t

    def banks(self, n):
        assert n <= 8
        out = [self.psum[(self._pb + i) % 8] for i in range(n)]
        self._pb = (self._pb + n) % 8
        return out

    DEBUG = False

    def dbg(self, name, ap_or_handle, reads):
        if not KB.DEBUG:
            return
        ap = ap_or_handle if hasattr(ap_or_handle, "tensor") else ap_or_handle.ap()
        shape = list(ap.shape)
        t = self.nc.dram_tensor("dbg_" + name, shape, ap.tensor.dtype, kind="ExternalOutput")
        self.dma("sp", t.ap(), ap, tag="dbg_" + name, extra_reads=reads)

    def dma(self, engine, out_ap, in_ap, tag="dma", extra_reads=(), slow_ok=False, dmaclass=None):
        self.b.emit(engine,
                    lambda e, o=out_ap, i=in_ap, s=slow_ok:
                    e.dma_start(out=o, in_=i, allow_slow_non_contiguous=s),
                    reads=[in_ap] + list(extra_reads), writes=[out_ap], dma=True, tag=tag,
                    dmaclass=dmaclass)

    def mm_group(self, bankT, terms, out_slices=None, tag="mm"):
        n = len(terms)
        for j, (o, l, r) in enumerate(terms):
            st = (j == 0)
            sp = (j == n - 1)
            self.b.emit("pe",
                        lambda e, o=o, l=l, r=r, st=st, sp=sp: e.matmul(o, l, r, start=st, stop=sp),
                        reads=[l, r], writes=[bankT], tag=tag)

    def ev(self, engine, fn, reads, writes, tag="ev"):
        self.b.emit(engine, fn, reads=reads, writes=writes, tag=tag)


def ceil_div(a, b):
    return (a + b - 1) // b


def tile_fi(w):
    r, n = w.shape
    return np.ascontiguousarray(w.reshape(r // 128, 128, n).transpose(1, 0, 2))


def tile_fo(w, ncols=128):
    r, n = w.shape
    nch = r // 128
    ns = n // ncols
    a = w.reshape(nch, 128, ns, ncols).transpose(2, 1, 0, 3)
    return np.ascontiguousarray(a.reshape(ns, 128, nch * ncols))



class Ctx:
    pass


def setup_core(kb):
    nc = kb.nc
    c = Ctx()
    c.kb = kb
    ar = kb.ar

    c.X = [ar.alloc(f"Xc{i}", [128, T], F32) for i in range(NCHUNK)]
    c.NB = None
    c.xspill = kb.nc.dram_tensor("xspill", [D, T], F32, kind="Internal").ap()

    def const_in(name, shape, dtype=F32):
        ap = kb.din(name, shape, dtype)
        t = ar.alloc("s" + name, shape, dtype)
        kb.dma("sp", t.ap(), ap, tag=name)
        return t

    c.ident_bf = const_in("c_ident_bf", [128, 128], BF16)
    c.ident_f32 = const_in("c_ident_f32", [128, 128], F32)
    c.ones_km = const_in("c_ones_km", [128, 128], BF16)
    c.ones_row = const_in("c_ones_row", [1, 128], BF16)
    c.ones_row_f32 = const_in("c_ones_row_f32", [1, 128], F32)
    c.utri = const_in("c_utri", [128, 128], BF16)
    c.causal = const_in("c_causal", [128, 128], BF16)
    c.eidx = const_in("c_eidx", [128, 8], F32)
    c.sel8 = const_in("c_sel8", [8, 8 * 128], BF16)
    c.prefmask = const_in("c_prefmask", [8, 128], F32)
    c.pswap = const_in("c_pswap", [128, 128], BF16)
    c.swap32 = const_in("c_swap32", [128, 128], BF16)
    c.dmask_strict = const_in("c_dmask_strict", [128, 128], F32)
    c.dmask_incl = const_in("c_dmask_incl", [128, 128], F32)
    c.caus512_in = kb.din("c_caus512", [128, 4, 512], BF16)

    c.x_in = kb.din("x_fm", [D, T], F32)
    for i in range(NCHUNK):
        kb.dma("sp", c.X[i].ap()[:, :], c.x_in[128 * i:128 * (i + 1), :], tag=f"ldX{i}")

    c.out = kb.dout("y_fm", [D, T], F32)
    return c


def store_out(c):
    kb = c.kb
    for i in range(NCHUNK):
        kb.dma("sp", c.out[128 * i:128 * (i + 1), :], c.X[i].ap()[:, :], tag=f"stX{i}")


def emit_ln(c, tag=""):
    kb = c.kb
    ar = kb.ar
    eps = 1e-5
    c.NB = [ar.alloc(kb.nid("NB"), [128, T], BF16, high=True) for _ in range(NCHUNK)]
    xb = [ar.alloc(kb.nid("ln_xb"), [128, 512], BF16) for _ in range(2)]
    xq = [ar.alloc(kb.nid("ln_xq"), [128, 512], BF16) for _ in range(2)]
    mu = ar.alloc(kb.nid("ln_mu"), [128, 512], F32)
    sv = ar.alloc(kb.nid("ln_sv"), [128, 512], F32)
    tf = ar.alloc(kb.nid("ln_tf"), [128, 512], F32)
    for tc in range(4):
        ts = slice(512 * tc, 512 * (tc + 1))
        bs = kb.bank()
        bq = kb.bank()
        for ci in range(NCHUNK):
            xbt, xqt = xb[ci % 2], xq[ci % 2]
            kb.ev("dve", lambda e, o=xbt.ap()[:, :], i=c.X[ci].ap()[:, ts]: e.tensor_copy(o, i),
                  reads=[c.X[ci]], writes=[xbt], tag="lncast")
            kb.ev("act", lambda e, o=xqt.ap()[:, :], i=c.X[ci].ap()[:, ts]: e.activation(o, i, AF.Square),
                  reads=[c.X[ci]], writes=[xqt], tag="lnsq")
            st, sp = (ci == 0), (ci == NCHUNK - 1)
            kb.b.emit("pe", lambda e, o=bs.ap()[:, :], l=c.ones_km.ap()[:, :], r=xbt.ap()[:, :], st=st, sp=sp:
                      e.matmul(o, l, r, start=st, stop=sp),
                      reads=[c.ones_km, xbt], writes=[bs], tag="lnmmS")
            kb.b.emit("pe", lambda e, o=bq.ap()[:, :], l=c.ones_km.ap()[:, :], r=xqt.ap()[:, :], st=st, sp=sp:
                      e.matmul(o, l, r, start=st, stop=sp),
                      reads=[c.ones_km, xqt], writes=[bq], tag="lnmmQ")
        kb.ev("dve", lambda e, o=mu.ap()[:, :], i=bs.ap()[:, :]: e.tensor_scalar_mul(o, i, 1.0 / D),
              reads=[bs], writes=[mu], tag="lnmu")
        kb.ev("act", lambda e, o=sv.ap()[:, :], i=mu.ap()[:, :]: e.activation(o, i, AF.Square),
              reads=[mu], writes=[sv], tag="lnmsq")
        kb.ev("dve", lambda e, o=sv.ap()[:, :], i=bq.ap()[:, :]:
              e.scalar_tensor_tensor(o, i, 1.0 / D, sv.ap()[:, :], ALU.mult, ALU.subtract),
              reads=[bq, sv], writes=[sv], tag="lnvar")
        kb.ev("dve", lambda e, o=sv.ap()[:, :]: e.tensor_scalar_add(o, sv.ap()[:, :], eps),
              reads=[sv], writes=[sv], tag="lneps")
        kb.ev("act", lambda e, o=sv.ap()[:, :]: e.activation(o, sv.ap()[:, :], AF.Sqrt),
              reads=[sv], writes=[sv], tag="lnsd")
        kb.ev("dve", lambda e, o=sv.ap()[:, :]: e.reciprocal(o, sv.ap()[:, :]),
              reads=[sv], writes=[sv], tag="lnrec")
        for ci in range(NCHUNK):
            kb.ev("dve", lambda e, o=tf.ap()[:, :], x=c.X[ci].ap()[:, ts]: e.tensor_sub(o, x, mu.ap()[:, :]),
                  reads=[c.X[ci], mu], writes=[tf], tag="lnsub")
            kb.ev("dve", lambda e, o=c.NB[ci].ap()[:, ts]: e.tensor_mul(o, tf.ap()[:, :], sv.ap()[:, :]),
                  reads=[tf, sv], writes=[c.NB[ci]], tag="lnmul")
    ar.free(xb[0], xb[1], xq[0], xq[1], mu, sv, tf)


def setup_l1ffn(c):
    kb = c.kb
    c.l1f_w1t_t = kb.din("l1f_w1t_t", [32, 128, 1024], F32)
    c.l1f_w1n = kb.din("l1f_w1n", [4096, 1024], F32)
    c.l1f_w2t = kb.din("l1f_w2t", [4096, 1024], F32)
    c.l1f_w2n = kb.din("l1f_w2n", [1024, 4096], F32)
    c.l1f_b1c = kb.din("l1f_b1c", [128, 32], F32)
    c.l1f_b2c = kb.din("l1f_b2c", [128, 8], F32)
    ar = kb.ar
    c.l1f_a1 = ar.alloc("l1f_a1", [128, 32], F32)
    c.l1f_a2 = ar.alloc("l1f_a2", [128, 8], F32)
    c.l1f_b1s = ar.alloc("l1f_b1s", [128, 32], F32)
    c.l1f_b2s = ar.alloc("l1f_b2s", [128, 8], F32)
    kb.dma("sp", c.l1f_b1s.ap()[:, :], c.l1f_b1c[:, :], tag="l1fb1")
    kb.dma("sp", c.l1f_b2s.ap()[:, :], c.l1f_b2c[:, :], tag="l1fb2")
    c.l1f_alphas_done = False


def emit_l1ffn_alphas(c):
    kb = c.kb
    ar = kb.ar
    wb = [ar.alloc(kb.nid("l1f_wn"), [128, 4096], F32) for _ in range(2)]
    for k in range(32):
        t = wb[k % 2]
        kb.dma("sp", t.ap()[:, 0:1024], c.l1f_w1n[128 * k:128 * (k + 1), :], tag="l1f_an1")
        kb.ev("dve", lambda e, o=c.l1f_a1.ap()[:, k:k + 1], i=t.ap()[:, 0:1024]:
              e.tensor_reduce(o, i, AX.X, ALU.add, apply_absolute_value=True),
              reads=[t], writes=[c.l1f_a1], tag="l1f_red1")
    for k in range(8):
        t = wb[k % 2]
        kb.dma("sp", t.ap()[:, :], c.l1f_w2n[128 * k:128 * (k + 1), :], tag="l1f_an2")
        kb.ev("dve", lambda e, o=c.l1f_a2.ap()[:, k:k + 1], i=t.ap()[:, :]:
              e.tensor_reduce(o, i, AX.X, ALU.add, apply_absolute_value=True),
              reads=[t], writes=[c.l1f_a2], tag="l1f_red2")
    kb.ev("dve", lambda e: e.tensor_scalar_mul(c.l1f_a1.ap()[:, :], c.l1f_a1.ap()[:, :], 1.0 / 1024.0),
          reads=[c.l1f_a1], writes=[c.l1f_a1], tag="l1f_as1")
    kb.ev("dve", lambda e: e.tensor_scalar_mul(c.l1f_a2.ap()[:, :], c.l1f_a2.ap()[:, :], 1.0 / 4096.0),
          reads=[c.l1f_a2], writes=[c.l1f_a2], tag="l1f_as2")
    ar.free(wb[0], wb[1])


def emit_l1ffn(c):
    kb = c.kb
    ar = kb.ar
    if not c.l1f_alphas_done:
        emit_l1ffn_alphas(c)
        c.l1f_alphas_done = True
    sg1 = [ar.alloc(kb.nid("l1f_sg1"), [128, 1024], BF16) for _ in range(2)]
    sg2 = [ar.alloc(kb.nid("l1f_sg2"), [128, 1024], BF16) for _ in range(2)]
    hbuf = ar.alloc(kb.nid("l1f_h"), [128, 32, 1024], BF16)
    yt = ar.alloc(kb.nid("l1f_yt"), [128, 512], F32)
    for tcg in range(2):
        for j in range(32):
            t = sg1[j % 2]
            kb.dma("pool", t.ap()[:, :], c.l1f_w1t_t[j, :, :], tag="l1f_w1l")
            kb.ev("act", lambda e, a=t.ap()[:, :]: e.activation(a, a, AF.Sign),
                  reads=[t], writes=[t], tag="l1f_sgn1")
            for tc2 in range(2):
                bk = kb.bank()
                ts = slice(1024 * tcg + 512 * tc2, 1024 * tcg + 512 * (tc2 + 1))
                for ci in range(8):
                    kb.b.emit("pe", lambda e, o=bk.ap()[:, :], l=t.ap()[:, 128 * ci:128 * (ci + 1)],
                              r=c.NB[ci].ap()[:, ts], st=(ci == 0), sp=(ci == 7):
                              e.matmul(o, l, r, start=st, stop=sp),
                              reads=[t, c.NB[ci]], writes=[bk], tag="l1f_hmm")
                kb.ev("act", lambda e, o=hbuf.ap()[:, j, 512 * tc2:512 * (tc2 + 1)], i=bk.ap()[:, :],
                      s=c.l1f_a1.ap()[:, j:j + 1], b=c.l1f_b1s.ap()[:, j:j + 1]:
                      e.activation(o, i, AF.Gelu, bias=b, scale=s),
                      reads=[bk, c.l1f_a1, c.l1f_b1s], writes=[hbuf], tag="l1f_hev")
        if tcg == 1:
            for t_ in c.NB:
                ar.free(t_)
            c.NB = None
        for fo2h in range(2):
            bks = kb.banks(8)
            for k2 in range(32):
                t = sg2[k2 % 2]
                kb.dma("pool", t.ap()[:, :], c.l1f_w2t[128 * k2:128 * (k2 + 1), :], tag="l1f_w2l")
                kb.ev("act", lambda e, a=t.ap()[:, :]: e.activation(a, a, AF.Sign),
                      reads=[t], writes=[t], tag="l1f_sgn2")
                for tc2 in range(2):
                    ts = slice(1024 * tcg + 512 * tc2, 1024 * tcg + 512 * (tc2 + 1))
                    for f2 in range(4):
                        fo2 = fo2h * 4 + f2
                        bk = bks[tc2 * 4 + f2]
                        kb.b.emit("pe", lambda e, o=bk.ap()[:, :], l=t.ap()[:, 128 * fo2:128 * (fo2 + 1)],
                                  r=hbuf.ap()[:, k2, 512 * tc2:512 * (tc2 + 1)],
                                  st=(k2 == 0), sp=(k2 == 31):
                                  e.matmul(o, l, r, start=st, stop=sp),
                                  reads=[t, hbuf], writes=[bk], tag="l1f_ymm")
            for tc2 in range(2):
                ts = slice(1024 * tcg + 512 * tc2, 1024 * tcg + 512 * (tc2 + 1))
                for f2 in range(4):
                    fo2 = fo2h * 4 + f2
                    bk = bks[tc2 * 4 + f2]
                    kb.ev("act", lambda e, o=yt.ap()[:, :], i=bk.ap()[:, :],
                          s=c.l1f_a2.ap()[:, fo2:fo2 + 1], b=c.l1f_b2s.ap()[:, fo2:fo2 + 1]:
                          e.activation(o, i, AF.Identity, bias=b, scale=s),
                          reads=[bk, c.l1f_a2, c.l1f_b2s], writes=[yt], tag="l1f_yev")
                    kb.ev("dve", lambda e, o=c.X[fo2].ap()[:, ts], i=yt.ap()[:, :]:
                          e.tensor_add(o, o, i),
                          reads=[yt, c.X[fo2]], writes=[c.X[fo2]], tag="l1f_xadd")
    ar.free(sg1[0], sg1[1], sg2[0], sg2[1], hbuf, yt)


def setup_attn_a0(c):
    kb = c.kb
    c.a0_wqT_t = kb.din("a0_wqT_t", [2, 128, 4096], F32)
    c.a0_wkT_t = kb.din("a0_wkT_t", [2, 128, 4096], F32)
    c.a0_wvT_t = kb.din("a0_wvT_t", [2, 128, 4096], F32)
    c.a0_ropec = kb.din("c_rope0_c", [D, T], F32)
    c.a0_ropes = kb.din("c_rope0_s", [D, T], F32)


A0_STOP = 9
A0_ROPE = True


def emit_attn_a0(c):
    kb = c.kb
    ar = kb.ar
    scale = 1.0 / math.sqrt(D)
    for ci in range(NCHUNK):
        kb.dma("sp", c.xspill[128 * ci:128 * (ci + 1), :], c.X[ci].ap()[:, :], tag="a0_spill")
    for ci in range(NCHUNK):
        ar.free(c.X[ci])
    c.X = None

    if A0_STOP <= 0:
        c.X = [ar.alloc(f"Xc{i}_r", [128, T], F32, high=True) for i in range(NCHUNK)]
        for ci in range(NCHUNK):
            kb.dma("sp", c.X[ci].ap()[:, :], c.xspill[128 * ci:128 * (ci + 1), :], tag="a0_xld")
        for t_ in c.NB:
            ar.free(t_)
        c.NB = None
        return

    qr = [ar.alloc(kb.nid("a0_q"), [128, T], BF16) for _ in range(NCHUNK)]
    kr = [ar.alloc(kb.nid("a0_k"), [128, T], BF16) for _ in range(NCHUNK)]
    vt = ar.alloc(kb.nid("a0_v"), [128, 16, 1024], BF16)
    wbuf = [ar.alloc(kb.nid("a0_w"), [128, 8, 512], BF16) for _ in range(2)]
    ctab = [ar.alloc(kb.nid("a0_ct"), [128, 512], F32) for _ in range(2)]
    stab = [ar.alloc(kb.nid("a0_st"), [128, 512], F32) for _ in range(2)]
    t1 = ar.alloc(kb.nid("a0_t1"), [128, 512], F32)
    t2 = ar.alloc(kb.nid("a0_t2"), [128, 512], F32)
    sqb = [ar.alloc(kb.nid("a0_sq"), [128, 512], BF16) for _ in range(2)]

    for (wt, dst) in ((c.a0_wqT_t, qr), (c.a0_wkT_t, kr), (c.a0_wvT_t, None)):
        for jh in range(2):
            wb = wbuf[jh % 2]
            kb.dma("pool", wb.ap()[:, :, :], wt[jh, :, :], tag="a0_wl")
            if dst is not None:
                for f in range(4):
                    fo = jh * 4 + f
                    for tc in range(4):
                        ts = slice(512 * tc, 512 * (tc + 1))
                        bk = kb.bank()
                        for ci in range(NCHUNK):
                            kb.b.emit("pe", lambda e, o=bk.ap()[:, :], l=wb.ap()[:, ci, 128 * f:128 * (f + 1)],
                                      r=c.NB[ci].ap()[:, ts], st=(ci == 0), sp=(ci == 7):
                                      e.matmul(o, l, r, start=st, stop=sp),
                                      reads=[wb, c.NB[ci]], writes=[bk], tag="a0_qkmm")
                        if not A0_ROPE:
                            kb.ev("act", lambda e, o=dst[fo].ap()[:, ts], i=bk.ap()[:, :]:
                                  e.activation(o, i, AF.Identity),
                                  reads=[bk], writes=[dst[fo]], tag="a0_qc0")
                            continue
                        qtile = sqb[(f * 4 + tc) % 2]
                        kb.ev("act", lambda e, o=qtile.ap()[:, :], i=bk.ap()[:, :]: e.activation(o, i, AF.Identity),
                              reads=[bk], writes=[qtile], tag="a0_qc")
                        bk2 = kb.bank()
                        kb.b.emit("pe", lambda e, o=bk2.ap()[:, :], l=c.pswap.ap()[:, :], r=qtile.ap()[:, :]:
                                  e.matmul(o, l, r, start=True, stop=True),
                                  reads=[c.pswap, qtile], writes=[bk2], tag="a0_swap")
                        ct, st_ = ctab[tc % 2], stab[tc % 2]
                        kb.dma("sp", ct.ap()[:, :], c.a0_ropec[128 * fo:128 * (fo + 1), ts], tag="a0_ctl")
                        kb.dma("sp", st_.ap()[:, :], c.a0_ropes[128 * fo:128 * (fo + 1), ts], tag="a0_stl")
                        kb.ev("dve", lambda e, o=t1.ap()[:, :], i=bk.ap()[:, :], m=ct.ap()[:, :]:
                              e.tensor_mul(o, i, m), reads=[bk, ct], writes=[t1], tag="a0_rc")
                        kb.ev("dve", lambda e, o=t2.ap()[:, :], i=bk2.ap()[:, :], m=st_.ap()[:, :]:
                              e.tensor_mul(o, i, m), reads=[bk2, st_], writes=[t2], tag="a0_rs")
                        kb.ev("dve", lambda e, o=dst[fo].ap()[:, ts]: e.tensor_add(o, t1.ap()[:, :], t2.ap()[:, :]),
                              reads=[t1, t2], writes=[dst[fo]], tag="a0_radd")
            else:
                for tt in range(16):
                    bk = kb.bank()
                    for ci in range(NCHUNK):
                        kb.b.emit("pe", lambda e, o=bk.ap()[:, :], l=c.NB[ci].ap()[:, 128 * tt:128 * (tt + 1)],
                                  r=wb.ap()[:, ci, :], st=(ci == 0), sp=(ci == 7):
                                  e.matmul(o, l, r, start=st, stop=sp),
                                  reads=[wb, c.NB[ci]], writes=[bk], tag="a0_vmm")
                    kb.ev("act", lambda e, o=vt.ap()[:, tt, 512 * jh:512 * (jh + 1)], i=bk.ap()[:, :]:
                          e.activation(o, i, AF.Identity),
                          reads=[bk], writes=[vt], tag="a0_vev")
    for t_ in c.NB:
        ar.free(t_)
    c.NB = None
    ar.free(wbuf[0], wbuf[1], ctab[0], ctab[1], stab[0], stab[1], t1, t2)
    if A0_STOP <= 1:
        c.X = [ar.alloc(f"Xc{i}_r", [128, T], F32, high=True) for i in range(NCHUNK)]
        for ci in range(NCHUNK):
            kb.dma("sp", c.X[ci].ap()[:, :], c.xspill[128 * ci:128 * (ci + 1), :], tag="a0_xld")
        return

    qn = ar.alloc(kb.nid("a0_qn"), [128, T], F32)
    kn = ar.alloc(kb.nid("a0_kn"), [128, T], F32)
    mneg = ar.alloc(kb.nid("a0_mneg"), [128, T], BF16)
    kmax = ar.alloc(kb.nid("a0_kmax"), [128, 5], F32)
    for (src_, dst) in ((qr, qn), (kr, kn)):
        for tc in range(4):
            ts = slice(512 * tc, 512 * (tc + 1))
            bk = kb.bank()
            for ci in range(NCHUNK):
                sq = sqb[ci % 2]
                kb.ev("act", lambda e, o=sq.ap()[:, :], i=src_[ci].ap()[:, ts]: e.activation(o, i, AF.Square),
                      reads=[src_[ci]], writes=[sq], tag="a0_nsq")
                kb.b.emit("pe", lambda e, o=bk.ap()[:, :], l=c.ones_km.ap()[:, :], r=sq.ap()[:, :],
                          st=(ci == 0), sp=(ci == 7): e.matmul(o, l, r, start=st, stop=sp),
                          reads=[c.ones_km, sq], writes=[bk], tag="a0_nmm")
            kb.ev("act", lambda e, o=dst.ap()[:, ts], i=bk.ap()[:, :]: e.activation(o, i, AF.Sqrt),
                  reads=[bk], writes=[dst], tag="a0_nsqrt")
            if dst is kn:
                kb.ev("dve", lambda e, o=kmax.ap()[:, tc:tc + 1], i=dst.ap()[:, ts]:
                      e.tensor_reduce(o, i, AX.X, ALU.max), reads=[dst], writes=[kmax], tag="a0_kmax")
    kb.ev("dve", lambda e, o=kmax.ap()[:, 4:5], i=kmax.ap()[:, 0:4]:
          e.tensor_reduce(o, i, AX.X, ALU.max), reads=[kmax], writes=[kmax], tag="a0_kmax2")
    for tc in range(4):
        ts = slice(512 * tc, 512 * (tc + 1))
        kb.ev("dve", lambda e, o=mneg.ap()[:, ts], i=qn.ap()[:, ts]:
              e.tensor_scalar(o, i, kmax.ap()[:, 4:5], -1.0, ALU.mult, ALU.mult),
              reads=[qn, kmax], writes=[mneg], tag="a0_mneg")
    ar.free(qn, kn, sqb[0], sqb[1], kmax)
    if A0_STOP <= 2:
        c.X = [ar.alloc(f"Xc{i}_r", [128, T], F32, high=True) for i in range(NCHUNK)]
        for ci in range(NCHUNK):
            kb.dma("sp", c.X[ci].ap()[:, :], c.xspill[128 * ci:128 * (ci + 1), :], tag="a0_xld")
        return

    pts = [ar.alloc(kb.nid("a0_pt"), [128, 16, 512], BF16) for _ in range(4)]
    dinv = ar.alloc(kb.nid("a0_dinv"), [128, 4, 512], F32)
    for qc in range(4):
        qs = slice(512 * qc, 512 * (qc + 1))
        bdn = kb.bank()
        for kt in range(16):
            bst = kb.bank()
            if bst is bdn:
                bst = kb.bank()
            for ci in range(NCHUNK):
                kb.b.emit("pe", lambda e, o=bst.ap()[:, :], l=kr[ci].ap()[:, 128 * kt:128 * (kt + 1)],
                          r=qr[ci].ap()[:, qs], st=(ci == 0), sp=False:
                          e.matmul(o, l, r, start=st, stop=sp),
                          reads=[kr[ci], qr[ci]], writes=[bst], tag="a0_smm")
            kb.b.emit("pe", lambda e, o=bst.ap()[:, :], l=c.ones_row.ap()[:, :], r=mneg.ap()[0:1, qs]:
                      e.matmul(o, l, r, start=False, stop=True),
                      reads=[c.ones_row, mneg], writes=[bst], tag="a0_smneg")
            kb.ev("act", lambda e, o=pts[qc].ap()[:, kt, :], i=bst.ap()[:, :]:
                  e.activation(o, i, AF.Exp, scale=scale),
                  reads=[bst], writes=[pts[qc]], tag="a0_exp")
            kb.b.emit("pe", lambda e, o=bdn.ap()[:, :], l=c.ones_km.ap()[:, :], r=pts[qc].ap()[:, kt, :],
                      st=(kt == 0), sp=(kt == 15): e.matmul(o, l, r, start=st, stop=sp),
                      reads=[c.ones_km, pts[qc]], writes=[bdn], tag="a0_dmm")
        kb.ev("dve", lambda e, o=dinv.ap()[:, qc, :], i=bdn.ap()[:, :]: e.reciprocal(o, i),
              reads=[bdn], writes=[dinv], tag="a0_dinv")
    ar.free(mneg)
    for ci in range(NCHUNK):
        ar.free(qr[ci], kr[ci])
    if A0_STOP <= 3:
        c.X = [ar.alloc(f"Xc{i}_r", [128, T], F32, high=True) for i in range(NCHUNK)]
        for ci in range(NCHUNK):
            kb.dma("sp", c.X[ci].ap()[:, :], c.xspill[128 * ci:128 * (ci + 1), :], tag="a0_xld")
        return

    newX = []
    ot = ar.alloc(kb.nid("a0_ot"), [128, 512], F32)
    for dvc in range(NCHUNK):
        xt = ar.alloc(f"Xc{dvc}_r", [128, T], F32, high=True)
        newX.append(xt)
        kb.dma("sp", xt.ap()[:, :], c.xspill[128 * dvc:128 * (dvc + 1), :], tag="a0_xld")
        for qc in range(4):
            qs = slice(512 * qc, 512 * (qc + 1))
            bo = kb.bank()
            for kt in range(16):
                kb.b.emit("pe", lambda e, o=bo.ap()[:, :], l=vt.ap()[:, kt, 128 * dvc:128 * (dvc + 1)],
                          r=pts[qc].ap()[:, kt, :], st=(kt == 0), sp=(kt == 15):
                          e.matmul(o, l, r, start=st, stop=sp),
                          reads=[vt, pts[qc]], writes=[bo], tag="a0_omm")
            kb.ev("dve", lambda e, o=ot.ap()[:, :], i=bo.ap()[:, :], dv=dinv.ap()[:, qc, :]:
                  e.tensor_mul(o, i, dv),
                  reads=[bo, dinv], writes=[ot], tag="a0_onorm")
            kb.ev("dve", lambda e, o=xt.ap()[:, qs], i=ot.ap()[:, :]: e.tensor_add(o, o, i),
                  reads=[ot, xt], writes=[xt], tag="a0_xadd")
    c.X = newX
    for p in pts:
        ar.free(p)
    ar.free(dinv, ot, vt)


def setup_l1attn_a(c):
    kb = c.kb
    c.aa_wq_t = kb.din("aa_wq_t", [2, 128, 4096], F32)
    c.aa_wkv_t = kb.din("aa_wkv_t", [4, 128, 4096], F32)
    c.aa_wo_t = kb.din("aa_wo_t", [2, 128, 4096], F32)
    c.aa_bqc = kb.din("aa_bqc", [128, 8], F32)
    c.aa_bkc = kb.din("aa_bkc", [128, 8], F32)
    c.aa_bvr = kb.din("aa_bvr", [1, 1024], F32)
    c.aa_boc = kb.din("aa_boc", [128, 8], F32)
    c.aa_ropec = kb.din("c_rope1_c", [128, T], F32)
    c.aa_ropes = kb.din("c_rope1_s", [128, T], F32)


def setup_l1attn_b(c):
    kb = c.kb
    c.ab_wq_t = kb.din("ab_wq_t", [2, 128, 4096], F32)
    c.ab_wk_t = kb.din("ab_wk_t", [2, 128, 4096], F32)
    c.ab_wv_t = kb.din("ab_wv_t", [2, 128, 4096], F32)
    c.ab_wo_t = kb.din("ab_wo_t", [2, 128, 4096], F32)


def emit_l1attn(c, variant):
    kb = c.kb
    ar = kb.ar
    scale = 0.125
    isa = variant == "a"
    for ci in range(NCHUNK):
        kb.dma("sp", c.xspill[128 * ci:128 * (ci + 1), :], c.X[ci].ap()[:, :], tag="l1_spill")
    for ci in range(NCHUNK):
        ar.free(c.X[ci])
    c.X = None
    qfm = [ar.alloc(kb.nid("l1q"), [128, T], BF16) for _ in range(NCHUNK)]
    kfm = [ar.alloc(kb.nid("l1k"), [128, T], BF16) for _ in range(NCHUNK)]
    vtm = ar.alloc(kb.nid("l1v"), [128, 16, 1024], BF16)
    wbuf = [ar.alloc(kb.nid("l1w"), [128, 4096], BF16) for _ in range(2)]
    bsb = None
    if isa:
        bsb = {}
        for nm, ap_ in (("bq", c.aa_bqc), ("bk", c.aa_bkc), ("bo", c.aa_boc)):
            t_ = ar.alloc(kb.nid("l1b" + nm), [128, 8], F32)
            kb.dma("sp", t_.ap(), ap_, tag="l1bias")
            bsb[nm] = t_
        bvr = ar.alloc(kb.nid("l1bvr"), [1, 1024], BF16)
        kb.dma("pool", bvr.ap(), c.aa_bvr, tag="l1bvr")
        ctab = ar.alloc(kb.nid("l1ct"), [128, T], F32)
        stab = ar.alloc(kb.nid("l1st"), [128, T], F32)
        kb.dma("sp", ctab.ap(), c.aa_ropec, tag="l1ctl")
        kb.dma("sp", stab.ap(), c.aa_ropes, tag="l1stl")
        t1 = ar.alloc(kb.nid("l1t1"), [128, 512], F32)
        t2 = ar.alloc(kb.nid("l1t2"), [128, 512], F32)
        qtl = ar.alloc(kb.nid("l1qt"), [128, 512], BF16)

    def proj_qk(wt_ap, jslices, dst, biascols):
        for jj, jsl in enumerate(jslices):
            wb = wbuf[jj % 2]
            kb.dma("pool", wb.ap(), wt_ap[jsl, :, :], tag="l1wl")
            for f in range(4):
                fo = jj * 4 + f
                for tc in range(4):
                    ts = slice(512 * tc, 512 * (tc + 1))
                    bk = kb.bank()
                    for ci in range(NCHUNK):
                        kb.b.emit("pe", lambda e, o=bk.ap()[:, :], l=wb.ap()[:, 512 * ci + 128 * f:512 * ci + 128 * (f + 1)],
                                  r=c.NB[ci].ap()[:, ts], st=(ci == 0), sp=(ci == 7):
                                  e.matmul(o, l, r, start=st, stop=sp),
                                  reads=[wb, c.NB[ci]], writes=[bk], tag="l1qkmm")
                    if not isa:
                        kb.ev("act", lambda e, o=dst[fo].ap()[:, ts], i=bk.ap()[:, :]:
                              e.activation(o, i, AF.Identity),
                              reads=[bk], writes=[dst[fo]], tag="l1qkev")
                        continue
                    kb.ev("act", lambda e, o=qtl.ap()[:, :], i=bk.ap()[:, :], b=biascols.ap()[:, fo:fo + 1]:
                          e.activation(o, i, AF.Identity, bias=b),
                          reads=[bk, biascols], writes=[qtl], tag="l1qkbias")
                    bk2 = kb.bank()
                    kb.b.emit("pe", lambda e, o=bk2.ap()[:, :], l=c.swap32.ap()[:, :], r=qtl.ap()[:, :]:
                              e.matmul(o, l, r, start=True, stop=True),
                              reads=[c.swap32, qtl], writes=[bk2], tag="l1swap")
                    kb.ev("dve", lambda e, o=t1.ap()[:, :], i=qtl.ap()[:, :], m=ctab.ap()[:, ts]:
                          e.tensor_mul(o, i, m), reads=[qtl, ctab], writes=[t1], tag="l1rc")
                    kb.ev("dve", lambda e, o=t2.ap()[:, :], i=bk2.ap()[:, :], m=stab.ap()[:, ts]:
                          e.tensor_mul(o, i, m), reads=[bk2, stab], writes=[t2], tag="l1rs")
                    kb.ev("dve", lambda e, o=dst[fo].ap()[:, ts]: e.tensor_add(o, t1.ap()[:, :], t2.ap()[:, :]),
                          reads=[t1, t2], writes=[dst[fo]], tag="l1radd")

    if isa:
        proj_qk(c.aa_wq_t, (0, 1), qfm, bsb["bq"])
        proj_qk(c.aa_wkv_t, (0, 1), kfm, bsb["bk"])
        for jj, jsl in enumerate((2, 3)):
            wb = wbuf[jj % 2]
            kb.dma("pool", wb.ap(), c.aa_wkv_t[jsl, :, :], tag="l1wvl")
            for tt in range(16):
                bk = kb.bank()
                for ci in range(NCHUNK):
                    kb.b.emit("pe", lambda e, o=bk.ap()[:, :], l=c.NB[ci].ap()[:, 128 * tt:128 * (tt + 1)],
                              r=wb.ap()[:, 512 * ci:512 * (ci + 1)], st=(ci == 0), sp=False:
                              e.matmul(o, l, r, start=st, stop=sp),
                              reads=[wb, c.NB[ci]], writes=[bk], tag="l1vmm")
                kb.b.emit("pe", lambda e, o=bk.ap()[:, :], l=c.ones_row.ap()[:, :], r=bvr.ap()[0:1, 512 * jj:512 * (jj + 1)]:
                          e.matmul(o, l, r, start=False, stop=True),
                          reads=[c.ones_row, bvr], writes=[bk], tag="l1vbias")
                kb.ev("act", lambda e, o=vtm.ap()[:, tt, 512 * jj:512 * (jj + 1)], i=bk.ap()[:, :]:
                      e.activation(o, i, AF.Identity),
                      reads=[bk], writes=[vtm], tag="l1vev")
    else:
        proj_qk(c.ab_wq_t, (0, 1), qfm, None)
        proj_qk(c.ab_wk_t, (0, 1), kfm, None)
        for jj in range(2):
            wb = wbuf[jj % 2]
            kb.dma("pool", wb.ap(), c.ab_wv_t[jj, :, :], tag="l1wvl")
            for tt in range(16):
                bk = kb.bank()
                for ci in range(NCHUNK):
                    kb.b.emit("pe", lambda e, o=bk.ap()[:, :], l=c.NB[ci].ap()[:, 128 * tt:128 * (tt + 1)],
                              r=wb.ap()[:, 512 * ci:512 * (ci + 1)], st=(ci == 0), sp=(ci == 7):
                              e.matmul(o, l, r, start=st, stop=sp),
                              reads=[wb, c.NB[ci]], writes=[bk], tag="l1vmm")
                kb.ev("act", lambda e, o=vtm.ap()[:, tt, 512 * jj:512 * (jj + 1)], i=bk.ap()[:, :]:
                      e.activation(o, i, AF.Identity),
                      reads=[bk], writes=[vtm], tag="l1vev")

    for t_ in c.NB:
        ar.free(t_)
    c.NB = None
    if isa:
        ar.free(ctab, stab, t1, t2, qtl, bvr)

    sq = ar.alloc(kb.nid("l1sq"), [128, 512], BF16)
    qmax2 = ar.alloc(kb.nid("l1qm"), [128, 16], F32)
    kmax2 = ar.alloc(kb.nid("l1km"), [128, 16], F32)
    mh = ar.alloc(kb.nid("l1mh"), [128, 16], F32)
    for (src_, dst) in ((qfm, qmax2), (kfm, kmax2)):
        for ch in range(NCHUNK):
            for tc in range(4):
                ts = slice(512 * tc, 512 * (tc + 1))
                kb.ev("act", lambda e, o=sq.ap()[:, :], i=src_[ch].ap()[:, ts]: e.activation(o, i, AF.Square),
                      reads=[src_[ch]], writes=[sq], tag="l1nsq")
                for hb in range(2):
                    h = 2 * ch + hb
                    bk = kb.bank()
                    kb.b.emit("pe", lambda e, o=bk.ap()[:, :], l=c.ones_km.ap()[64 * hb:64 * (hb + 1), :],
                              r=sq.ap()[64 * hb:64 * (hb + 1), :]:
                              e.matmul(o, l, r, start=True, stop=True),
                              reads=[c.ones_km, sq], writes=[bk], tag="l1nmm")
                    if tc == 0:
                        kb.ev("dve", lambda e, o=dst.ap()[:, h:h + 1], i=bk.ap()[:, :]:
                              e.tensor_reduce(o, i, AX.X, ALU.max),
                              reads=[bk], writes=[dst], tag="l1nred")
                    else:
                        kb.ev("dve", lambda e, o=mh.ap()[:, h:h + 1], i=bk.ap()[:, :]:
                              e.tensor_reduce(o, i, AX.X, ALU.max),
                              reads=[bk], writes=[mh], tag="l1nred2")
                        kb.ev("dve", lambda e, o=dst.ap()[:, h:h + 1], a=dst.ap()[:, h:h + 1], b=mh.ap()[:, h:h + 1]:
                              e.tensor_max(o, a, b),
                              reads=[dst, mh], writes=[dst], tag="l1nmax")
    kb.ev("act", lambda e: e.activation(qmax2.ap()[:, :], qmax2.ap()[:, :], AF.Sqrt),
          reads=[qmax2], writes=[qmax2], tag="l1nsqrt")
    kb.ev("act", lambda e: e.activation(kmax2.ap()[:, :], kmax2.ap()[:, :], AF.Sqrt),
          reads=[kmax2], writes=[kmax2], tag="l1nsqrt2")
    kb.ev("dve", lambda e: e.tensor_mul(mh.ap()[:, :], qmax2.ap()[:, :], kmax2.ap()[:, :]),
          reads=[qmax2, kmax2], writes=[mh], tag="l1mh")
    kb.ev("dve", lambda e: e.tensor_scalar_mul(mh.ap()[:, :], mh.ap()[:, :], -scale),
          reads=[mh], writes=[mh], tag="l1mh2")
    ar.free(sq)

    ofm = [ar.alloc(kb.nid("l1o"), [128, T], BF16) for _ in range(NCHUNK)]
    ptsb = [ar.alloc(kb.nid("l1pt"), [128, 16, 512], BF16) for _ in range(2)]
    dinv = ar.alloc(kb.nid("l1dv"), [128, 512], F32)
    caus512 = ar.alloc(kb.nid("l1cz"), [128, 4, 512], BF16)
    kb.dma("sp", caus512.ap(), c.caus512_in, tag="l1czl")

    def spass(h, qc, pts):
        ch = h // 2
        hs = slice(64 * (h % 2), 64 * (h % 2 + 1))
        qs = slice(512 * qc, 512 * (qc + 1))
        for kt in range(4 * qc + 4):
            bst = kb.bank()
            kb.b.emit("pe", lambda e, o=bst.ap()[:, :], l=kfm[ch].ap()[hs, 128 * kt:128 * (kt + 1)],
                      r=qfm[ch].ap()[hs, qs]:
                      e.matmul(o, l, r, start=True, stop=True),
                      reads=[kfm[ch], qfm[ch]], writes=[bst], tag="l1smm")
            kb.ev("act", lambda e, o=pts.ap()[:, kt, :], i=bst.ap()[:, :], b=mh.ap()[:, h:h + 1]:
                  e.activation(o, i, AF.Exp, bias=b, scale=scale),
                  reads=[bst, mh], writes=[pts], tag="l1exp")
            if kt >= 4 * qc:
                j = kt - 4 * qc
                kb.ev("dve", lambda e, o=pts.ap()[:, kt, :], m=caus512.ap()[:, j, :]:
                      e.tensor_mul(o, o, m),
                      reads=[pts, caus512], writes=[pts], tag="l1mask")

    def opass(h, qc, pts):
        ch = h // 2
        hs = slice(64 * (h % 2), 64 * (h % 2 + 1))
        qs = slice(512 * qc, 512 * (qc + 1))
        nkt = 4 * qc + 4
        bdn = kb.bank()
        for kt in range(nkt):
            kb.b.emit("pe", lambda e, o=bdn.ap()[:, :], l=c.ones_km.ap()[:, :], r=pts.ap()[:, kt, :],
                      st=(kt == 0), sp=(kt == nkt - 1): e.matmul(o, l, r, start=st, stop=sp),
                      reads=[c.ones_km, pts], writes=[bdn], tag="l1dmm")
        kb.ev("dve", lambda e, o=dinv.ap()[:, :], i=bdn.ap()[:, :]: e.reciprocal(o, i),
              reads=[bdn], writes=[dinv], tag="l1dinv")
        bo = kb.bank()
        for kt in range(nkt):
            kb.b.emit("pe", lambda e, o=bo.ap()[hs, :], l=vtm.ap()[:, kt, 64 * h:64 * (h + 1)],
                      r=pts.ap()[:, kt, :], st=(kt == 0), sp=(kt == nkt - 1):
                      e.matmul(o, l, r, start=st, stop=sp),
                      reads=[vtm, pts], writes=[bo], tag="l1omm")
        kb.ev("dve", lambda e, o=ofm[ch].ap()[hs, qs], i=bo.ap()[hs, :], dv=dinv.ap()[hs, :]:
              e.tensor_mul(o, i, dv),
              reads=[bo, dinv], writes=[ofm[ch]], tag="l1oev")

    hqs = [(h, qc) for h in range(16) for qc in range(4)]
    for i, (h, qc) in enumerate(hqs):
        spass(h, qc, ptsb[i % 2])
        if i > 0:
            opass(*hqs[i - 1], ptsb[(i - 1) % 2])
    opass(*hqs[-1], ptsb[(len(hqs) - 1) % 2])
    ar.free(ptsb[0], ptsb[1], dinv, vtm, qmax2, kmax2, mh, caus512)
    for ci in range(NCHUNK):
        ar.free(qfm[ci], kfm[ci])

    newX = [None] * NCHUNK
    wo_t = c.aa_wo_t if isa else c.ab_wo_t
    for jj in range(2):
        wb = wbuf[jj % 2]
        kb.dma("pool", wb.ap(), wo_t[jj, :, :], tag="l1wol")
        for f in range(4):
            fo = jj * 4 + f
            xt = ar.alloc(kb.nid(f"Xl{fo}"), [128, T], F32)
            newX[fo] = xt
            kb.dma("sp", xt.ap()[:, :], c.xspill[128 * fo:128 * (fo + 1), :], tag="l1_xld")
            for tc in range(4):
                ts = slice(512 * tc, 512 * (tc + 1))
                bk = kb.bank()
                for ci in range(NCHUNK):
                    kb.b.emit("pe", lambda e, o=bk.ap()[:, :], l=wb.ap()[:, 512 * ci + 128 * f:512 * ci + 128 * (f + 1)],
                              r=ofm[ci].ap()[:, ts], st=(ci == 0), sp=(ci == 7):
                              e.matmul(o, l, r, start=st, stop=sp),
                              reads=[wb, ofm[ci]], writes=[bk], tag="l1pmm")
                if isa:
                    kb.ev("dve", lambda e, o=xt.ap()[:, ts], i=bk.ap()[:, :], b=bsb["bo"].ap()[:, fo:fo + 1]:
                          e.scalar_tensor_tensor(o, i, b, o, ALU.add, ALU.add),
                          reads=[bk, bsb["bo"], xt], writes=[xt], tag="l1xadd")
                else:
                    kb.ev("dve", lambda e, o=xt.ap()[:, ts], i=bk.ap()[:, :]:
                          e.tensor_add(o, o, i),
                          reads=[bk, xt], writes=[xt], tag="l1xadd")
    c.X = newX
    for ci in range(NCHUNK):
        ar.free(ofm[ci])
    ar.free(wbuf[0], wbuf[1])
    if isa:
        ar.free(bsb["bq"], bsb["bk"], bsb["bo"])


def setup_moe(c):
    kb = c.kb
    nc = kb.nc
    c.moe_gate_t = kb.din("moe_gate_t", [128, 8, 8], F32)
    c.moe_gateb = kb.din("moe_gateb", [1, 8], F32)
    c.moe_w1 = kb.din("moe_w1", [8, 16, 128, 2048], F32)
    c.moe_w2 = kb.din("moe_w2", [8, 16, 128, 2048], F32)
    c.moe_b1c = kb.din("moe_b1c", [8, 128, 32], F32)
    c.moe_b2c = kb.din("moe_b2c", [8, 128, 8], F32)
    c.moe_w1bf = nc.dram_tensor("moe_w1bf", [8, 16, 128, 2048], BF16, kind="Internal").ap()
    c.moe_w2bf = nc.dram_tensor("moe_w2bf", [8, 16, 128, 2048], BF16, kind="Internal").ap()
    c.coll_in = nc.dram_tensor("moe_cin", [1, 16], F32, kind="Internal").ap()
    c.coll_out = nc.dram_tensor("moe_cout", [8, 16], F32, kind="Internal").ap()
    c.moe_call = 0
    ar = kb.ar
    c.gate_sb = ar.alloc("moe_gate_sb", [128, 8, 8], BF16)
    c.gateb_sb = ar.alloc("moe_gateb_sb", [1, 8], BF16)
    kb.dma("pool", c.gate_sb.ap(), c.moe_gate_t, tag="moe_gl")
    kb.dma("pool", c.gateb_sb.ap(), c.moe_gateb, tag="moe_gbl")


def emit_moe_prepass(c):
    kb = c.kb
    for e in range(8):
        kb.dma("pool", c.moe_w1bf[e], c.moe_w1[e], tag="moe_pp1", dmaclass=6)
        kb.dma("pool", c.moe_w2bf[e], c.moe_w2[e], tag="moe_pp2", dmaclass=6)


def emit_moe(c):
    kb = c.kb
    ar = kb.ar
    E = 8
    logit = ar.alloc(kb.nid("moe_lg"), [128, 128], F32)
    ohbuf = ar.alloc(kb.nid("moe_oh"), [128, 256], BF16)
    cum = ar.alloc(kb.nid("moe_cum"), [128, 256], F32)
    pw = ar.alloc(kb.nid("moe_pw"), [128, 32], F32)
    sm = ar.alloc(kb.nid("moe_sm"), [128, 4], F32)
    top8 = ar.alloc(kb.nid("moe_t8"), [128, 8], F32)
    idx8 = ar.alloc(kb.nid("moe_i8"), [128, 8], mybir.dt.uint32)
    idx8f = ar.alloc(kb.nid("moe_i8f"), [128, 8], F32)
    rt = ar.alloc(kb.nid("moe_rt"), [128, 256], F32)
    csel = ar.alloc(kb.nid("moe_cs"), [128, 32], F32)
    psel = ar.alloc(kb.nid("moe_ps"), [128, 32], F32)
    cnt = ar.alloc(kb.nid("moe_cnt"), [8, 256], F32)
    wexp = ar.alloc(kb.nid("moe_we"), [128, 128], F32)
    wT = ar.alloc(kb.nid("moe_wT"), [8, T], BF16)

    blg = kb.bank()
    for ti in range(16):
        tsl = slice(128 * ti, 128 * (ti + 1))
        for ci in range(NCHUNK):
            kb.b.emit("pe", lambda e, o=blg.ap()[:, 8 * ti:8 * (ti + 1)], l=c.NB[ci].ap()[:, tsl],
                      r=c.gate_sb.ap()[:, ci, :], st=(ti == 0 and ci == 0), sp=False:
                      e.matmul(o, l, r, start=st, stop=sp),
                      reads=[c.NB[ci], c.gate_sb], writes=[blg], tag="moe_gmm")
        kb.b.emit("pe", lambda e, o=blg.ap()[:, 8 * ti:8 * (ti + 1)], l=c.ones_row.ap()[:, :],
                  r=c.gateb_sb.ap()[0:1, :], st=False, sp=(ti == 15):
                  e.matmul(o, l, r, start=False, stop=sp),
                  reads=[c.ones_row, c.gateb_sb], writes=[blg], tag="moe_gbm")
    kb.ev("dve", lambda e: e.tensor_copy(logit.ap()[:, :], blg.ap()[:, 0:128]),
          reads=[blg], writes=[logit], tag="moe_gev")
    for ti in range(16):
        L8 = logit.ap()[:, 8 * ti:8 * (ti + 1)]
        kb.ev("dve", lambda e, L8=L8: e.reduce_max(sm.ap()[:, 0:1], L8, AX.X),
              reads=[logit], writes=[sm], tag="moe_smx")
        kb.ev("dve", lambda e, L8=L8: e.tensor_scalar(L8, L8, sm.ap()[:, 0:1], None, ALU.subtract),
              reads=[logit, sm], writes=[logit], tag="moe_ssub")
        kb.ev("act", lambda e, L8=L8: e.activation(L8, L8, AF.Exp, accum_out=sm.ap()[:, 1:2]),
              reads=[logit], writes=[logit, sm], tag="moe_sexp")
        kb.ev("dve", lambda e: e.reciprocal(sm.ap()[:, 2:3], sm.ap()[:, 1:2]),
              reads=[sm], writes=[sm], tag="moe_srec")
        kb.ev("dve", lambda e, L8=L8: e.tensor_scalar(L8, L8, sm.ap()[:, 2:3], None, ALU.mult),
              reads=[logit, sm], writes=[logit], tag="moe_smul")
        kb.ev("dve", lambda e, L8=L8: e.max(top8.ap()[:, :], L8),
              reads=[logit], writes=[top8], tag="moe_top8")
        kb.ev("dve", lambda e, L8=L8: e.max_index(idx8.ap()[:, :], top8.ap()[:, :], L8),
              reads=[logit, top8], writes=[idx8], tag="moe_idx8")
        kb.ev("dve", lambda e: e.tensor_copy(idx8f.ap()[:, 0:2], idx8.ap()[:, 0:2]),
              reads=[idx8], writes=[idx8f], tag="moe_idxc")
        kb.ev("dve", lambda e, o=pw.ap()[:, 2 * ti:2 * ti + 2]: e.tensor_copy(o, top8.ap()[:, 0:2]),
              reads=[top8], writes=[pw], tag="moe_pwc")
        for cc in range(2):
            kb.ev("dve", lambda e, o=ohbuf.ap()[:, 16 * ti + 8 * cc:16 * ti + 8 * (cc + 1)], ic=cc:
                  e.tensor_scalar(o, c.eidx.ap()[:, :], idx8f.ap()[:, ic:ic + 1], None, ALU.is_equal),
                  reads=[c.eidx, idx8f], writes=[ohbuf], tag="moe_oh")
    bcm = kb.bank()
    for ti in range(16):
        for j in range(ti + 1):
            lj = c.utri if j == ti else c.ones_km
            kb.b.emit("pe", lambda e, o=bcm.ap()[:, 16 * ti:16 * (ti + 1)], l=lj.ap()[:, :],
                      r=ohbuf.ap()[:, 16 * j:16 * (j + 1)],
                      st=(ti == 0 and j == 0), sp=(ti == 15 and j == 15):
                      e.matmul(o, l, r, start=st, stop=sp),
                      reads=[lj, ohbuf], writes=[bcm], tag="moe_cmm")
    kb.ev("dve", lambda e: e.tensor_copy(cum.ap()[:, :], bcm.ap()[:, 0:256]),
          reads=[bcm], writes=[cum], tag="moe_cev")
    kb.dma("sp", c.coll_in[:, :], cum.ap()[127:128, 240:256], tag="moe_cst")
    kb.b.emit("pool", lambda e: e.collective_compute("AllGather", ALU.bypass,
                                                     [[0, 1, 2, 3, 4, 5, 6, 7]],
                                                     [c.coll_in[:, :]], [c.coll_out[:, :]]),
              reads=[c.coll_in], writes=[c.coll_out], tag="moe_ag")
    kb.dma("sp", cnt.ap()[:, 0:16], c.coll_out[:, :], tag="moe_cld")
    for k in range(1, 16):
        kb.ev("dve", lambda e, o=cnt.ap()[:, 16 * k:16 * (k + 1)]: e.tensor_copy(o, cnt.ap()[:, 0:16]),
              reads=[cnt], writes=[cnt], tag="moe_ctile")
    bpf = kb.bank()
    kb.b.emit("pe", lambda e: e.matmul(bpf.ap()[:, 0:256], c.prefmask.ap()[:, :], cnt.ap()[:, :],
                                       start=True, stop=True),
              reads=[c.prefmask, cnt], writes=[bpf], tag="moe_pfmm")
    kb.ev("dve", lambda e: e.tensor_mul(rt.ap()[:, :], cum.ap()[:, :], ohbuf.ap()[:, :]),
          reads=[cum, ohbuf], writes=[rt], tag="moe_cm")
    kb.ev("dve", lambda e: e.tensor_reduce(csel.ap()[:, :],
                                           rt.ap()[:, :].rearrange("p (a b) -> p a b", b=8), AX.X, ALU.add),
          reads=[rt], writes=[csel], tag="moe_cr")
    kb.ev("dve", lambda e: e.tensor_mul(rt.ap()[:, :], bpf.ap()[:, 0:256], ohbuf.ap()[:, :]),
          reads=[bpf, ohbuf], writes=[rt], tag="moe_pm")
    kb.ev("dve", lambda e: e.tensor_reduce(psel.ap()[:, :],
                                           rt.ap()[:, :].rearrange("p (a b) -> p a b", b=8), AX.X, ALU.add),
          reads=[rt], writes=[psel], tag="moe_pr")
    kb.ev("dve", lambda e: e.scalar_tensor_tensor(csel.ap()[:, :], csel.ap()[:, :], -1.0,
                                                  psel.ap()[:, :], ALU.add, ALU.add),
          reads=[csel, psel], writes=[csel], tag="moe_pos")
    kb.ev("dve", lambda e: e.tensor_single_scalar(csel.ap()[:, :], csel.ap()[:, :], float(CAP), ALU.is_lt),
          reads=[csel], writes=[csel], tag="moe_keep")
    kb.ev("dve", lambda e: e.tensor_mul(pw.ap()[:, :], pw.ap()[:, :], csel.ap()[:, :]),
          reads=[pw, csel], writes=[pw], tag="moe_w")
    for ti in range(16):
        for cc in range(2):
            dst = wexp.ap()[:, 8 * ti:8 * (ti + 1)]
            ohs = ohbuf.ap()[:, 16 * ti + 8 * cc:16 * ti + 8 * (cc + 1)]
            if cc == 0:
                kb.ev("dve", lambda e, dst=dst, ohs=ohs, w=pw.ap()[:, 2 * ti:2 * ti + 1]:
                      e.tensor_scalar(dst, ohs, w, None, ALU.mult),
                      reads=[ohbuf, pw], writes=[wexp], tag="moe_wx0")
            else:
                kb.ev("dve", lambda e, dst=dst, ohs=ohs, w=pw.ap()[:, 2 * ti + 1:2 * ti + 2]:
                      e.scalar_tensor_tensor(dst, ohs, w, dst, ALU.mult, ALU.add),
                      reads=[ohbuf, pw, wexp], writes=[wexp], tag="moe_wx1")
    for ti in range(16):
        bt = kb.bank()
        kb.b.emit("pe", lambda e, o=bt.ap()[0:8, 0:128], i=wexp.ap()[:, 8 * ti:8 * (ti + 1)]:
                  e.transpose(o, i, c.ident_f32.ap()[:, :]),
                  reads=[wexp, c.ident_f32], writes=[bt], tag="moe_wtt")
        kb.ev("dve", lambda e, o=wT.ap()[:, 128 * ti:128 * (ti + 1)], i=bt.ap()[0:8, 0:128]:
              e.tensor_copy(o, i),
              reads=[bt], writes=[wT], tag="moe_wtev")
    ar.free(logit, cum, pw, sm, top8, idx8, idx8f, rt, csel, psel, cnt, wexp, ohbuf)

    NW1, NW2 = 2, 4
    hbufs = [ar.alloc(kb.nid("moe_h"), [128, 8, 1024], BF16) for _ in range(4)]
    wred = ar.alloc(kb.nid("moe_wr"), [128, 1024], BF16)
    w1sl = [ar.alloc(kb.nid("moe_w1s"), [128, 2048], BF16) for _ in range(NW1)]
    w2sl = [ar.alloc(kb.nid("moe_w2s"), [128, 2048], BF16) for _ in range(NW2)]
    b1c = ar.alloc(kb.nid("moe_b1s"), [128, 32], F32)
    b2c = ar.alloc(kb.nid("moe_b2s"), [128, 8], F32)
    nw1 = nw2 = 0
    for ex in range(E):
        kb.dma("sp", b1c.ap()[:, :], c.moe_b1c[ex, :, :], tag="moe_b1l")
        kb.dma("sp", b2c.ap()[:, :], c.moe_b2c[ex, :, :], tag="moe_b2l")
        for tcg in range(2):
            tb = 1024 * tcg
            for qc2 in range(2):
                bw = kb.bank()
                kb.b.emit("pe", lambda e, o=bw.ap()[:, :], l=c.sel8.ap()[:, 128 * ex:128 * (ex + 1)],
                          r=wT.ap()[:, tb + 512 * qc2:tb + 512 * (qc2 + 1)]:
                          e.matmul(o, l, r, start=True, stop=True),
                          reads=[c.sel8, wT], writes=[bw], tag="moe_wrmm")
                kb.ev("dve", lambda e, o=wred.ap()[:, 512 * qc2:512 * (qc2 + 1)], i=bw.ap()[:, :]:
                      e.tensor_copy(o, i),
                      reads=[bw], writes=[wred], tag="moe_wrev")
            for jj in range(16):
                wsl = w1sl[nw1 % NW1]
                nw1 += 1
                kb.dma("sp" if nw1 % 2 == 0 else "pool", wsl.ap()[:, :], c.moe_w1bf[ex, jj, :, :],
                       tag="moe_w1l", extra_reads=["moe_w1bf"])
                for u in range(2):
                    j = 2 * jj + u
                    for tc2 in range(2):
                        ts = slice(tb + 512 * tc2, tb + 512 * (tc2 + 1))
                        bk = kb.bank()
                        for ci in range(NCHUNK):
                            kb.b.emit("pe", lambda e, o=bk.ap()[:, :],
                                      l=wsl.ap()[:, 1024 * u + 128 * ci:1024 * u + 128 * (ci + 1)],
                                      r=c.NB[ci].ap()[:, ts], st=(ci == 0), sp=(ci == 7):
                                      e.matmul(o, l, r, start=st, stop=sp),
                                      reads=[wsl, c.NB[ci]], writes=[bk], tag="moe_hmm")
                        hb = hbufs[j // 8]
                        kb.ev("act", lambda e, o=hb.ap()[:, j % 8, 512 * tc2:512 * (tc2 + 1)],
                              i=bk.ap()[:, :], b=b1c.ap()[:, j:j + 1]:
                              e.activation(o, i, AF.Relu, bias=b),
                              reads=[bk, b1c], writes=[hb], tag="moe_hev")
            for tc2 in range(2):
                ts = slice(tb + 512 * tc2, tb + 512 * (tc2 + 1))
                hs = slice(512 * tc2, 512 * (tc2 + 1))
                bks = kb.banks(8)
                for jj in range(16):
                    wsl = w2sl[nw2 % NW2]
                    nw2 += 1
                    kb.dma("sp" if nw2 % 2 == 0 else "pool", wsl.ap()[:, :], c.moe_w2bf[ex, jj, :, :],
                           tag="moe_w2l", extra_reads=["moe_w2bf"])
                    for u in range(2):
                        k2 = 2 * jj + u
                        hb = hbufs[k2 // 8]
                        for fo2 in range(8):
                            kb.b.emit("pe", lambda e, o=bks[fo2].ap()[:, :],
                                      l=wsl.ap()[:, 1024 * u + 128 * fo2:1024 * u + 128 * (fo2 + 1)],
                                      r=hb.ap()[:, k2 % 8, hs], st=(k2 == 0), sp=(k2 == 31):
                                      e.matmul(o, l, r, start=st, stop=sp),
                                      reads=[wsl, hb], writes=[bks[fo2]], tag="moe_ymm")
                for fo2 in range(8):
                    bk = bks[fo2]
                    kb.ev("dve", lambda e, o=bk.ap()[:, :], i=bk.ap()[:, :], b=b2c.ap()[:, fo2:fo2 + 1],
                          w=wred.ap()[:, hs]:
                          e.scalar_tensor_tensor(o, i, b, w, ALU.add, ALU.mult),
                          reads=[bk, b2c, wred], writes=[bk], tag="moe_yev")
                    kb.ev("dve", lambda e, o=c.X[fo2].ap()[:, ts], i=bk.ap()[:, :]:
                          e.tensor_add(o, o, i),
                          reads=[bk, c.X[fo2]], writes=[c.X[fo2]], tag="moe_xadd")
    for t_ in c.NB:
        ar.free(t_)
    c.NB = None
    ar.free(wred, *w1sl, *w2sl, *hbufs, b1c, b2c, wT)
    c.moe_call += 1


def setup_attn_b0(c):
    kb = c.kb
    nc = kb.nc
    c.b0_win = kb.din("b0_win", [128, 8, 2048], F32)
    c.b0_wq = kb.din("b0_wq", [6, 128, 1024], F32)
    c.b0_wk = kb.din("b0_wk", [6, 128, 1024], F32)
    c.b0_wk_fi = kb.din("b0_wk_fi", [128, 8, 768], F32)
    c.b0_wv_fi = kb.din("b0_wv_fi", [128, 8, 1536], F32)
    c.b0_wg = kb.din("b0_wg", [12, 128, 1024], F32)
    c.b0_wgkb = kb.din("b0_wgkb", [128, 8, 8], F32)
    c.b0_wso = kb.din("b0_wso", [8, 128, 1536], F32)
    c.b0_wout = kb.din("b0_wout", [8, 128, 2048], F32)
    c.b0_convw = kb.din("b0_convw", [128, 8, 8], F32)
    c.b0_convb = kb.din("b0_convb", [128, 16], F32)
    c.b0_alog = kb.din("b0_alog", [4, 1], F32)
    c.b0_dtb = kb.din("b0_dtb", [4, 1], F32)
    c.b0_bb = kb.din("b0_bb", [4, 1], F32)
    c.b0_gnc = kb.din("b0_gnc", [128, 3], F32)
    c.c_sel4 = kb.din("c_sel4", [4, 512], F32)
    c.c_umask_s = kb.din("c_umask_s", [128, 128], F32)
    c.c_umask_i = kb.din("c_umask_i", [128, 128], F32)
    c.c_lmask_s = kb.din("c_lmask_s", [128, 128], F32)
    c.b0_wqb_b = nc.dram_tensor("b0_wqb", [6, 128, 1024], BF16, kind="Internal").ap()
    c.b0_wkb_b = nc.dram_tensor("b0_wkb", [6, 128, 1024], BF16, kind="Internal").ap()
    c.b0_wkfib_b = nc.dram_tensor("b0_wkfib", [128, 8, 768], BF16, kind="Internal").ap()
    c.b0_wvfib_b = nc.dram_tensor("b0_wvfib", [128, 8, 1536], BF16, kind="Internal").ap()
    c.b0_wgb_b = nc.dram_tensor("b0_wgb", [12, 128, 1024], BF16, kind="Internal").ap()
    c.b0_wsob_b = nc.dram_tensor("b0_wsob", [8, 128, 1536], BF16, kind="Internal").ap()
    c.b0_woutb_b = nc.dram_tensor("b0_woutb", [8, 128, 2048], BF16, kind="Internal").ap()
    c.zspill = nc.dram_tensor("zspill", [128, 8, 2048], BF16, kind="Internal").ap()


def emit_b0_prepass(c):
    kb = c.kb
    for (dst, src_) in ((c.b0_wqb_b, c.b0_wq), (c.b0_wkb_b, c.b0_wk), (c.b0_wkfib_b, c.b0_wk_fi),
                        (c.b0_wvfib_b, c.b0_wv_fi), (c.b0_wgb_b, c.b0_wg), (c.b0_wsob_b, c.b0_wso),
                        (c.b0_woutb_b, c.b0_wout)):
        kb.dma("pool", dst, src_, tag="b0_pp", dmaclass=6)


def emit_attn_b0(c):
    kb = c.kb
    ar = kb.ar
    H, DK, DV = 4, 192, 384
    SEG, NSEG, CH = 512, 4, 128

    for ci in range(NCHUNK):
        kb.dma("sp", c.xspill[128 * ci:128 * (ci + 1), :], c.X[ci].ap()[:, :], tag="b0_spill")
    for ci in range(NCHUNK):
        ar.free(c.X[ci])
    c.X = None
    winb = ar.alloc(kb.nid("b0_winb"), [128, 8, 2048], BF16)
    kb.dma("pool", winb.ap(), c.b0_win, tag="b0_winl")
    xcb = ar.alloc(kb.nid("b0_xcb"), [128, 8, T], BF16)
    zcb = ar.alloc(kb.nid("b0_zcb"), [128, 8, T], BF16)
    for half, dst in ((0, xcb), (1, zcb)):
        for f in range(8):
            fo = half * 8 + f
            for tc in range(4):
                ts = slice(512 * tc, 512 * (tc + 1))
                bk = kb.bank()
                for ci in range(NCHUNK):
                    kb.b.emit("pe", lambda e, o=bk.ap()[:, :], l=winb.ap()[:, ci, 128 * fo:128 * (fo + 1)],
                              r=c.NB[ci].ap()[:, ts], st=(ci == 0), sp=(ci == 7):
                              e.matmul(o, l, r, start=st, stop=sp),
                              reads=[winb, c.NB[ci]], writes=[bk], tag="b0_xzmm")
                kb.ev("act", lambda e, o=dst.ap()[:, f, ts], i=bk.ap()[:, :]:
                      e.activation(o, i, AF.Identity),
                      reads=[bk], writes=[dst], tag="b0_xzev")
    for t_ in c.NB:
        ar.free(t_)
    c.NB = None
    ar.free(winb)
    cvw = ar.alloc(kb.nid("b0_cw"), [128, 8, 8], F32)
    cvb = ar.alloc(kb.nid("b0_cb"), [128, 16], F32)
    kb.dma("sp", cvw.ap(), c.b0_convw, tag="b0_cwl")
    kb.dma("sp", cvb.ap(), c.b0_convb, tag="b0_cbl")
    xsb = ar.alloc(kb.nid("b0_xsb"), [128, 8, T], BF16)
    zsb = ar.alloc(kb.nid("b0_zsb"), [128, 8, T], BF16)
    acc = ar.alloc(kb.nid("b0_acc"), [128, T], F32)
    for half, srcb, dstb in ((0, xcb, xsb), (1, zcb, zsb)):
        for f in range(8):
            xa = srcb.ap()
            w0 = cvw.ap()[:, f, 4 * half + 0:4 * half + 1]
            w1 = cvw.ap()[:, f, 4 * half + 1:4 * half + 2]
            w2 = cvw.ap()[:, f, 4 * half + 2:4 * half + 3]
            w3 = cvw.ap()[:, f, 4 * half + 3:4 * half + 4]
            kb.ev("dve", lambda e, o=acc.ap()[:, :], i=xa[:, f, :], s=w1:
                  e.tensor_scalar(o, i, s, None, ALU.mult),
                  reads=[srcb, cvw], writes=[acc], tag="b0_cv1")
            kb.ev("dve", lambda e, o=acc.ap()[:, 1:T], i=xa[:, f, 0:T - 1], s=w0:
                  e.scalar_tensor_tensor(o, i, s, acc.ap()[:, 1:T], ALU.mult, ALU.add),
                  reads=[srcb, cvw, acc], writes=[acc], tag="b0_cv0")
            kb.ev("dve", lambda e, o=acc.ap()[:, 0:T - 1], i=xa[:, f, 1:T], s=w2:
                  e.scalar_tensor_tensor(o, i, s, acc.ap()[:, 0:T - 1], ALU.mult, ALU.add),
                  reads=[srcb, cvw, acc], writes=[acc], tag="b0_cv2")
            kb.ev("dve", lambda e, o=acc.ap()[:, 0:T - 2], i=xa[:, f, 2:T], s=w3:
                  e.scalar_tensor_tensor(o, i, s, acc.ap()[:, 0:T - 2], ALU.mult, ALU.add),
                  reads=[srcb, cvw, acc], writes=[acc], tag="b0_cv3")
            kb.ev("act", lambda e, o=dstb.ap()[:, f, :], i=acc.ap()[:, :],
                  b=cvb.ap()[:, 8 * half + f:8 * half + f + 1]:
                  e.activation(o, i, AF.Silu, bias=b),
                  reads=[acc, cvb], writes=[dstb], tag="b0_cvs")
    kb.dbg("xsb", xsb, [xsb])
    ar.free(xcb, zcb, acc, cvw, cvb)
    kb.dma("sp", c.zspill, zsb.ap(), tag="b0_zsp")
    ar.free(zsb)

    alog = ar.alloc(kb.nid("b0_al"), [4, 1], F32)
    dtb = ar.alloc(kb.nid("b0_dt"), [4, 1], F32)
    bbc = ar.alloc(kb.nid("b0_bbc"), [4, 1], F32)
    gnc = ar.alloc(kb.nid("b0_gn"), [128, 3], F32)
    onescol = ar.alloc(kb.nid("b0_onec"), [128, 1], F32)
    epscol = ar.alloc(kb.nid("b0_epsc"), [128, 1], F32)
    sel4 = ar.alloc(kb.nid("b0_sel4"), [4, 512], F32)
    umask_s = ar.alloc(kb.nid("b0_ums"), [128, 128], F32)
    umask_i = ar.alloc(kb.nid("b0_umi"), [128, 128], F32)
    lmask_s = ar.alloc(kb.nid("b0_lms"), [128, 128], F32)
    wgkb = ar.alloc(kb.nid("b0_wgkbs"), [128, 8, 8], BF16)
    for (t_, ap_) in ((alog, c.b0_alog), (dtb, c.b0_dtb), (bbc, c.b0_bb), (gnc, c.b0_gnc),
                      (sel4, c.c_sel4), (umask_s, c.c_umask_s), (umask_i, c.c_umask_i),
                      (lmask_s, c.c_lmask_s)):
        kb.dma("sp", t_.ap(), ap_, tag="b0c")
    kb.dma("pool", wgkb.ap(), c.b0_wgkb, tag="b0c2")
    kb.ev("dve", lambda e: e.memset(onescol.ap()[:, :], 1.0), reads=[], writes=[onescol], tag="b0_ones")
    kb.ev("dve", lambda e: e.memset(epscol.ap()[:, :], 1e-5), reads=[], writes=[epscol], tag="b0_eps")
    negA = ar.alloc(kb.nid("b0_negA"), [4, 1], F32)
    kb.ev("act", lambda e: e.activation(negA.ap()[:, :], alog.ap()[:, :], AF.Exp),
          reads=[alog], writes=[negA], tag="b0_expA")
    kb.ev("dve", lambda e: e.tensor_scalar_mul(negA.ap()[:, :], negA.ap()[:, :], -1.0),
          reads=[negA], writes=[negA], tag="b0_negA")

    Sa = ar.alloc(kb.nid("b0_Sa"), [128, 4, DV], F32)
    Sb = ar.alloc(kb.nid("b0_Sb"), [128, 2, DV], F32)
    Sab = ar.alloc(kb.nid("b0_Sab"), [128, 4, DV], BF16)
    Sbb = ar.alloc(kb.nid("b0_Sbb"), [128, 2, DV], BF16)
    for t_ in (Sa, Sb, Sab, Sbb):
        kb.ev("dve", lambda e, a=t_.ap()[:, :, :]: e.memset(a, 0.0), reads=[], writes=[t_], tag="b0_s0")

    fullc = [0, 2, 3, 5]
    halfc = [1, 1, 4, 4]
    halfb = [0, 64, 0, 64]

    qn = ar.alloc(kb.nid("b0_qn"), [128, 6, SEG], BF16)
    qg = ar.alloc(kb.nid("b0_qg"), [128, 6, SEG], BF16)
    kn = ar.alloc(kb.nid("b0_kn"), [128, 6, SEG], BF16)
    knr = ar.alloc(kb.nid("b0_knr"), [128, 6, SEG], BF16)
    ktm = ar.alloc(kb.nid("b0_ktm"), [128, 4, 768], BF16)
    vtm = ar.alloc(kb.nid("b0_vtm"), [128, 4, 1536], BF16)
    ofm = ar.alloc(kb.nid("b0_ofm"), [128, 12, SEG], BF16)
    gkb2 = ar.alloc(kb.nid("b0_gkb2"), [4, 2 * SEG], F32)
    cum = ar.alloc(kb.nid("b0_cum"), [4, SEG], F32)
    grow = ar.alloc(kb.nid("b0_grow"), [4, SEG], F32)
    ratr = ar.alloc(kb.nid("b0_ratr"), [4, SEG], F32)
    wqb = [ar.alloc(kb.nid("b0_wqb"), [128, 1024], BF16) for _ in range(2)]
    wfi = [ar.alloc(kb.nid("b0_wfi"), [128, 8, 512], BF16) for _ in range(2)]
    scl = ar.alloc(kb.nid("b0_scl"), [128, SEG], F32)
    sq = ar.alloc(kb.nid("b0_sq"), [128, SEG], BF16)
    nrm = ar.alloc(kb.nid("b0_nrm"), [128, SEG], F32)
    cumc = ar.alloc(kb.nid("b0_cumc"), [128, 1], F32)
    ratc = ar.alloc(kb.nid("b0_ratc"), [128, 1], F32)
    bnc = ar.alloc(kb.nid("b0_bnc"), [128, 1], F32)
    gcc = ar.alloc(kb.nid("b0_gcc"), [128, 1], F32)
    ncol = ar.alloc(kb.nid("b0_ncol"), [128, 1], F32)
    gcolx = ar.alloc(kb.nid("b0_gcolx"), [128, 1], F32)
    row0 = ar.alloc(kb.nid("b0_row0"), [1, 128], F32)
    row1 = ar.alloc(kb.nid("b0_row1"), [1, 128], F32)
    row2 = ar.alloc(kb.nid("b0_row2"), [1, 128], F32)
    gend = ar.alloc(kb.nid("b0_gend"), [1, 16], F32)
    em = ar.alloc(kb.nid("b0_em"), [128, CH], F32)
    emL = ar.alloc(kb.nid("b0_emL"), [128, CH], F32)
    emU = ar.alloc(kb.nid("b0_emU"), [128, CH], F32)
    emI = ar.alloc(kb.nid("b0_emI"), [128, CH], F32)
    NTm = ar.alloc(kb.nid("b0_NTm"), [128, CH], F32)
    Lm = ar.alloc(kb.nid("b0_Lm"), [128, CH], F32)
    NT2 = ar.alloc(kb.nid("b0_NT2"), [128, CH], F32)
    L2 = ar.alloc(kb.nid("b0_L2"), [128, CH], F32)
    MqT = ar.alloc(kb.nid("b0_MqT"), [128, CH], BF16)
    rhs = ar.alloc(kb.nid("b0_rhs"), [128, DV], F32)
    rv = ar.alloc(kb.nid("b0_rv"), [128, DV], F32)
    rvb = ar.alloc(kb.nid("b0_rvb"), [128, DV], BF16)
    kpc = ar.alloc(kb.nid("b0_kpc"), [128, DK], BF16)
    oscr = ar.alloc(kb.nid("b0_oscr"), [128, SEG], F32)
    ysob = ar.alloc(kb.nid("b0_ysob"), [128, 8, SEG], BF16)
    zss = ar.alloc(kb.nid("b0_zss"), [128, 8, SEG], BF16)
    wsob = [ar.alloc(kb.nid("b0_wsos"), [128, 1536], BF16) for _ in range(2)]
    woutb = [ar.alloc(kb.nid("b0_wouts"), [128, 2048], BF16) for _ in range(2)]
    wgb = [ar.alloc(kb.nid("b0_wgs"), [128, 1024], BF16) for _ in range(2)]
    gsil = ar.alloc(kb.nid("b0_gsil"), [128, SEG], BF16)
    xseg = [ar.alloc(kb.nid("b0_xseg"), [128, SEG], F32) for _ in range(2)]

    for s in range(NSEG):
        ss = slice(SEG * s, SEG * (s + 1))
        for (wt, dst) in ((c.b0_wqb_b, qn), (c.b0_wkb_b, kn)):
            for j in range(6):
                wb = wqb[j % 2]
                kb.dma("sp", wb.ap(), wt[j, :, :], tag="b0_wql")
                bk = kb.bank()
                for ci in range(NCHUNK):
                    kb.b.emit("pe", lambda e, o=bk.ap()[:, :], l=wb.ap()[:, 128 * ci:128 * (ci + 1)],
                              r=xsb.ap()[:, ci, ss], st=(ci == 0), sp=(ci == 7):
                              e.matmul(o, l, r, start=st, stop=sp),
                              reads=[wb, xsb], writes=[bk], tag="b0_qkmm")
                kb.ev("act", lambda e, o=dst.ap()[:, j, :], i=bk.ap()[:, :]:
                      e.activation(o, i, AF.Identity),
                      reads=[bk], writes=[dst], tag="b0_qkev")
        wix = 0
        for (wsrc, dst, nd) in ((c.b0_wkfib_b, ktm, 768), (c.b0_wvfib_b, vtm, 1536)):
            for np_ in range(ceil_div(nd, 512)):
                n0 = 512 * np_
                n1 = min(512 * (np_ + 1), nd)
                wb2 = wfi[wix % 2]
                wix += 1
                kb.dma("sp", wb2.ap()[:, :, 0:n1 - n0], wsrc[:, :, n0:n1], tag="b0_wfil")
                for tt in range(4):
                    tsl = slice(SEG * s + 128 * tt, SEG * s + 128 * (tt + 1))
                    bk = kb.bank()
                    for ci in range(NCHUNK):
                        kb.b.emit("pe", lambda e, o=bk.ap()[:, 0:n1 - n0], l=xsb.ap()[:, ci, tsl],
                                  r=wb2.ap()[:, ci, 0:n1 - n0], st=(ci == 0), sp=(ci == 7):
                                  e.matmul(o, l, r, start=st, stop=sp),
                                  reads=[xsb, wb2], writes=[bk], tag="b0_tmmm")
                    kb.ev("act", lambda e, o=dst.ap()[:, tt, n0:n1], i=bk.ap()[:, 0:n1 - n0]:
                          e.activation(o, i, AF.Identity),
                          reads=[bk], writes=[dst], tag="b0_tmev")
        bgk = kb.bank()
        bbt = kb.bank()
        for half in range(2):
            bx = bgk if half == 0 else bbt
            for ci in range(NCHUNK):
                kb.b.emit("pe", lambda e, o=bx.ap()[0:4, 0:SEG],
                          l=wgkb.ap()[:, ci, 4 * half:4 * half + 4],
                          r=xsb.ap()[:, ci, ss], st=(ci == 0), sp=(ci == 7):
                          e.matmul(o, l, r, start=st, stop=sp),
                          reads=[wgkb, xsb], writes=[bx], tag="b0_gkmm")
        kb.ev("act", lambda e, o=gkb2.ap()[:, 0:SEG], i=bgk.ap()[0:4, 0:SEG], b=dtb.ap()[:, :]:
              e.activation(o, i, AF.Identity, bias=b),
              reads=[bgk, dtb], writes=[gkb2], tag="b0_gk0")
        kb.ev("act", lambda e, o=gkb2.ap()[:, SEG:2 * SEG], i=bbt.ap()[0:4, 0:SEG], b=bbc.ap()[:, :]:
              e.activation(o, i, AF.Sigmoid, bias=b),
              reads=[bbt, bbc], writes=[gkb2], tag="b0_beta")
        kb.ev("act", lambda e, o=grow.ap()[:, :], i=gkb2.ap()[:, 0:SEG]:
              e.activation(o, i, AF.Abs), reads=[gkb2], writes=[grow], tag="b0_abs")
        kb.ev("act", lambda e, o=grow.ap()[:, :]: e.activation(o, grow.ap()[:, :], AF.Exp, scale=-1.0),
              reads=[grow], writes=[grow], tag="b0_expn")
        kb.ev("act", lambda e, o=grow.ap()[:, :], b=onescol.ap()[0:4, :]:
              e.activation(o, grow.ap()[:, :], AF.Ln, bias=b),
              reads=[grow, onescol], writes=[grow], tag="b0_ln1p")
        kb.ev("dve", lambda e, o=cum.ap()[:, :], i=gkb2.ap()[:, 0:SEG]: e.tensor_relu(o, i),
              reads=[gkb2], writes=[cum], tag="b0_relu")
        kb.ev("dve", lambda e, o=cum.ap()[:, :]: e.tensor_add(o, o, grow.ap()[:, :]),
              reads=[cum, grow], writes=[cum], tag="b0_spadd")
        kb.ev("dve", lambda e, o=cum.ap()[:, :], s=negA.ap()[:, :]:
              e.tensor_scalar(o, cum.ap()[:, :], s, None, ALU.mult),
              reads=[cum, negA], writes=[cum], tag="b0_gkmul")
        for cc in range(4):
            cs = slice(128 * cc, 128 * (cc + 1))
            kb.ev("dve", lambda e, o=cum.ap()[:, cs], i=cum.ap()[:, cs]:
                  e.tensor_tensor_scan(o, i, i, 0.0, ALU.add, ALU.bypass),
                  reads=[cum], writes=[cum], tag="b0_scan")
        kb.ev("act", lambda e, o=grow.ap()[:, :], i=cum.ap()[:, :]: e.activation(o, i, AF.Exp),
              reads=[cum], writes=[grow], tag="b0_gamma")
        for cc in range(4):
            cs = slice(128 * cc, 128 * (cc + 1))
            kb.ev("dve", lambda e, o=ratr.ap()[:, cs], i=cum.ap()[:, cs],
                  s=cum.ap()[:, 128 * cc + 127:128 * cc + 128]:
                  e.tensor_scalar(o, i, s, None, ALU.subtract),
                  reads=[cum], writes=[ratr], tag="b0_ratd")
        kb.ev("act", lambda e, o=ratr.ap()[:, :]: e.activation(o, ratr.ap()[:, :], AF.Exp, scale=-1.0),
              reads=[ratr], writes=[ratr], tag="b0_rate")
        for h in range(H):
            kb.dma("sp", gend.ap()[0:1, 4 * h:4 * (h + 1)], grow.ap()[h:h + 1, 127:512:128],
                   tag="b0_gend", slow_ok=True)
        if s == 0:
            kb.dbg("cum0", cum, [cum])
            kb.dbg("beta0", gkb2.ap()[:, SEG:2 * SEG], [gkb2])
        for h in range(H):
            fc, hc, hb = fullc[h], halfc[h], halfb[h]
            hsl = slice(hb, hb + 64)

            def pieces(buf):
                return ((buf.ap()[:, fc, :], 128, slice(0, 128)),
                        (buf.ap()[hsl, hc, :], 64, hsl))

            bkn = kb.bank()
            first = True
            for (ksrc, npart, psl) in pieces(kn):
                kb.ev("act", lambda e, o=sq.ap()[0:npart, :], i=ksrc: e.activation(o, i, AF.Square),
                      reads=[kn], writes=[sq], tag="b0_ksq")
                kb.b.emit("pe", lambda e, o=bkn.ap()[:, :], l=c.ones_km.ap()[0:npart, :],
                          r=sq.ap()[0:npart, :], st=first, sp=(not first):
                          e.matmul(o, l, r, start=st, stop=sp),
                          reads=[c.ones_km, sq], writes=[bkn], tag="b0_knmm")
                first = False
            kb.ev("dve", lambda e, o=nrm.ap()[:, :], i=bkn.ap()[:, :]: e.tensor_scalar_add(o, i, 1e-6),
                  reads=[bkn], writes=[nrm], tag="b0_keps")
            kb.ev("act", lambda e, o=nrm.ap()[:, :]: e.activation(o, nrm.ap()[:, :], AF.Sqrt),
                  reads=[nrm], writes=[nrm], tag="b0_ksqrt")
            kb.ev("dve", lambda e, o=nrm.ap()[:, :]: e.reciprocal(o, nrm.ap()[:, :]),
                  reads=[nrm], writes=[nrm], tag="b0_krec")
            bkr = kb.bank()
            kb.b.emit("pe", lambda e, o=bkr.ap()[:, :], l=sel4.ap()[:, 128 * h:128 * (h + 1)],
                      r=gkb2.ap()[:, SEG:2 * SEG]: e.matmul(o, l, r, start=True, stop=True),
                      reads=[sel4, gkb2], writes=[bkr], tag="b0_bred")
            kb.ev("dve", lambda e, o=scl.ap()[:, :], i=bkr.ap()[:, :]: e.tensor_mul(o, i, nrm.ap()[:, :]),
                  reads=[bkr, nrm], writes=[scl], tag="b0_bnr")
            for (ksrc, npart, psl) in pieces(kn):
                kdst = knr.ap()[:, fc, :] if npart == 128 else knr.ap()[hsl, hc, :]
                kb.ev("dve", lambda e, o=kdst, i=ksrc, m=scl.ap()[0:npart, :] if npart == 128 else scl.ap()[psl, :]:
                      e.tensor_mul(o, i, m), reads=[kn, scl], writes=[knr], tag="b0_knr")
            for (ksrc, npart, psl) in pieces(kn):
                kb.ev("dve", lambda e, o=ksrc, i=ksrc, m=nrm.ap()[0:npart, :] if npart == 128 else nrm.ap()[psl, :]:
                      e.tensor_mul(o, i, m), reads=[kn, nrm], writes=[kn], tag="b0_knn")
            for tt in range(4):
                btc = kb.bank()
                kb.b.emit("pe", lambda e, o=btc.ap()[0:128, 0:1], i=nrm.ap()[0:1, 128 * tt:128 * (tt + 1)]:
                          e.transpose(o, i, c.ident_f32.ap()[0:1, 0:1]),
                          reads=[nrm, c.ident_f32], writes=[btc], tag="b0_ntr")
                kb.ev("dve", lambda e, o=ncol.ap()[:, :], i=btc.ap()[0:128, 0:1]: e.tensor_copy(o, i),
                      reads=[btc], writes=[ncol], tag="b0_ntc")
                ksl = slice(192 * h, 192 * (h + 1))
                kb.ev("dve", lambda e, o=ktm.ap()[:, tt, ksl], s=ncol.ap()[:, 0:1]:
                      e.tensor_scalar(o, o, s, None, ALU.mult),
                      reads=[ktm, ncol], writes=[ktm], tag="b0_ktn")
            bq = kb.bank()
            first = True
            for (qsrc, npart, psl) in pieces(qn):
                kb.ev("act", lambda e, o=sq.ap()[0:npart, :], i=qsrc: e.activation(o, i, AF.Square),
                      reads=[qn], writes=[sq], tag="b0_qsq")
                kb.b.emit("pe", lambda e, o=bq.ap()[:, :], l=c.ones_km.ap()[0:npart, :],
                          r=sq.ap()[0:npart, :], st=first, sp=(not first):
                          e.matmul(o, l, r, start=st, stop=sp),
                          reads=[c.ones_km, sq], writes=[bq], tag="b0_qnmm")
                first = False
            kb.ev("dve", lambda e, o=nrm.ap()[:, :], i=bq.ap()[:, :]: e.tensor_scalar_add(o, i, 1e-6),
                  reads=[bq], writes=[nrm], tag="b0_qeps")
            kb.ev("act", lambda e, o=nrm.ap()[:, :]: e.activation(o, nrm.ap()[:, :], AF.Sqrt),
                  reads=[nrm], writes=[nrm], tag="b0_qsqrt")
            kb.ev("dve", lambda e, o=nrm.ap()[:, :]: e.reciprocal(o, nrm.ap()[:, :]),
                  reads=[nrm], writes=[nrm], tag="b0_qrec")
            bq2 = kb.bank()
            kb.b.emit("pe", lambda e, o=bq2.ap()[:, :], l=sel4.ap()[:, 128 * h:128 * (h + 1)],
                      r=grow.ap()[:, :]: e.matmul(o, l, r, start=True, stop=True),
                      reads=[sel4, grow], writes=[bq2], tag="b0_gred")
            kb.ev("dve", lambda e, o=scl.ap()[:, :], i=bq2.ap()[:, :]: e.tensor_mul(o, i, nrm.ap()[:, :]),
                  reads=[bq2, nrm], writes=[scl], tag="b0_gnr")
            for (qsrc, npart, psl) in pieces(qn):
                qdst = qg.ap()[:, fc, :] if npart == 128 else qg.ap()[hsl, hc, :]
                kb.ev("dve", lambda e, o=qdst, i=qsrc, m=scl.ap()[0:npart, :] if npart == 128 else scl.ap()[psl, :]:
                      e.tensor_mul(o, i, m), reads=[qn, scl], writes=[qg], tag="b0_qg")
            for (qsrc, npart, psl) in pieces(qn):
                kb.ev("dve", lambda e, o=qsrc, i=qsrc, m=nrm.ap()[0:npart, :] if npart == 128 else nrm.ap()[psl, :]:
                      e.tensor_mul(o, i, m), reads=[qn, nrm], writes=[qn], tag="b0_qn2")
        if s == 0:
            kb.dbg("kn0", kn, [kn])
            kb.dbg("qg0", qg, [qg])
            kb.dbg("ktm0", ktm, [ktm])
            kb.dbg("vtm0", vtm, [vtm])
        for ccc in range(4):
            csl = slice(128 * ccc, 128 * (ccc + 1))
            for h in range(H):
                fc, hc, hb = fullc[h], halfc[h], halfb[h]
                hsl = slice(hb, hb + 64)
                vsl = slice(384 * h, 384 * (h + 1))
                kb.dma("sp", row0.ap(), cum.ap()[h:h + 1, csl], tag="b0_r0")
                kb.dma("sp", row1.ap(), ratr.ap()[h:h + 1, csl], tag="b0_r1")
                kb.dma("sp", row2.ap(), gkb2.ap()[h:h + 1, SEG + 128 * ccc:SEG + 128 * (ccc + 1)], tag="b0_r2")
                bt = kb.bank()
                kb.b.emit("pe", lambda e, o=bt.ap()[0:128, 0:1], i=row0.ap()[0:1, :]:
                          e.transpose(o, i, c.ident_f32.ap()[0:1, 0:1]),
                          reads=[row0, c.ident_f32], writes=[bt], tag="b0_ctr")
                kb.ev("dve", lambda e, o=cumc.ap()[:, :], i=bt.ap()[0:128, 0:1]: e.tensor_copy(o, i),
                      reads=[bt], writes=[cumc], tag="b0_ctc")
                bt2 = kb.bank()
                kb.b.emit("pe", lambda e, o=bt2.ap()[0:128, 0:1], i=row1.ap()[0:1, :]:
                          e.transpose(o, i, c.ident_f32.ap()[0:1, 0:1]),
                          reads=[row1, c.ident_f32], writes=[bt2], tag="b0_rtr")
                kb.ev("dve", lambda e, o=ratc.ap()[:, :], i=bt2.ap()[0:128, 0:1]: e.tensor_copy(o, i),
                      reads=[bt2], writes=[ratc], tag="b0_rtc")
                bt3 = kb.bank()
                kb.b.emit("pe", lambda e, o=bt3.ap()[0:128, 0:1], i=row2.ap()[0:1, :]:
                          e.transpose(o, i, c.ident_f32.ap()[0:1, 0:1]),
                          reads=[row2, c.ident_f32], writes=[bt3], tag="b0_btr")
                kb.ev("dve", lambda e, o=bnc.ap()[:, :], i=bt3.ap()[0:128, 0:1]:
                      e.tensor_scalar_mul(o, i, -1.0),
                      reads=[bt3], writes=[bnc], tag="b0_btc")
                kb.ev("act", lambda e, o=gcolx.ap()[:, :], i=cumc.ap()[:, :]: e.activation(o, i, AF.Exp),
                      reads=[cumc], writes=[gcolx], tag="b0_gcol")
                brr = kb.bank()
                kb.b.emit("pe", lambda e, o=brr.ap()[:, 0:128], l=c.ones_row_f32.ap()[:, :], r=row0.ap()[0:1, :]:
                          e.matmul(o, l, r, start=True, stop=True),
                          reads=[c.ones_row_f32, row0], writes=[brr], tag="b0_crr")
                kb.ev("dve", lambda e, o=em.ap()[:, :], i=brr.ap()[:, 0:128]:
                      e.tensor_scalar(o, i, cumc.ap()[:, 0:1], 0.0, ALU.subtract, ALU.min),
                      reads=[brr, cumc], writes=[em], tag="b0_dmin")
                kb.ev("dve", lambda e, o=emL.ap()[:, :], i=brr.ap()[:, 0:128]:
                      e.tensor_scalar(o, i, cumc.ap()[:, 0:1], 0.0, ALU.subtract, ALU.max),
                      reads=[brr, cumc], writes=[emL], tag="b0_dmax")
                kb.ev("act", lambda e, o=em.ap()[:, :]: e.activation(o, em.ap()[:, :], AF.Exp),
                      reads=[em], writes=[em], tag="b0_eexp")
                kb.ev("act", lambda e, o=emL.ap()[:, :]: e.activation(o, emL.ap()[:, :], AF.Exp, scale=-1.0),
                      reads=[emL], writes=[emL], tag="b0_eexpL")
                kb.ev("dve", lambda e, o=emU.ap()[:, :]: e.tensor_mul(o, em.ap()[:, :], umask_s.ap()[:, :]),
                      reads=[em, umask_s], writes=[emU], tag="b0_emU")
                kb.ev("dve", lambda e, o=emI.ap()[:, :]: e.tensor_mul(o, em.ap()[:, :], umask_i.ap()[:, :]),
                      reads=[em, umask_i], writes=[emI], tag="b0_emI")
                kb.ev("dve", lambda e, o=emL.ap()[:, :]: e.tensor_mul(o, emL.ap()[:, :], lmask_s.ap()[:, :]),
                      reads=[emL, lmask_s], writes=[emL], tag="b0_emL")
                bkk = kb.bank()
                kb.b.emit("pe", lambda e, o=bkk.ap()[:, 0:128], l=kn.ap()[:, fc, csl], r=knr.ap()[:, fc, csl]:
                          e.matmul(o, l, r, start=True, stop=False),
                          reads=[kn, knr], writes=[bkk], tag="b0_kk1")
                kb.b.emit("pe", lambda e, o=bkk.ap()[:, 0:128], l=kn.ap()[hsl, hc, csl], r=knr.ap()[hsl, hc, csl]:
                          e.matmul(o, l, r, start=False, stop=True),
                          reads=[kn, knr], writes=[bkk], tag="b0_kk2")
                if KB.DEBUG and s == 0 and ccc == 0 and h == 0:
                    dbt = ar.alloc(kb.nid("b0_dbkk"), [128, 128], F32)
                    kb.ev("dve", lambda e, o=dbt.ap()[:, :], i=bkk.ap()[:, 0:128]: e.tensor_copy(o, i),
                          reads=[bkk], writes=[dbt], tag="dbg_bkkcp")
                    kb.dbg("bkk00", dbt, [dbt])
                kb.ev("dve", lambda e, o=NTm.ap()[:, :], i=bkk.ap()[:, 0:128]:
                      e.tensor_mul(o, i, emU.ap()[:, :]),
                      reads=[bkk, emU], writes=[NTm], tag="b0_NT")
                bkl = kb.bank()
                kb.b.emit("pe", lambda e, o=bkl.ap()[:, 0:128], l=knr.ap()[:, fc, csl], r=kn.ap()[:, fc, csl]:
                          e.matmul(o, l, r, start=True, stop=False),
                          reads=[kn, knr], writes=[bkl], tag="b0_kkl1")
                kb.b.emit("pe", lambda e, o=bkl.ap()[:, 0:128], l=knr.ap()[hsl, hc, csl], r=kn.ap()[hsl, hc, csl]:
                          e.matmul(o, l, r, start=False, stop=True),
                          reads=[kn, knr], writes=[bkl], tag="b0_kkl2")
                kb.ev("dve", lambda e, o=Lm.ap()[:, :], i=bkl.ap()[:, 0:128]:
                      e.tensor_mul(o, i, emL.ap()[:, :]),
                      reads=[bkl, emL], writes=[Lm], tag="b0_L")
                bqk = kb.bank()
                kb.b.emit("pe", lambda e, o=bqk.ap()[:, 0:128], l=kn.ap()[:, fc, csl], r=qn.ap()[:, fc, csl]:
                          e.matmul(o, l, r, start=True, stop=False),
                          reads=[kn, qn], writes=[bqk], tag="b0_qk1")
                kb.b.emit("pe", lambda e, o=bqk.ap()[:, 0:128], l=kn.ap()[hsl, hc, csl], r=qn.ap()[hsl, hc, csl]:
                          e.matmul(o, l, r, start=False, stop=True),
                          reads=[kn, qn], writes=[bqk], tag="b0_qk2")
                kb.ev("dve", lambda e, o=MqT.ap()[:, :], i=bqk.ap()[:, 0:128]:
                      e.tensor_mul(o, i, emI.ap()[:, :]),
                      reads=[bqk, emI], writes=[MqT], tag="b0_Mq")
                bks = kb.bank()
                kb.b.emit("pe", lambda e, o=bks.ap()[:, 0:DV], l=kn.ap()[:, fc, csl], r=Sab.ap()[:, h, :]:
                          e.matmul(o, l, r, start=True, stop=False),
                          reads=[kn, Sab], writes=[bks], tag="b0_ks1")
                kb.b.emit("pe", lambda e, o=bks.ap()[:, 0:DV], l=kn.ap()[hsl, hc, csl], r=Sbb.ap()[hsl, h // 2, :]:
                          e.matmul(o, l, r, start=False, stop=True),
                          reads=[kn, Sbb], writes=[bks], tag="b0_ks2")
                kb.ev("dve", lambda e, o=oscr.ap()[:, 0:DV], i=bks.ap()[:, 0:DV], s=gcolx.ap()[:, 0:1]:
                      e.tensor_scalar(o, i, s, None, ALU.mult),
                      reads=[bks, gcolx], writes=[oscr], tag="b0_rh1")
                kb.ev("dve", lambda e, o=oscr.ap()[:, 0:DV], m=vtm.ap()[:, ccc, vsl]:
                      e.tensor_sub(o, oscr.ap()[:, 0:DV], m),
                      reads=[oscr, vtm], writes=[oscr], tag="b0_rh2")
                kb.ev("dve", lambda e, o=rhs.ap()[:, :], i=oscr.ap()[:, 0:DV], s=bnc.ap()[:, 0:1]:
                      e.tensor_scalar(o, i, s, None, ALU.mult),
                      reads=[oscr, bnc], writes=[rhs], tag="b0_rh3")
                bsv = kb.bank()
                kb.b.emit("pe", lambda e, o=bsv.ap()[:, 0:DV], l=NTm.ap()[:, :], r=rhs.ap()[:, :]:
                          e.matmul(o, l, r, start=True, stop=True),
                          reads=[NTm, rhs], writes=[bsv], tag="b0_ap0")
                kb.ev("dve", lambda e, o=rv.ap()[:, :], i=rhs.ap()[:, :], bb=bsv.ap()[:, 0:DV]:
                      e.tensor_sub(o, i, bb),
                      reads=[rhs, bsv], writes=[rv], tag="b0_apv0")
                cN, cL, nN, nL = NTm, Lm, NT2, L2
                for lev in range(6):
                    bq1 = kb.bank()
                    kb.b.emit("pe", lambda e, o=bq1.ap()[:, 0:128], l=cL.ap()[:, :], r=cN.ap()[:, :]:
                              e.matmul(o, l, r, start=True, stop=True),
                              reads=[cL, cN], writes=[bq1], tag="b0_sqN")
                    bq2m = kb.bank()
                    kb.b.emit("pe", lambda e, o=bq2m.ap()[:, 0:128], l=cN.ap()[:, :], r=cL.ap()[:, :]:
                              e.matmul(o, l, r, start=True, stop=True),
                              reads=[cN, cL], writes=[bq2m], tag="b0_sqL")
                    kb.ev("dve", lambda e, o=nN.ap()[:, :], i=bq1.ap()[:, 0:128]: e.tensor_copy(o, i),
                          reads=[bq1], writes=[nN], tag="b0_sqNe")
                    kb.ev("dve", lambda e, o=nL.ap()[:, :], i=bq2m.ap()[:, 0:128]: e.tensor_copy(o, i),
                          reads=[bq2m], writes=[nL], tag="b0_sqLe")
                    bap = kb.bank()
                    kb.b.emit("pe", lambda e, o=bap.ap()[:, 0:DV], l=nN.ap()[:, :], r=rv.ap()[:, :]:
                              e.matmul(o, l, r, start=True, stop=True),
                              reads=[nN, rv], writes=[bap], tag="b0_ap")
                    kb.ev("dve", lambda e, o=rv.ap()[:, :], i=bap.ap()[:, 0:DV]:
                          e.tensor_add(o, rv.ap()[:, :], i),
                          reads=[rv, bap], writes=[rv], tag="b0_apv")
                    cN, cL, nN, nL = nN, nL, cN, cL
                kb.ev("dve", lambda e, o=rvb.ap()[:, :], i=rv.ap()[:, :]: e.tensor_copy(o, i),
                      reads=[rv], writes=[rvb], tag="b0_rvc")
                bo = kb.bank()
                for dvc in range(3):
                    dsl = slice(128 * dvc, 128 * (dvc + 1))
                    kb.b.emit("pe", lambda e, o=bo.ap()[:, dsl], l=Sab.ap()[:, h, dsl], r=qg.ap()[:, fc, csl],
                              st=(dvc == 0), sp=False: e.matmul(o, l, r, start=st, stop=sp),
                              reads=[Sab, qg], writes=[bo], tag="b0_oi1")
                    kb.b.emit("pe", lambda e, o=bo.ap()[:, dsl], l=Sbb.ap()[hsl, h // 2, dsl], r=qg.ap()[hsl, hc, csl]:
                              e.matmul(o, l, r, start=False, stop=False),
                              reads=[Sbb, qg], writes=[bo], tag="b0_oi2")
                    kb.b.emit("pe", lambda e, o=bo.ap()[:, dsl], l=rvb.ap()[:, dsl], r=MqT.ap()[:, :],
                              sp=(dvc == 2): e.matmul(o, l, r, start=False, stop=sp),
                              reads=[rvb, MqT], writes=[bo], tag="b0_oMq")
                for dvc in range(3):
                    kb.ev("dve", lambda e, o=ofm.ap()[:, 3 * h + dvc, csl], i=bo.ap()[:, 128 * dvc:128 * (dvc + 1)]:
                          e.tensor_copy(o, i),
                          reads=[bo], writes=[ofm], tag="b0_oev")
                kb.ev("dve", lambda e, o=kpc.ap()[:, :], i=ktm.ap()[:, ccc, 192 * h:192 * (h + 1)],
                      s=ratc.ap()[:, 0:1]: e.tensor_scalar(o, i, s, None, ALU.mult),
                      reads=[ktm, ratc], writes=[kpc], tag="b0_kpr")
                bgc = kb.bank()
                kb.b.emit("pe", lambda e, o=bgc.ap()[:, 0:1], l=c.ones_row_f32.ap()[:, :],
                          r=gend.ap()[0:1, 4 * h + ccc:4 * h + ccc + 1]:
                          e.matmul(o, l, r, start=True, stop=True),
                          reads=[c.ones_row_f32, gend], writes=[bgc], tag="b0_gcr")
                kb.ev("dve", lambda e, o=gcc.ap()[:, :], i=bgc.ap()[:, 0:1]: e.tensor_copy(o, i),
                      reads=[bgc], writes=[gcc], tag="b0_gcc")
                bst = kb.bank()
                fcoff = 0 if h % 2 == 0 else 64
                hcoff = 128 if h % 2 == 0 else 0
                kb.b.emit("pe", lambda e, o=bst.ap()[:, 0:DV], l=kpc.ap()[:, fcoff:fcoff + 128], r=rvb.ap()[:, :]:
                          e.matmul(o, l, r, start=True, stop=True),
                          reads=[kpc, rvb], writes=[bst], tag="b0_su1")
                kb.ev("dve", lambda e, o=Sa.ap()[:, h, :], i=Sa.ap()[:, h, :], s=gcc.ap()[:, 0:1],
                      bb=bst.ap()[:, 0:DV]:
                      e.scalar_tensor_tensor(o, i, s, bb, ALU.mult, ALU.add),
                      reads=[Sa, gcc, bst], writes=[Sa], tag="b0_sup1")
                bst2 = kb.bank()
                kb.b.emit("pe", lambda e, o=bst2.ap()[hsl, 0:DV], l=kpc.ap()[:, hcoff:hcoff + 64], r=rvb.ap()[:, :]:
                          e.matmul(o, l, r, start=True, stop=True),
                          reads=[kpc, rvb], writes=[bst2], tag="b0_su2")
                kb.ev("dve", lambda e, o=Sb.ap()[hsl, h // 2, :], i=Sb.ap()[hsl, h // 2, :],
                      s=gcc.ap()[hsl, 0:1], bb=bst2.ap()[hsl, 0:DV]:
                      e.scalar_tensor_tensor(o, i, s, bb, ALU.mult, ALU.add),
                      reads=[Sb, gcc, bst2], writes=[Sb], tag="b0_sup2")
                if s == 0 and ccc == 0 and h == 0:
                    kb.dbg("rv00", rv, [rv])
                    kb.dbg("NT00", NTm, [NTm])
                    kb.dbg("emU00", emU, [emU])
                    kb.dbg("rhs00", rhs, [rhs])
                    kb.dbg("Sa00", Sa.ap()[:, 0, :], [Sa])
                kb.ev("dve", lambda e, o=Sab.ap()[:, h, :], i=Sa.ap()[:, h, :]: e.tensor_copy(o, i),
                      reads=[Sa], writes=[Sab], tag="b0_scp1")
                kb.ev("dve", lambda e, o=Sbb.ap()[hsl, h // 2, :], i=Sb.ap()[hsl, h // 2, :]: e.tensor_copy(o, i),
                      reads=[Sb], writes=[Sbb], tag="b0_scp2")
        if s == 0:
            kb.dbg("ofm0", ofm, [ofm])
        rinvh = []
        for h in range(H):
            brm = kb.bank()
            for dvc in range(3):
                kb.ev("act", lambda e, o=sq.ap()[:, :], i=ofm.ap()[:, 3 * h + dvc, :]:
                      e.activation(o, i, AF.Square),
                      reads=[ofm], writes=[sq], tag="b0_osq")
                kb.b.emit("pe", lambda e, o=brm.ap()[:, :], l=c.ones_km.ap()[:, :], r=sq.ap()[:, :],
                          st=(dvc == 0), sp=(dvc == 2): e.matmul(o, l, r, start=st, stop=sp),
                          reads=[c.ones_km, sq], writes=[brm], tag="b0_omm")
            rh = ar.alloc(kb.nid("b0_rinv"), [128, SEG], BF16)
            rinvh.append(rh)
            kb.ev("act", lambda e, o=nrm.ap()[:, :], i=brm.ap()[:, :], b=epscol.ap()[:, :]:
                  e.activation(o, i, AF.Sqrt, bias=b, scale=1.0 / DV),
                  reads=[brm, epscol], writes=[nrm], tag="b0_orms")
            kb.ev("dve", lambda e, o=nrm.ap()[:, :]: e.reciprocal(o, o),
                  reads=[nrm], writes=[nrm], tag="b0_orec")
            kb.ev("dve", lambda e, o=rh.ap()[:, :], i=nrm.ap()[:, :]: e.tensor_copy(o, i),
                  reads=[nrm], writes=[rh], tag="b0_orcp")
        for j12 in range(12):
            wb = wgb[j12 % 2]
            kb.dma("sp", wb.ap(), c.b0_wgb_b[j12, :, :], tag="b0_wgl")
            bk = kb.bank()
            for ci in range(NCHUNK):
                kb.b.emit("pe", lambda e, o=bk.ap()[:, :], l=wb.ap()[:, 128 * ci:128 * (ci + 1)],
                          r=xsb.ap()[:, ci, ss], st=(ci == 0), sp=(ci == 7):
                          e.matmul(o, l, r, start=st, stop=sp),
                          reads=[wb, xsb], writes=[bk], tag="b0_gmm")
            kb.ev("act", lambda e, o=gsil.ap()[:, :], i=bk.ap()[:, :]:
                  e.activation(o, i, AF.Silu),
                  reads=[bk], writes=[gsil], tag="b0_gsil")
            h = j12 // 3
            dvc = j12 % 3
            kb.ev("dve", lambda e, o=oscr.ap()[:, :], i=ofm.ap()[:, j12, :], s=gnc.ap()[:, dvc:dvc + 1],
                  rr=rinvh[h].ap()[:, :]:
                  e.scalar_tensor_tensor(o, i, s, rr, ALU.mult, ALU.mult),
                  reads=[ofm, gnc, rinvh[h]], writes=[oscr], tag="b0_osc")
            kb.ev("dve", lambda e, o=ofm.ap()[:, j12, :], i=oscr.ap()[:, :]:
                  e.tensor_mul(o, i, gsil.ap()[:, :]),
                  reads=[oscr, gsil], writes=[ofm], tag="b0_ogate")
        for rh in rinvh:
            ar.free(rh)
        for j in range(8):
            wb = wsob[j % 2]
            kb.dma("sp", wb.ap(), c.b0_wsob_b[j, :, :], tag="b0_wsol")
            bk = kb.bank()
            for ci2 in range(12):
                kb.b.emit("pe", lambda e, o=bk.ap()[:, :], l=wb.ap()[:, 128 * ci2:128 * (ci2 + 1)],
                          r=ofm.ap()[:, ci2, :], st=(ci2 == 0), sp=(ci2 == 11):
                          e.matmul(o, l, r, start=st, stop=sp),
                          reads=[wb, ofm], writes=[bk], tag="b0_somm")
            kb.ev("act", lambda e, o=ysob.ap()[:, j, :], i=bk.ap()[:, :]:
                  e.activation(o, i, AF.Identity),
                  reads=[bk], writes=[ysob], tag="b0_soev")
        for f in range(8):
            kb.dma("sp", zss.ap()[:, f, :], c.zspill[:, f, ss], tag="b0_zsl")
        for fo in range(8):
            xg = xseg[fo % 2]
            kb.dma("sp", xg.ap()[:, :], c.xspill[128 * fo:128 * (fo + 1), ss], tag="b0_xld")
            wb = woutb[fo % 2]
            kb.dma("sp", wb.ap(), c.b0_woutb_b[fo, :, :], tag="b0_woutl")
            bk = kb.bank()
            for ci2 in range(16):
                rsrc = ysob.ap()[:, ci2, :] if ci2 < 8 else zss.ap()[:, ci2 - 8, :]
                rkey = ysob if ci2 < 8 else zss
                kb.b.emit("pe", lambda e, o=bk.ap()[:, :], l=wb.ap()[:, 128 * ci2:128 * (ci2 + 1)],
                          r=rsrc, st=(ci2 == 0), sp=(ci2 == 15):
                          e.matmul(o, l, r, start=st, stop=sp),
                          reads=[wb, rkey], writes=[bk], tag="b0_outmm")
            kb.ev("dve", lambda e, o=xg.ap()[:, :], i=bk.ap()[:, :]:
                  e.tensor_add(o, o, i),
                  reads=[bk, xg], writes=[xg], tag="b0_xadd")
            kb.dma("sp", c.xspill[128 * fo:128 * (fo + 1), ss], xg.ap()[:, :], tag="b0_xst")
    for t_ in (xsb, qn, qg, kn, knr, ktm, vtm, ofm, ysob, zss):
        ar.free(t_)
    c.X = []
    for fo in range(8):
        xt = ar.alloc(f"Xb{fo}_r", [128, T], F32, high=True)
        c.X.append(xt)
        kb.dma("sp", xt.ap()[:, :], c.xspill[128 * fo:128 * (fo + 1), :], tag="b0_xfin")
    ar.free(gkb2, cum, grow, ratr,
            wqb[0], wqb[1], wfi[0], wfi[1], scl, sq, nrm, cumc, ratc, bnc, gcc,
            ncol, gcolx, row0, row1, row2, gend, em, emL, emU, emI, NTm, Lm,
            NT2, L2, MqT, rhs, rv, rvb, kpc, oscr, wsob[0], wsob[1],
            woutb[0], woutb[1], wgb[0], wgb[1], gsil, xseg[0], xseg[1],
            alog, dtb, bbc, gnc, onescol, epscol, sel4, umask_s, umask_i,
            lmask_s, wgkb, negA, Sa, Sb, Sab, Sbb)



SEM_NAMES = ["pe", "act", "dve", "sp", "pool", "qsp", "qpool"]


def assemble(kb):
    kb.b.finalize()
    nc = kb.nc
    sems = {}
    for n in ("pe", "act", "dve", "pool", "sp"):
        sems[n] = nc.alloc_semaphore("sem_" + n)
    for qe in ("sp", "pool"):
        for j in range(NDMACLASS):
            sems[f"q{qe}{j}"] = nc.alloc_semaphore(f"sem_q{qe}{j}")
    kb.b.build(nc, sems)
    return nc


def shard_inputs(full, core_id):
    m = dict(host_constants(core_id))
    m["x_fm"] = np.ascontiguousarray(full["L0_attn_a_x"][core_id].T)
    w1 = np.asarray(full["L1_ffn_w1"])
    w2 = np.asarray(full["L1_ffn_w2"])
    m["l1f_w1t_t"] = tile_fo(np.ascontiguousarray(w1.T))
    m["l1f_w1n"] = np.ascontiguousarray(w1)
    m["l1f_w2t"] = np.ascontiguousarray(w2.T)
    m["l1f_w2n"] = np.ascontiguousarray(w2)
    m["l1f_b1c"] = np.ascontiguousarray(np.asarray(full["L1_ffn_b1"]).reshape(32, 128).T)
    m["l1f_b2c"] = np.ascontiguousarray(np.asarray(full["L1_ffn_b2"]).reshape(8, 128).T)
    m["a0_wqT_t"] = tile_fo(np.ascontiguousarray(np.asarray(full["L0_attn_a_wq"]).T), 512)
    m["a0_wkT_t"] = tile_fo(np.ascontiguousarray(np.asarray(full["L0_attn_a_wk"]).T), 512)
    m["a0_wvT_t"] = tile_fo(np.ascontiguousarray(np.asarray(full["L0_attn_a_wv"]).T), 512)
    m["aa_wq_t"] = tile_fo(np.asarray(full["L1_attn_a_Wq"]), 512)
    m["aa_wkv_t"] = tile_fo(np.asarray(full["L1_attn_a_Wkv"]), 512)
    m["aa_wo_t"] = tile_fo(np.asarray(full["L1_attn_a_Wo"]), 512)
    m["aa_bqc"] = np.ascontiguousarray(np.asarray(full["L1_attn_a_bq"]).reshape(8, 128).T)
    kvb = np.asarray(full["L1_attn_a_bkv"])
    m["aa_bkc"] = np.ascontiguousarray(kvb[:1024].reshape(8, 128).T)
    m["aa_bvr"] = np.ascontiguousarray(kvb[1024:].reshape(1, 1024))
    m["aa_boc"] = np.ascontiguousarray(np.asarray(full["L1_attn_a_bo"]).reshape(8, 128).T)
    wqc = np.asarray(full["L1_attn_b_w_q"]).transpose(1, 0, 2).reshape(1024, 1024)
    wkc = np.asarray(full["L1_attn_b_w_k"]).transpose(1, 0, 2).reshape(1024, 1024)
    wvc = np.asarray(full["L1_attn_b_w_v"]).transpose(1, 0, 2).reshape(1024, 1024)
    m["ab_wq_t"] = tile_fo(wqc, 512)
    m["ab_wk_t"] = tile_fo(wkc, 512)
    m["ab_wv_t"] = tile_fo(wvc, 512)
    m["ab_wo_t"] = tile_fo(np.asarray(full["L1_attn_b_w_o"]), 512)
    m["moe_gate_t"] = tile_fi(np.ascontiguousarray(np.asarray(full["L0_ffn_gate_w"]).T))
    m["moe_gateb"] = np.ascontiguousarray(np.asarray(full["L0_ffn_gate_b"]).reshape(1, 8))
    w1t = np.stack([tile_fo(np.asarray(full["L0_ffn_W1"])[e], 128).reshape(32, 128, 1024)
                    for e in range(8)])
    m["moe_w1"] = np.ascontiguousarray(
        w1t.reshape(8, 16, 2, 128, 1024).transpose(0, 1, 3, 2, 4).reshape(8, 16, 128, 2048))
    w2t = np.asarray(full["L0_ffn_W2"]).reshape(8, 32, 128, 1024)
    m["moe_w2"] = np.ascontiguousarray(
        w2t.reshape(8, 16, 2, 128, 1024).transpose(0, 1, 3, 2, 4).reshape(8, 16, 128, 2048))
    m["moe_b1c"] = np.ascontiguousarray(np.asarray(full["L0_ffn_b1"]).reshape(8, 32, 128).transpose(0, 2, 1))
    m["moe_b2c"] = np.ascontiguousarray(np.asarray(full["L0_ffn_b2"]).reshape(8, 8, 128).transpose(0, 2, 1))
    m["b0_win"] = tile_fi(np.asarray(full["L0_attn_b_W_in"]))
    m["b0_wq"] = tile_fo(np.asarray(full["L0_attn_b_W_q"]), 128)
    m["b0_wk"] = tile_fo(np.asarray(full["L0_attn_b_W_k"]), 128)
    m["b0_wk_fi"] = tile_fi(np.asarray(full["L0_attn_b_W_k"]))
    m["b0_wv_fi"] = tile_fi(np.asarray(full["L0_attn_b_W_v"]))
    m["b0_wg"] = tile_fo(np.asarray(full["L0_attn_b_W_g"]), 128)
    m["b0_wgkb"] = tile_fi(np.concatenate([np.asarray(full["L0_attn_b_W_gk"]),
                                           np.asarray(full["L0_attn_b_W_b"])], axis=1))
    m["b0_wso"] = tile_fo(np.asarray(full["L0_attn_b_W_so"]), 128)
    m["b0_wout"] = tile_fo(np.asarray(full["L0_attn_b_W_out"]), 128)
    cxw = np.asarray(full["L0_attn_b_conv_x_w"])[:, 0, :]
    czw = np.asarray(full["L0_attn_b_conv_z_w"])[:, 0, :]
    m["b0_convw"] = np.concatenate([tile_fi(cxw), tile_fi(czw)], axis=2)
    m["b0_convb"] = np.concatenate([
        np.asarray(full["L0_attn_b_conv_x_b"]).reshape(8, 128).T,
        np.asarray(full["L0_attn_b_conv_z_b"]).reshape(8, 128).T], axis=1)
    m["b0_convb"] = np.ascontiguousarray(m["b0_convb"])
    m["b0_alog"] = np.asarray(full["L0_attn_b_A_log"]).reshape(4, 1)
    m["b0_dtb"] = np.asarray(full["L0_attn_b_dt_bias"]).reshape(4, 1)
    m["b0_bb"] = np.asarray(full["L0_attn_b_b_b"]).reshape(4, 1)
    m["b0_gnc"] = np.ascontiguousarray(np.asarray(full["L0_attn_b_g_norm_w"]).reshape(3, 128).T)
    return m


def build_nc(blocks):
    nc = bass.Bass("TRN2", target_bir_lowering=False)
    kb = KB(nc)
    c = setup_core(kb)
    if "l1ffn" in blocks:
        setup_l1ffn(c)
    if "attn_a0" in blocks:
        setup_attn_a0(c)
    if "l1attn_a" in blocks:
        setup_l1attn_a(c)
    if "l1attn_b" in blocks:
        setup_l1attn_b(c)
    if "moe" in blocks:
        setup_moe(c)
        emit_moe_prepass(c)
    if "attn_b0" in blocks:
        setup_attn_b0(c)
        emit_b0_prepass(c)
    for blk in blocks:
        emit_ln(c)
        if blk == "l1ffn":
            emit_l1ffn(c)
        elif blk == "attn_a0":
            emit_attn_a0(c)
        elif blk == "l1attn_a":
            emit_l1attn(c, "a")
        elif blk == "l1attn_b":
            emit_l1attn(c, "b")
        elif blk == "moe":
            emit_moe(c)
        elif blk == "attn_b0":
            emit_attn_b0(c)
        else:
            raise ValueError(blk)
    store_out(c)
    return assemble(kb), c


FULL_CHAIN = ["attn_a0", "moe", "attn_b0", "moe", "l1attn_a", "l1ffn", "l1attn_b", "l1ffn"]


def kernel(**inputs):
    blocks = FULL_CHAIN
    nc, c = build_nc(blocks)
    declared = set(c.kb.inputs.keys())
    in_maps = []
    for i in range(8):
        m = shard_inputs(inputs, i)
        in_maps.append({k: v for k, v in m.items() if k in declared})
        missing = declared - set(in_maps[-1].keys())
        assert not missing, f"missing inputs: {missing}"
    res = run_bass_kernel_spmd(nc, in_maps, core_ids=list(range(8)))
    out = np.stack([res.results[i]["y_fm"].T for i in range(8)], axis=0)
    return np.ascontiguousarray(out.astype(np.float32))


# version 07
# speedup vs baseline: 1.8769x; 1.0173x over previous
import math
import numpy as np
import ml_dtypes

import concourse.bass as bass
import concourse.mybir as mybir
from concourse.bass_utils import run_bass_kernel_spmd

F32 = mybir.dt.float32
BF16 = mybir.dt.bfloat16
AF = mybir.ActivationFunctionType
ALU = mybir.AluOpType
AX = mybir.AxisListType

BF = ml_dtypes.bfloat16

B, T, D = 8, 2048, 1024
NCHUNK = D // 128
CAP = 2560

ENGINES = ("pe", "act", "dve", "sp", "pool")
DMA_ENGINES = {"sp": "spq", "pool": "poolq"}


NDMACLASS = 7
RRDMACLASS = 6


class Op:
    __slots__ = ("engine", "fn", "reads", "writes", "dma", "tag", "seq", "idx",
                 "need_inc", "deps", "dmaclass", "dmapos", "same_dep")

    def __init__(self, engine, fn, reads, writes, dma, tag, seq, idx):
        self.engine = engine
        self.fn = fn
        self.reads = reads
        self.writes = writes
        self.dma = dma
        self.tag = tag
        self.seq = seq
        self.idx = idx
        self.need_inc = dma
        self.deps = []
        self.dmaclass = -1
        self.dmapos = 0
        self.same_dep = False


def _key(x):
    if isinstance(x, str):
        return x
    t = getattr(x, "tensor", x)
    return t.name


class Builder:
    def __init__(self):
        self.ops = {e: [] for e in ENGINES}
        self.seq = 0
        self.res = {}
        self.dmacount = {e: 0 for e in ENGINES}
        self.dmaposc = {}

    def _r(self, k):
        return self.res.setdefault(k, {"w": None, "r": {}})

    def emit(self, engine, fn, reads=(), writes=(), dma=False, tag="", dmaclass=None):
        assert engine in ENGINES
        idx = len(self.ops[engine])
        op = Op(engine, fn, [_key(r) for r in reads], [_key(w) for w in writes],
                dma, tag, self.seq, idx)
        self.seq += 1
        deps = {}

        def add_dep(src):
            if src is None:
                return
            (e, i, sdma) = src
            if e == engine and not sdma:
                op.same_dep = True
                return
            deps[(e, i)] = (e, i, sdma)

        for k in op.reads:
            r = self._r(k)
            add_dep(r["w"])
            if k.startswith("psb"):
                for (e, sdma), i in r["r"].items():
                    add_dep((e, i, sdma))
        for k in op.writes:
            r = self._r(k)
            add_dep(r["w"])
            for (e, sdma), i in r["r"].items():
                add_dep((e, i, sdma))
        op.deps = [d for d in deps.values() if not (d[0] == engine and d[2] == dma and d[1] == idx)]
        if dma:
            if dmaclass is None:
                n = self.dmacount[engine]
                self.dmacount[engine] = n + 1
                dmaclass = n % RRDMACLASS
            op.dmaclass = dmaclass
            pos = self.dmaposc.setdefault(engine, {}).get(dmaclass, 0) + 1
            self.dmaposc[engine][dmaclass] = pos
            op.dmapos = pos
        for k in op.reads:
            r = self._r(k)
            if k.startswith("psb"):
                r["w"] = (engine, idx, dma)
                r["r"] = {}
            cur = r["r"].get((engine, dma))
            if cur is None or cur < idx:
                r["r"][(engine, dma)] = idx
        for k in op.writes:
            r = self._r(k)
            r["w"] = (engine, idx, dma)
            r["r"] = {}
        self.ops[engine].append(op)
        return op

    def haunt(self, newkey, oldkeys):
        r = self._r(newkey)
        for ok in oldkeys:
            o = self.res.get(ok)
            if not o:
                continue
            if o["w"] is not None:
                e, i, sdma = o["w"]
                cur = r["r"].get((e, sdma))
                if cur is None or cur < i:
                    r["r"][(e, sdma)] = i
            for (e, sdma), i in o["r"].items():
                cur = r["r"].get((e, sdma))
                if cur is None or cur < i:
                    r["r"][(e, sdma)] = i

    def finalize(self):
        for e in ENGINES:
            for op in self.ops[e]:
                for (se, si, sdma) in op.deps:
                    self.ops[se][si].need_inc = True
        self.cum = {}
        for e in ENGINES:
            cc, cd = 0, 0
            cums = []
            for op in self.ops[e]:
                if op.need_inc:
                    if op.dma:
                        cd += 16
                    else:
                        cc += 1
                cums.append((cc, cd))
            self.cum[e] = cums

    def wait_value(self, se, si, sdma):
        if sdma:
            return 16 * self.ops[se][si].dmapos
        cc, cd = self.cum[se][si]
        return cc

    def wait_sem(self, se, si, sdma):
        if sdma:
            return f"q{se}{self.ops[se][si].dmaclass}"
        return se

    def stats(self):
        out = {}
        for e in ENGINES:
            n = len(self.ops[e])
            ninc = sum(1 for o in self.ops[e] if o.need_inc)
            nw = sum(len(o.deps) for o in self.ops[e])
            out[e] = (n, ninc, nw)
        return out

    def build(self, nc, sems):
        bld = self

        with nc.Block() as block:
            def run_engine(engobj, ename):
                seen = {}

                def wait(semkey, val):
                    if val <= 0 or seen.get(semkey, 0) >= val:
                        return
                    seen[semkey] = val
                    engobj.wait_ge(sems[semkey], val)

                for op in bld.ops[ename]:
                    if op.same_dep and not op.dma and ename != "pe":
                        engobj.drain()
                    if op.dma and op.dmapos > 1:
                        wait(f"q{ename}{op.dmaclass}", 16 * (op.dmapos - 1))
                    for (se, si, sdma) in op.deps:
                        wait(bld.wait_sem(se, si, sdma), bld.wait_value(se, si, sdma))
                    inst = op.fn(engobj)
                    if op.need_inc:
                        assert inst is not None, f"op {op.tag} on {ename} needs inc but fn returned None"
                        if op.dma:
                            inst.then_inc(sems[f"q{ename}{op.dmaclass}"], 16)
                        else:
                            inst.then_inc(sems[ename], 1)

            if bld.ops["sp"]:
                block.sync(lambda e: run_engine(e, "sp"))
            if bld.ops["pool"]:
                block.gpsimd(lambda e: run_engine(e, "pool"))
            if bld.ops["pe"]:
                block.tensor(lambda e: run_engine(e, "pe"))
            if bld.ops["act"]:
                block.scalar(lambda e: run_engine(e, "act"))
            if bld.ops["dve"]:
                block.vector(lambda e: run_engine(e, "dve"))


SBUF_ARENA_BASE = 16512
SBUF_ARENA_END = 229300


class Arena:

    def __init__(self, nc, builder, base=SBUF_ARENA_BASE, end=SBUF_ARENA_END):
        self.nc = nc
        self.b = builder
        self.live = {}
        self.free_l = [(base, end - base)]
        self.ghosts = []
        self.hwm = 0

    def alloc(self, name, shape, dtype, align=64, high=False):
        per_part = int(np.prod(shape[1:])) * mybir.dt.size(dtype)
        per_part = (per_part + align - 1) // align * align
        ranges = enumerate(self.free_l)
        if high:
            ranges = reversed(list(ranges))
        for i, (off, sz) in ranges:
            hoff = (off + sz - per_part) // align * align
            if sz >= per_part and (not high or hoff >= off):
                if high:
                    tail = off + sz - (hoff + per_part)
                    self.free_l[i] = (off, hoff - off)
                    if tail:
                        self.free_l.append((hoff + per_part, tail))
                        self.free_l.sort()
                    off = hoff
                else:
                    self.free_l[i] = (off + per_part, sz - per_part)
                break
        else:
            raise MemoryError(f"arena OOM for {name}: {per_part}B; free={self.free_l}")
        t = self.nc.alloc_sbuf_tensor_at(name, list(shape), dtype, offset=off)
        self.live[name] = (off, per_part, t)
        self.hwm = max(self.hwm, sum(v[1] for v in self.live.values()))
        olds = [g[2] for g in self.ghosts if g[0] < off + per_part and off < g[0] + g[1]]
        if olds:
            self.b.haunt(t.name, olds)
        return t

    def free(self, *tensors):
        for t in tensors:
            name = None
            for k, (off, sz, h) in self.live.items():
                if h is t or k == t:
                    name = k
                    break
            assert name is not None, f"free of unknown tensor {t}"
            off, sz, h = self.live.pop(name)
            self.free_l.append((off, sz))
            self.ghosts.append((off, sz, h.name))
        self.free_l.sort()
        out = []
        for off, sz in self.free_l:
            if out and out[-1][0] + out[-1][1] == off:
                out[-1] = (out[-1][0], out[-1][1] + sz)
            else:
                out.append((off, sz))
        self.free_l = [tuple(x) for x in out]




def rope_tables_l0():
    d = D
    i = np.arange(d // 2, dtype=np.float64)
    theta = np.power(10000.0, -2.0 * (i - 1.0) / d)
    t = np.arange(T, dtype=np.float64)
    ang = t[None, :] * theta[:, None]
    cos = np.cos(ang)
    sin = np.sin(ang)
    C = np.empty((d, T), np.float32)
    S = np.empty((d, T), np.float32)
    C[0::2] = cos
    C[1::2] = cos
    S[0::2] = sin
    S[1::2] = -sin
    return C, S


def rope_tables_l1():
    dqk = 64
    inv = 1.0 / (10000.0 ** (np.arange(0, dqk, 2, dtype=np.float64) / dqk))
    t = np.arange(T, dtype=np.float64)
    ang = t[None, :] * inv[:, None]
    cos = np.cos(ang).astype(np.float32)
    sin = np.sin(ang).astype(np.float32)
    C2 = np.empty((128, T), np.float32)
    S2 = np.empty((128, T), np.float32)
    for h in range(2):
        b = h * 64
        C2[b:b + 32] = cos
        C2[b + 32:b + 64] = cos
        S2[b:b + 32] = -sin
        S2[b + 32:b + 64] = sin
    return C2, S2


def pair_swap_matrix():
    P = np.zeros((128, 128), np.float32)
    for i in range(0, 128, 2):
        P[i, i + 1] = 1.0
        P[i + 1, i] = 1.0
    return P


def swap32_matrix():
    P = np.zeros((128, 128), np.float32)
    for h in range(2):
        b = h * 64
        for i in range(32):
            P[b + i, b + 32 + i] = 1.0
            P[b + 32 + i, b + i] = 1.0
    return P


def host_constants(core_id):
    c = {}
    c["c_ident_bf"] = np.eye(128, dtype=np.float32).astype(BF)
    c["c_ident_f32"] = np.eye(128, dtype=np.float32)
    c["c_ones_km"] = np.ones((128, 128), np.float32).astype(BF)
    c["c_ones_row"] = np.ones((1, 128), np.float32).astype(BF)
    c["c_ones_row_f32"] = np.ones((1, 128), np.float32)
    c["c_utri"] = np.triu(np.ones((128, 128), np.float32)).astype(BF)
    c["c_causal"] = np.tril(np.ones((128, 128), np.float32)).astype(BF)
    c["c_eidx"] = np.tile(np.arange(8, dtype=np.float32), (128, 1))
    sel = np.zeros((8, 8 * 128), np.float32)
    for e in range(8):
        sel[e, e * 128:(e + 1) * 128] = 1.0
    c["c_sel8"] = sel.astype(BF)
    pref = np.zeros((8, 128), np.float32)
    pref[:core_id, :] = 1.0
    c["c_prefmask"] = pref
    C0, S0 = rope_tables_l0()
    c["c_rope0_c"] = C0
    c["c_rope0_s"] = S0
    C2, S2 = rope_tables_l1()
    c["c_rope1_c"] = C2
    c["c_rope1_s"] = S2
    c["c_pswap"] = pair_swap_matrix().astype(BF)
    c["c_swap32"] = swap32_matrix().astype(BF)
    m = np.zeros((128, 128), np.float32)
    mi = np.zeros((128, 128), np.float32)
    for hb in range(2):
        b = hb * 64
        for i in range(64):
            for j in range(64):
                if j < i:
                    m[b + i, b + j] = 1.0
                if j <= i:
                    mi[b + i, b + j] = 1.0
    c["c_dmask_strict"] = m
    c["c_dmask_incl"] = mi
    cz = np.zeros((128, 4, 512), np.float32)
    for j in range(4):
        for kk in range(128):
            cz[kk, j, :] = (128 * j + kk <= np.arange(512)).astype(np.float32)
    c["c_caus512"] = cz.astype(BF)
    s4 = np.zeros((4, 512), np.float32)
    for h in range(4):
        s4[h, 128 * h:128 * (h + 1)] = 1.0
    c["c_sel4"] = s4
    jj = np.arange(128)
    c["c_umask_s"] = (jj[:, None] < jj[None, :]).astype(np.float32)
    c["c_umask_i"] = (jj[:, None] <= jj[None, :]).astype(np.float32)
    c["c_lmask_s"] = (jj[:, None] > jj[None, :]).astype(np.float32)
    return c



class KB:

    def __init__(self, nc):
        self.nc = nc
        self.b = Builder()
        self.ar = Arena(nc, self.b)
        self.uid = 0
        self.inputs = {}
        self.psum = []
        for i in range(8):
            t = nc.alloc_psum_tensor(f"psb{i}", [128, 512], F32)
            self.psum.append(t)
        self._pb = 0

    def nid(self, prefix):
        self.uid += 1
        return f"{prefix}_{self.uid}"

    def din(self, name, shape, dtype=F32):
        t = self.nc.dram_tensor(name, list(shape), dtype, kind="ExternalInput")
        ap = t.ap()
        self.inputs[name] = ap
        return ap

    def dout(self, name, shape, dtype=F32):
        t = self.nc.dram_tensor(name, list(shape), dtype, kind="ExternalOutput")
        return t.ap()

    def bank(self):
        t = self.psum[self._pb]
        self._pb = (self._pb + 1) % 8
        return t

    def banks(self, n):
        assert n <= 8
        out = [self.psum[(self._pb + i) % 8] for i in range(n)]
        self._pb = (self._pb + n) % 8
        return out

    DEBUG = False

    def dbg(self, name, ap_or_handle, reads):
        if not KB.DEBUG:
            return
        ap = ap_or_handle if hasattr(ap_or_handle, "tensor") else ap_or_handle.ap()
        shape = list(ap.shape)
        t = self.nc.dram_tensor("dbg_" + name, shape, ap.tensor.dtype, kind="ExternalOutput")
        self.dma("sp", t.ap(), ap, tag="dbg_" + name, extra_reads=reads)

    def dma(self, engine, out_ap, in_ap, tag="dma", extra_reads=(), slow_ok=False, dmaclass=None):
        self.b.emit(engine,
                    lambda e, o=out_ap, i=in_ap, s=slow_ok:
                    e.dma_start(out=o, in_=i, allow_slow_non_contiguous=s),
                    reads=[in_ap] + list(extra_reads), writes=[out_ap], dma=True, tag=tag,
                    dmaclass=dmaclass)

    def mm_group(self, bankT, terms, out_slices=None, tag="mm"):
        n = len(terms)
        for j, (o, l, r) in enumerate(terms):
            st = (j == 0)
            sp = (j == n - 1)
            self.b.emit("pe",
                        lambda e, o=o, l=l, r=r, st=st, sp=sp: e.matmul(o, l, r, start=st, stop=sp),
                        reads=[l, r], writes=[bankT], tag=tag)

    def ev(self, engine, fn, reads, writes, tag="ev"):
        self.b.emit(engine, fn, reads=reads, writes=writes, tag=tag)


def ceil_div(a, b):
    return (a + b - 1) // b


def tile_fi(w):
    r, n = w.shape
    return np.ascontiguousarray(w.reshape(r // 128, 128, n).transpose(1, 0, 2))


def tile_fo(w, ncols=128):
    r, n = w.shape
    nch = r // 128
    ns = n // ncols
    a = w.reshape(nch, 128, ns, ncols).transpose(2, 1, 0, 3)
    return np.ascontiguousarray(a.reshape(ns, 128, nch * ncols))



class Ctx:
    pass


def setup_core(kb):
    nc = kb.nc
    c = Ctx()
    c.kb = kb
    ar = kb.ar

    c.X = [ar.alloc(f"Xc{i}", [128, T], F32) for i in range(NCHUNK)]
    c.NB = None
    c.xspill = kb.nc.dram_tensor("xspill", [D, T], F32, kind="Internal").ap()

    def const_in(name, shape, dtype=F32):
        ap = kb.din(name, shape, dtype)
        t = ar.alloc("s" + name, shape, dtype)
        kb.dma("sp", t.ap(), ap, tag=name)
        return t

    c.ident_bf = const_in("c_ident_bf", [128, 128], BF16)
    c.ident_f32 = const_in("c_ident_f32", [128, 128], F32)
    c.ones_km = const_in("c_ones_km", [128, 128], BF16)
    c.ones_row = const_in("c_ones_row", [1, 128], BF16)
    c.ones_row_f32 = const_in("c_ones_row_f32", [1, 128], F32)
    c.utri = const_in("c_utri", [128, 128], BF16)
    c.causal = const_in("c_causal", [128, 128], BF16)
    c.eidx = const_in("c_eidx", [128, 8], F32)
    c.sel8 = const_in("c_sel8", [8, 8 * 128], BF16)
    c.prefmask = const_in("c_prefmask", [8, 128], F32)
    c.pswap = const_in("c_pswap", [128, 128], BF16)
    c.swap32 = const_in("c_swap32", [128, 128], BF16)
    c.dmask_strict = const_in("c_dmask_strict", [128, 128], F32)
    c.dmask_incl = const_in("c_dmask_incl", [128, 128], F32)
    c.caus512_in = kb.din("c_caus512", [128, 4, 512], BF16)

    c.x_in = kb.din("x_fm", [D, T], F32)
    for i in range(NCHUNK):
        kb.dma("sp", c.X[i].ap()[:, :], c.x_in[128 * i:128 * (i + 1), :], tag=f"ldX{i}")

    c.out = kb.dout("y_fm", [D, T], F32)
    return c


def store_out(c):
    kb = c.kb
    for i in range(NCHUNK):
        kb.dma("sp", c.out[128 * i:128 * (i + 1), :], c.X[i].ap()[:, :], tag=f"stX{i}")


def emit_ln(c, tag=""):
    kb = c.kb
    ar = kb.ar
    eps = 1e-5
    c.NB = [ar.alloc(kb.nid("NB"), [128, T], BF16, high=True) for _ in range(NCHUNK)]
    xb = [ar.alloc(kb.nid("ln_xb"), [128, 512], BF16) for _ in range(2)]
    xq = [ar.alloc(kb.nid("ln_xq"), [128, 512], BF16) for _ in range(2)]
    mu = ar.alloc(kb.nid("ln_mu"), [128, 512], F32)
    sv = ar.alloc(kb.nid("ln_sv"), [128, 512], F32)
    tf = ar.alloc(kb.nid("ln_tf"), [128, 512], F32)
    for tc in range(4):
        ts = slice(512 * tc, 512 * (tc + 1))
        bs = kb.bank()
        bq = kb.bank()
        for ci in range(NCHUNK):
            xbt, xqt = xb[ci % 2], xq[ci % 2]
            kb.ev("dve", lambda e, o=xbt.ap()[:, :], i=c.X[ci].ap()[:, ts]: e.tensor_copy(o, i),
                  reads=[c.X[ci]], writes=[xbt], tag="lncast")
            kb.ev("act", lambda e, o=xqt.ap()[:, :], i=c.X[ci].ap()[:, ts]: e.activation(o, i, AF.Square),
                  reads=[c.X[ci]], writes=[xqt], tag="lnsq")
            st, sp = (ci == 0), (ci == NCHUNK - 1)
            kb.b.emit("pe", lambda e, o=bs.ap()[:, :], l=c.ones_km.ap()[:, :], r=xbt.ap()[:, :], st=st, sp=sp:
                      e.matmul(o, l, r, start=st, stop=sp),
                      reads=[c.ones_km, xbt], writes=[bs], tag="lnmmS")
            kb.b.emit("pe", lambda e, o=bq.ap()[:, :], l=c.ones_km.ap()[:, :], r=xqt.ap()[:, :], st=st, sp=sp:
                      e.matmul(o, l, r, start=st, stop=sp),
                      reads=[c.ones_km, xqt], writes=[bq], tag="lnmmQ")
        kb.ev("dve", lambda e, o=mu.ap()[:, :], i=bs.ap()[:, :]: e.tensor_scalar_mul(o, i, 1.0 / D),
              reads=[bs], writes=[mu], tag="lnmu")
        kb.ev("act", lambda e, o=sv.ap()[:, :], i=mu.ap()[:, :]: e.activation(o, i, AF.Square),
              reads=[mu], writes=[sv], tag="lnmsq")
        kb.ev("dve", lambda e, o=sv.ap()[:, :], i=bq.ap()[:, :]:
              e.scalar_tensor_tensor(o, i, 1.0 / D, sv.ap()[:, :], ALU.mult, ALU.subtract),
              reads=[bq, sv], writes=[sv], tag="lnvar")
        kb.ev("dve", lambda e, o=sv.ap()[:, :]: e.tensor_scalar_add(o, sv.ap()[:, :], eps),
              reads=[sv], writes=[sv], tag="lneps")
        kb.ev("act", lambda e, o=sv.ap()[:, :]: e.activation(o, sv.ap()[:, :], AF.Sqrt),
              reads=[sv], writes=[sv], tag="lnsd")
        kb.ev("dve", lambda e, o=sv.ap()[:, :]: e.reciprocal(o, sv.ap()[:, :]),
              reads=[sv], writes=[sv], tag="lnrec")
        for ci in range(NCHUNK):
            kb.ev("dve", lambda e, o=tf.ap()[:, :], x=c.X[ci].ap()[:, ts]: e.tensor_sub(o, x, mu.ap()[:, :]),
                  reads=[c.X[ci], mu], writes=[tf], tag="lnsub")
            kb.ev("dve", lambda e, o=c.NB[ci].ap()[:, ts]: e.tensor_mul(o, tf.ap()[:, :], sv.ap()[:, :]),
                  reads=[tf, sv], writes=[c.NB[ci]], tag="lnmul")
    ar.free(xb[0], xb[1], xq[0], xq[1], mu, sv, tf)


def setup_l1ffn(c):
    kb = c.kb
    c.l1f_w1t_t = kb.din("l1f_w1t_t", [32, 128, 1024], F32)
    c.l1f_w1n = kb.din("l1f_w1n", [4096, 1024], F32)
    c.l1f_w2t = kb.din("l1f_w2t", [4096, 1024], F32)
    c.l1f_w2n = kb.din("l1f_w2n", [1024, 4096], F32)
    c.l1f_b1c = kb.din("l1f_b1c", [128, 32], F32)
    c.l1f_b2c = kb.din("l1f_b2c", [128, 8], F32)
    ar = kb.ar
    c.l1f_a1 = ar.alloc("l1f_a1", [128, 32], F32)
    c.l1f_a2 = ar.alloc("l1f_a2", [128, 8], F32)
    c.l1f_b1s = ar.alloc("l1f_b1s", [128, 32], F32)
    c.l1f_b2s = ar.alloc("l1f_b2s", [128, 8], F32)
    kb.dma("sp", c.l1f_b1s.ap()[:, :], c.l1f_b1c[:, :], tag="l1fb1")
    kb.dma("sp", c.l1f_b2s.ap()[:, :], c.l1f_b2c[:, :], tag="l1fb2")
    c.l1f_alphas_done = False


def emit_l1ffn_alphas(c):
    kb = c.kb
    ar = kb.ar
    wb = [ar.alloc(kb.nid("l1f_wn"), [128, 4096], F32) for _ in range(2)]
    for k in range(32):
        t = wb[k % 2]
        kb.dma("sp", t.ap()[:, 0:1024], c.l1f_w1n[128 * k:128 * (k + 1), :], tag="l1f_an1")
        kb.ev("dve", lambda e, o=c.l1f_a1.ap()[:, k:k + 1], i=t.ap()[:, 0:1024]:
              e.tensor_reduce(o, i, AX.X, ALU.add, apply_absolute_value=True),
              reads=[t], writes=[c.l1f_a1], tag="l1f_red1")
    for k in range(8):
        t = wb[k % 2]
        kb.dma("sp", t.ap()[:, :], c.l1f_w2n[128 * k:128 * (k + 1), :], tag="l1f_an2")
        kb.ev("dve", lambda e, o=c.l1f_a2.ap()[:, k:k + 1], i=t.ap()[:, :]:
              e.tensor_reduce(o, i, AX.X, ALU.add, apply_absolute_value=True),
              reads=[t], writes=[c.l1f_a2], tag="l1f_red2")
    kb.ev("dve", lambda e: e.tensor_scalar_mul(c.l1f_a1.ap()[:, :], c.l1f_a1.ap()[:, :], 1.0 / 1024.0),
          reads=[c.l1f_a1], writes=[c.l1f_a1], tag="l1f_as1")
    kb.ev("dve", lambda e: e.tensor_scalar_mul(c.l1f_a2.ap()[:, :], c.l1f_a2.ap()[:, :], 1.0 / 4096.0),
          reads=[c.l1f_a2], writes=[c.l1f_a2], tag="l1f_as2")
    ar.free(wb[0], wb[1])


def emit_l1ffn(c):
    kb = c.kb
    ar = kb.ar
    if not c.l1f_alphas_done:
        emit_l1ffn_alphas(c)
        c.l1f_alphas_done = True
    sg1 = [ar.alloc(kb.nid("l1f_sg1"), [128, 1024], BF16) for _ in range(2)]
    sg2 = [ar.alloc(kb.nid("l1f_sg2"), [128, 1024], BF16) for _ in range(2)]
    hbuf = ar.alloc(kb.nid("l1f_h"), [128, 32, 1024], BF16)
    yt = ar.alloc(kb.nid("l1f_yt"), [128, 512], F32)
    for tcg in range(2):
        for j in range(32):
            t = sg1[j % 2]
            kb.dma("pool", t.ap()[:, :], c.l1f_w1t_t[j, :, :], tag="l1f_w1l")
            kb.ev("act", lambda e, a=t.ap()[:, :]: e.activation(a, a, AF.Sign),
                  reads=[t], writes=[t], tag="l1f_sgn1")
            for tc2 in range(2):
                bk = kb.bank()
                ts = slice(1024 * tcg + 512 * tc2, 1024 * tcg + 512 * (tc2 + 1))
                for ci in range(8):
                    kb.b.emit("pe", lambda e, o=bk.ap()[:, :], l=t.ap()[:, 128 * ci:128 * (ci + 1)],
                              r=c.NB[ci].ap()[:, ts], st=(ci == 0), sp=(ci == 7):
                              e.matmul(o, l, r, start=st, stop=sp),
                              reads=[t, c.NB[ci]], writes=[bk], tag="l1f_hmm")
                kb.ev("act", lambda e, o=hbuf.ap()[:, j, 512 * tc2:512 * (tc2 + 1)], i=bk.ap()[:, :],
                      s=c.l1f_a1.ap()[:, j:j + 1], b=c.l1f_b1s.ap()[:, j:j + 1]:
                      e.activation(o, i, AF.Gelu, bias=b, scale=s),
                      reads=[bk, c.l1f_a1, c.l1f_b1s], writes=[hbuf], tag="l1f_hev")
        if tcg == 1:
            for t_ in c.NB:
                ar.free(t_)
            c.NB = None
        for fo2h in range(2):
            bks = kb.banks(8)
            for k2 in range(32):
                t = sg2[k2 % 2]
                kb.dma("pool", t.ap()[:, :], c.l1f_w2t[128 * k2:128 * (k2 + 1), :], tag="l1f_w2l")
                kb.ev("act", lambda e, a=t.ap()[:, :]: e.activation(a, a, AF.Sign),
                      reads=[t], writes=[t], tag="l1f_sgn2")
                for tc2 in range(2):
                    ts = slice(1024 * tcg + 512 * tc2, 1024 * tcg + 512 * (tc2 + 1))
                    for f2 in range(4):
                        fo2 = fo2h * 4 + f2
                        bk = bks[tc2 * 4 + f2]
                        kb.b.emit("pe", lambda e, o=bk.ap()[:, :], l=t.ap()[:, 128 * fo2:128 * (fo2 + 1)],
                                  r=hbuf.ap()[:, k2, 512 * tc2:512 * (tc2 + 1)],
                                  st=(k2 == 0), sp=(k2 == 31):
                                  e.matmul(o, l, r, start=st, stop=sp),
                                  reads=[t, hbuf], writes=[bk], tag="l1f_ymm")
            for tc2 in range(2):
                ts = slice(1024 * tcg + 512 * tc2, 1024 * tcg + 512 * (tc2 + 1))
                for f2 in range(4):
                    fo2 = fo2h * 4 + f2
                    bk = bks[tc2 * 4 + f2]
                    kb.ev("act", lambda e, o=yt.ap()[:, :], i=bk.ap()[:, :],
                          s=c.l1f_a2.ap()[:, fo2:fo2 + 1], b=c.l1f_b2s.ap()[:, fo2:fo2 + 1]:
                          e.activation(o, i, AF.Identity, bias=b, scale=s),
                          reads=[bk, c.l1f_a2, c.l1f_b2s], writes=[yt], tag="l1f_yev")
                    kb.ev("dve", lambda e, o=c.X[fo2].ap()[:, ts], i=yt.ap()[:, :]:
                          e.tensor_add(o, o, i),
                          reads=[yt, c.X[fo2]], writes=[c.X[fo2]], tag="l1f_xadd")
    ar.free(sg1[0], sg1[1], sg2[0], sg2[1], hbuf, yt)


def setup_attn_a0(c):
    kb = c.kb
    c.a0_wqT_t = kb.din("a0_wqT_t", [2, 128, 4096], F32)
    c.a0_wkT_t = kb.din("a0_wkT_t", [2, 128, 4096], F32)
    c.a0_wvT_t = kb.din("a0_wvT_t", [2, 128, 4096], F32)
    c.a0_ropec = kb.din("c_rope0_c", [D, T], F32)
    c.a0_ropes = kb.din("c_rope0_s", [D, T], F32)


A0_STOP = 9
A0_ROPE = True


def emit_attn_a0(c):
    kb = c.kb
    ar = kb.ar
    scale = 1.0 / math.sqrt(D)
    for ci in range(NCHUNK):
        kb.dma("sp", c.xspill[128 * ci:128 * (ci + 1), :], c.X[ci].ap()[:, :], tag="a0_spill")
    for ci in range(NCHUNK):
        ar.free(c.X[ci])
    c.X = None

    if A0_STOP <= 0:
        c.X = [ar.alloc(f"Xc{i}_r", [128, T], F32, high=True) for i in range(NCHUNK)]
        for ci in range(NCHUNK):
            kb.dma("sp", c.X[ci].ap()[:, :], c.xspill[128 * ci:128 * (ci + 1), :], tag="a0_xld")
        for t_ in c.NB:
            ar.free(t_)
        c.NB = None
        return

    qr = [ar.alloc(kb.nid("a0_q"), [128, T], BF16) for _ in range(NCHUNK)]
    kr = [ar.alloc(kb.nid("a0_k"), [128, T], BF16) for _ in range(NCHUNK)]
    vt = ar.alloc(kb.nid("a0_v"), [128, 16, 1024], BF16)
    wbuf = [ar.alloc(kb.nid("a0_w"), [128, 8, 512], BF16) for _ in range(2)]
    ctab = [ar.alloc(kb.nid("a0_ct"), [128, 512], F32) for _ in range(2)]
    stab = [ar.alloc(kb.nid("a0_st"), [128, 512], F32) for _ in range(2)]
    t1 = ar.alloc(kb.nid("a0_t1"), [128, 512], F32)
    t2 = ar.alloc(kb.nid("a0_t2"), [128, 512], F32)
    sqb = [ar.alloc(kb.nid("a0_sq"), [128, 512], BF16) for _ in range(2)]

    for (wt, dst) in ((c.a0_wqT_t, qr), (c.a0_wkT_t, kr), (c.a0_wvT_t, None)):
        for jh in range(2):
            wb = wbuf[jh % 2]
            kb.dma("pool", wb.ap()[:, :, :], wt[jh, :, :], tag="a0_wl")
            if dst is not None:
                for f in range(4):
                    fo = jh * 4 + f
                    for tc in range(4):
                        ts = slice(512 * tc, 512 * (tc + 1))
                        bk = kb.bank()
                        for ci in range(NCHUNK):
                            kb.b.emit("pe", lambda e, o=bk.ap()[:, :], l=wb.ap()[:, ci, 128 * f:128 * (f + 1)],
                                      r=c.NB[ci].ap()[:, ts], st=(ci == 0), sp=(ci == 7):
                                      e.matmul(o, l, r, start=st, stop=sp),
                                      reads=[wb, c.NB[ci]], writes=[bk], tag="a0_qkmm")
                        if not A0_ROPE:
                            kb.ev("act", lambda e, o=dst[fo].ap()[:, ts], i=bk.ap()[:, :]:
                                  e.activation(o, i, AF.Identity),
                                  reads=[bk], writes=[dst[fo]], tag="a0_qc0")
                            continue
                        qtile = sqb[(f * 4 + tc) % 2]
                        kb.ev("act", lambda e, o=qtile.ap()[:, :], i=bk.ap()[:, :]: e.activation(o, i, AF.Identity),
                              reads=[bk], writes=[qtile], tag="a0_qc")
                        bk2 = kb.bank()
                        kb.b.emit("pe", lambda e, o=bk2.ap()[:, :], l=c.pswap.ap()[:, :], r=qtile.ap()[:, :]:
                                  e.matmul(o, l, r, start=True, stop=True),
                                  reads=[c.pswap, qtile], writes=[bk2], tag="a0_swap")
                        ct, st_ = ctab[tc % 2], stab[tc % 2]
                        kb.dma("sp", ct.ap()[:, :], c.a0_ropec[128 * fo:128 * (fo + 1), ts], tag="a0_ctl")
                        kb.dma("sp", st_.ap()[:, :], c.a0_ropes[128 * fo:128 * (fo + 1), ts], tag="a0_stl")
                        kb.ev("dve", lambda e, o=t1.ap()[:, :], i=bk.ap()[:, :], m=ct.ap()[:, :]:
                              e.tensor_mul(o, i, m), reads=[bk, ct], writes=[t1], tag="a0_rc")
                        kb.ev("dve", lambda e, o=t2.ap()[:, :], i=bk2.ap()[:, :], m=st_.ap()[:, :]:
                              e.tensor_mul(o, i, m), reads=[bk2, st_], writes=[t2], tag="a0_rs")
                        kb.ev("dve", lambda e, o=dst[fo].ap()[:, ts]: e.tensor_add(o, t1.ap()[:, :], t2.ap()[:, :]),
                              reads=[t1, t2], writes=[dst[fo]], tag="a0_radd")
            else:
                for tt in range(16):
                    bk = kb.bank()
                    for ci in range(NCHUNK):
                        kb.b.emit("pe", lambda e, o=bk.ap()[:, :], l=c.NB[ci].ap()[:, 128 * tt:128 * (tt + 1)],
                                  r=wb.ap()[:, ci, :], st=(ci == 0), sp=(ci == 7):
                                  e.matmul(o, l, r, start=st, stop=sp),
                                  reads=[wb, c.NB[ci]], writes=[bk], tag="a0_vmm")
                    kb.ev("act", lambda e, o=vt.ap()[:, tt, 512 * jh:512 * (jh + 1)], i=bk.ap()[:, :]:
                          e.activation(o, i, AF.Identity),
                          reads=[bk], writes=[vt], tag="a0_vev")
    for t_ in c.NB:
        ar.free(t_)
    c.NB = None
    ar.free(wbuf[0], wbuf[1], ctab[0], ctab[1], stab[0], stab[1], t1, t2)
    if A0_STOP <= 1:
        c.X = [ar.alloc(f"Xc{i}_r", [128, T], F32, high=True) for i in range(NCHUNK)]
        for ci in range(NCHUNK):
            kb.dma("sp", c.X[ci].ap()[:, :], c.xspill[128 * ci:128 * (ci + 1), :], tag="a0_xld")
        return

    qn = ar.alloc(kb.nid("a0_qn"), [128, T], F32)
    kn = ar.alloc(kb.nid("a0_kn"), [128, T], F32)
    mneg = ar.alloc(kb.nid("a0_mneg"), [128, T], BF16)
    kmax = ar.alloc(kb.nid("a0_kmax"), [128, 5], F32)
    for (src_, dst) in ((qr, qn), (kr, kn)):
        for tc in range(4):
            ts = slice(512 * tc, 512 * (tc + 1))
            bk = kb.bank()
            for ci in range(NCHUNK):
                sq = sqb[ci % 2]
                kb.ev("act", lambda e, o=sq.ap()[:, :], i=src_[ci].ap()[:, ts]: e.activation(o, i, AF.Square),
                      reads=[src_[ci]], writes=[sq], tag="a0_nsq")
                kb.b.emit("pe", lambda e, o=bk.ap()[:, :], l=c.ones_km.ap()[:, :], r=sq.ap()[:, :],
                          st=(ci == 0), sp=(ci == 7): e.matmul(o, l, r, start=st, stop=sp),
                          reads=[c.ones_km, sq], writes=[bk], tag="a0_nmm")
            kb.ev("act", lambda e, o=dst.ap()[:, ts], i=bk.ap()[:, :]: e.activation(o, i, AF.Sqrt),
                  reads=[bk], writes=[dst], tag="a0_nsqrt")
            if dst is kn:
                kb.ev("dve", lambda e, o=kmax.ap()[:, tc:tc + 1], i=dst.ap()[:, ts]:
                      e.tensor_reduce(o, i, AX.X, ALU.max), reads=[dst], writes=[kmax], tag="a0_kmax")
    kb.ev("dve", lambda e, o=kmax.ap()[:, 4:5], i=kmax.ap()[:, 0:4]:
          e.tensor_reduce(o, i, AX.X, ALU.max), reads=[kmax], writes=[kmax], tag="a0_kmax2")
    for tc in range(4):
        ts = slice(512 * tc, 512 * (tc + 1))
        kb.ev("dve", lambda e, o=mneg.ap()[:, ts], i=qn.ap()[:, ts]:
              e.tensor_scalar(o, i, kmax.ap()[:, 4:5], -1.0, ALU.mult, ALU.mult),
              reads=[qn, kmax], writes=[mneg], tag="a0_mneg")
    ar.free(qn, kn, sqb[0], sqb[1], kmax)
    if A0_STOP <= 2:
        c.X = [ar.alloc(f"Xc{i}_r", [128, T], F32, high=True) for i in range(NCHUNK)]
        for ci in range(NCHUNK):
            kb.dma("sp", c.X[ci].ap()[:, :], c.xspill[128 * ci:128 * (ci + 1), :], tag="a0_xld")
        return

    pts = [ar.alloc(kb.nid("a0_pt"), [128, 16, 512], BF16) for _ in range(4)]
    dinv = ar.alloc(kb.nid("a0_dinv"), [128, 4, 512], F32)
    for qc in range(4):
        qs = slice(512 * qc, 512 * (qc + 1))
        bdn = kb.bank()
        for kt in range(16):
            bst = kb.bank()
            if bst is bdn:
                bst = kb.bank()
            for ci in range(NCHUNK):
                kb.b.emit("pe", lambda e, o=bst.ap()[:, :], l=kr[ci].ap()[:, 128 * kt:128 * (kt + 1)],
                          r=qr[ci].ap()[:, qs], st=(ci == 0), sp=False:
                          e.matmul(o, l, r, start=st, stop=sp),
                          reads=[kr[ci], qr[ci]], writes=[bst], tag="a0_smm")
            kb.b.emit("pe", lambda e, o=bst.ap()[:, :], l=c.ones_row.ap()[:, :], r=mneg.ap()[0:1, qs]:
                      e.matmul(o, l, r, start=False, stop=True),
                      reads=[c.ones_row, mneg], writes=[bst], tag="a0_smneg")
            kb.ev("act", lambda e, o=pts[qc].ap()[:, kt, :], i=bst.ap()[:, :]:
                  e.activation(o, i, AF.Exp, scale=scale),
                  reads=[bst], writes=[pts[qc]], tag="a0_exp")
            kb.b.emit("pe", lambda e, o=bdn.ap()[:, :], l=c.ones_km.ap()[:, :], r=pts[qc].ap()[:, kt, :],
                      st=(kt == 0), sp=(kt == 15): e.matmul(o, l, r, start=st, stop=sp),
                      reads=[c.ones_km, pts[qc]], writes=[bdn], tag="a0_dmm")
        kb.ev("dve", lambda e, o=dinv.ap()[:, qc, :], i=bdn.ap()[:, :]: e.reciprocal(o, i),
              reads=[bdn], writes=[dinv], tag="a0_dinv")
    ar.free(mneg)
    for ci in range(NCHUNK):
        ar.free(qr[ci], kr[ci])
    if A0_STOP <= 3:
        c.X = [ar.alloc(f"Xc{i}_r", [128, T], F32, high=True) for i in range(NCHUNK)]
        for ci in range(NCHUNK):
            kb.dma("sp", c.X[ci].ap()[:, :], c.xspill[128 * ci:128 * (ci + 1), :], tag="a0_xld")
        return

    newX = []
    ot = ar.alloc(kb.nid("a0_ot"), [128, 512], F32)
    for dvc in range(NCHUNK):
        xt = ar.alloc(f"Xc{dvc}_r", [128, T], F32, high=True)
        newX.append(xt)
        kb.dma("sp", xt.ap()[:, :], c.xspill[128 * dvc:128 * (dvc + 1), :], tag="a0_xld")
        for qc in range(4):
            qs = slice(512 * qc, 512 * (qc + 1))
            bo = kb.bank()
            for kt in range(16):
                kb.b.emit("pe", lambda e, o=bo.ap()[:, :], l=vt.ap()[:, kt, 128 * dvc:128 * (dvc + 1)],
                          r=pts[qc].ap()[:, kt, :], st=(kt == 0), sp=(kt == 15):
                          e.matmul(o, l, r, start=st, stop=sp),
                          reads=[vt, pts[qc]], writes=[bo], tag="a0_omm")
            kb.ev("dve", lambda e, o=ot.ap()[:, :], i=bo.ap()[:, :], dv=dinv.ap()[:, qc, :]:
                  e.tensor_mul(o, i, dv),
                  reads=[bo, dinv], writes=[ot], tag="a0_onorm")
            kb.ev("dve", lambda e, o=xt.ap()[:, qs], i=ot.ap()[:, :]: e.tensor_add(o, o, i),
                  reads=[ot, xt], writes=[xt], tag="a0_xadd")
    c.X = newX
    for p in pts:
        ar.free(p)
    ar.free(dinv, ot, vt)


def setup_l1attn_a(c):
    kb = c.kb
    c.aa_wq_t = kb.din("aa_wq_t", [2, 128, 4096], F32)
    c.aa_wkv_t = kb.din("aa_wkv_t", [4, 128, 4096], F32)
    c.aa_wo_t = kb.din("aa_wo_t", [2, 128, 4096], F32)
    c.aa_bqc = kb.din("aa_bqc", [128, 8], F32)
    c.aa_bkc = kb.din("aa_bkc", [128, 8], F32)
    c.aa_bvr = kb.din("aa_bvr", [1, 1024], F32)
    c.aa_boc = kb.din("aa_boc", [128, 8], F32)
    c.aa_ropec = kb.din("c_rope1_c", [128, T], F32)
    c.aa_ropes = kb.din("c_rope1_s", [128, T], F32)


def setup_l1attn_b(c):
    kb = c.kb
    c.ab_wq_t = kb.din("ab_wq_t", [2, 128, 4096], F32)
    c.ab_wk_t = kb.din("ab_wk_t", [2, 128, 4096], F32)
    c.ab_wv_t = kb.din("ab_wv_t", [2, 128, 4096], F32)
    c.ab_wo_t = kb.din("ab_wo_t", [2, 128, 4096], F32)


def emit_l1attn(c, variant):
    kb = c.kb
    ar = kb.ar
    scale = 0.125
    isa = variant == "a"
    for ci in range(NCHUNK):
        kb.dma("sp", c.xspill[128 * ci:128 * (ci + 1), :], c.X[ci].ap()[:, :], tag="l1_spill")
    for ci in range(NCHUNK):
        ar.free(c.X[ci])
    c.X = None
    qfm = [ar.alloc(kb.nid("l1q"), [128, T], BF16) for _ in range(NCHUNK)]
    kfm = [ar.alloc(kb.nid("l1k"), [128, T], BF16) for _ in range(NCHUNK)]
    vtm = ar.alloc(kb.nid("l1v"), [128, 16, 1024], BF16)
    wbuf = [ar.alloc(kb.nid("l1w"), [128, 4096], BF16) for _ in range(2)]
    bsb = None
    if isa:
        bsb = {}
        for nm, ap_ in (("bq", c.aa_bqc), ("bk", c.aa_bkc), ("bo", c.aa_boc)):
            t_ = ar.alloc(kb.nid("l1b" + nm), [128, 8], F32)
            kb.dma("sp", t_.ap(), ap_, tag="l1bias")
            bsb[nm] = t_
        bvr = ar.alloc(kb.nid("l1bvr"), [1, 1024], BF16)
        kb.dma("pool", bvr.ap(), c.aa_bvr, tag="l1bvr")
        ctab = ar.alloc(kb.nid("l1ct"), [128, T], F32)
        stab = ar.alloc(kb.nid("l1st"), [128, T], F32)
        kb.dma("sp", ctab.ap(), c.aa_ropec, tag="l1ctl")
        kb.dma("sp", stab.ap(), c.aa_ropes, tag="l1stl")
        t1 = ar.alloc(kb.nid("l1t1"), [128, 512], F32)
        t2 = ar.alloc(kb.nid("l1t2"), [128, 512], F32)
        qtl = ar.alloc(kb.nid("l1qt"), [128, 512], BF16)

    def proj_qk(wt_ap, jslices, dst, biascols):
        for jj, jsl in enumerate(jslices):
            wb = wbuf[jj % 2]
            kb.dma("pool", wb.ap(), wt_ap[jsl, :, :], tag="l1wl")
            for f in range(4):
                fo = jj * 4 + f
                for tc in range(4):
                    ts = slice(512 * tc, 512 * (tc + 1))
                    bk = kb.bank()
                    for ci in range(NCHUNK):
                        kb.b.emit("pe", lambda e, o=bk.ap()[:, :], l=wb.ap()[:, 512 * ci + 128 * f:512 * ci + 128 * (f + 1)],
                                  r=c.NB[ci].ap()[:, ts], st=(ci == 0), sp=(ci == 7):
                                  e.matmul(o, l, r, start=st, stop=sp),
                                  reads=[wb, c.NB[ci]], writes=[bk], tag="l1qkmm")
                    if not isa:
                        kb.ev("act", lambda e, o=dst[fo].ap()[:, ts], i=bk.ap()[:, :]:
                              e.activation(o, i, AF.Identity),
                              reads=[bk], writes=[dst[fo]], tag="l1qkev")
                        continue
                    kb.ev("act", lambda e, o=qtl.ap()[:, :], i=bk.ap()[:, :], b=biascols.ap()[:, fo:fo + 1]:
                          e.activation(o, i, AF.Identity, bias=b),
                          reads=[bk, biascols], writes=[qtl], tag="l1qkbias")
                    bk2 = kb.bank()
                    kb.b.emit("pe", lambda e, o=bk2.ap()[:, :], l=c.swap32.ap()[:, :], r=qtl.ap()[:, :]:
                              e.matmul(o, l, r, start=True, stop=True),
                              reads=[c.swap32, qtl], writes=[bk2], tag="l1swap")
                    kb.ev("dve", lambda e, o=t1.ap()[:, :], i=qtl.ap()[:, :], m=ctab.ap()[:, ts]:
                          e.tensor_mul(o, i, m), reads=[qtl, ctab], writes=[t1], tag="l1rc")
                    kb.ev("dve", lambda e, o=t2.ap()[:, :], i=bk2.ap()[:, :], m=stab.ap()[:, ts]:
                          e.tensor_mul(o, i, m), reads=[bk2, stab], writes=[t2], tag="l1rs")
                    kb.ev("dve", lambda e, o=dst[fo].ap()[:, ts]: e.tensor_add(o, t1.ap()[:, :], t2.ap()[:, :]),
                          reads=[t1, t2], writes=[dst[fo]], tag="l1radd")

    if isa:
        proj_qk(c.aa_wq_t, (0, 1), qfm, bsb["bq"])
        proj_qk(c.aa_wkv_t, (0, 1), kfm, bsb["bk"])
        for jj, jsl in enumerate((2, 3)):
            wb = wbuf[jj % 2]
            kb.dma("pool", wb.ap(), c.aa_wkv_t[jsl, :, :], tag="l1wvl")
            for tt in range(16):
                bk = kb.bank()
                for ci in range(NCHUNK):
                    kb.b.emit("pe", lambda e, o=bk.ap()[:, :], l=c.NB[ci].ap()[:, 128 * tt:128 * (tt + 1)],
                              r=wb.ap()[:, 512 * ci:512 * (ci + 1)], st=(ci == 0), sp=False:
                              e.matmul(o, l, r, start=st, stop=sp),
                              reads=[wb, c.NB[ci]], writes=[bk], tag="l1vmm")
                kb.b.emit("pe", lambda e, o=bk.ap()[:, :], l=c.ones_row.ap()[:, :], r=bvr.ap()[0:1, 512 * jj:512 * (jj + 1)]:
                          e.matmul(o, l, r, start=False, stop=True),
                          reads=[c.ones_row, bvr], writes=[bk], tag="l1vbias")
                kb.ev("act", lambda e, o=vtm.ap()[:, tt, 512 * jj:512 * (jj + 1)], i=bk.ap()[:, :]:
                      e.activation(o, i, AF.Identity),
                      reads=[bk], writes=[vtm], tag="l1vev")
    else:
        proj_qk(c.ab_wq_t, (0, 1), qfm, None)
        proj_qk(c.ab_wk_t, (0, 1), kfm, None)
        for jj in range(2):
            wb = wbuf[jj % 2]
            kb.dma("pool", wb.ap(), c.ab_wv_t[jj, :, :], tag="l1wvl")
            for tt in range(16):
                bk = kb.bank()
                for ci in range(NCHUNK):
                    kb.b.emit("pe", lambda e, o=bk.ap()[:, :], l=c.NB[ci].ap()[:, 128 * tt:128 * (tt + 1)],
                              r=wb.ap()[:, 512 * ci:512 * (ci + 1)], st=(ci == 0), sp=(ci == 7):
                              e.matmul(o, l, r, start=st, stop=sp),
                              reads=[wb, c.NB[ci]], writes=[bk], tag="l1vmm")
                kb.ev("act", lambda e, o=vtm.ap()[:, tt, 512 * jj:512 * (jj + 1)], i=bk.ap()[:, :]:
                      e.activation(o, i, AF.Identity),
                      reads=[bk], writes=[vtm], tag="l1vev")

    for t_ in c.NB:
        ar.free(t_)
    c.NB = None
    if isa:
        ar.free(ctab, stab, t1, t2, qtl, bvr)

    sq = ar.alloc(kb.nid("l1sq"), [128, 512], BF16)
    qmax2 = ar.alloc(kb.nid("l1qm"), [128, 16], F32)
    kmax2 = ar.alloc(kb.nid("l1km"), [128, 16], F32)
    mh = ar.alloc(kb.nid("l1mh"), [128, 16], F32)
    for (src_, dst) in ((qfm, qmax2), (kfm, kmax2)):
        for ch in range(NCHUNK):
            for tc in range(4):
                ts = slice(512 * tc, 512 * (tc + 1))
                kb.ev("act", lambda e, o=sq.ap()[:, :], i=src_[ch].ap()[:, ts]: e.activation(o, i, AF.Square),
                      reads=[src_[ch]], writes=[sq], tag="l1nsq")
                for hb in range(2):
                    h = 2 * ch + hb
                    bk = kb.bank()
                    kb.b.emit("pe", lambda e, o=bk.ap()[:, :], l=c.ones_km.ap()[64 * hb:64 * (hb + 1), :],
                              r=sq.ap()[64 * hb:64 * (hb + 1), :]:
                              e.matmul(o, l, r, start=True, stop=True),
                              reads=[c.ones_km, sq], writes=[bk], tag="l1nmm")
                    if tc == 0:
                        kb.ev("dve", lambda e, o=dst.ap()[:, h:h + 1], i=bk.ap()[:, :]:
                              e.tensor_reduce(o, i, AX.X, ALU.max),
                              reads=[bk], writes=[dst], tag="l1nred")
                    else:
                        kb.ev("dve", lambda e, o=mh.ap()[:, h:h + 1], i=bk.ap()[:, :]:
                              e.tensor_reduce(o, i, AX.X, ALU.max),
                              reads=[bk], writes=[mh], tag="l1nred2")
                        kb.ev("dve", lambda e, o=dst.ap()[:, h:h + 1], a=dst.ap()[:, h:h + 1], b=mh.ap()[:, h:h + 1]:
                              e.tensor_max(o, a, b),
                              reads=[dst, mh], writes=[dst], tag="l1nmax")
    kb.ev("act", lambda e: e.activation(qmax2.ap()[:, :], qmax2.ap()[:, :], AF.Sqrt),
          reads=[qmax2], writes=[qmax2], tag="l1nsqrt")
    kb.ev("act", lambda e: e.activation(kmax2.ap()[:, :], kmax2.ap()[:, :], AF.Sqrt),
          reads=[kmax2], writes=[kmax2], tag="l1nsqrt2")
    kb.ev("dve", lambda e: e.tensor_mul(mh.ap()[:, :], qmax2.ap()[:, :], kmax2.ap()[:, :]),
          reads=[qmax2, kmax2], writes=[mh], tag="l1mh")
    kb.ev("dve", lambda e: e.tensor_scalar_mul(mh.ap()[:, :], mh.ap()[:, :], -scale),
          reads=[mh], writes=[mh], tag="l1mh2")
    ar.free(sq)

    ofm = [ar.alloc(kb.nid("l1o"), [128, T], BF16) for _ in range(NCHUNK)]
    ptsb = [ar.alloc(kb.nid("l1pt"), [128, 16, 512], BF16) for _ in range(2)]
    dinv = ar.alloc(kb.nid("l1dv"), [128, 512], F32)
    caus512 = ar.alloc(kb.nid("l1cz"), [128, 4, 512], BF16)
    kb.dma("sp", caus512.ap(), c.caus512_in, tag="l1czl")

    def spass(h, qc, pts):
        ch = h // 2
        hs = slice(64 * (h % 2), 64 * (h % 2 + 1))
        qs = slice(512 * qc, 512 * (qc + 1))
        for kt in range(4 * qc + 4):
            bst = kb.bank()
            kb.b.emit("pe", lambda e, o=bst.ap()[:, :], l=kfm[ch].ap()[hs, 128 * kt:128 * (kt + 1)],
                      r=qfm[ch].ap()[hs, qs]:
                      e.matmul(o, l, r, start=True, stop=True),
                      reads=[kfm[ch], qfm[ch]], writes=[bst], tag="l1smm")
            kb.ev("act", lambda e, o=pts.ap()[:, kt, :], i=bst.ap()[:, :], b=mh.ap()[:, h:h + 1]:
                  e.activation(o, i, AF.Exp, bias=b, scale=scale),
                  reads=[bst, mh], writes=[pts], tag="l1exp")
            if kt >= 4 * qc:
                j = kt - 4 * qc
                kb.ev("dve", lambda e, o=pts.ap()[:, kt, :], m=caus512.ap()[:, j, :]:
                      e.tensor_mul(o, o, m),
                      reads=[pts, caus512], writes=[pts], tag="l1mask")

    def opass(h, qc, pts):
        ch = h // 2
        hs = slice(64 * (h % 2), 64 * (h % 2 + 1))
        qs = slice(512 * qc, 512 * (qc + 1))
        nkt = 4 * qc + 4
        bdn = kb.bank()
        for kt in range(nkt):
            kb.b.emit("pe", lambda e, o=bdn.ap()[:, :], l=c.ones_km.ap()[:, :], r=pts.ap()[:, kt, :],
                      st=(kt == 0), sp=(kt == nkt - 1): e.matmul(o, l, r, start=st, stop=sp),
                      reads=[c.ones_km, pts], writes=[bdn], tag="l1dmm")
        kb.ev("dve", lambda e, o=dinv.ap()[:, :], i=bdn.ap()[:, :]: e.reciprocal(o, i),
              reads=[bdn], writes=[dinv], tag="l1dinv")
        bo = kb.bank()
        for kt in range(nkt):
            kb.b.emit("pe", lambda e, o=bo.ap()[hs, :], l=vtm.ap()[:, kt, 64 * h:64 * (h + 1)],
                      r=pts.ap()[:, kt, :], st=(kt == 0), sp=(kt == nkt - 1):
                      e.matmul(o, l, r, start=st, stop=sp),
                      reads=[vtm, pts], writes=[bo], tag="l1omm")
        kb.ev("dve", lambda e, o=ofm[ch].ap()[hs, qs], i=bo.ap()[hs, :], dv=dinv.ap()[hs, :]:
              e.tensor_mul(o, i, dv),
              reads=[bo, dinv], writes=[ofm[ch]], tag="l1oev")

    hqs = [(h, qc) for h in range(16) for qc in range(4)]
    for i, (h, qc) in enumerate(hqs):
        spass(h, qc, ptsb[i % 2])
        if i > 0:
            opass(*hqs[i - 1], ptsb[(i - 1) % 2])
    opass(*hqs[-1], ptsb[(len(hqs) - 1) % 2])
    ar.free(ptsb[0], ptsb[1], dinv, vtm, qmax2, kmax2, mh, caus512)
    for ci in range(NCHUNK):
        ar.free(qfm[ci], kfm[ci])

    newX = [None] * NCHUNK
    wo_t = c.aa_wo_t if isa else c.ab_wo_t
    for jj in range(2):
        wb = wbuf[jj % 2]
        kb.dma("pool", wb.ap(), wo_t[jj, :, :], tag="l1wol")
        for f in range(4):
            fo = jj * 4 + f
            xt = ar.alloc(kb.nid(f"Xl{fo}"), [128, T], F32)
            newX[fo] = xt
            kb.dma("sp", xt.ap()[:, :], c.xspill[128 * fo:128 * (fo + 1), :], tag="l1_xld")
            for tc in range(4):
                ts = slice(512 * tc, 512 * (tc + 1))
                bk = kb.bank()
                for ci in range(NCHUNK):
                    kb.b.emit("pe", lambda e, o=bk.ap()[:, :], l=wb.ap()[:, 512 * ci + 128 * f:512 * ci + 128 * (f + 1)],
                              r=ofm[ci].ap()[:, ts], st=(ci == 0), sp=(ci == 7):
                              e.matmul(o, l, r, start=st, stop=sp),
                              reads=[wb, ofm[ci]], writes=[bk], tag="l1pmm")
                if isa:
                    kb.ev("dve", lambda e, o=xt.ap()[:, ts], i=bk.ap()[:, :], b=bsb["bo"].ap()[:, fo:fo + 1]:
                          e.scalar_tensor_tensor(o, i, b, o, ALU.add, ALU.add),
                          reads=[bk, bsb["bo"], xt], writes=[xt], tag="l1xadd")
                else:
                    kb.ev("dve", lambda e, o=xt.ap()[:, ts], i=bk.ap()[:, :]:
                          e.tensor_add(o, o, i),
                          reads=[bk, xt], writes=[xt], tag="l1xadd")
    c.X = newX
    for ci in range(NCHUNK):
        ar.free(ofm[ci])
    ar.free(wbuf[0], wbuf[1])
    if isa:
        ar.free(bsb["bq"], bsb["bk"], bsb["bo"])


def setup_moe(c):
    kb = c.kb
    nc = kb.nc
    c.moe_gate_t = kb.din("moe_gate_t", [128, 8, 8], F32)
    c.moe_gateb = kb.din("moe_gateb", [1, 8], F32)
    c.moe_w1 = kb.din("moe_w1", [8, 16, 128, 2048], F32)
    c.moe_w2 = kb.din("moe_w2", [8, 16, 128, 2048], F32)
    c.moe_b1c = kb.din("moe_b1c", [8, 128, 32], F32)
    c.moe_b2c = kb.din("moe_b2c", [8, 128, 8], F32)
    c.moe_w1bf = nc.dram_tensor("moe_w1bf", [8, 16, 128, 2048], BF16, kind="Internal").ap()
    c.moe_w2bf = nc.dram_tensor("moe_w2bf", [8, 16, 128, 2048], BF16, kind="Internal").ap()
    c.coll_in = nc.dram_tensor("moe_cin", [1, 16], F32, kind="Internal").ap()
    c.coll_out = nc.dram_tensor("moe_cout", [8, 16], F32, kind="Internal").ap()
    c.moe_call = 0
    ar = kb.ar
    c.gate_sb = ar.alloc("moe_gate_sb", [128, 8, 8], BF16)
    c.gateb_sb = ar.alloc("moe_gateb_sb", [1, 8], BF16)
    kb.dma("pool", c.gate_sb.ap(), c.moe_gate_t, tag="moe_gl")
    kb.dma("pool", c.gateb_sb.ap(), c.moe_gateb, tag="moe_gbl")


def emit_moe_prepass(c, experts):
    kb = c.kb
    for e in experts:
        kb.dma("pool", c.moe_w1bf[e], c.moe_w1[e], tag="moe_pp1", dmaclass=6)
        kb.dma("pool", c.moe_w2bf[e], c.moe_w2[e], tag="moe_pp2", dmaclass=6)


def emit_moe(c):
    kb = c.kb
    ar = kb.ar
    E = 8
    logit = ar.alloc(kb.nid("moe_lg"), [128, 128], F32)
    ohbuf = ar.alloc(kb.nid("moe_oh"), [128, 256], BF16)
    cum = ar.alloc(kb.nid("moe_cum"), [128, 256], F32)
    pw = ar.alloc(kb.nid("moe_pw"), [128, 32], F32)
    sm = ar.alloc(kb.nid("moe_sm"), [128, 4], F32)
    top8 = ar.alloc(kb.nid("moe_t8"), [128, 8], F32)
    idx8 = ar.alloc(kb.nid("moe_i8"), [128, 8], mybir.dt.uint32)
    idx8f = ar.alloc(kb.nid("moe_i8f"), [128, 8], F32)
    rt = ar.alloc(kb.nid("moe_rt"), [128, 256], F32)
    csel = ar.alloc(kb.nid("moe_cs"), [128, 32], F32)
    psel = ar.alloc(kb.nid("moe_ps"), [128, 32], F32)
    cnt = ar.alloc(kb.nid("moe_cnt"), [8, 256], F32)
    wexp = ar.alloc(kb.nid("moe_we"), [128, 128], F32)
    wT = ar.alloc(kb.nid("moe_wT"), [8, T], BF16)

    blg = kb.bank()
    for ti in range(16):
        tsl = slice(128 * ti, 128 * (ti + 1))
        for ci in range(NCHUNK):
            kb.b.emit("pe", lambda e, o=blg.ap()[:, 8 * ti:8 * (ti + 1)], l=c.NB[ci].ap()[:, tsl],
                      r=c.gate_sb.ap()[:, ci, :], st=(ti == 0 and ci == 0), sp=False:
                      e.matmul(o, l, r, start=st, stop=sp),
                      reads=[c.NB[ci], c.gate_sb], writes=[blg], tag="moe_gmm")
        kb.b.emit("pe", lambda e, o=blg.ap()[:, 8 * ti:8 * (ti + 1)], l=c.ones_row.ap()[:, :],
                  r=c.gateb_sb.ap()[0:1, :], st=False, sp=(ti == 15):
                  e.matmul(o, l, r, start=False, stop=sp),
                  reads=[c.ones_row, c.gateb_sb], writes=[blg], tag="moe_gbm")
    kb.ev("dve", lambda e: e.tensor_copy(logit.ap()[:, :], blg.ap()[:, 0:128]),
          reads=[blg], writes=[logit], tag="moe_gev")
    for ti in range(16):
        L8 = logit.ap()[:, 8 * ti:8 * (ti + 1)]
        kb.ev("dve", lambda e, L8=L8: e.reduce_max(sm.ap()[:, 0:1], L8, AX.X),
              reads=[logit], writes=[sm], tag="moe_smx")
        kb.ev("dve", lambda e, L8=L8: e.tensor_scalar(L8, L8, sm.ap()[:, 0:1], None, ALU.subtract),
              reads=[logit, sm], writes=[logit], tag="moe_ssub")
        kb.ev("act", lambda e, L8=L8: e.activation(L8, L8, AF.Exp, accum_out=sm.ap()[:, 1:2]),
              reads=[logit], writes=[logit, sm], tag="moe_sexp")
        kb.ev("dve", lambda e: e.reciprocal(sm.ap()[:, 2:3], sm.ap()[:, 1:2]),
              reads=[sm], writes=[sm], tag="moe_srec")
        kb.ev("dve", lambda e, L8=L8: e.tensor_scalar(L8, L8, sm.ap()[:, 2:3], None, ALU.mult),
              reads=[logit, sm], writes=[logit], tag="moe_smul")
        kb.ev("dve", lambda e, L8=L8: e.max(top8.ap()[:, :], L8),
              reads=[logit], writes=[top8], tag="moe_top8")
        kb.ev("dve", lambda e, L8=L8: e.max_index(idx8.ap()[:, :], top8.ap()[:, :], L8),
              reads=[logit, top8], writes=[idx8], tag="moe_idx8")
        kb.ev("dve", lambda e: e.tensor_copy(idx8f.ap()[:, 0:2], idx8.ap()[:, 0:2]),
              reads=[idx8], writes=[idx8f], tag="moe_idxc")
        kb.ev("dve", lambda e, o=pw.ap()[:, 2 * ti:2 * ti + 2]: e.tensor_copy(o, top8.ap()[:, 0:2]),
              reads=[top8], writes=[pw], tag="moe_pwc")
        for cc in range(2):
            kb.ev("dve", lambda e, o=ohbuf.ap()[:, 16 * ti + 8 * cc:16 * ti + 8 * (cc + 1)], ic=cc:
                  e.tensor_scalar(o, c.eidx.ap()[:, :], idx8f.ap()[:, ic:ic + 1], None, ALU.is_equal),
                  reads=[c.eidx, idx8f], writes=[ohbuf], tag="moe_oh")
    bcm = kb.bank()
    for ti in range(16):
        for j in range(ti + 1):
            lj = c.utri if j == ti else c.ones_km
            kb.b.emit("pe", lambda e, o=bcm.ap()[:, 16 * ti:16 * (ti + 1)], l=lj.ap()[:, :],
                      r=ohbuf.ap()[:, 16 * j:16 * (j + 1)],
                      st=(ti == 0 and j == 0), sp=(ti == 15 and j == 15):
                      e.matmul(o, l, r, start=st, stop=sp),
                      reads=[lj, ohbuf], writes=[bcm], tag="moe_cmm")
    kb.ev("dve", lambda e: e.tensor_copy(cum.ap()[:, :], bcm.ap()[:, 0:256]),
          reads=[bcm], writes=[cum], tag="moe_cev")
    kb.dma("sp", c.coll_in[:, :], cum.ap()[127:128, 240:256], tag="moe_cst")
    kb.b.emit("pool", lambda e: e.collective_compute("AllGather", ALU.bypass,
                                                     [[0, 1, 2, 3, 4, 5, 6, 7]],
                                                     [c.coll_in[:, :]], [c.coll_out[:, :]]),
              reads=[c.coll_in], writes=[c.coll_out], tag="moe_ag")
    kb.dma("sp", cnt.ap()[:, 0:16], c.coll_out[:, :], tag="moe_cld")
    for k in range(1, 16):
        kb.ev("dve", lambda e, o=cnt.ap()[:, 16 * k:16 * (k + 1)]: e.tensor_copy(o, cnt.ap()[:, 0:16]),
              reads=[cnt], writes=[cnt], tag="moe_ctile")
    bpf = kb.bank()
    kb.b.emit("pe", lambda e: e.matmul(bpf.ap()[:, 0:256], c.prefmask.ap()[:, :], cnt.ap()[:, :],
                                       start=True, stop=True),
              reads=[c.prefmask, cnt], writes=[bpf], tag="moe_pfmm")
    kb.ev("dve", lambda e: e.tensor_mul(rt.ap()[:, :], cum.ap()[:, :], ohbuf.ap()[:, :]),
          reads=[cum, ohbuf], writes=[rt], tag="moe_cm")
    kb.ev("dve", lambda e: e.tensor_reduce(csel.ap()[:, :],
                                           rt.ap()[:, :].rearrange("p (a b) -> p a b", b=8), AX.X, ALU.add),
          reads=[rt], writes=[csel], tag="moe_cr")
    kb.ev("dve", lambda e: e.tensor_mul(rt.ap()[:, :], bpf.ap()[:, 0:256], ohbuf.ap()[:, :]),
          reads=[bpf, ohbuf], writes=[rt], tag="moe_pm")
    kb.ev("dve", lambda e: e.tensor_reduce(psel.ap()[:, :],
                                           rt.ap()[:, :].rearrange("p (a b) -> p a b", b=8), AX.X, ALU.add),
          reads=[rt], writes=[psel], tag="moe_pr")
    kb.ev("dve", lambda e: e.scalar_tensor_tensor(csel.ap()[:, :], csel.ap()[:, :], -1.0,
                                                  psel.ap()[:, :], ALU.add, ALU.add),
          reads=[csel, psel], writes=[csel], tag="moe_pos")
    kb.ev("dve", lambda e: e.tensor_single_scalar(csel.ap()[:, :], csel.ap()[:, :], float(CAP), ALU.is_lt),
          reads=[csel], writes=[csel], tag="moe_keep")
    kb.ev("dve", lambda e: e.tensor_mul(pw.ap()[:, :], pw.ap()[:, :], csel.ap()[:, :]),
          reads=[pw, csel], writes=[pw], tag="moe_w")
    for ti in range(16):
        for cc in range(2):
            dst = wexp.ap()[:, 8 * ti:8 * (ti + 1)]
            ohs = ohbuf.ap()[:, 16 * ti + 8 * cc:16 * ti + 8 * (cc + 1)]
            if cc == 0:
                kb.ev("dve", lambda e, dst=dst, ohs=ohs, w=pw.ap()[:, 2 * ti:2 * ti + 1]:
                      e.tensor_scalar(dst, ohs, w, None, ALU.mult),
                      reads=[ohbuf, pw], writes=[wexp], tag="moe_wx0")
            else:
                kb.ev("dve", lambda e, dst=dst, ohs=ohs, w=pw.ap()[:, 2 * ti + 1:2 * ti + 2]:
                      e.scalar_tensor_tensor(dst, ohs, w, dst, ALU.mult, ALU.add),
                      reads=[ohbuf, pw, wexp], writes=[wexp], tag="moe_wx1")
    for ti in range(16):
        bt = kb.bank()
        kb.b.emit("pe", lambda e, o=bt.ap()[0:8, 0:128], i=wexp.ap()[:, 8 * ti:8 * (ti + 1)]:
                  e.transpose(o, i, c.ident_f32.ap()[:, :]),
                  reads=[wexp, c.ident_f32], writes=[bt], tag="moe_wtt")
        kb.ev("dve", lambda e, o=wT.ap()[:, 128 * ti:128 * (ti + 1)], i=bt.ap()[0:8, 0:128]:
              e.tensor_copy(o, i),
              reads=[bt], writes=[wT], tag="moe_wtev")
    ar.free(logit, cum, pw, sm, top8, idx8, idx8f, rt, csel, psel, cnt, wexp, ohbuf)

    NW1, NW2 = 2, 4
    hbufs = [ar.alloc(kb.nid("moe_h"), [128, 8, 1024], BF16) for _ in range(4)]
    wred = ar.alloc(kb.nid("moe_wr"), [128, 1024], BF16)
    w1sl = [ar.alloc(kb.nid("moe_w1s"), [128, 2048], BF16) for _ in range(NW1)]
    w2sl = [ar.alloc(kb.nid("moe_w2s"), [128, 2048], BF16) for _ in range(NW2)]
    b1c = ar.alloc(kb.nid("moe_b1s"), [128, 32], F32)
    b2c = ar.alloc(kb.nid("moe_b2s"), [128, 8], F32)
    nw1 = nw2 = 0
    for ex in range(E):
        if c.moe_call == 0 and c.moe_spread and ex < 6:
            emit_moe_prepass(c, [ex + 2])
        if c.moe_call == 0 and c.b0_spread and ex < 7:
            emit_b0_prepass(c, [ex])
        kb.dma("sp", b1c.ap()[:, :], c.moe_b1c[ex, :, :], tag="moe_b1l")
        kb.dma("sp", b2c.ap()[:, :], c.moe_b2c[ex, :, :], tag="moe_b2l")
        for tcg in range(2):
            tb = 1024 * tcg
            for qc2 in range(2):
                bw = kb.bank()
                kb.b.emit("pe", lambda e, o=bw.ap()[:, :], l=c.sel8.ap()[:, 128 * ex:128 * (ex + 1)],
                          r=wT.ap()[:, tb + 512 * qc2:tb + 512 * (qc2 + 1)]:
                          e.matmul(o, l, r, start=True, stop=True),
                          reads=[c.sel8, wT], writes=[bw], tag="moe_wrmm")
                kb.ev("dve", lambda e, o=wred.ap()[:, 512 * qc2:512 * (qc2 + 1)], i=bw.ap()[:, :]:
                      e.tensor_copy(o, i),
                      reads=[bw], writes=[wred], tag="moe_wrev")
            for jj in range(16):
                wsl = w1sl[nw1 % NW1]
                nw1 += 1
                kb.dma("sp" if nw1 % 2 == 0 else "pool", wsl.ap()[:, :], c.moe_w1bf[ex, jj, :, :],
                       tag="moe_w1l", extra_reads=["moe_w1bf"])
                for u in range(2):
                    j = 2 * jj + u
                    for tc2 in range(2):
                        ts = slice(tb + 512 * tc2, tb + 512 * (tc2 + 1))
                        bk = kb.bank()
                        for ci in range(NCHUNK):
                            kb.b.emit("pe", lambda e, o=bk.ap()[:, :],
                                      l=wsl.ap()[:, 1024 * u + 128 * ci:1024 * u + 128 * (ci + 1)],
                                      r=c.NB[ci].ap()[:, ts], st=(ci == 0), sp=(ci == 7):
                                      e.matmul(o, l, r, start=st, stop=sp),
                                      reads=[wsl, c.NB[ci]], writes=[bk], tag="moe_hmm")
                        hb = hbufs[j // 8]
                        kb.ev("act", lambda e, o=hb.ap()[:, j % 8, 512 * tc2:512 * (tc2 + 1)],
                              i=bk.ap()[:, :], b=b1c.ap()[:, j:j + 1]:
                              e.activation(o, i, AF.Relu, bias=b),
                              reads=[bk, b1c], writes=[hb], tag="moe_hev")
            for tc2 in range(2):
                ts = slice(tb + 512 * tc2, tb + 512 * (tc2 + 1))
                hs = slice(512 * tc2, 512 * (tc2 + 1))
                bks = kb.banks(8)
                for jj in range(16):
                    wsl = w2sl[nw2 % NW2]
                    nw2 += 1
                    kb.dma("sp" if nw2 % 2 == 0 else "pool", wsl.ap()[:, :], c.moe_w2bf[ex, jj, :, :],
                           tag="moe_w2l", extra_reads=["moe_w2bf"])
                    for u in range(2):
                        k2 = 2 * jj + u
                        hb = hbufs[k2 // 8]
                        for fo2 in range(8):
                            kb.b.emit("pe", lambda e, o=bks[fo2].ap()[:, :],
                                      l=wsl.ap()[:, 1024 * u + 128 * fo2:1024 * u + 128 * (fo2 + 1)],
                                      r=hb.ap()[:, k2 % 8, hs], st=(k2 == 0), sp=(k2 == 31):
                                      e.matmul(o, l, r, start=st, stop=sp),
                                      reads=[wsl, hb], writes=[bks[fo2]], tag="moe_ymm")
                for fo2 in range(8):
                    bk = bks[fo2]
                    kb.ev("dve", lambda e, o=bk.ap()[:, :], i=bk.ap()[:, :], b=b2c.ap()[:, fo2:fo2 + 1],
                          w=wred.ap()[:, hs]:
                          e.scalar_tensor_tensor(o, i, b, w, ALU.add, ALU.mult),
                          reads=[bk, b2c, wred], writes=[bk], tag="moe_yev")
                    kb.ev("dve", lambda e, o=c.X[fo2].ap()[:, ts], i=bk.ap()[:, :]:
                          e.tensor_add(o, o, i),
                          reads=[bk, c.X[fo2]], writes=[c.X[fo2]], tag="moe_xadd")
    for t_ in c.NB:
        ar.free(t_)
    c.NB = None
    ar.free(wred, *w1sl, *w2sl, *hbufs, b1c, b2c, wT)
    c.moe_call += 1


def setup_attn_b0(c):
    kb = c.kb
    nc = kb.nc
    c.b0_win = kb.din("b0_win", [128, 8, 2048], F32)
    c.b0_wq = kb.din("b0_wq", [6, 128, 1024], F32)
    c.b0_wk = kb.din("b0_wk", [6, 128, 1024], F32)
    c.b0_wk_fi = kb.din("b0_wk_fi", [128, 8, 768], F32)
    c.b0_wv_fi = kb.din("b0_wv_fi", [128, 8, 1536], F32)
    c.b0_wg = kb.din("b0_wg", [12, 128, 1024], F32)
    c.b0_wgkb = kb.din("b0_wgkb", [128, 8, 8], F32)
    c.b0_wso = kb.din("b0_wso", [8, 128, 1536], F32)
    c.b0_wout = kb.din("b0_wout", [8, 128, 2048], F32)
    c.b0_convw = kb.din("b0_convw", [128, 8, 8], F32)
    c.b0_convb = kb.din("b0_convb", [128, 16], F32)
    c.b0_alog = kb.din("b0_alog", [4, 1], F32)
    c.b0_dtb = kb.din("b0_dtb", [4, 1], F32)
    c.b0_bb = kb.din("b0_bb", [4, 1], F32)
    c.b0_gnc = kb.din("b0_gnc", [128, 3], F32)
    c.c_sel4 = kb.din("c_sel4", [4, 512], F32)
    c.c_umask_s = kb.din("c_umask_s", [128, 128], F32)
    c.c_umask_i = kb.din("c_umask_i", [128, 128], F32)
    c.c_lmask_s = kb.din("c_lmask_s", [128, 128], F32)
    c.b0_wqb_b = nc.dram_tensor("b0_wqb", [6, 128, 1024], BF16, kind="Internal").ap()
    c.b0_wkb_b = nc.dram_tensor("b0_wkb", [6, 128, 1024], BF16, kind="Internal").ap()
    c.b0_wkfib_b = nc.dram_tensor("b0_wkfib", [128, 8, 768], BF16, kind="Internal").ap()
    c.b0_wvfib_b = nc.dram_tensor("b0_wvfib", [128, 8, 1536], BF16, kind="Internal").ap()
    c.b0_wgb_b = nc.dram_tensor("b0_wgb", [12, 128, 1024], BF16, kind="Internal").ap()
    c.b0_wsob_b = nc.dram_tensor("b0_wsob", [8, 128, 1536], BF16, kind="Internal").ap()
    c.b0_woutb_b = nc.dram_tensor("b0_woutb", [8, 128, 2048], BF16, kind="Internal").ap()
    c.zspill = nc.dram_tensor("zspill", [128, 8, 2048], BF16, kind="Internal").ap()


def emit_b0_prepass(c, idxs):
    kb = c.kb
    pairs = ((c.b0_wqb_b, c.b0_wq), (c.b0_wkb_b, c.b0_wk), (c.b0_wkfib_b, c.b0_wk_fi),
             (c.b0_wvfib_b, c.b0_wv_fi), (c.b0_wgb_b, c.b0_wg), (c.b0_wsob_b, c.b0_wso),
             (c.b0_woutb_b, c.b0_wout))
    for i in idxs:
        dst, src_ = pairs[i]
        kb.dma("pool", dst, src_, tag="b0_pp", dmaclass=6)


def emit_attn_b0(c):
    kb = c.kb
    ar = kb.ar
    H, DK, DV = 4, 192, 384
    SEG, NSEG, CH = 512, 4, 128

    for ci in range(NCHUNK):
        kb.dma("sp", c.xspill[128 * ci:128 * (ci + 1), :], c.X[ci].ap()[:, :], tag="b0_spill")
    for ci in range(NCHUNK):
        ar.free(c.X[ci])
    c.X = None
    winb = ar.alloc(kb.nid("b0_winb"), [128, 8, 2048], BF16)
    kb.dma("pool", winb.ap(), c.b0_win, tag="b0_winl")
    xcb = ar.alloc(kb.nid("b0_xcb"), [128, 8, T], BF16)
    zcb = ar.alloc(kb.nid("b0_zcb"), [128, 8, T], BF16)
    for half, dst in ((0, xcb), (1, zcb)):
        for f in range(8):
            fo = half * 8 + f
            for tc in range(4):
                ts = slice(512 * tc, 512 * (tc + 1))
                bk = kb.bank()
                for ci in range(NCHUNK):
                    kb.b.emit("pe", lambda e, o=bk.ap()[:, :], l=winb.ap()[:, ci, 128 * fo:128 * (fo + 1)],
                              r=c.NB[ci].ap()[:, ts], st=(ci == 0), sp=(ci == 7):
                              e.matmul(o, l, r, start=st, stop=sp),
                              reads=[winb, c.NB[ci]], writes=[bk], tag="b0_xzmm")
                kb.ev("act", lambda e, o=dst.ap()[:, f, ts], i=bk.ap()[:, :]:
                      e.activation(o, i, AF.Identity),
                      reads=[bk], writes=[dst], tag="b0_xzev")
    for t_ in c.NB:
        ar.free(t_)
    c.NB = None
    ar.free(winb)
    cvw = ar.alloc(kb.nid("b0_cw"), [128, 8, 8], F32)
    cvb = ar.alloc(kb.nid("b0_cb"), [128, 16], F32)
    kb.dma("sp", cvw.ap(), c.b0_convw, tag="b0_cwl")
    kb.dma("sp", cvb.ap(), c.b0_convb, tag="b0_cbl")
    xsb = ar.alloc(kb.nid("b0_xsb"), [128, 8, T], BF16)
    zsb = ar.alloc(kb.nid("b0_zsb"), [128, 8, T], BF16)
    acc = ar.alloc(kb.nid("b0_acc"), [128, T], F32)
    for half, srcb, dstb in ((0, xcb, xsb), (1, zcb, zsb)):
        for f in range(8):
            xa = srcb.ap()
            w0 = cvw.ap()[:, f, 4 * half + 0:4 * half + 1]
            w1 = cvw.ap()[:, f, 4 * half + 1:4 * half + 2]
            w2 = cvw.ap()[:, f, 4 * half + 2:4 * half + 3]
            w3 = cvw.ap()[:, f, 4 * half + 3:4 * half + 4]
            kb.ev("dve", lambda e, o=acc.ap()[:, :], i=xa[:, f, :], s=w1:
                  e.tensor_scalar(o, i, s, None, ALU.mult),
                  reads=[srcb, cvw], writes=[acc], tag="b0_cv1")
            kb.ev("dve", lambda e, o=acc.ap()[:, 1:T], i=xa[:, f, 0:T - 1], s=w0:
                  e.scalar_tensor_tensor(o, i, s, acc.ap()[:, 1:T], ALU.mult, ALU.add),
                  reads=[srcb, cvw, acc], writes=[acc], tag="b0_cv0")
            kb.ev("dve", lambda e, o=acc.ap()[:, 0:T - 1], i=xa[:, f, 1:T], s=w2:
                  e.scalar_tensor_tensor(o, i, s, acc.ap()[:, 0:T - 1], ALU.mult, ALU.add),
                  reads=[srcb, cvw, acc], writes=[acc], tag="b0_cv2")
            kb.ev("dve", lambda e, o=acc.ap()[:, 0:T - 2], i=xa[:, f, 2:T], s=w3:
                  e.scalar_tensor_tensor(o, i, s, acc.ap()[:, 0:T - 2], ALU.mult, ALU.add),
                  reads=[srcb, cvw, acc], writes=[acc], tag="b0_cv3")
            kb.ev("act", lambda e, o=dstb.ap()[:, f, :], i=acc.ap()[:, :],
                  b=cvb.ap()[:, 8 * half + f:8 * half + f + 1]:
                  e.activation(o, i, AF.Silu, bias=b),
                  reads=[acc, cvb], writes=[dstb], tag="b0_cvs")
    kb.dbg("xsb", xsb, [xsb])
    ar.free(xcb, zcb, acc, cvw, cvb)
    kb.dma("sp", c.zspill, zsb.ap(), tag="b0_zsp")
    ar.free(zsb)

    alog = ar.alloc(kb.nid("b0_al"), [4, 1], F32)
    dtb = ar.alloc(kb.nid("b0_dt"), [4, 1], F32)
    bbc = ar.alloc(kb.nid("b0_bbc"), [4, 1], F32)
    gnc = ar.alloc(kb.nid("b0_gn"), [128, 3], F32)
    onescol = ar.alloc(kb.nid("b0_onec"), [128, 1], F32)
    epscol = ar.alloc(kb.nid("b0_epsc"), [128, 1], F32)
    sel4 = ar.alloc(kb.nid("b0_sel4"), [4, 512], F32)
    umask_s = ar.alloc(kb.nid("b0_ums"), [128, 128], F32)
    umask_i = ar.alloc(kb.nid("b0_umi"), [128, 128], F32)
    lmask_s = ar.alloc(kb.nid("b0_lms"), [128, 128], F32)
    wgkb = ar.alloc(kb.nid("b0_wgkbs"), [128, 8, 8], BF16)
    for (t_, ap_) in ((alog, c.b0_alog), (dtb, c.b0_dtb), (bbc, c.b0_bb), (gnc, c.b0_gnc),
                      (sel4, c.c_sel4), (umask_s, c.c_umask_s), (umask_i, c.c_umask_i),
                      (lmask_s, c.c_lmask_s)):
        kb.dma("sp", t_.ap(), ap_, tag="b0c")
    kb.dma("pool", wgkb.ap(), c.b0_wgkb, tag="b0c2")
    kb.ev("dve", lambda e: e.memset(onescol.ap()[:, :], 1.0), reads=[], writes=[onescol], tag="b0_ones")
    kb.ev("dve", lambda e: e.memset(epscol.ap()[:, :], 1e-5), reads=[], writes=[epscol], tag="b0_eps")
    negA = ar.alloc(kb.nid("b0_negA"), [4, 1], F32)
    kb.ev("act", lambda e: e.activation(negA.ap()[:, :], alog.ap()[:, :], AF.Exp),
          reads=[alog], writes=[negA], tag="b0_expA")
    kb.ev("dve", lambda e: e.tensor_scalar_mul(negA.ap()[:, :], negA.ap()[:, :], -1.0),
          reads=[negA], writes=[negA], tag="b0_negA")

    Sa = ar.alloc(kb.nid("b0_Sa"), [128, 4, DV], F32)
    Sb = ar.alloc(kb.nid("b0_Sb"), [128, 2, DV], F32)
    Sab = ar.alloc(kb.nid("b0_Sab"), [128, 4, DV], BF16)
    Sbb = ar.alloc(kb.nid("b0_Sbb"), [128, 2, DV], BF16)
    for t_ in (Sa, Sb, Sab, Sbb):
        kb.ev("dve", lambda e, a=t_.ap()[:, :, :]: e.memset(a, 0.0), reads=[], writes=[t_], tag="b0_s0")

    fullc = [0, 2, 3, 5]
    halfc = [1, 1, 4, 4]
    halfb = [0, 64, 0, 64]

    qn = ar.alloc(kb.nid("b0_qn"), [128, 6, SEG], BF16)
    qg = ar.alloc(kb.nid("b0_qg"), [128, 6, SEG], BF16)
    kn = ar.alloc(kb.nid("b0_kn"), [128, 6, SEG], BF16)
    knr = ar.alloc(kb.nid("b0_knr"), [128, 6, SEG], BF16)
    ktm = ar.alloc(kb.nid("b0_ktm"), [128, 4, 768], BF16)
    vtm = ar.alloc(kb.nid("b0_vtm"), [128, 4, 1536], BF16)
    ofm = ar.alloc(kb.nid("b0_ofm"), [128, 12, SEG], BF16)
    gkb2 = ar.alloc(kb.nid("b0_gkb2"), [4, 2 * SEG], F32)
    cum = ar.alloc(kb.nid("b0_cum"), [4, SEG], F32)
    grow = ar.alloc(kb.nid("b0_grow"), [4, SEG], F32)
    ratr = ar.alloc(kb.nid("b0_ratr"), [4, SEG], F32)
    wqb = [ar.alloc(kb.nid("b0_wqb"), [128, 1024], BF16) for _ in range(2)]
    wfi = [ar.alloc(kb.nid("b0_wfi"), [128, 8, 512], BF16) for _ in range(2)]
    scl = ar.alloc(kb.nid("b0_scl"), [128, SEG], F32)
    sq = ar.alloc(kb.nid("b0_sq"), [128, SEG], BF16)
    nrm = ar.alloc(kb.nid("b0_nrm"), [128, SEG], F32)
    cumc = ar.alloc(kb.nid("b0_cumc"), [128, 1], F32)
    ratc = ar.alloc(kb.nid("b0_ratc"), [128, 1], F32)
    bnc = ar.alloc(kb.nid("b0_bnc"), [128, 1], F32)
    gcc = ar.alloc(kb.nid("b0_gcc"), [128, 1], F32)
    ncol = ar.alloc(kb.nid("b0_ncol"), [128, 1], F32)
    gcolx = ar.alloc(kb.nid("b0_gcolx"), [128, 1], F32)
    row0 = ar.alloc(kb.nid("b0_row0"), [1, 128], F32)
    row1 = ar.alloc(kb.nid("b0_row1"), [1, 128], F32)
    row2 = ar.alloc(kb.nid("b0_row2"), [1, 128], F32)
    gend = ar.alloc(kb.nid("b0_gend"), [1, 16], F32)
    em = ar.alloc(kb.nid("b0_em"), [128, CH], F32)
    emL = ar.alloc(kb.nid("b0_emL"), [128, CH], F32)
    emU = ar.alloc(kb.nid("b0_emU"), [128, CH], F32)
    emI = ar.alloc(kb.nid("b0_emI"), [128, CH], F32)
    NTm = ar.alloc(kb.nid("b0_NTm"), [128, CH], F32)
    Lm = ar.alloc(kb.nid("b0_Lm"), [128, CH], F32)
    NT2 = ar.alloc(kb.nid("b0_NT2"), [128, CH], F32)
    L2 = ar.alloc(kb.nid("b0_L2"), [128, CH], F32)
    MqT = ar.alloc(kb.nid("b0_MqT"), [128, CH], BF16)
    rhs = ar.alloc(kb.nid("b0_rhs"), [128, DV], F32)
    rv = ar.alloc(kb.nid("b0_rv"), [128, DV], F32)
    rvb = ar.alloc(kb.nid("b0_rvb"), [128, DV], BF16)
    kpc = ar.alloc(kb.nid("b0_kpc"), [128, DK], BF16)
    oscr = ar.alloc(kb.nid("b0_oscr"), [128, SEG], F32)
    ysob = ar.alloc(kb.nid("b0_ysob"), [128, 8, SEG], BF16)
    zss = ar.alloc(kb.nid("b0_zss"), [128, 8, SEG], BF16)
    wsob = [ar.alloc(kb.nid("b0_wsos"), [128, 1536], BF16) for _ in range(2)]
    woutb = [ar.alloc(kb.nid("b0_wouts"), [128, 2048], BF16) for _ in range(2)]
    wgb = [ar.alloc(kb.nid("b0_wgs"), [128, 1024], BF16) for _ in range(2)]
    gsil = ar.alloc(kb.nid("b0_gsil"), [128, SEG], BF16)
    xseg = [ar.alloc(kb.nid("b0_xseg"), [128, SEG], F32) for _ in range(2)]

    for s in range(NSEG):
        ss = slice(SEG * s, SEG * (s + 1))
        for (wt, dst) in ((c.b0_wqb_b, qn), (c.b0_wkb_b, kn)):
            for j in range(6):
                wb = wqb[j % 2]
                kb.dma("sp", wb.ap(), wt[j, :, :], tag="b0_wql")
                bk = kb.bank()
                for ci in range(NCHUNK):
                    kb.b.emit("pe", lambda e, o=bk.ap()[:, :], l=wb.ap()[:, 128 * ci:128 * (ci + 1)],
                              r=xsb.ap()[:, ci, ss], st=(ci == 0), sp=(ci == 7):
                              e.matmul(o, l, r, start=st, stop=sp),
                              reads=[wb, xsb], writes=[bk], tag="b0_qkmm")
                kb.ev("act", lambda e, o=dst.ap()[:, j, :], i=bk.ap()[:, :]:
                      e.activation(o, i, AF.Identity),
                      reads=[bk], writes=[dst], tag="b0_qkev")
        wix = 0
        for (wsrc, dst, nd) in ((c.b0_wkfib_b, ktm, 768), (c.b0_wvfib_b, vtm, 1536)):
            for np_ in range(ceil_div(nd, 512)):
                n0 = 512 * np_
                n1 = min(512 * (np_ + 1), nd)
                wb2 = wfi[wix % 2]
                wix += 1
                kb.dma("sp", wb2.ap()[:, :, 0:n1 - n0], wsrc[:, :, n0:n1], tag="b0_wfil")
                for tt in range(4):
                    tsl = slice(SEG * s + 128 * tt, SEG * s + 128 * (tt + 1))
                    bk = kb.bank()
                    for ci in range(NCHUNK):
                        kb.b.emit("pe", lambda e, o=bk.ap()[:, 0:n1 - n0], l=xsb.ap()[:, ci, tsl],
                                  r=wb2.ap()[:, ci, 0:n1 - n0], st=(ci == 0), sp=(ci == 7):
                                  e.matmul(o, l, r, start=st, stop=sp),
                                  reads=[xsb, wb2], writes=[bk], tag="b0_tmmm")
                    kb.ev("act", lambda e, o=dst.ap()[:, tt, n0:n1], i=bk.ap()[:, 0:n1 - n0]:
                          e.activation(o, i, AF.Identity),
                          reads=[bk], writes=[dst], tag="b0_tmev")
        bgk = kb.bank()
        bbt = kb.bank()
        for half in range(2):
            bx = bgk if half == 0 else bbt
            for ci in range(NCHUNK):
                kb.b.emit("pe", lambda e, o=bx.ap()[0:4, 0:SEG],
                          l=wgkb.ap()[:, ci, 4 * half:4 * half + 4],
                          r=xsb.ap()[:, ci, ss], st=(ci == 0), sp=(ci == 7):
                          e.matmul(o, l, r, start=st, stop=sp),
                          reads=[wgkb, xsb], writes=[bx], tag="b0_gkmm")
        kb.ev("act", lambda e, o=gkb2.ap()[:, 0:SEG], i=bgk.ap()[0:4, 0:SEG], b=dtb.ap()[:, :]:
              e.activation(o, i, AF.Identity, bias=b),
              reads=[bgk, dtb], writes=[gkb2], tag="b0_gk0")
        kb.ev("act", lambda e, o=gkb2.ap()[:, SEG:2 * SEG], i=bbt.ap()[0:4, 0:SEG], b=bbc.ap()[:, :]:
              e.activation(o, i, AF.Sigmoid, bias=b),
              reads=[bbt, bbc], writes=[gkb2], tag="b0_beta")
        kb.ev("act", lambda e, o=grow.ap()[:, :], i=gkb2.ap()[:, 0:SEG]:
              e.activation(o, i, AF.Abs), reads=[gkb2], writes=[grow], tag="b0_abs")
        kb.ev("act", lambda e, o=grow.ap()[:, :]: e.activation(o, grow.ap()[:, :], AF.Exp, scale=-1.0),
              reads=[grow], writes=[grow], tag="b0_expn")
        kb.ev("act", lambda e, o=grow.ap()[:, :], b=onescol.ap()[0:4, :]:
              e.activation(o, grow.ap()[:, :], AF.Ln, bias=b),
              reads=[grow, onescol], writes=[grow], tag="b0_ln1p")
        kb.ev("dve", lambda e, o=cum.ap()[:, :], i=gkb2.ap()[:, 0:SEG]: e.tensor_relu(o, i),
              reads=[gkb2], writes=[cum], tag="b0_relu")
        kb.ev("dve", lambda e, o=cum.ap()[:, :]: e.tensor_add(o, o, grow.ap()[:, :]),
              reads=[cum, grow], writes=[cum], tag="b0_spadd")
        kb.ev("dve", lambda e, o=cum.ap()[:, :], s=negA.ap()[:, :]:
              e.tensor_scalar(o, cum.ap()[:, :], s, None, ALU.mult),
              reads=[cum, negA], writes=[cum], tag="b0_gkmul")
        for cc in range(4):
            cs = slice(128 * cc, 128 * (cc + 1))
            kb.ev("dve", lambda e, o=cum.ap()[:, cs], i=cum.ap()[:, cs]:
                  e.tensor_tensor_scan(o, i, i, 0.0, ALU.add, ALU.bypass),
                  reads=[cum], writes=[cum], tag="b0_scan")
        kb.ev("act", lambda e, o=grow.ap()[:, :], i=cum.ap()[:, :]: e.activation(o, i, AF.Exp),
              reads=[cum], writes=[grow], tag="b0_gamma")
        for cc in range(4):
            cs = slice(128 * cc, 128 * (cc + 1))
            kb.ev("dve", lambda e, o=ratr.ap()[:, cs], i=cum.ap()[:, cs],
                  s=cum.ap()[:, 128 * cc + 127:128 * cc + 128]:
                  e.tensor_scalar(o, i, s, None, ALU.subtract),
                  reads=[cum], writes=[ratr], tag="b0_ratd")
        kb.ev("act", lambda e, o=ratr.ap()[:, :]: e.activation(o, ratr.ap()[:, :], AF.Exp, scale=-1.0),
              reads=[ratr], writes=[ratr], tag="b0_rate")
        for h in range(H):
            kb.dma("sp", gend.ap()[0:1, 4 * h:4 * (h + 1)], grow.ap()[h:h + 1, 127:512:128],
                   tag="b0_gend", slow_ok=True)
        if s == 0:
            kb.dbg("cum0", cum, [cum])
            kb.dbg("beta0", gkb2.ap()[:, SEG:2 * SEG], [gkb2])
        for h in range(H):
            fc, hc, hb = fullc[h], halfc[h], halfb[h]
            hsl = slice(hb, hb + 64)

            def pieces(buf):
                return ((buf.ap()[:, fc, :], 128, slice(0, 128)),
                        (buf.ap()[hsl, hc, :], 64, hsl))

            bkn = kb.bank()
            first = True
            for (ksrc, npart, psl) in pieces(kn):
                kb.ev("act", lambda e, o=sq.ap()[0:npart, :], i=ksrc: e.activation(o, i, AF.Square),
                      reads=[kn], writes=[sq], tag="b0_ksq")
                kb.b.emit("pe", lambda e, o=bkn.ap()[:, :], l=c.ones_km.ap()[0:npart, :],
                          r=sq.ap()[0:npart, :], st=first, sp=(not first):
                          e.matmul(o, l, r, start=st, stop=sp),
                          reads=[c.ones_km, sq], writes=[bkn], tag="b0_knmm")
                first = False
            kb.ev("dve", lambda e, o=nrm.ap()[:, :], i=bkn.ap()[:, :]: e.tensor_scalar_add(o, i, 1e-6),
                  reads=[bkn], writes=[nrm], tag="b0_keps")
            kb.ev("act", lambda e, o=nrm.ap()[:, :]: e.activation(o, nrm.ap()[:, :], AF.Sqrt),
                  reads=[nrm], writes=[nrm], tag="b0_ksqrt")
            kb.ev("dve", lambda e, o=nrm.ap()[:, :]: e.reciprocal(o, nrm.ap()[:, :]),
                  reads=[nrm], writes=[nrm], tag="b0_krec")
            bkr = kb.bank()
            kb.b.emit("pe", lambda e, o=bkr.ap()[:, :], l=sel4.ap()[:, 128 * h:128 * (h + 1)],
                      r=gkb2.ap()[:, SEG:2 * SEG]: e.matmul(o, l, r, start=True, stop=True),
                      reads=[sel4, gkb2], writes=[bkr], tag="b0_bred")
            kb.ev("dve", lambda e, o=scl.ap()[:, :], i=bkr.ap()[:, :]: e.tensor_mul(o, i, nrm.ap()[:, :]),
                  reads=[bkr, nrm], writes=[scl], tag="b0_bnr")
            for (ksrc, npart, psl) in pieces(kn):
                kdst = knr.ap()[:, fc, :] if npart == 128 else knr.ap()[hsl, hc, :]
                kb.ev("dve", lambda e, o=kdst, i=ksrc, m=scl.ap()[0:npart, :] if npart == 128 else scl.ap()[psl, :]:
                      e.tensor_mul(o, i, m), reads=[kn, scl], writes=[knr], tag="b0_knr")
            for (ksrc, npart, psl) in pieces(kn):
                kb.ev("dve", lambda e, o=ksrc, i=ksrc, m=nrm.ap()[0:npart, :] if npart == 128 else nrm.ap()[psl, :]:
                      e.tensor_mul(o, i, m), reads=[kn, nrm], writes=[kn], tag="b0_knn")
            for tt in range(4):
                btc = kb.bank()
                kb.b.emit("pe", lambda e, o=btc.ap()[0:128, 0:1], i=nrm.ap()[0:1, 128 * tt:128 * (tt + 1)]:
                          e.transpose(o, i, c.ident_f32.ap()[0:1, 0:1]),
                          reads=[nrm, c.ident_f32], writes=[btc], tag="b0_ntr")
                kb.ev("dve", lambda e, o=ncol.ap()[:, :], i=btc.ap()[0:128, 0:1]: e.tensor_copy(o, i),
                      reads=[btc], writes=[ncol], tag="b0_ntc")
                ksl = slice(192 * h, 192 * (h + 1))
                kb.ev("dve", lambda e, o=ktm.ap()[:, tt, ksl], s=ncol.ap()[:, 0:1]:
                      e.tensor_scalar(o, o, s, None, ALU.mult),
                      reads=[ktm, ncol], writes=[ktm], tag="b0_ktn")
            bq = kb.bank()
            first = True
            for (qsrc, npart, psl) in pieces(qn):
                kb.ev("act", lambda e, o=sq.ap()[0:npart, :], i=qsrc: e.activation(o, i, AF.Square),
                      reads=[qn], writes=[sq], tag="b0_qsq")
                kb.b.emit("pe", lambda e, o=bq.ap()[:, :], l=c.ones_km.ap()[0:npart, :],
                          r=sq.ap()[0:npart, :], st=first, sp=(not first):
                          e.matmul(o, l, r, start=st, stop=sp),
                          reads=[c.ones_km, sq], writes=[bq], tag="b0_qnmm")
                first = False
            kb.ev("dve", lambda e, o=nrm.ap()[:, :], i=bq.ap()[:, :]: e.tensor_scalar_add(o, i, 1e-6),
                  reads=[bq], writes=[nrm], tag="b0_qeps")
            kb.ev("act", lambda e, o=nrm.ap()[:, :]: e.activation(o, nrm.ap()[:, :], AF.Sqrt),
                  reads=[nrm], writes=[nrm], tag="b0_qsqrt")
            kb.ev("dve", lambda e, o=nrm.ap()[:, :]: e.reciprocal(o, nrm.ap()[:, :]),
                  reads=[nrm], writes=[nrm], tag="b0_qrec")
            bq2 = kb.bank()
            kb.b.emit("pe", lambda e, o=bq2.ap()[:, :], l=sel4.ap()[:, 128 * h:128 * (h + 1)],
                      r=grow.ap()[:, :]: e.matmul(o, l, r, start=True, stop=True),
                      reads=[sel4, grow], writes=[bq2], tag="b0_gred")
            kb.ev("dve", lambda e, o=scl.ap()[:, :], i=bq2.ap()[:, :]: e.tensor_mul(o, i, nrm.ap()[:, :]),
                  reads=[bq2, nrm], writes=[scl], tag="b0_gnr")
            for (qsrc, npart, psl) in pieces(qn):
                qdst = qg.ap()[:, fc, :] if npart == 128 else qg.ap()[hsl, hc, :]
                kb.ev("dve", lambda e, o=qdst, i=qsrc, m=scl.ap()[0:npart, :] if npart == 128 else scl.ap()[psl, :]:
                      e.tensor_mul(o, i, m), reads=[qn, scl], writes=[qg], tag="b0_qg")
            for (qsrc, npart, psl) in pieces(qn):
                kb.ev("dve", lambda e, o=qsrc, i=qsrc, m=nrm.ap()[0:npart, :] if npart == 128 else nrm.ap()[psl, :]:
                      e.tensor_mul(o, i, m), reads=[qn, nrm], writes=[qn], tag="b0_qn2")
        if s == 0:
            kb.dbg("kn0", kn, [kn])
            kb.dbg("qg0", qg, [qg])
            kb.dbg("ktm0", ktm, [ktm])
            kb.dbg("vtm0", vtm, [vtm])
        for ccc in range(4):
            csl = slice(128 * ccc, 128 * (ccc + 1))
            for h in range(H):
                fc, hc, hb = fullc[h], halfc[h], halfb[h]
                hsl = slice(hb, hb + 64)
                vsl = slice(384 * h, 384 * (h + 1))
                kb.dma("sp", row0.ap(), cum.ap()[h:h + 1, csl], tag="b0_r0")
                kb.dma("sp", row1.ap(), ratr.ap()[h:h + 1, csl], tag="b0_r1")
                kb.dma("sp", row2.ap(), gkb2.ap()[h:h + 1, SEG + 128 * ccc:SEG + 128 * (ccc + 1)], tag="b0_r2")
                bt = kb.bank()
                kb.b.emit("pe", lambda e, o=bt.ap()[0:128, 0:1], i=row0.ap()[0:1, :]:
                          e.transpose(o, i, c.ident_f32.ap()[0:1, 0:1]),
                          reads=[row0, c.ident_f32], writes=[bt], tag="b0_ctr")
                kb.ev("dve", lambda e, o=cumc.ap()[:, :], i=bt.ap()[0:128, 0:1]: e.tensor_copy(o, i),
                      reads=[bt], writes=[cumc], tag="b0_ctc")
                bt2 = kb.bank()
                kb.b.emit("pe", lambda e, o=bt2.ap()[0:128, 0:1], i=row1.ap()[0:1, :]:
                          e.transpose(o, i, c.ident_f32.ap()[0:1, 0:1]),
                          reads=[row1, c.ident_f32], writes=[bt2], tag="b0_rtr")
                kb.ev("dve", lambda e, o=ratc.ap()[:, :], i=bt2.ap()[0:128, 0:1]: e.tensor_copy(o, i),
                      reads=[bt2], writes=[ratc], tag="b0_rtc")
                bt3 = kb.bank()
                kb.b.emit("pe", lambda e, o=bt3.ap()[0:128, 0:1], i=row2.ap()[0:1, :]:
                          e.transpose(o, i, c.ident_f32.ap()[0:1, 0:1]),
                          reads=[row2, c.ident_f32], writes=[bt3], tag="b0_btr")
                kb.ev("dve", lambda e, o=bnc.ap()[:, :], i=bt3.ap()[0:128, 0:1]:
                      e.tensor_scalar_mul(o, i, -1.0),
                      reads=[bt3], writes=[bnc], tag="b0_btc")
                kb.ev("act", lambda e, o=gcolx.ap()[:, :], i=cumc.ap()[:, :]: e.activation(o, i, AF.Exp),
                      reads=[cumc], writes=[gcolx], tag="b0_gcol")
                brr = kb.bank()
                kb.b.emit("pe", lambda e, o=brr.ap()[:, 0:128], l=c.ones_row_f32.ap()[:, :], r=row0.ap()[0:1, :]:
                          e.matmul(o, l, r, start=True, stop=True),
                          reads=[c.ones_row_f32, row0], writes=[brr], tag="b0_crr")
                kb.ev("dve", lambda e, o=em.ap()[:, :], i=brr.ap()[:, 0:128]:
                      e.tensor_scalar(o, i, cumc.ap()[:, 0:1], 0.0, ALU.subtract, ALU.min),
                      reads=[brr, cumc], writes=[em], tag="b0_dmin")
                kb.ev("dve", lambda e, o=emL.ap()[:, :], i=brr.ap()[:, 0:128]:
                      e.tensor_scalar(o, i, cumc.ap()[:, 0:1], 0.0, ALU.subtract, ALU.max),
                      reads=[brr, cumc], writes=[emL], tag="b0_dmax")
                kb.ev("act", lambda e, o=em.ap()[:, :]: e.activation(o, em.ap()[:, :], AF.Exp),
                      reads=[em], writes=[em], tag="b0_eexp")
                kb.ev("act", lambda e, o=emL.ap()[:, :]: e.activation(o, emL.ap()[:, :], AF.Exp, scale=-1.0),
                      reads=[emL], writes=[emL], tag="b0_eexpL")
                kb.ev("dve", lambda e, o=emU.ap()[:, :]: e.tensor_mul(o, em.ap()[:, :], umask_s.ap()[:, :]),
                      reads=[em, umask_s], writes=[emU], tag="b0_emU")
                kb.ev("dve", lambda e, o=emI.ap()[:, :]: e.tensor_mul(o, em.ap()[:, :], umask_i.ap()[:, :]),
                      reads=[em, umask_i], writes=[emI], tag="b0_emI")
                kb.ev("dve", lambda e, o=emL.ap()[:, :]: e.tensor_mul(o, emL.ap()[:, :], lmask_s.ap()[:, :]),
                      reads=[emL, lmask_s], writes=[emL], tag="b0_emL")
                bkk = kb.bank()
                kb.b.emit("pe", lambda e, o=bkk.ap()[:, 0:128], l=kn.ap()[:, fc, csl], r=knr.ap()[:, fc, csl]:
                          e.matmul(o, l, r, start=True, stop=False),
                          reads=[kn, knr], writes=[bkk], tag="b0_kk1")
                kb.b.emit("pe", lambda e, o=bkk.ap()[:, 0:128], l=kn.ap()[hsl, hc, csl], r=knr.ap()[hsl, hc, csl]:
                          e.matmul(o, l, r, start=False, stop=True),
                          reads=[kn, knr], writes=[bkk], tag="b0_kk2")
                if KB.DEBUG and s == 0 and ccc == 0 and h == 0:
                    dbt = ar.alloc(kb.nid("b0_dbkk"), [128, 128], F32)
                    kb.ev("dve", lambda e, o=dbt.ap()[:, :], i=bkk.ap()[:, 0:128]: e.tensor_copy(o, i),
                          reads=[bkk], writes=[dbt], tag="dbg_bkkcp")
                    kb.dbg("bkk00", dbt, [dbt])
                kb.ev("dve", lambda e, o=NTm.ap()[:, :], i=bkk.ap()[:, 0:128]:
                      e.tensor_mul(o, i, emU.ap()[:, :]),
                      reads=[bkk, emU], writes=[NTm], tag="b0_NT")
                bkl = kb.bank()
                kb.b.emit("pe", lambda e, o=bkl.ap()[:, 0:128], l=knr.ap()[:, fc, csl], r=kn.ap()[:, fc, csl]:
                          e.matmul(o, l, r, start=True, stop=False),
                          reads=[kn, knr], writes=[bkl], tag="b0_kkl1")
                kb.b.emit("pe", lambda e, o=bkl.ap()[:, 0:128], l=knr.ap()[hsl, hc, csl], r=kn.ap()[hsl, hc, csl]:
                          e.matmul(o, l, r, start=False, stop=True),
                          reads=[kn, knr], writes=[bkl], tag="b0_kkl2")
                kb.ev("dve", lambda e, o=Lm.ap()[:, :], i=bkl.ap()[:, 0:128]:
                      e.tensor_mul(o, i, emL.ap()[:, :]),
                      reads=[bkl, emL], writes=[Lm], tag="b0_L")
                bqk = kb.bank()
                kb.b.emit("pe", lambda e, o=bqk.ap()[:, 0:128], l=kn.ap()[:, fc, csl], r=qn.ap()[:, fc, csl]:
                          e.matmul(o, l, r, start=True, stop=False),
                          reads=[kn, qn], writes=[bqk], tag="b0_qk1")
                kb.b.emit("pe", lambda e, o=bqk.ap()[:, 0:128], l=kn.ap()[hsl, hc, csl], r=qn.ap()[hsl, hc, csl]:
                          e.matmul(o, l, r, start=False, stop=True),
                          reads=[kn, qn], writes=[bqk], tag="b0_qk2")
                kb.ev("dve", lambda e, o=MqT.ap()[:, :], i=bqk.ap()[:, 0:128]:
                      e.tensor_mul(o, i, emI.ap()[:, :]),
                      reads=[bqk, emI], writes=[MqT], tag="b0_Mq")
                bks = kb.bank()
                kb.b.emit("pe", lambda e, o=bks.ap()[:, 0:DV], l=kn.ap()[:, fc, csl], r=Sab.ap()[:, h, :]:
                          e.matmul(o, l, r, start=True, stop=False),
                          reads=[kn, Sab], writes=[bks], tag="b0_ks1")
                kb.b.emit("pe", lambda e, o=bks.ap()[:, 0:DV], l=kn.ap()[hsl, hc, csl], r=Sbb.ap()[hsl, h // 2, :]:
                          e.matmul(o, l, r, start=False, stop=True),
                          reads=[kn, Sbb], writes=[bks], tag="b0_ks2")
                kb.ev("dve", lambda e, o=oscr.ap()[:, 0:DV], i=bks.ap()[:, 0:DV], s=gcolx.ap()[:, 0:1]:
                      e.tensor_scalar(o, i, s, None, ALU.mult),
                      reads=[bks, gcolx], writes=[oscr], tag="b0_rh1")
                kb.ev("dve", lambda e, o=oscr.ap()[:, 0:DV], m=vtm.ap()[:, ccc, vsl]:
                      e.tensor_sub(o, oscr.ap()[:, 0:DV], m),
                      reads=[oscr, vtm], writes=[oscr], tag="b0_rh2")
                kb.ev("dve", lambda e, o=rhs.ap()[:, :], i=oscr.ap()[:, 0:DV], s=bnc.ap()[:, 0:1]:
                      e.tensor_scalar(o, i, s, None, ALU.mult),
                      reads=[oscr, bnc], writes=[rhs], tag="b0_rh3")
                bsv = kb.bank()
                kb.b.emit("pe", lambda e, o=bsv.ap()[:, 0:DV], l=NTm.ap()[:, :], r=rhs.ap()[:, :]:
                          e.matmul(o, l, r, start=True, stop=True),
                          reads=[NTm, rhs], writes=[bsv], tag="b0_ap0")
                kb.ev("dve", lambda e, o=rv.ap()[:, :], i=rhs.ap()[:, :], bb=bsv.ap()[:, 0:DV]:
                      e.tensor_sub(o, i, bb),
                      reads=[rhs, bsv], writes=[rv], tag="b0_apv0")
                cN, cL, nN, nL = NTm, Lm, NT2, L2
                for lev in range(6):
                    bq1 = kb.bank()
                    kb.b.emit("pe", lambda e, o=bq1.ap()[:, 0:128], l=cL.ap()[:, :], r=cN.ap()[:, :]:
                              e.matmul(o, l, r, start=True, stop=True),
                              reads=[cL, cN], writes=[bq1], tag="b0_sqN")
                    bq2m = kb.bank()
                    kb.b.emit("pe", lambda e, o=bq2m.ap()[:, 0:128], l=cN.ap()[:, :], r=cL.ap()[:, :]:
                              e.matmul(o, l, r, start=True, stop=True),
                              reads=[cN, cL], writes=[bq2m], tag="b0_sqL")
                    kb.ev("dve", lambda e, o=nN.ap()[:, :], i=bq1.ap()[:, 0:128]: e.tensor_copy(o, i),
                          reads=[bq1], writes=[nN], tag="b0_sqNe")
                    kb.ev("dve", lambda e, o=nL.ap()[:, :], i=bq2m.ap()[:, 0:128]: e.tensor_copy(o, i),
                          reads=[bq2m], writes=[nL], tag="b0_sqLe")
                    bap = kb.bank()
                    kb.b.emit("pe", lambda e, o=bap.ap()[:, 0:DV], l=nN.ap()[:, :], r=rv.ap()[:, :]:
                              e.matmul(o, l, r, start=True, stop=True),
                              reads=[nN, rv], writes=[bap], tag="b0_ap")
                    kb.ev("dve", lambda e, o=rv.ap()[:, :], i=bap.ap()[:, 0:DV]:
                          e.tensor_add(o, rv.ap()[:, :], i),
                          reads=[rv, bap], writes=[rv], tag="b0_apv")
                    cN, cL, nN, nL = nN, nL, cN, cL
                kb.ev("dve", lambda e, o=rvb.ap()[:, :], i=rv.ap()[:, :]: e.tensor_copy(o, i),
                      reads=[rv], writes=[rvb], tag="b0_rvc")
                bo = kb.bank()
                for dvc in range(3):
                    dsl = slice(128 * dvc, 128 * (dvc + 1))
                    kb.b.emit("pe", lambda e, o=bo.ap()[:, dsl], l=Sab.ap()[:, h, dsl], r=qg.ap()[:, fc, csl],
                              st=(dvc == 0), sp=False: e.matmul(o, l, r, start=st, stop=sp),
                              reads=[Sab, qg], writes=[bo], tag="b0_oi1")
                    kb.b.emit("pe", lambda e, o=bo.ap()[:, dsl], l=Sbb.ap()[hsl, h // 2, dsl], r=qg.ap()[hsl, hc, csl]:
                              e.matmul(o, l, r, start=False, stop=False),
                              reads=[Sbb, qg], writes=[bo], tag="b0_oi2")
                    kb.b.emit("pe", lambda e, o=bo.ap()[:, dsl], l=rvb.ap()[:, dsl], r=MqT.ap()[:, :],
                              sp=(dvc == 2): e.matmul(o, l, r, start=False, stop=sp),
                              reads=[rvb, MqT], writes=[bo], tag="b0_oMq")
                for dvc in range(3):
                    kb.ev("dve", lambda e, o=ofm.ap()[:, 3 * h + dvc, csl], i=bo.ap()[:, 128 * dvc:128 * (dvc + 1)]:
                          e.tensor_copy(o, i),
                          reads=[bo], writes=[ofm], tag="b0_oev")
                kb.ev("dve", lambda e, o=kpc.ap()[:, :], i=ktm.ap()[:, ccc, 192 * h:192 * (h + 1)],
                      s=ratc.ap()[:, 0:1]: e.tensor_scalar(o, i, s, None, ALU.mult),
                      reads=[ktm, ratc], writes=[kpc], tag="b0_kpr")
                bgc = kb.bank()
                kb.b.emit("pe", lambda e, o=bgc.ap()[:, 0:1], l=c.ones_row_f32.ap()[:, :],
                          r=gend.ap()[0:1, 4 * h + ccc:4 * h + ccc + 1]:
                          e.matmul(o, l, r, start=True, stop=True),
                          reads=[c.ones_row_f32, gend], writes=[bgc], tag="b0_gcr")
                kb.ev("dve", lambda e, o=gcc.ap()[:, :], i=bgc.ap()[:, 0:1]: e.tensor_copy(o, i),
                      reads=[bgc], writes=[gcc], tag="b0_gcc")
                bst = kb.bank()
                fcoff = 0 if h % 2 == 0 else 64
                hcoff = 128 if h % 2 == 0 else 0
                kb.b.emit("pe", lambda e, o=bst.ap()[:, 0:DV], l=kpc.ap()[:, fcoff:fcoff + 128], r=rvb.ap()[:, :]:
                          e.matmul(o, l, r, start=True, stop=True),
                          reads=[kpc, rvb], writes=[bst], tag="b0_su1")
                kb.ev("dve", lambda e, o=Sa.ap()[:, h, :], i=Sa.ap()[:, h, :], s=gcc.ap()[:, 0:1],
                      bb=bst.ap()[:, 0:DV]:
                      e.scalar_tensor_tensor(o, i, s, bb, ALU.mult, ALU.add),
                      reads=[Sa, gcc, bst], writes=[Sa], tag="b0_sup1")
                bst2 = kb.bank()
                kb.b.emit("pe", lambda e, o=bst2.ap()[hsl, 0:DV], l=kpc.ap()[:, hcoff:hcoff + 64], r=rvb.ap()[:, :]:
                          e.matmul(o, l, r, start=True, stop=True),
                          reads=[kpc, rvb], writes=[bst2], tag="b0_su2")
                kb.ev("dve", lambda e, o=Sb.ap()[hsl, h // 2, :], i=Sb.ap()[hsl, h // 2, :],
                      s=gcc.ap()[hsl, 0:1], bb=bst2.ap()[hsl, 0:DV]:
                      e.scalar_tensor_tensor(o, i, s, bb, ALU.mult, ALU.add),
                      reads=[Sb, gcc, bst2], writes=[Sb], tag="b0_sup2")
                if s == 0 and ccc == 0 and h == 0:
                    kb.dbg("rv00", rv, [rv])
                    kb.dbg("NT00", NTm, [NTm])
                    kb.dbg("emU00", emU, [emU])
                    kb.dbg("rhs00", rhs, [rhs])
                    kb.dbg("Sa00", Sa.ap()[:, 0, :], [Sa])
                kb.ev("dve", lambda e, o=Sab.ap()[:, h, :], i=Sa.ap()[:, h, :]: e.tensor_copy(o, i),
                      reads=[Sa], writes=[Sab], tag="b0_scp1")
                kb.ev("dve", lambda e, o=Sbb.ap()[hsl, h // 2, :], i=Sb.ap()[hsl, h // 2, :]: e.tensor_copy(o, i),
                      reads=[Sb], writes=[Sbb], tag="b0_scp2")
        if s == 0:
            kb.dbg("ofm0", ofm, [ofm])
        rinvh = []
        for h in range(H):
            brm = kb.bank()
            for dvc in range(3):
                kb.ev("act", lambda e, o=sq.ap()[:, :], i=ofm.ap()[:, 3 * h + dvc, :]:
                      e.activation(o, i, AF.Square),
                      reads=[ofm], writes=[sq], tag="b0_osq")
                kb.b.emit("pe", lambda e, o=brm.ap()[:, :], l=c.ones_km.ap()[:, :], r=sq.ap()[:, :],
                          st=(dvc == 0), sp=(dvc == 2): e.matmul(o, l, r, start=st, stop=sp),
                          reads=[c.ones_km, sq], writes=[brm], tag="b0_omm")
            rh = ar.alloc(kb.nid("b0_rinv"), [128, SEG], BF16)
            rinvh.append(rh)
            kb.ev("act", lambda e, o=nrm.ap()[:, :], i=brm.ap()[:, :], b=epscol.ap()[:, :]:
                  e.activation(o, i, AF.Sqrt, bias=b, scale=1.0 / DV),
                  reads=[brm, epscol], writes=[nrm], tag="b0_orms")
            kb.ev("dve", lambda e, o=nrm.ap()[:, :]: e.reciprocal(o, o),
                  reads=[nrm], writes=[nrm], tag="b0_orec")
            kb.ev("dve", lambda e, o=rh.ap()[:, :], i=nrm.ap()[:, :]: e.tensor_copy(o, i),
                  reads=[nrm], writes=[rh], tag="b0_orcp")
        for j12 in range(12):
            wb = wgb[j12 % 2]
            kb.dma("sp", wb.ap(), c.b0_wgb_b[j12, :, :], tag="b0_wgl")
            bk = kb.bank()
            for ci in range(NCHUNK):
                kb.b.emit("pe", lambda e, o=bk.ap()[:, :], l=wb.ap()[:, 128 * ci:128 * (ci + 1)],
                          r=xsb.ap()[:, ci, ss], st=(ci == 0), sp=(ci == 7):
                          e.matmul(o, l, r, start=st, stop=sp),
                          reads=[wb, xsb], writes=[bk], tag="b0_gmm")
            kb.ev("act", lambda e, o=gsil.ap()[:, :], i=bk.ap()[:, :]:
                  e.activation(o, i, AF.Silu),
                  reads=[bk], writes=[gsil], tag="b0_gsil")
            h = j12 // 3
            dvc = j12 % 3
            kb.ev("dve", lambda e, o=oscr.ap()[:, :], i=ofm.ap()[:, j12, :], s=gnc.ap()[:, dvc:dvc + 1],
                  rr=rinvh[h].ap()[:, :]:
                  e.scalar_tensor_tensor(o, i, s, rr, ALU.mult, ALU.mult),
                  reads=[ofm, gnc, rinvh[h]], writes=[oscr], tag="b0_osc")
            kb.ev("dve", lambda e, o=ofm.ap()[:, j12, :], i=oscr.ap()[:, :]:
                  e.tensor_mul(o, i, gsil.ap()[:, :]),
                  reads=[oscr, gsil], writes=[ofm], tag="b0_ogate")
        for rh in rinvh:
            ar.free(rh)
        for j in range(8):
            wb = wsob[j % 2]
            kb.dma("sp", wb.ap(), c.b0_wsob_b[j, :, :], tag="b0_wsol")
            bk = kb.bank()
            for ci2 in range(12):
                kb.b.emit("pe", lambda e, o=bk.ap()[:, :], l=wb.ap()[:, 128 * ci2:128 * (ci2 + 1)],
                          r=ofm.ap()[:, ci2, :], st=(ci2 == 0), sp=(ci2 == 11):
                          e.matmul(o, l, r, start=st, stop=sp),
                          reads=[wb, ofm], writes=[bk], tag="b0_somm")
            kb.ev("act", lambda e, o=ysob.ap()[:, j, :], i=bk.ap()[:, :]:
                  e.activation(o, i, AF.Identity),
                  reads=[bk], writes=[ysob], tag="b0_soev")
        for f in range(8):
            kb.dma("sp", zss.ap()[:, f, :], c.zspill[:, f, ss], tag="b0_zsl")
        for fo in range(8):
            xg = xseg[fo % 2]
            kb.dma("sp", xg.ap()[:, :], c.xspill[128 * fo:128 * (fo + 1), ss], tag="b0_xld")
            wb = woutb[fo % 2]
            kb.dma("sp", wb.ap(), c.b0_woutb_b[fo, :, :], tag="b0_woutl")
            bk = kb.bank()
            for ci2 in range(16):
                rsrc = ysob.ap()[:, ci2, :] if ci2 < 8 else zss.ap()[:, ci2 - 8, :]
                rkey = ysob if ci2 < 8 else zss
                kb.b.emit("pe", lambda e, o=bk.ap()[:, :], l=wb.ap()[:, 128 * ci2:128 * (ci2 + 1)],
                          r=rsrc, st=(ci2 == 0), sp=(ci2 == 15):
                          e.matmul(o, l, r, start=st, stop=sp),
                          reads=[wb, rkey], writes=[bk], tag="b0_outmm")
            kb.ev("dve", lambda e, o=xg.ap()[:, :], i=bk.ap()[:, :]:
                  e.tensor_add(o, o, i),
                  reads=[bk, xg], writes=[xg], tag="b0_xadd")
            kb.dma("sp", c.xspill[128 * fo:128 * (fo + 1), ss], xg.ap()[:, :], tag="b0_xst")
    for t_ in (xsb, qn, qg, kn, knr, ktm, vtm, ofm, ysob, zss):
        ar.free(t_)
    c.X = []
    for fo in range(8):
        xt = ar.alloc(f"Xb{fo}_r", [128, T], F32, high=True)
        c.X.append(xt)
        kb.dma("sp", xt.ap()[:, :], c.xspill[128 * fo:128 * (fo + 1), :], tag="b0_xfin")
    ar.free(gkb2, cum, grow, ratr,
            wqb[0], wqb[1], wfi[0], wfi[1], scl, sq, nrm, cumc, ratc, bnc, gcc,
            ncol, gcolx, row0, row1, row2, gend, em, emL, emU, emI, NTm, Lm,
            NT2, L2, MqT, rhs, rv, rvb, kpc, oscr, wsob[0], wsob[1],
            woutb[0], woutb[1], wgb[0], wgb[1], gsil, xseg[0], xseg[1],
            alog, dtb, bbc, gnc, onescol, epscol, sel4, umask_s, umask_i,
            lmask_s, wgkb, negA, Sa, Sb, Sab, Sbb)



SEM_NAMES = ["pe", "act", "dve", "sp", "pool", "qsp", "qpool"]


def assemble(kb):
    kb.b.finalize()
    nc = kb.nc
    sems = {}
    for n in ("pe", "act", "dve", "pool", "sp"):
        sems[n] = nc.alloc_semaphore("sem_" + n)
    for qe in ("sp", "pool"):
        for j in range(NDMACLASS):
            sems[f"q{qe}{j}"] = nc.alloc_semaphore(f"sem_q{qe}{j}")
    kb.b.build(nc, sems)
    return nc


def shard_inputs(full, core_id):
    m = dict(host_constants(core_id))
    m["x_fm"] = np.ascontiguousarray(full["L0_attn_a_x"][core_id].T)
    w1 = np.asarray(full["L1_ffn_w1"])
    w2 = np.asarray(full["L1_ffn_w2"])
    m["l1f_w1t_t"] = tile_fo(np.ascontiguousarray(w1.T))
    m["l1f_w1n"] = np.ascontiguousarray(w1)
    m["l1f_w2t"] = np.ascontiguousarray(w2.T)
    m["l1f_w2n"] = np.ascontiguousarray(w2)
    m["l1f_b1c"] = np.ascontiguousarray(np.asarray(full["L1_ffn_b1"]).reshape(32, 128).T)
    m["l1f_b2c"] = np.ascontiguousarray(np.asarray(full["L1_ffn_b2"]).reshape(8, 128).T)
    m["a0_wqT_t"] = tile_fo(np.ascontiguousarray(np.asarray(full["L0_attn_a_wq"]).T), 512)
    m["a0_wkT_t"] = tile_fo(np.ascontiguousarray(np.asarray(full["L0_attn_a_wk"]).T), 512)
    m["a0_wvT_t"] = tile_fo(np.ascontiguousarray(np.asarray(full["L0_attn_a_wv"]).T), 512)
    m["aa_wq_t"] = tile_fo(np.asarray(full["L1_attn_a_Wq"]), 512)
    m["aa_wkv_t"] = tile_fo(np.asarray(full["L1_attn_a_Wkv"]), 512)
    m["aa_wo_t"] = tile_fo(np.asarray(full["L1_attn_a_Wo"]), 512)
    m["aa_bqc"] = np.ascontiguousarray(np.asarray(full["L1_attn_a_bq"]).reshape(8, 128).T)
    kvb = np.asarray(full["L1_attn_a_bkv"])
    m["aa_bkc"] = np.ascontiguousarray(kvb[:1024].reshape(8, 128).T)
    m["aa_bvr"] = np.ascontiguousarray(kvb[1024:].reshape(1, 1024))
    m["aa_boc"] = np.ascontiguousarray(np.asarray(full["L1_attn_a_bo"]).reshape(8, 128).T)
    wqc = np.asarray(full["L1_attn_b_w_q"]).transpose(1, 0, 2).reshape(1024, 1024)
    wkc = np.asarray(full["L1_attn_b_w_k"]).transpose(1, 0, 2).reshape(1024, 1024)
    wvc = np.asarray(full["L1_attn_b_w_v"]).transpose(1, 0, 2).reshape(1024, 1024)
    m["ab_wq_t"] = tile_fo(wqc, 512)
    m["ab_wk_t"] = tile_fo(wkc, 512)
    m["ab_wv_t"] = tile_fo(wvc, 512)
    m["ab_wo_t"] = tile_fo(np.asarray(full["L1_attn_b_w_o"]), 512)
    m["moe_gate_t"] = tile_fi(np.ascontiguousarray(np.asarray(full["L0_ffn_gate_w"]).T))
    m["moe_gateb"] = np.ascontiguousarray(np.asarray(full["L0_ffn_gate_b"]).reshape(1, 8))
    w1t = np.stack([tile_fo(np.asarray(full["L0_ffn_W1"])[e], 128).reshape(32, 128, 1024)
                    for e in range(8)])
    m["moe_w1"] = np.ascontiguousarray(
        w1t.reshape(8, 16, 2, 128, 1024).transpose(0, 1, 3, 2, 4).reshape(8, 16, 128, 2048))
    w2t = np.asarray(full["L0_ffn_W2"]).reshape(8, 32, 128, 1024)
    m["moe_w2"] = np.ascontiguousarray(
        w2t.reshape(8, 16, 2, 128, 1024).transpose(0, 1, 3, 2, 4).reshape(8, 16, 128, 2048))
    m["moe_b1c"] = np.ascontiguousarray(np.asarray(full["L0_ffn_b1"]).reshape(8, 32, 128).transpose(0, 2, 1))
    m["moe_b2c"] = np.ascontiguousarray(np.asarray(full["L0_ffn_b2"]).reshape(8, 8, 128).transpose(0, 2, 1))
    m["b0_win"] = tile_fi(np.asarray(full["L0_attn_b_W_in"]))
    m["b0_wq"] = tile_fo(np.asarray(full["L0_attn_b_W_q"]), 128)
    m["b0_wk"] = tile_fo(np.asarray(full["L0_attn_b_W_k"]), 128)
    m["b0_wk_fi"] = tile_fi(np.asarray(full["L0_attn_b_W_k"]))
    m["b0_wv_fi"] = tile_fi(np.asarray(full["L0_attn_b_W_v"]))
    m["b0_wg"] = tile_fo(np.asarray(full["L0_attn_b_W_g"]), 128)
    m["b0_wgkb"] = tile_fi(np.concatenate([np.asarray(full["L0_attn_b_W_gk"]),
                                           np.asarray(full["L0_attn_b_W_b"])], axis=1))
    m["b0_wso"] = tile_fo(np.asarray(full["L0_attn_b_W_so"]), 128)
    m["b0_wout"] = tile_fo(np.asarray(full["L0_attn_b_W_out"]), 128)
    cxw = np.asarray(full["L0_attn_b_conv_x_w"])[:, 0, :]
    czw = np.asarray(full["L0_attn_b_conv_z_w"])[:, 0, :]
    m["b0_convw"] = np.concatenate([tile_fi(cxw), tile_fi(czw)], axis=2)
    m["b0_convb"] = np.concatenate([
        np.asarray(full["L0_attn_b_conv_x_b"]).reshape(8, 128).T,
        np.asarray(full["L0_attn_b_conv_z_b"]).reshape(8, 128).T], axis=1)
    m["b0_convb"] = np.ascontiguousarray(m["b0_convb"])
    m["b0_alog"] = np.asarray(full["L0_attn_b_A_log"]).reshape(4, 1)
    m["b0_dtb"] = np.asarray(full["L0_attn_b_dt_bias"]).reshape(4, 1)
    m["b0_bb"] = np.asarray(full["L0_attn_b_b_b"]).reshape(4, 1)
    m["b0_gnc"] = np.ascontiguousarray(np.asarray(full["L0_attn_b_g_norm_w"]).reshape(3, 128).T)
    return m


def build_nc(blocks):
    nc = bass.Bass("TRN2", target_bir_lowering=False)
    kb = KB(nc)
    c = setup_core(kb)
    if "l1ffn" in blocks:
        setup_l1ffn(c)
    if "attn_a0" in blocks:
        setup_attn_a0(c)
    if "l1attn_a" in blocks:
        setup_l1attn_a(c)
    if "l1attn_b" in blocks:
        setup_l1attn_b(c)
    if "moe" in blocks:
        setup_moe(c)
    if "attn_b0" in blocks:
        setup_attn_b0(c)
    c.moe_spread = "moe" in blocks and blocks.index("moe") > 0
    c.b0_spread = ("attn_b0" in blocks and "moe" in blocks
                   and blocks.index("moe") < blocks.index("attn_b0"))
    if "moe" in blocks and not c.moe_spread:
        emit_moe_prepass(c, range(8))
    if "attn_b0" in blocks and not c.b0_spread:
        emit_b0_prepass(c, range(7))
    for bi, blk in enumerate(blocks):
        if bi == 1 and c.moe_spread:
            emit_moe_prepass(c, [0, 1])
        emit_ln(c)
        if blk == "l1ffn":
            emit_l1ffn(c)
        elif blk == "attn_a0":
            emit_attn_a0(c)
        elif blk == "l1attn_a":
            emit_l1attn(c, "a")
        elif blk == "l1attn_b":
            emit_l1attn(c, "b")
        elif blk == "moe":
            emit_moe(c)
        elif blk == "attn_b0":
            emit_attn_b0(c)
        else:
            raise ValueError(blk)
    store_out(c)
    return assemble(kb), c


FULL_CHAIN = ["attn_a0", "moe", "attn_b0", "moe", "l1attn_a", "l1ffn", "l1attn_b", "l1ffn"]


def kernel(**inputs):
    blocks = FULL_CHAIN
    nc, c = build_nc(blocks)
    declared = set(c.kb.inputs.keys())
    in_maps = []
    for i in range(8):
        m = shard_inputs(inputs, i)
        in_maps.append({k: v for k, v in m.items() if k in declared})
        missing = declared - set(in_maps[-1].keys())
        assert not missing, f"missing inputs: {missing}"
    res = run_bass_kernel_spmd(nc, in_maps, core_ids=list(range(8)))
    out = np.stack([res.results[i]["y_fm"].T for i in range(8)], axis=0)
    return np.ascontiguousarray(out.astype(np.float32))


# version 08
# speedup vs baseline: 1.9009x; 1.0128x over previous
import math
import numpy as np
import ml_dtypes

import concourse.bass as bass
import concourse.mybir as mybir
from concourse.bass_utils import run_bass_kernel_spmd

F32 = mybir.dt.float32
BF16 = mybir.dt.bfloat16
AF = mybir.ActivationFunctionType
ALU = mybir.AluOpType
AX = mybir.AxisListType

BF = ml_dtypes.bfloat16

B, T, D = 8, 2048, 1024
NCHUNK = D // 128
CAP = 2560

ENGINES = ("pe", "act", "dve", "sp", "pool")
DMA_ENGINES = {"sp": "spq", "pool": "poolq"}


NDMACLASS = 8
RRDMACLASS = 6


class Op:
    __slots__ = ("engine", "fn", "reads", "writes", "dma", "tag", "seq", "idx",
                 "need_inc", "deps", "dmaclass", "dmapos", "same_dep")

    def __init__(self, engine, fn, reads, writes, dma, tag, seq, idx):
        self.engine = engine
        self.fn = fn
        self.reads = reads
        self.writes = writes
        self.dma = dma
        self.tag = tag
        self.seq = seq
        self.idx = idx
        self.need_inc = dma
        self.deps = []
        self.dmaclass = -1
        self.dmapos = 0
        self.same_dep = False


def _key(x):
    if isinstance(x, str):
        return x
    t = getattr(x, "tensor", x)
    return t.name


class Builder:
    def __init__(self):
        self.ops = {e: [] for e in ENGINES}
        self.seq = 0
        self.res = {}
        self.dmacount = {e: 0 for e in ENGINES}
        self.dmaposc = {}

    def _r(self, k):
        return self.res.setdefault(k, {"w": None, "r": {}})

    def emit(self, engine, fn, reads=(), writes=(), dma=False, tag="", dmaclass=None):
        assert engine in ENGINES
        idx = len(self.ops[engine])
        op = Op(engine, fn, [_key(r) for r in reads], [_key(w) for w in writes],
                dma, tag, self.seq, idx)
        self.seq += 1
        deps = {}

        def add_dep(src):
            if src is None:
                return
            (e, i, sdma) = src
            if e == engine and not sdma:
                op.same_dep = True
                return
            deps[(e, i)] = (e, i, sdma)

        for k in op.reads:
            r = self._r(k)
            add_dep(r["w"])
            if k.startswith("psb"):
                for (e, sdma), i in r["r"].items():
                    add_dep((e, i, sdma))
        for k in op.writes:
            r = self._r(k)
            add_dep(r["w"])
            for (e, sdma), i in r["r"].items():
                add_dep((e, i, sdma))
        op.deps = [d for d in deps.values() if not (d[0] == engine and d[2] == dma and d[1] == idx)]
        if dma:
            if dmaclass is None:
                n = self.dmacount[engine]
                self.dmacount[engine] = n + 1
                dmaclass = n % RRDMACLASS
            op.dmaclass = dmaclass
            pos = self.dmaposc.setdefault(engine, {}).get(dmaclass, 0) + 1
            self.dmaposc[engine][dmaclass] = pos
            op.dmapos = pos
        for k in op.reads:
            r = self._r(k)
            if k.startswith("psb"):
                r["w"] = (engine, idx, dma)
                r["r"] = {}
            cur = r["r"].get((engine, dma))
            if cur is None or cur < idx:
                r["r"][(engine, dma)] = idx
        for k in op.writes:
            r = self._r(k)
            r["w"] = (engine, idx, dma)
            r["r"] = {}
        self.ops[engine].append(op)
        return op

    def haunt(self, newkey, oldkeys):
        r = self._r(newkey)
        for ok in oldkeys:
            o = self.res.get(ok)
            if not o:
                continue
            if o["w"] is not None:
                e, i, sdma = o["w"]
                cur = r["r"].get((e, sdma))
                if cur is None or cur < i:
                    r["r"][(e, sdma)] = i
            for (e, sdma), i in o["r"].items():
                cur = r["r"].get((e, sdma))
                if cur is None or cur < i:
                    r["r"][(e, sdma)] = i

    def finalize(self):
        for e in ENGINES:
            for op in self.ops[e]:
                for (se, si, sdma) in op.deps:
                    self.ops[se][si].need_inc = True
        self.cum = {}
        for e in ENGINES:
            cc, cd = 0, 0
            cums = []
            for op in self.ops[e]:
                if op.need_inc:
                    if op.dma:
                        cd += 16
                    else:
                        cc += 1
                cums.append((cc, cd))
            self.cum[e] = cums

    def wait_value(self, se, si, sdma):
        if sdma:
            return 16 * self.ops[se][si].dmapos
        cc, cd = self.cum[se][si]
        return cc

    def wait_sem(self, se, si, sdma):
        if sdma:
            return f"q{se}{self.ops[se][si].dmaclass}"
        return se

    def stats(self):
        out = {}
        for e in ENGINES:
            n = len(self.ops[e])
            ninc = sum(1 for o in self.ops[e] if o.need_inc)
            nw = sum(len(o.deps) for o in self.ops[e])
            out[e] = (n, ninc, nw)
        return out

    def build(self, nc, sems):
        bld = self

        with nc.Block() as block:
            def run_engine(engobj, ename):
                seen = {}

                def wait(semkey, val):
                    if val <= 0 or seen.get(semkey, 0) >= val:
                        return
                    seen[semkey] = val
                    engobj.wait_ge(sems[semkey], val)

                for op in bld.ops[ename]:
                    if op.same_dep and not op.dma and ename != "pe":
                        engobj.drain()
                    if op.dma and op.dmapos > 1:
                        wait(f"q{ename}{op.dmaclass}", 16 * (op.dmapos - 1))
                    for (se, si, sdma) in op.deps:
                        wait(bld.wait_sem(se, si, sdma), bld.wait_value(se, si, sdma))
                    inst = op.fn(engobj)
                    if op.need_inc:
                        assert inst is not None, f"op {op.tag} on {ename} needs inc but fn returned None"
                        if op.dma:
                            inst.then_inc(sems[f"q{ename}{op.dmaclass}"], 16)
                        else:
                            inst.then_inc(sems[ename], 1)

            if bld.ops["sp"]:
                block.sync(lambda e: run_engine(e, "sp"))
            if bld.ops["pool"]:
                block.gpsimd(lambda e: run_engine(e, "pool"))
            if bld.ops["pe"]:
                block.tensor(lambda e: run_engine(e, "pe"))
            if bld.ops["act"]:
                block.scalar(lambda e: run_engine(e, "act"))
            if bld.ops["dve"]:
                block.vector(lambda e: run_engine(e, "dve"))


SBUF_ARENA_BASE = 16512
SBUF_ARENA_END = 229300


class Arena:

    def __init__(self, nc, builder, base=SBUF_ARENA_BASE, end=SBUF_ARENA_END):
        self.nc = nc
        self.b = builder
        self.live = {}
        self.free_l = [(base, end - base)]
        self.ghosts = []
        self.hwm = 0

    def alloc(self, name, shape, dtype, align=64, high=False):
        per_part = int(np.prod(shape[1:])) * mybir.dt.size(dtype)
        per_part = (per_part + align - 1) // align * align
        ranges = enumerate(self.free_l)
        if high:
            ranges = reversed(list(ranges))
        for i, (off, sz) in ranges:
            hoff = (off + sz - per_part) // align * align
            if sz >= per_part and (not high or hoff >= off):
                if high:
                    tail = off + sz - (hoff + per_part)
                    self.free_l[i] = (off, hoff - off)
                    if tail:
                        self.free_l.append((hoff + per_part, tail))
                        self.free_l.sort()
                    off = hoff
                else:
                    self.free_l[i] = (off + per_part, sz - per_part)
                break
        else:
            raise MemoryError(f"arena OOM for {name}: {per_part}B; free={self.free_l}")
        t = self.nc.alloc_sbuf_tensor_at(name, list(shape), dtype, offset=off)
        self.live[name] = (off, per_part, t)
        self.hwm = max(self.hwm, sum(v[1] for v in self.live.values()))
        olds = [g[2] for g in self.ghosts if g[0] < off + per_part and off < g[0] + g[1]]
        if olds:
            self.b.haunt(t.name, olds)
        return t

    def free(self, *tensors):
        for t in tensors:
            name = None
            for k, (off, sz, h) in self.live.items():
                if h is t or k == t:
                    name = k
                    break
            assert name is not None, f"free of unknown tensor {t}"
            off, sz, h = self.live.pop(name)
            self.free_l.append((off, sz))
            self.ghosts.append((off, sz, h.name))
        self.free_l.sort()
        out = []
        for off, sz in self.free_l:
            if out and out[-1][0] + out[-1][1] == off:
                out[-1] = (out[-1][0], out[-1][1] + sz)
            else:
                out.append((off, sz))
        self.free_l = [tuple(x) for x in out]




def rope_tables_l0():
    d = D
    i = np.arange(d // 2, dtype=np.float64)
    theta = np.power(10000.0, -2.0 * (i - 1.0) / d)
    t = np.arange(T, dtype=np.float64)
    ang = t[None, :] * theta[:, None]
    cos = np.cos(ang)
    sin = np.sin(ang)
    C = np.empty((d, T), np.float32)
    S = np.empty((d, T), np.float32)
    C[0::2] = cos
    C[1::2] = cos
    S[0::2] = sin
    S[1::2] = -sin
    return C, S


def rope_tables_l1():
    dqk = 64
    inv = 1.0 / (10000.0 ** (np.arange(0, dqk, 2, dtype=np.float64) / dqk))
    t = np.arange(T, dtype=np.float64)
    ang = t[None, :] * inv[:, None]
    cos = np.cos(ang).astype(np.float32)
    sin = np.sin(ang).astype(np.float32)
    C2 = np.empty((128, T), np.float32)
    S2 = np.empty((128, T), np.float32)
    for h in range(2):
        b = h * 64
        C2[b:b + 32] = cos
        C2[b + 32:b + 64] = cos
        S2[b:b + 32] = -sin
        S2[b + 32:b + 64] = sin
    return C2, S2


def pair_swap_matrix():
    P = np.zeros((128, 128), np.float32)
    for i in range(0, 128, 2):
        P[i, i + 1] = 1.0
        P[i + 1, i] = 1.0
    return P


def swap32_matrix():
    P = np.zeros((128, 128), np.float32)
    for h in range(2):
        b = h * 64
        for i in range(32):
            P[b + i, b + 32 + i] = 1.0
            P[b + 32 + i, b + i] = 1.0
    return P


def host_constants(core_id):
    c = {}
    c["c_ident_bf"] = np.eye(128, dtype=np.float32).astype(BF)
    c["c_ident_f32"] = np.eye(128, dtype=np.float32)
    c["c_ones_km"] = np.ones((128, 128), np.float32).astype(BF)
    c["c_ones_row"] = np.ones((1, 128), np.float32).astype(BF)
    c["c_ones_row_f32"] = np.ones((1, 128), np.float32)
    c["c_utri"] = np.triu(np.ones((128, 128), np.float32)).astype(BF)
    c["c_causal"] = np.tril(np.ones((128, 128), np.float32)).astype(BF)
    c["c_eidx"] = np.tile(np.arange(8, dtype=np.float32), (128, 1))
    sel = np.zeros((8, 8 * 128), np.float32)
    for e in range(8):
        sel[e, e * 128:(e + 1) * 128] = 1.0
    c["c_sel8"] = sel.astype(BF)
    pref = np.zeros((8, 128), np.float32)
    pref[:core_id, :] = 1.0
    c["c_prefmask"] = pref
    C0, S0 = rope_tables_l0()
    c["c_rope0_c"] = C0
    c["c_rope0_s"] = S0
    C2, S2 = rope_tables_l1()
    c["c_rope1_c"] = C2
    c["c_rope1_s"] = S2
    c["c_pswap"] = pair_swap_matrix().astype(BF)
    c["c_swap32"] = swap32_matrix().astype(BF)
    m = np.zeros((128, 128), np.float32)
    mi = np.zeros((128, 128), np.float32)
    for hb in range(2):
        b = hb * 64
        for i in range(64):
            for j in range(64):
                if j < i:
                    m[b + i, b + j] = 1.0
                if j <= i:
                    mi[b + i, b + j] = 1.0
    c["c_dmask_strict"] = m
    c["c_dmask_incl"] = mi
    cz = np.zeros((128, 4, 512), np.float32)
    for j in range(4):
        for kk in range(128):
            cz[kk, j, :] = (128 * j + kk <= np.arange(512)).astype(np.float32)
    c["c_caus512"] = cz.astype(BF)
    s4 = np.zeros((4, 512), np.float32)
    for h in range(4):
        s4[h, 128 * h:128 * (h + 1)] = 1.0
    c["c_sel4"] = s4
    jj = np.arange(128)
    c["c_umask_s"] = (jj[:, None] < jj[None, :]).astype(np.float32)
    c["c_umask_i"] = (jj[:, None] <= jj[None, :]).astype(np.float32)
    c["c_lmask_s"] = (jj[:, None] > jj[None, :]).astype(np.float32)
    return c



class KB:

    def __init__(self, nc):
        self.nc = nc
        self.b = Builder()
        self.ar = Arena(nc, self.b)
        self.uid = 0
        self.inputs = {}
        self.psum = []
        for i in range(8):
            t = nc.alloc_psum_tensor(f"psb{i}", [128, 512], F32)
            self.psum.append(t)
        self._pb = 0

    def nid(self, prefix):
        self.uid += 1
        return f"{prefix}_{self.uid}"

    def din(self, name, shape, dtype=F32):
        t = self.nc.dram_tensor(name, list(shape), dtype, kind="ExternalInput")
        ap = t.ap()
        self.inputs[name] = ap
        return ap

    def dout(self, name, shape, dtype=F32):
        t = self.nc.dram_tensor(name, list(shape), dtype, kind="ExternalOutput")
        return t.ap()

    def bank(self):
        t = self.psum[self._pb]
        self._pb = (self._pb + 1) % 8
        return t

    def banks(self, n):
        assert n <= 8
        out = [self.psum[(self._pb + i) % 8] for i in range(n)]
        self._pb = (self._pb + n) % 8
        return out

    DEBUG = False

    def dbg(self, name, ap_or_handle, reads):
        if not KB.DEBUG:
            return
        ap = ap_or_handle if hasattr(ap_or_handle, "tensor") else ap_or_handle.ap()
        shape = list(ap.shape)
        t = self.nc.dram_tensor("dbg_" + name, shape, ap.tensor.dtype, kind="ExternalOutput")
        self.dma("sp", t.ap(), ap, tag="dbg_" + name, extra_reads=reads)

    def dma(self, engine, out_ap, in_ap, tag="dma", extra_reads=(), slow_ok=False, dmaclass=None):
        self.b.emit(engine,
                    lambda e, o=out_ap, i=in_ap, s=slow_ok:
                    e.dma_start(out=o, in_=i, allow_slow_non_contiguous=s),
                    reads=[in_ap] + list(extra_reads), writes=[out_ap], dma=True, tag=tag,
                    dmaclass=dmaclass)

    def mm_group(self, bankT, terms, out_slices=None, tag="mm"):
        n = len(terms)
        for j, (o, l, r) in enumerate(terms):
            st = (j == 0)
            sp = (j == n - 1)
            self.b.emit("pe",
                        lambda e, o=o, l=l, r=r, st=st, sp=sp: e.matmul(o, l, r, start=st, stop=sp),
                        reads=[l, r], writes=[bankT], tag=tag)

    def ev(self, engine, fn, reads, writes, tag="ev"):
        self.b.emit(engine, fn, reads=reads, writes=writes, tag=tag)


def ceil_div(a, b):
    return (a + b - 1) // b


def tile_fi(w):
    r, n = w.shape
    return np.ascontiguousarray(w.reshape(r // 128, 128, n).transpose(1, 0, 2))


def tile_fo(w, ncols=128):
    r, n = w.shape
    nch = r // 128
    ns = n // ncols
    a = w.reshape(nch, 128, ns, ncols).transpose(2, 1, 0, 3)
    return np.ascontiguousarray(a.reshape(ns, 128, nch * ncols))



class Ctx:
    pass


def setup_core(kb):
    nc = kb.nc
    c = Ctx()
    c.kb = kb
    ar = kb.ar

    c.X = [ar.alloc(f"Xc{i}", [128, T], F32) for i in range(NCHUNK)]
    c.NB = None
    c.xspill = kb.nc.dram_tensor("xspill", [D, T], F32, kind="Internal").ap()

    def const_in(name, shape, dtype=F32):
        ap = kb.din(name, shape, dtype)
        t = ar.alloc("s" + name, shape, dtype)
        kb.dma("sp", t.ap(), ap, tag=name)
        return t

    c.ident_bf = const_in("c_ident_bf", [128, 128], BF16)
    c.ident_f32 = const_in("c_ident_f32", [128, 128], F32)
    c.ones_km = const_in("c_ones_km", [128, 128], BF16)
    c.ones_row = const_in("c_ones_row", [1, 128], BF16)
    c.ones_row_f32 = const_in("c_ones_row_f32", [1, 128], F32)
    c.utri = const_in("c_utri", [128, 128], BF16)
    c.causal = const_in("c_causal", [128, 128], BF16)
    c.eidx = const_in("c_eidx", [128, 8], F32)
    c.sel8 = const_in("c_sel8", [8, 8 * 128], BF16)
    c.prefmask = const_in("c_prefmask", [8, 128], F32)
    c.pswap = const_in("c_pswap", [128, 128], BF16)
    c.swap32 = const_in("c_swap32", [128, 128], BF16)
    c.dmask_strict = const_in("c_dmask_strict", [128, 128], F32)
    c.dmask_incl = const_in("c_dmask_incl", [128, 128], F32)
    c.caus512_in = kb.din("c_caus512", [128, 4, 512], BF16)

    c.x_in = kb.din("x_fm", [D, T], F32)
    for i in range(NCHUNK):
        kb.dma("sp", c.X[i].ap()[:, :], c.x_in[128 * i:128 * (i + 1), :], tag=f"ldX{i}")

    c.out = kb.dout("y_fm", [D, T], F32)
    return c


def store_out(c):
    kb = c.kb
    for i in range(NCHUNK):
        kb.dma("sp", c.out[128 * i:128 * (i + 1), :], c.X[i].ap()[:, :], tag=f"stX{i}")


def emit_ln(c, tag=""):
    kb = c.kb
    ar = kb.ar
    eps = 1e-5
    c.NB = [ar.alloc(kb.nid("NB"), [128, T], BF16, high=True) for _ in range(NCHUNK)]
    xb = [ar.alloc(kb.nid("ln_xb"), [128, 512], BF16) for _ in range(2)]
    xq = [ar.alloc(kb.nid("ln_xq"), [128, 512], BF16) for _ in range(2)]
    mu = ar.alloc(kb.nid("ln_mu"), [128, 512], F32)
    sv = ar.alloc(kb.nid("ln_sv"), [128, 512], F32)
    tf = ar.alloc(kb.nid("ln_tf"), [128, 512], F32)
    for tc in range(4):
        ts = slice(512 * tc, 512 * (tc + 1))
        bs = kb.bank()
        bq = kb.bank()
        for ci in range(NCHUNK):
            xbt, xqt = xb[ci % 2], xq[ci % 2]
            kb.ev("dve", lambda e, o=xbt.ap()[:, :], i=c.X[ci].ap()[:, ts]: e.tensor_copy(o, i),
                  reads=[c.X[ci]], writes=[xbt], tag="lncast")
            kb.ev("act", lambda e, o=xqt.ap()[:, :], i=c.X[ci].ap()[:, ts]: e.activation(o, i, AF.Square),
                  reads=[c.X[ci]], writes=[xqt], tag="lnsq")
            st, sp = (ci == 0), (ci == NCHUNK - 1)
            kb.b.emit("pe", lambda e, o=bs.ap()[:, :], l=c.ones_km.ap()[:, :], r=xbt.ap()[:, :], st=st, sp=sp:
                      e.matmul(o, l, r, start=st, stop=sp),
                      reads=[c.ones_km, xbt], writes=[bs], tag="lnmmS")
            kb.b.emit("pe", lambda e, o=bq.ap()[:, :], l=c.ones_km.ap()[:, :], r=xqt.ap()[:, :], st=st, sp=sp:
                      e.matmul(o, l, r, start=st, stop=sp),
                      reads=[c.ones_km, xqt], writes=[bq], tag="lnmmQ")
        kb.ev("dve", lambda e, o=mu.ap()[:, :], i=bs.ap()[:, :]: e.tensor_scalar_mul(o, i, 1.0 / D),
              reads=[bs], writes=[mu], tag="lnmu")
        kb.ev("act", lambda e, o=sv.ap()[:, :], i=mu.ap()[:, :]: e.activation(o, i, AF.Square),
              reads=[mu], writes=[sv], tag="lnmsq")
        kb.ev("dve", lambda e, o=sv.ap()[:, :], i=bq.ap()[:, :]:
              e.scalar_tensor_tensor(o, i, 1.0 / D, sv.ap()[:, :], ALU.mult, ALU.subtract),
              reads=[bq, sv], writes=[sv], tag="lnvar")
        kb.ev("dve", lambda e, o=sv.ap()[:, :]: e.tensor_scalar_add(o, sv.ap()[:, :], eps),
              reads=[sv], writes=[sv], tag="lneps")
        kb.ev("act", lambda e, o=sv.ap()[:, :]: e.activation(o, sv.ap()[:, :], AF.Sqrt),
              reads=[sv], writes=[sv], tag="lnsd")
        kb.ev("dve", lambda e, o=sv.ap()[:, :]: e.reciprocal(o, sv.ap()[:, :]),
              reads=[sv], writes=[sv], tag="lnrec")
        for ci in range(NCHUNK):
            kb.ev("dve", lambda e, o=tf.ap()[:, :], x=c.X[ci].ap()[:, ts]: e.tensor_sub(o, x, mu.ap()[:, :]),
                  reads=[c.X[ci], mu], writes=[tf], tag="lnsub")
            kb.ev("dve", lambda e, o=c.NB[ci].ap()[:, ts]: e.tensor_mul(o, tf.ap()[:, :], sv.ap()[:, :]),
                  reads=[tf, sv], writes=[c.NB[ci]], tag="lnmul")
    ar.free(xb[0], xb[1], xq[0], xq[1], mu, sv, tf)


def setup_l1ffn(c):
    kb = c.kb
    c.l1f_w1t_t = kb.din("l1f_w1t_t", [32, 128, 1024], F32)
    c.l1f_w1n = kb.din("l1f_w1n", [4096, 1024], F32)
    c.l1f_w2t = kb.din("l1f_w2t", [4096, 1024], F32)
    c.l1f_w2n = kb.din("l1f_w2n", [1024, 4096], F32)
    c.l1f_b1c = kb.din("l1f_b1c", [128, 32], F32)
    c.l1f_b2c = kb.din("l1f_b2c", [128, 8], F32)
    ar = kb.ar
    c.l1f_a1 = ar.alloc("l1f_a1", [128, 32], F32)
    c.l1f_a2 = ar.alloc("l1f_a2", [128, 8], F32)
    c.l1f_b1s = ar.alloc("l1f_b1s", [128, 32], F32)
    c.l1f_b2s = ar.alloc("l1f_b2s", [128, 8], F32)
    kb.dma("sp", c.l1f_b1s.ap()[:, :], c.l1f_b1c[:, :], tag="l1fb1")
    kb.dma("sp", c.l1f_b2s.ap()[:, :], c.l1f_b2c[:, :], tag="l1fb2")
    c.l1f_alphas_done = False


def emit_l1ffn_alphas(c):
    kb = c.kb
    ar = kb.ar
    wb = [ar.alloc(kb.nid("l1f_wn"), [128, 4096], F32) for _ in range(2)]
    for k in range(32):
        t = wb[k % 2]
        kb.dma("sp", t.ap()[:, 0:1024], c.l1f_w1n[128 * k:128 * (k + 1), :], tag="l1f_an1")
        kb.ev("dve", lambda e, o=c.l1f_a1.ap()[:, k:k + 1], i=t.ap()[:, 0:1024]:
              e.tensor_reduce(o, i, AX.X, ALU.add, apply_absolute_value=True),
              reads=[t], writes=[c.l1f_a1], tag="l1f_red1")
    for k in range(8):
        t = wb[k % 2]
        kb.dma("sp", t.ap()[:, :], c.l1f_w2n[128 * k:128 * (k + 1), :], tag="l1f_an2")
        kb.ev("dve", lambda e, o=c.l1f_a2.ap()[:, k:k + 1], i=t.ap()[:, :]:
              e.tensor_reduce(o, i, AX.X, ALU.add, apply_absolute_value=True),
              reads=[t], writes=[c.l1f_a2], tag="l1f_red2")
    kb.ev("dve", lambda e: e.tensor_scalar_mul(c.l1f_a1.ap()[:, :], c.l1f_a1.ap()[:, :], 1.0 / 1024.0),
          reads=[c.l1f_a1], writes=[c.l1f_a1], tag="l1f_as1")
    kb.ev("dve", lambda e: e.tensor_scalar_mul(c.l1f_a2.ap()[:, :], c.l1f_a2.ap()[:, :], 1.0 / 4096.0),
          reads=[c.l1f_a2], writes=[c.l1f_a2], tag="l1f_as2")
    ar.free(wb[0], wb[1])


def emit_l1ffn(c):
    kb = c.kb
    ar = kb.ar
    if not c.l1f_alphas_done:
        emit_l1ffn_alphas(c)
        c.l1f_alphas_done = True
    sg1 = [ar.alloc(kb.nid("l1f_sg1"), [128, 1024], BF16) for _ in range(2)]
    sg2 = [ar.alloc(kb.nid("l1f_sg2"), [128, 1024], BF16) for _ in range(2)]
    hbuf = ar.alloc(kb.nid("l1f_h"), [128, 32, 1024], BF16)
    yt = ar.alloc(kb.nid("l1f_yt"), [128, 512], F32)
    for tcg in range(2):
        for j in range(32):
            t = sg1[j % 2]
            kb.dma("pool", t.ap()[:, :], c.l1f_w1t_t[j, :, :], tag="l1f_w1l")
            kb.ev("act", lambda e, a=t.ap()[:, :]: e.activation(a, a, AF.Sign),
                  reads=[t], writes=[t], tag="l1f_sgn1")
            for tc2 in range(2):
                bk = kb.bank()
                ts = slice(1024 * tcg + 512 * tc2, 1024 * tcg + 512 * (tc2 + 1))
                for ci in range(8):
                    kb.b.emit("pe", lambda e, o=bk.ap()[:, :], l=t.ap()[:, 128 * ci:128 * (ci + 1)],
                              r=c.NB[ci].ap()[:, ts], st=(ci == 0), sp=(ci == 7):
                              e.matmul(o, l, r, start=st, stop=sp),
                              reads=[t, c.NB[ci]], writes=[bk], tag="l1f_hmm")
                kb.ev("act", lambda e, o=hbuf.ap()[:, j, 512 * tc2:512 * (tc2 + 1)], i=bk.ap()[:, :],
                      s=c.l1f_a1.ap()[:, j:j + 1], b=c.l1f_b1s.ap()[:, j:j + 1]:
                      e.activation(o, i, AF.Gelu, bias=b, scale=s),
                      reads=[bk, c.l1f_a1, c.l1f_b1s], writes=[hbuf], tag="l1f_hev")
        if tcg == 1:
            for t_ in c.NB:
                ar.free(t_)
            c.NB = None
        for fo2h in range(2):
            bks = kb.banks(8)
            for k2 in range(32):
                t = sg2[k2 % 2]
                kb.dma("pool", t.ap()[:, :], c.l1f_w2t[128 * k2:128 * (k2 + 1), :], tag="l1f_w2l")
                kb.ev("act", lambda e, a=t.ap()[:, :]: e.activation(a, a, AF.Sign),
                      reads=[t], writes=[t], tag="l1f_sgn2")
                for tc2 in range(2):
                    ts = slice(1024 * tcg + 512 * tc2, 1024 * tcg + 512 * (tc2 + 1))
                    for f2 in range(4):
                        fo2 = fo2h * 4 + f2
                        bk = bks[tc2 * 4 + f2]
                        kb.b.emit("pe", lambda e, o=bk.ap()[:, :], l=t.ap()[:, 128 * fo2:128 * (fo2 + 1)],
                                  r=hbuf.ap()[:, k2, 512 * tc2:512 * (tc2 + 1)],
                                  st=(k2 == 0), sp=(k2 == 31):
                                  e.matmul(o, l, r, start=st, stop=sp),
                                  reads=[t, hbuf], writes=[bk], tag="l1f_ymm")
            for tc2 in range(2):
                ts = slice(1024 * tcg + 512 * tc2, 1024 * tcg + 512 * (tc2 + 1))
                for f2 in range(4):
                    fo2 = fo2h * 4 + f2
                    bk = bks[tc2 * 4 + f2]
                    kb.ev("act", lambda e, o=yt.ap()[:, :], i=bk.ap()[:, :],
                          s=c.l1f_a2.ap()[:, fo2:fo2 + 1], b=c.l1f_b2s.ap()[:, fo2:fo2 + 1]:
                          e.activation(o, i, AF.Identity, bias=b, scale=s),
                          reads=[bk, c.l1f_a2, c.l1f_b2s], writes=[yt], tag="l1f_yev")
                    kb.ev("dve", lambda e, o=c.X[fo2].ap()[:, ts], i=yt.ap()[:, :]:
                          e.tensor_add(o, o, i),
                          reads=[yt, c.X[fo2]], writes=[c.X[fo2]], tag="l1f_xadd")
    ar.free(sg1[0], sg1[1], sg2[0], sg2[1], hbuf, yt)


def setup_attn_a0(c):
    kb = c.kb
    c.a0_wqT_t = kb.din("a0_wqT_t", [2, 128, 4096], F32)
    c.a0_wkT_t = kb.din("a0_wkT_t", [2, 128, 4096], F32)
    c.a0_wvT_t = kb.din("a0_wvT_t", [2, 128, 4096], F32)
    c.a0_ropec = kb.din("c_rope0_c", [D, T], F32)
    c.a0_ropes = kb.din("c_rope0_s", [D, T], F32)


A0_STOP = 9
A0_ROPE = True


def emit_attn_a0(c):
    kb = c.kb
    ar = kb.ar
    scale = 1.0 / math.sqrt(D)
    for ci in range(NCHUNK):
        kb.dma("sp", c.xspill[128 * ci:128 * (ci + 1), :], c.X[ci].ap()[:, :], tag="a0_spill")
    for ci in range(NCHUNK):
        ar.free(c.X[ci])
    c.X = None

    if A0_STOP <= 0:
        c.X = [ar.alloc(f"Xc{i}_r", [128, T], F32, high=True) for i in range(NCHUNK)]
        for ci in range(NCHUNK):
            kb.dma("sp", c.X[ci].ap()[:, :], c.xspill[128 * ci:128 * (ci + 1), :], tag="a0_xld")
        for t_ in c.NB:
            ar.free(t_)
        c.NB = None
        return

    qr = [ar.alloc(kb.nid("a0_q"), [128, T], BF16) for _ in range(NCHUNK)]
    kr = [ar.alloc(kb.nid("a0_k"), [128, T], BF16) for _ in range(NCHUNK)]
    vt = ar.alloc(kb.nid("a0_v"), [128, 16, 1024], BF16)
    wbuf = [ar.alloc(kb.nid("a0_w"), [128, 8, 512], BF16) for _ in range(2)]
    ctab = [ar.alloc(kb.nid("a0_ct"), [128, 512], F32) for _ in range(2)]
    stab = [ar.alloc(kb.nid("a0_st"), [128, 512], F32) for _ in range(2)]
    t1 = ar.alloc(kb.nid("a0_t1"), [128, 512], F32)
    t2 = ar.alloc(kb.nid("a0_t2"), [128, 512], F32)
    sqb = [ar.alloc(kb.nid("a0_sq"), [128, 512], BF16) for _ in range(2)]

    for (wt, dst) in ((c.a0_wqT_t, qr), (c.a0_wkT_t, kr), (c.a0_wvT_t, None)):
        for jh in range(2):
            wb = wbuf[jh % 2]
            kb.dma("pool", wb.ap()[:, :, :], wt[jh, :, :], tag="a0_wl")
            if dst is not None:
                for f in range(4):
                    fo = jh * 4 + f
                    for tc in range(4):
                        ts = slice(512 * tc, 512 * (tc + 1))
                        bk = kb.bank()
                        for ci in range(NCHUNK):
                            kb.b.emit("pe", lambda e, o=bk.ap()[:, :], l=wb.ap()[:, ci, 128 * f:128 * (f + 1)],
                                      r=c.NB[ci].ap()[:, ts], st=(ci == 0), sp=(ci == 7):
                                      e.matmul(o, l, r, start=st, stop=sp),
                                      reads=[wb, c.NB[ci]], writes=[bk], tag="a0_qkmm")
                        if not A0_ROPE:
                            kb.ev("act", lambda e, o=dst[fo].ap()[:, ts], i=bk.ap()[:, :]:
                                  e.activation(o, i, AF.Identity),
                                  reads=[bk], writes=[dst[fo]], tag="a0_qc0")
                            continue
                        qtile = sqb[(f * 4 + tc) % 2]
                        kb.ev("act", lambda e, o=qtile.ap()[:, :], i=bk.ap()[:, :]: e.activation(o, i, AF.Identity),
                              reads=[bk], writes=[qtile], tag="a0_qc")
                        bk2 = kb.bank()
                        kb.b.emit("pe", lambda e, o=bk2.ap()[:, :], l=c.pswap.ap()[:, :], r=qtile.ap()[:, :]:
                                  e.matmul(o, l, r, start=True, stop=True),
                                  reads=[c.pswap, qtile], writes=[bk2], tag="a0_swap")
                        ct, st_ = ctab[tc % 2], stab[tc % 2]
                        kb.dma("sp", ct.ap()[:, :], c.a0_ropec[128 * fo:128 * (fo + 1), ts], tag="a0_ctl")
                        kb.dma("sp", st_.ap()[:, :], c.a0_ropes[128 * fo:128 * (fo + 1), ts], tag="a0_stl")
                        kb.ev("dve", lambda e, o=t1.ap()[:, :], i=bk.ap()[:, :], m=ct.ap()[:, :]:
                              e.tensor_mul(o, i, m), reads=[bk, ct], writes=[t1], tag="a0_rc")
                        kb.ev("dve", lambda e, o=t2.ap()[:, :], i=bk2.ap()[:, :], m=st_.ap()[:, :]:
                              e.tensor_mul(o, i, m), reads=[bk2, st_], writes=[t2], tag="a0_rs")
                        kb.ev("dve", lambda e, o=dst[fo].ap()[:, ts]: e.tensor_add(o, t1.ap()[:, :], t2.ap()[:, :]),
                              reads=[t1, t2], writes=[dst[fo]], tag="a0_radd")
            else:
                for tt in range(16):
                    bk = kb.bank()
                    for ci in range(NCHUNK):
                        kb.b.emit("pe", lambda e, o=bk.ap()[:, :], l=c.NB[ci].ap()[:, 128 * tt:128 * (tt + 1)],
                                  r=wb.ap()[:, ci, :], st=(ci == 0), sp=(ci == 7):
                                  e.matmul(o, l, r, start=st, stop=sp),
                                  reads=[wb, c.NB[ci]], writes=[bk], tag="a0_vmm")
                    kb.ev("act", lambda e, o=vt.ap()[:, tt, 512 * jh:512 * (jh + 1)], i=bk.ap()[:, :]:
                          e.activation(o, i, AF.Identity),
                          reads=[bk], writes=[vt], tag="a0_vev")
    for t_ in c.NB:
        ar.free(t_)
    c.NB = None
    ar.free(wbuf[0], wbuf[1], ctab[0], ctab[1], stab[0], stab[1], t1, t2)
    if A0_STOP <= 1:
        c.X = [ar.alloc(f"Xc{i}_r", [128, T], F32, high=True) for i in range(NCHUNK)]
        for ci in range(NCHUNK):
            kb.dma("sp", c.X[ci].ap()[:, :], c.xspill[128 * ci:128 * (ci + 1), :], tag="a0_xld")
        return

    qn = ar.alloc(kb.nid("a0_qn"), [128, T], F32)
    kn = ar.alloc(kb.nid("a0_kn"), [128, T], F32)
    mneg = ar.alloc(kb.nid("a0_mneg"), [128, T], BF16)
    kmax = ar.alloc(kb.nid("a0_kmax"), [128, 5], F32)
    for (src_, dst) in ((qr, qn), (kr, kn)):
        for tc in range(4):
            ts = slice(512 * tc, 512 * (tc + 1))
            bk = kb.bank()
            for ci in range(NCHUNK):
                sq = sqb[ci % 2]
                kb.ev("act", lambda e, o=sq.ap()[:, :], i=src_[ci].ap()[:, ts]: e.activation(o, i, AF.Square),
                      reads=[src_[ci]], writes=[sq], tag="a0_nsq")
                kb.b.emit("pe", lambda e, o=bk.ap()[:, :], l=c.ones_km.ap()[:, :], r=sq.ap()[:, :],
                          st=(ci == 0), sp=(ci == 7): e.matmul(o, l, r, start=st, stop=sp),
                          reads=[c.ones_km, sq], writes=[bk], tag="a0_nmm")
            kb.ev("act", lambda e, o=dst.ap()[:, ts], i=bk.ap()[:, :]: e.activation(o, i, AF.Sqrt),
                  reads=[bk], writes=[dst], tag="a0_nsqrt")
            if dst is kn:
                kb.ev("dve", lambda e, o=kmax.ap()[:, tc:tc + 1], i=dst.ap()[:, ts]:
                      e.tensor_reduce(o, i, AX.X, ALU.max), reads=[dst], writes=[kmax], tag="a0_kmax")
    kb.ev("dve", lambda e, o=kmax.ap()[:, 4:5], i=kmax.ap()[:, 0:4]:
          e.tensor_reduce(o, i, AX.X, ALU.max), reads=[kmax], writes=[kmax], tag="a0_kmax2")
    for tc in range(4):
        ts = slice(512 * tc, 512 * (tc + 1))
        kb.ev("dve", lambda e, o=mneg.ap()[:, ts], i=qn.ap()[:, ts]:
              e.tensor_scalar(o, i, kmax.ap()[:, 4:5], -1.0, ALU.mult, ALU.mult),
              reads=[qn, kmax], writes=[mneg], tag="a0_mneg")
    ar.free(qn, kn, sqb[0], sqb[1], kmax)
    if A0_STOP <= 2:
        c.X = [ar.alloc(f"Xc{i}_r", [128, T], F32, high=True) for i in range(NCHUNK)]
        for ci in range(NCHUNK):
            kb.dma("sp", c.X[ci].ap()[:, :], c.xspill[128 * ci:128 * (ci + 1), :], tag="a0_xld")
        return

    pts = [ar.alloc(kb.nid("a0_pt"), [128, 16, 512], BF16) for _ in range(4)]
    dinv = ar.alloc(kb.nid("a0_dinv"), [128, 4, 512], F32)
    for qc in range(4):
        qs = slice(512 * qc, 512 * (qc + 1))
        bdn = kb.bank()
        for kt in range(16):
            bst = kb.bank()
            if bst is bdn:
                bst = kb.bank()
            for ci in range(NCHUNK):
                kb.b.emit("pe", lambda e, o=bst.ap()[:, :], l=kr[ci].ap()[:, 128 * kt:128 * (kt + 1)],
                          r=qr[ci].ap()[:, qs], st=(ci == 0), sp=False:
                          e.matmul(o, l, r, start=st, stop=sp),
                          reads=[kr[ci], qr[ci]], writes=[bst], tag="a0_smm")
            kb.b.emit("pe", lambda e, o=bst.ap()[:, :], l=c.ones_row.ap()[:, :], r=mneg.ap()[0:1, qs]:
                      e.matmul(o, l, r, start=False, stop=True),
                      reads=[c.ones_row, mneg], writes=[bst], tag="a0_smneg")
            kb.ev("act", lambda e, o=pts[qc].ap()[:, kt, :], i=bst.ap()[:, :]:
                  e.activation(o, i, AF.Exp, scale=scale),
                  reads=[bst], writes=[pts[qc]], tag="a0_exp")
            kb.b.emit("pe", lambda e, o=bdn.ap()[:, :], l=c.ones_km.ap()[:, :], r=pts[qc].ap()[:, kt, :],
                      st=(kt == 0), sp=(kt == 15): e.matmul(o, l, r, start=st, stop=sp),
                      reads=[c.ones_km, pts[qc]], writes=[bdn], tag="a0_dmm")
        kb.ev("dve", lambda e, o=dinv.ap()[:, qc, :], i=bdn.ap()[:, :]: e.reciprocal(o, i),
              reads=[bdn], writes=[dinv], tag="a0_dinv")
    ar.free(mneg)
    for ci in range(NCHUNK):
        ar.free(qr[ci], kr[ci])
    if A0_STOP <= 3:
        c.X = [ar.alloc(f"Xc{i}_r", [128, T], F32, high=True) for i in range(NCHUNK)]
        for ci in range(NCHUNK):
            kb.dma("sp", c.X[ci].ap()[:, :], c.xspill[128 * ci:128 * (ci + 1), :], tag="a0_xld")
        return

    newX = []
    ot = ar.alloc(kb.nid("a0_ot"), [128, 512], F32)
    for dvc in range(NCHUNK):
        xt = ar.alloc(f"Xc{dvc}_r", [128, T], F32, high=True)
        newX.append(xt)
        kb.dma("sp", xt.ap()[:, :], c.xspill[128 * dvc:128 * (dvc + 1), :], tag="a0_xld")
        for qc in range(4):
            qs = slice(512 * qc, 512 * (qc + 1))
            bo = kb.bank()
            for kt in range(16):
                kb.b.emit("pe", lambda e, o=bo.ap()[:, :], l=vt.ap()[:, kt, 128 * dvc:128 * (dvc + 1)],
                          r=pts[qc].ap()[:, kt, :], st=(kt == 0), sp=(kt == 15):
                          e.matmul(o, l, r, start=st, stop=sp),
                          reads=[vt, pts[qc]], writes=[bo], tag="a0_omm")
            kb.ev("dve", lambda e, o=ot.ap()[:, :], i=bo.ap()[:, :], dv=dinv.ap()[:, qc, :]:
                  e.tensor_mul(o, i, dv),
                  reads=[bo, dinv], writes=[ot], tag="a0_onorm")
            kb.ev("dve", lambda e, o=xt.ap()[:, qs], i=ot.ap()[:, :]: e.tensor_add(o, o, i),
                  reads=[ot, xt], writes=[xt], tag="a0_xadd")
    c.X = newX
    for p in pts:
        ar.free(p)
    ar.free(dinv, ot, vt)


def setup_l1attn_a(c):
    kb = c.kb
    c.aa_wq_t = kb.din("aa_wq_t", [2, 128, 4096], F32)
    c.aa_wkv_t = kb.din("aa_wkv_t", [4, 128, 4096], F32)
    c.aa_wo_t = kb.din("aa_wo_t", [2, 128, 4096], F32)
    c.aa_bqc = kb.din("aa_bqc", [128, 8], F32)
    c.aa_bkc = kb.din("aa_bkc", [128, 8], F32)
    c.aa_bvr = kb.din("aa_bvr", [1, 1024], F32)
    c.aa_boc = kb.din("aa_boc", [128, 8], F32)
    c.aa_ropec = kb.din("c_rope1_c", [128, T], F32)
    c.aa_ropes = kb.din("c_rope1_s", [128, T], F32)


def setup_l1attn_b(c):
    kb = c.kb
    c.ab_wq_t = kb.din("ab_wq_t", [2, 128, 4096], F32)
    c.ab_wk_t = kb.din("ab_wk_t", [2, 128, 4096], F32)
    c.ab_wv_t = kb.din("ab_wv_t", [2, 128, 4096], F32)
    c.ab_wo_t = kb.din("ab_wo_t", [2, 128, 4096], F32)


def emit_l1attn(c, variant):
    kb = c.kb
    ar = kb.ar
    scale = 0.125
    isa = variant == "a"
    for ci in range(NCHUNK):
        kb.dma("sp", c.xspill[128 * ci:128 * (ci + 1), :], c.X[ci].ap()[:, :], tag="l1_spill")
    for ci in range(NCHUNK):
        ar.free(c.X[ci])
    c.X = None
    qfm = [ar.alloc(kb.nid("l1q"), [128, T], BF16) for _ in range(NCHUNK)]
    kfm = [ar.alloc(kb.nid("l1k"), [128, T], BF16) for _ in range(NCHUNK)]
    vtm = ar.alloc(kb.nid("l1v"), [128, 16, 1024], BF16)
    wbuf = [ar.alloc(kb.nid("l1w"), [128, 4096], BF16) for _ in range(2)]
    bsb = None
    if isa:
        bsb = {}
        for nm, ap_ in (("bq", c.aa_bqc), ("bk", c.aa_bkc), ("bo", c.aa_boc)):
            t_ = ar.alloc(kb.nid("l1b" + nm), [128, 8], F32)
            kb.dma("sp", t_.ap(), ap_, tag="l1bias")
            bsb[nm] = t_
        bvr = ar.alloc(kb.nid("l1bvr"), [1, 1024], BF16)
        kb.dma("pool", bvr.ap(), c.aa_bvr, tag="l1bvr")
        ctab = ar.alloc(kb.nid("l1ct"), [128, T], F32)
        stab = ar.alloc(kb.nid("l1st"), [128, T], F32)
        kb.dma("sp", ctab.ap(), c.aa_ropec, tag="l1ctl")
        kb.dma("sp", stab.ap(), c.aa_ropes, tag="l1stl")
        t1 = ar.alloc(kb.nid("l1t1"), [128, 512], F32)
        t2 = ar.alloc(kb.nid("l1t2"), [128, 512], F32)
        qtl = ar.alloc(kb.nid("l1qt"), [128, 512], BF16)

    def proj_qk(wt_ap, jslices, dst, biascols):
        for jj, jsl in enumerate(jslices):
            wb = wbuf[jj % 2]
            kb.dma("pool", wb.ap(), wt_ap[jsl, :, :], tag="l1wl")
            for f in range(4):
                fo = jj * 4 + f
                for tc in range(4):
                    ts = slice(512 * tc, 512 * (tc + 1))
                    bk = kb.bank()
                    for ci in range(NCHUNK):
                        kb.b.emit("pe", lambda e, o=bk.ap()[:, :], l=wb.ap()[:, 512 * ci + 128 * f:512 * ci + 128 * (f + 1)],
                                  r=c.NB[ci].ap()[:, ts], st=(ci == 0), sp=(ci == 7):
                                  e.matmul(o, l, r, start=st, stop=sp),
                                  reads=[wb, c.NB[ci]], writes=[bk], tag="l1qkmm")
                    if not isa:
                        kb.ev("act", lambda e, o=dst[fo].ap()[:, ts], i=bk.ap()[:, :]:
                              e.activation(o, i, AF.Identity),
                              reads=[bk], writes=[dst[fo]], tag="l1qkev")
                        continue
                    kb.ev("act", lambda e, o=qtl.ap()[:, :], i=bk.ap()[:, :], b=biascols.ap()[:, fo:fo + 1]:
                          e.activation(o, i, AF.Identity, bias=b),
                          reads=[bk, biascols], writes=[qtl], tag="l1qkbias")
                    bk2 = kb.bank()
                    kb.b.emit("pe", lambda e, o=bk2.ap()[:, :], l=c.swap32.ap()[:, :], r=qtl.ap()[:, :]:
                              e.matmul(o, l, r, start=True, stop=True),
                              reads=[c.swap32, qtl], writes=[bk2], tag="l1swap")
                    kb.ev("dve", lambda e, o=t1.ap()[:, :], i=qtl.ap()[:, :], m=ctab.ap()[:, ts]:
                          e.tensor_mul(o, i, m), reads=[qtl, ctab], writes=[t1], tag="l1rc")
                    kb.ev("dve", lambda e, o=t2.ap()[:, :], i=bk2.ap()[:, :], m=stab.ap()[:, ts]:
                          e.tensor_mul(o, i, m), reads=[bk2, stab], writes=[t2], tag="l1rs")
                    kb.ev("dve", lambda e, o=dst[fo].ap()[:, ts]: e.tensor_add(o, t1.ap()[:, :], t2.ap()[:, :]),
                          reads=[t1, t2], writes=[dst[fo]], tag="l1radd")

    if isa:
        proj_qk(c.aa_wq_t, (0, 1), qfm, bsb["bq"])
        proj_qk(c.aa_wkv_t, (0, 1), kfm, bsb["bk"])
        for jj, jsl in enumerate((2, 3)):
            wb = wbuf[jj % 2]
            kb.dma("pool", wb.ap(), c.aa_wkv_t[jsl, :, :], tag="l1wvl")
            for tt in range(16):
                bk = kb.bank()
                for ci in range(NCHUNK):
                    kb.b.emit("pe", lambda e, o=bk.ap()[:, :], l=c.NB[ci].ap()[:, 128 * tt:128 * (tt + 1)],
                              r=wb.ap()[:, 512 * ci:512 * (ci + 1)], st=(ci == 0), sp=False:
                              e.matmul(o, l, r, start=st, stop=sp),
                              reads=[wb, c.NB[ci]], writes=[bk], tag="l1vmm")
                kb.b.emit("pe", lambda e, o=bk.ap()[:, :], l=c.ones_row.ap()[:, :], r=bvr.ap()[0:1, 512 * jj:512 * (jj + 1)]:
                          e.matmul(o, l, r, start=False, stop=True),
                          reads=[c.ones_row, bvr], writes=[bk], tag="l1vbias")
                kb.ev("act", lambda e, o=vtm.ap()[:, tt, 512 * jj:512 * (jj + 1)], i=bk.ap()[:, :]:
                      e.activation(o, i, AF.Identity),
                      reads=[bk], writes=[vtm], tag="l1vev")
    else:
        proj_qk(c.ab_wq_t, (0, 1), qfm, None)
        proj_qk(c.ab_wk_t, (0, 1), kfm, None)
        for jj in range(2):
            wb = wbuf[jj % 2]
            kb.dma("pool", wb.ap(), c.ab_wv_t[jj, :, :], tag="l1wvl")
            for tt in range(16):
                bk = kb.bank()
                for ci in range(NCHUNK):
                    kb.b.emit("pe", lambda e, o=bk.ap()[:, :], l=c.NB[ci].ap()[:, 128 * tt:128 * (tt + 1)],
                              r=wb.ap()[:, 512 * ci:512 * (ci + 1)], st=(ci == 0), sp=(ci == 7):
                              e.matmul(o, l, r, start=st, stop=sp),
                              reads=[wb, c.NB[ci]], writes=[bk], tag="l1vmm")
                kb.ev("act", lambda e, o=vtm.ap()[:, tt, 512 * jj:512 * (jj + 1)], i=bk.ap()[:, :]:
                      e.activation(o, i, AF.Identity),
                      reads=[bk], writes=[vtm], tag="l1vev")

    for t_ in c.NB:
        ar.free(t_)
    c.NB = None
    if isa:
        ar.free(ctab, stab, t1, t2, qtl, bvr)

    sq = ar.alloc(kb.nid("l1sq"), [128, 512], BF16)
    qmax2 = ar.alloc(kb.nid("l1qm"), [128, 16], F32)
    kmax2 = ar.alloc(kb.nid("l1km"), [128, 16], F32)
    mh = ar.alloc(kb.nid("l1mh"), [128, 16], F32)
    for (src_, dst) in ((qfm, qmax2), (kfm, kmax2)):
        for ch in range(NCHUNK):
            for tc in range(4):
                ts = slice(512 * tc, 512 * (tc + 1))
                kb.ev("act", lambda e, o=sq.ap()[:, :], i=src_[ch].ap()[:, ts]: e.activation(o, i, AF.Square),
                      reads=[src_[ch]], writes=[sq], tag="l1nsq")
                for hb in range(2):
                    h = 2 * ch + hb
                    bk = kb.bank()
                    kb.b.emit("pe", lambda e, o=bk.ap()[:, :], l=c.ones_km.ap()[64 * hb:64 * (hb + 1), :],
                              r=sq.ap()[64 * hb:64 * (hb + 1), :]:
                              e.matmul(o, l, r, start=True, stop=True),
                              reads=[c.ones_km, sq], writes=[bk], tag="l1nmm")
                    if tc == 0:
                        kb.ev("dve", lambda e, o=dst.ap()[:, h:h + 1], i=bk.ap()[:, :]:
                              e.tensor_reduce(o, i, AX.X, ALU.max),
                              reads=[bk], writes=[dst], tag="l1nred")
                    else:
                        kb.ev("dve", lambda e, o=mh.ap()[:, h:h + 1], i=bk.ap()[:, :]:
                              e.tensor_reduce(o, i, AX.X, ALU.max),
                              reads=[bk], writes=[mh], tag="l1nred2")
                        kb.ev("dve", lambda e, o=dst.ap()[:, h:h + 1], a=dst.ap()[:, h:h + 1], b=mh.ap()[:, h:h + 1]:
                              e.tensor_max(o, a, b),
                              reads=[dst, mh], writes=[dst], tag="l1nmax")
    kb.ev("act", lambda e: e.activation(qmax2.ap()[:, :], qmax2.ap()[:, :], AF.Sqrt),
          reads=[qmax2], writes=[qmax2], tag="l1nsqrt")
    kb.ev("act", lambda e: e.activation(kmax2.ap()[:, :], kmax2.ap()[:, :], AF.Sqrt),
          reads=[kmax2], writes=[kmax2], tag="l1nsqrt2")
    kb.ev("dve", lambda e: e.tensor_mul(mh.ap()[:, :], qmax2.ap()[:, :], kmax2.ap()[:, :]),
          reads=[qmax2, kmax2], writes=[mh], tag="l1mh")
    kb.ev("dve", lambda e: e.tensor_scalar_mul(mh.ap()[:, :], mh.ap()[:, :], -scale),
          reads=[mh], writes=[mh], tag="l1mh2")
    ar.free(sq)

    ofm = [ar.alloc(kb.nid("l1o"), [128, T], BF16) for _ in range(NCHUNK)]
    ptsb = [ar.alloc(kb.nid("l1pt"), [128, 16, 512], BF16) for _ in range(2)]
    dinv = ar.alloc(kb.nid("l1dv"), [128, 512], F32)
    caus512 = ar.alloc(kb.nid("l1cz"), [128, 4, 512], BF16)
    kb.dma("sp", caus512.ap(), c.caus512_in, tag="l1czl")

    def spass(h, qc, pts):
        ch = h // 2
        hs = slice(64 * (h % 2), 64 * (h % 2 + 1))
        qs = slice(512 * qc, 512 * (qc + 1))
        for kt in range(4 * qc + 4):
            bst = kb.bank()
            kb.b.emit("pe", lambda e, o=bst.ap()[:, :], l=kfm[ch].ap()[hs, 128 * kt:128 * (kt + 1)],
                      r=qfm[ch].ap()[hs, qs]:
                      e.matmul(o, l, r, start=True, stop=True),
                      reads=[kfm[ch], qfm[ch]], writes=[bst], tag="l1smm")
            kb.ev("act", lambda e, o=pts.ap()[:, kt, :], i=bst.ap()[:, :], b=mh.ap()[:, h:h + 1]:
                  e.activation(o, i, AF.Exp, bias=b, scale=scale),
                  reads=[bst, mh], writes=[pts], tag="l1exp")
            if kt >= 4 * qc:
                j = kt - 4 * qc
                kb.ev("dve", lambda e, o=pts.ap()[:, kt, :], m=caus512.ap()[:, j, :]:
                      e.tensor_mul(o, o, m),
                      reads=[pts, caus512], writes=[pts], tag="l1mask")

    def opass(h, qc, pts):
        ch = h // 2
        hs = slice(64 * (h % 2), 64 * (h % 2 + 1))
        qs = slice(512 * qc, 512 * (qc + 1))
        nkt = 4 * qc + 4
        bdn = kb.bank()
        for kt in range(nkt):
            kb.b.emit("pe", lambda e, o=bdn.ap()[:, :], l=c.ones_km.ap()[:, :], r=pts.ap()[:, kt, :],
                      st=(kt == 0), sp=(kt == nkt - 1): e.matmul(o, l, r, start=st, stop=sp),
                      reads=[c.ones_km, pts], writes=[bdn], tag="l1dmm")
        kb.ev("dve", lambda e, o=dinv.ap()[:, :], i=bdn.ap()[:, :]: e.reciprocal(o, i),
              reads=[bdn], writes=[dinv], tag="l1dinv")
        bo = kb.bank()
        for kt in range(nkt):
            kb.b.emit("pe", lambda e, o=bo.ap()[hs, :], l=vtm.ap()[:, kt, 64 * h:64 * (h + 1)],
                      r=pts.ap()[:, kt, :], st=(kt == 0), sp=(kt == nkt - 1):
                      e.matmul(o, l, r, start=st, stop=sp),
                      reads=[vtm, pts], writes=[bo], tag="l1omm")
        kb.ev("dve", lambda e, o=ofm[ch].ap()[hs, qs], i=bo.ap()[hs, :], dv=dinv.ap()[hs, :]:
              e.tensor_mul(o, i, dv),
              reads=[bo, dinv], writes=[ofm[ch]], tag="l1oev")

    hqs = [(h, qc) for h in range(16) for qc in range(4)]
    for i, (h, qc) in enumerate(hqs):
        spass(h, qc, ptsb[i % 2])
        if i > 0:
            opass(*hqs[i - 1], ptsb[(i - 1) % 2])
    opass(*hqs[-1], ptsb[(len(hqs) - 1) % 2])
    ar.free(ptsb[0], ptsb[1], dinv, vtm, qmax2, kmax2, mh, caus512)
    for ci in range(NCHUNK):
        ar.free(qfm[ci], kfm[ci])

    newX = [None] * NCHUNK
    wo_t = c.aa_wo_t if isa else c.ab_wo_t
    for jj in range(2):
        wb = wbuf[jj % 2]
        kb.dma("pool", wb.ap(), wo_t[jj, :, :], tag="l1wol")
        for f in range(4):
            fo = jj * 4 + f
            xt = ar.alloc(kb.nid(f"Xl{fo}"), [128, T], F32)
            newX[fo] = xt
            kb.dma("sp", xt.ap()[:, :], c.xspill[128 * fo:128 * (fo + 1), :], tag="l1_xld")
            for tc in range(4):
                ts = slice(512 * tc, 512 * (tc + 1))
                bk = kb.bank()
                for ci in range(NCHUNK):
                    kb.b.emit("pe", lambda e, o=bk.ap()[:, :], l=wb.ap()[:, 512 * ci + 128 * f:512 * ci + 128 * (f + 1)],
                              r=ofm[ci].ap()[:, ts], st=(ci == 0), sp=(ci == 7):
                              e.matmul(o, l, r, start=st, stop=sp),
                              reads=[wb, ofm[ci]], writes=[bk], tag="l1pmm")
                if isa:
                    kb.ev("dve", lambda e, o=xt.ap()[:, ts], i=bk.ap()[:, :], b=bsb["bo"].ap()[:, fo:fo + 1]:
                          e.scalar_tensor_tensor(o, i, b, o, ALU.add, ALU.add),
                          reads=[bk, bsb["bo"], xt], writes=[xt], tag="l1xadd")
                else:
                    kb.ev("dve", lambda e, o=xt.ap()[:, ts], i=bk.ap()[:, :]:
                          e.tensor_add(o, o, i),
                          reads=[bk, xt], writes=[xt], tag="l1xadd")
    c.X = newX
    for ci in range(NCHUNK):
        ar.free(ofm[ci])
    ar.free(wbuf[0], wbuf[1])
    if isa:
        ar.free(bsb["bq"], bsb["bk"], bsb["bo"])


def setup_moe(c):
    kb = c.kb
    nc = kb.nc
    c.moe_gate_t = kb.din("moe_gate_t", [128, 8, 8], F32)
    c.moe_gateb = kb.din("moe_gateb", [1, 8], F32)
    c.moe_w1 = kb.din("moe_w1", [8, 16, 128, 2048], F32)
    c.moe_w2 = kb.din("moe_w2", [8, 16, 128, 2048], F32)
    c.moe_b1c = kb.din("moe_b1c", [8, 128, 32], F32)
    c.moe_b2c = kb.din("moe_b2c", [8, 128, 8], F32)
    c.moe_w1bf = nc.dram_tensor("moe_w1bf", [8, 16, 128, 2048], BF16, kind="Internal").ap()
    c.moe_w2bf = nc.dram_tensor("moe_w2bf", [8, 16, 128, 2048], BF16, kind="Internal").ap()
    c.coll_in = nc.dram_tensor("moe_cin", [1, 16], F32, kind="Internal").ap()
    c.coll_out = nc.dram_tensor("moe_cout", [8, 16], F32, kind="Internal").ap()
    c.moe_call = 0
    ar = kb.ar
    c.gate_sb = ar.alloc("moe_gate_sb", [128, 8, 8], BF16)
    c.gateb_sb = ar.alloc("moe_gateb_sb", [1, 8], BF16)
    kb.dma("pool", c.gate_sb.ap(), c.moe_gate_t, tag="moe_gl")
    kb.dma("pool", c.gateb_sb.ap(), c.moe_gateb, tag="moe_gbl")


def emit_moe_prepass(c, experts):
    kb = c.kb
    for e in experts:
        kb.dma("pool", c.moe_w1bf[e], c.moe_w1[e], tag="moe_pp1", dmaclass=6)
        kb.dma("pool", c.moe_w2bf[e], c.moe_w2[e], tag="moe_pp2", dmaclass=7)


def emit_moe(c):
    kb = c.kb
    ar = kb.ar
    E = 8
    logit = ar.alloc(kb.nid("moe_lg"), [128, 128], F32)
    ohbuf = ar.alloc(kb.nid("moe_oh"), [128, 256], BF16)
    cum = ar.alloc(kb.nid("moe_cum"), [128, 256], F32)
    pw = ar.alloc(kb.nid("moe_pw"), [128, 32], F32)
    sm = ar.alloc(kb.nid("moe_sm"), [128, 4], F32)
    top8 = ar.alloc(kb.nid("moe_t8"), [128, 8], F32)
    idx8 = ar.alloc(kb.nid("moe_i8"), [128, 8], mybir.dt.uint32)
    idx8f = ar.alloc(kb.nid("moe_i8f"), [128, 8], F32)
    rt = ar.alloc(kb.nid("moe_rt"), [128, 256], F32)
    csel = ar.alloc(kb.nid("moe_cs"), [128, 32], F32)
    psel = ar.alloc(kb.nid("moe_ps"), [128, 32], F32)
    cnt = ar.alloc(kb.nid("moe_cnt"), [8, 256], F32)
    wexp = ar.alloc(kb.nid("moe_we"), [128, 128], F32)
    wT = ar.alloc(kb.nid("moe_wT"), [8, T], BF16)

    blg = kb.bank()
    for ti in range(16):
        tsl = slice(128 * ti, 128 * (ti + 1))
        for ci in range(NCHUNK):
            kb.b.emit("pe", lambda e, o=blg.ap()[:, 8 * ti:8 * (ti + 1)], l=c.NB[ci].ap()[:, tsl],
                      r=c.gate_sb.ap()[:, ci, :], st=(ti == 0 and ci == 0), sp=False:
                      e.matmul(o, l, r, start=st, stop=sp),
                      reads=[c.NB[ci], c.gate_sb], writes=[blg], tag="moe_gmm")
        kb.b.emit("pe", lambda e, o=blg.ap()[:, 8 * ti:8 * (ti + 1)], l=c.ones_row.ap()[:, :],
                  r=c.gateb_sb.ap()[0:1, :], st=False, sp=(ti == 15):
                  e.matmul(o, l, r, start=False, stop=sp),
                  reads=[c.ones_row, c.gateb_sb], writes=[blg], tag="moe_gbm")
    kb.ev("dve", lambda e: e.tensor_copy(logit.ap()[:, :], blg.ap()[:, 0:128]),
          reads=[blg], writes=[logit], tag="moe_gev")
    for ti in range(16):
        L8 = logit.ap()[:, 8 * ti:8 * (ti + 1)]
        kb.ev("dve", lambda e, L8=L8: e.reduce_max(sm.ap()[:, 0:1], L8, AX.X),
              reads=[logit], writes=[sm], tag="moe_smx")
        kb.ev("dve", lambda e, L8=L8: e.tensor_scalar(L8, L8, sm.ap()[:, 0:1], None, ALU.subtract),
              reads=[logit, sm], writes=[logit], tag="moe_ssub")
        kb.ev("act", lambda e, L8=L8: e.activation(L8, L8, AF.Exp, accum_out=sm.ap()[:, 1:2]),
              reads=[logit], writes=[logit, sm], tag="moe_sexp")
        kb.ev("dve", lambda e: e.reciprocal(sm.ap()[:, 2:3], sm.ap()[:, 1:2]),
              reads=[sm], writes=[sm], tag="moe_srec")
        kb.ev("dve", lambda e, L8=L8: e.tensor_scalar(L8, L8, sm.ap()[:, 2:3], None, ALU.mult),
              reads=[logit, sm], writes=[logit], tag="moe_smul")
        kb.ev("dve", lambda e, L8=L8: e.max(top8.ap()[:, :], L8),
              reads=[logit], writes=[top8], tag="moe_top8")
        kb.ev("dve", lambda e, L8=L8: e.max_index(idx8.ap()[:, :], top8.ap()[:, :], L8),
              reads=[logit, top8], writes=[idx8], tag="moe_idx8")
        kb.ev("dve", lambda e: e.tensor_copy(idx8f.ap()[:, 0:2], idx8.ap()[:, 0:2]),
              reads=[idx8], writes=[idx8f], tag="moe_idxc")
        kb.ev("dve", lambda e, o=pw.ap()[:, 2 * ti:2 * ti + 2]: e.tensor_copy(o, top8.ap()[:, 0:2]),
              reads=[top8], writes=[pw], tag="moe_pwc")
        for cc in range(2):
            kb.ev("dve", lambda e, o=ohbuf.ap()[:, 16 * ti + 8 * cc:16 * ti + 8 * (cc + 1)], ic=cc:
                  e.tensor_scalar(o, c.eidx.ap()[:, :], idx8f.ap()[:, ic:ic + 1], None, ALU.is_equal),
                  reads=[c.eidx, idx8f], writes=[ohbuf], tag="moe_oh")
    bcm = kb.bank()
    for ti in range(16):
        for j in range(ti + 1):
            lj = c.utri if j == ti else c.ones_km
            kb.b.emit("pe", lambda e, o=bcm.ap()[:, 16 * ti:16 * (ti + 1)], l=lj.ap()[:, :],
                      r=ohbuf.ap()[:, 16 * j:16 * (j + 1)],
                      st=(ti == 0 and j == 0), sp=(ti == 15 and j == 15):
                      e.matmul(o, l, r, start=st, stop=sp),
                      reads=[lj, ohbuf], writes=[bcm], tag="moe_cmm")
    kb.ev("dve", lambda e: e.tensor_copy(cum.ap()[:, :], bcm.ap()[:, 0:256]),
          reads=[bcm], writes=[cum], tag="moe_cev")
    kb.dma("sp", c.coll_in[:, :], cum.ap()[127:128, 240:256], tag="moe_cst")
    kb.b.emit("pool", lambda e: e.collective_compute("AllGather", ALU.bypass,
                                                     [[0, 1, 2, 3, 4, 5, 6, 7]],
                                                     [c.coll_in[:, :]], [c.coll_out[:, :]]),
              reads=[c.coll_in], writes=[c.coll_out], tag="moe_ag")
    kb.dma("sp", cnt.ap()[:, 0:16], c.coll_out[:, :], tag="moe_cld")
    for k in range(1, 16):
        kb.ev("dve", lambda e, o=cnt.ap()[:, 16 * k:16 * (k + 1)]: e.tensor_copy(o, cnt.ap()[:, 0:16]),
              reads=[cnt], writes=[cnt], tag="moe_ctile")
    bpf = kb.bank()
    kb.b.emit("pe", lambda e: e.matmul(bpf.ap()[:, 0:256], c.prefmask.ap()[:, :], cnt.ap()[:, :],
                                       start=True, stop=True),
              reads=[c.prefmask, cnt], writes=[bpf], tag="moe_pfmm")
    kb.ev("dve", lambda e: e.tensor_mul(rt.ap()[:, :], cum.ap()[:, :], ohbuf.ap()[:, :]),
          reads=[cum, ohbuf], writes=[rt], tag="moe_cm")
    kb.ev("dve", lambda e: e.tensor_reduce(csel.ap()[:, :],
                                           rt.ap()[:, :].rearrange("p (a b) -> p a b", b=8), AX.X, ALU.add),
          reads=[rt], writes=[csel], tag="moe_cr")
    kb.ev("dve", lambda e: e.tensor_mul(rt.ap()[:, :], bpf.ap()[:, 0:256], ohbuf.ap()[:, :]),
          reads=[bpf, ohbuf], writes=[rt], tag="moe_pm")
    kb.ev("dve", lambda e: e.tensor_reduce(psel.ap()[:, :],
                                           rt.ap()[:, :].rearrange("p (a b) -> p a b", b=8), AX.X, ALU.add),
          reads=[rt], writes=[psel], tag="moe_pr")
    kb.ev("dve", lambda e: e.scalar_tensor_tensor(csel.ap()[:, :], csel.ap()[:, :], -1.0,
                                                  psel.ap()[:, :], ALU.add, ALU.add),
          reads=[csel, psel], writes=[csel], tag="moe_pos")
    kb.ev("dve", lambda e: e.tensor_single_scalar(csel.ap()[:, :], csel.ap()[:, :], float(CAP), ALU.is_lt),
          reads=[csel], writes=[csel], tag="moe_keep")
    kb.ev("dve", lambda e: e.tensor_mul(pw.ap()[:, :], pw.ap()[:, :], csel.ap()[:, :]),
          reads=[pw, csel], writes=[pw], tag="moe_w")
    for ti in range(16):
        for cc in range(2):
            dst = wexp.ap()[:, 8 * ti:8 * (ti + 1)]
            ohs = ohbuf.ap()[:, 16 * ti + 8 * cc:16 * ti + 8 * (cc + 1)]
            if cc == 0:
                kb.ev("dve", lambda e, dst=dst, ohs=ohs, w=pw.ap()[:, 2 * ti:2 * ti + 1]:
                      e.tensor_scalar(dst, ohs, w, None, ALU.mult),
                      reads=[ohbuf, pw], writes=[wexp], tag="moe_wx0")
            else:
                kb.ev("dve", lambda e, dst=dst, ohs=ohs, w=pw.ap()[:, 2 * ti + 1:2 * ti + 2]:
                      e.scalar_tensor_tensor(dst, ohs, w, dst, ALU.mult, ALU.add),
                      reads=[ohbuf, pw, wexp], writes=[wexp], tag="moe_wx1")
    for ti in range(16):
        bt = kb.bank()
        kb.b.emit("pe", lambda e, o=bt.ap()[0:8, 0:128], i=wexp.ap()[:, 8 * ti:8 * (ti + 1)]:
                  e.transpose(o, i, c.ident_f32.ap()[:, :]),
                  reads=[wexp, c.ident_f32], writes=[bt], tag="moe_wtt")
        kb.ev("dve", lambda e, o=wT.ap()[:, 128 * ti:128 * (ti + 1)], i=bt.ap()[0:8, 0:128]:
              e.tensor_copy(o, i),
              reads=[bt], writes=[wT], tag="moe_wtev")
    ar.free(logit, cum, pw, sm, top8, idx8, idx8f, rt, csel, psel, cnt, wexp, ohbuf)

    NW1, NW2 = 2, 4
    hbufs = [ar.alloc(kb.nid("moe_h"), [128, 8, 1024], BF16) for _ in range(4)]
    wred = ar.alloc(kb.nid("moe_wr"), [128, 1024], BF16)
    w1sl = [ar.alloc(kb.nid("moe_w1s"), [128, 2048], BF16) for _ in range(NW1)]
    w2sl = [ar.alloc(kb.nid("moe_w2s"), [128, 2048], BF16) for _ in range(NW2)]
    b1c = ar.alloc(kb.nid("moe_b1s"), [128, 32], F32)
    b2c = ar.alloc(kb.nid("moe_b2s"), [128, 8], F32)
    nw1 = nw2 = 0
    for ex in range(E):
        if c.moe_call == 0 and c.moe_spread and ex < 6:
            emit_moe_prepass(c, [ex + 2])
        if c.moe_call == 0 and c.b0_spread and ex < 7:
            emit_b0_prepass(c, [ex])
        kb.dma("sp", b1c.ap()[:, :], c.moe_b1c[ex, :, :], tag="moe_b1l")
        kb.dma("sp", b2c.ap()[:, :], c.moe_b2c[ex, :, :], tag="moe_b2l")
        for tcg in range(2):
            tb = 1024 * tcg
            for qc2 in range(2):
                bw = kb.bank()
                kb.b.emit("pe", lambda e, o=bw.ap()[:, :], l=c.sel8.ap()[:, 128 * ex:128 * (ex + 1)],
                          r=wT.ap()[:, tb + 512 * qc2:tb + 512 * (qc2 + 1)]:
                          e.matmul(o, l, r, start=True, stop=True),
                          reads=[c.sel8, wT], writes=[bw], tag="moe_wrmm")
                kb.ev("dve", lambda e, o=wred.ap()[:, 512 * qc2:512 * (qc2 + 1)], i=bw.ap()[:, :]:
                      e.tensor_copy(o, i),
                      reads=[bw], writes=[wred], tag="moe_wrev")
            for jj in range(16):
                wsl = w1sl[nw1 % NW1]
                nw1 += 1
                kb.dma("sp" if nw1 % 2 == 0 else "pool", wsl.ap()[:, :], c.moe_w1bf[ex, jj, :, :],
                       tag="moe_w1l", extra_reads=["moe_w1bf"])
                for u in range(2):
                    j = 2 * jj + u
                    for tc2 in range(2):
                        ts = slice(tb + 512 * tc2, tb + 512 * (tc2 + 1))
                        bk = kb.bank()
                        for ci in range(NCHUNK):
                            kb.b.emit("pe", lambda e, o=bk.ap()[:, :],
                                      l=wsl.ap()[:, 1024 * u + 128 * ci:1024 * u + 128 * (ci + 1)],
                                      r=c.NB[ci].ap()[:, ts], st=(ci == 0), sp=(ci == 7):
                                      e.matmul(o, l, r, start=st, stop=sp),
                                      reads=[wsl, c.NB[ci]], writes=[bk], tag="moe_hmm")
                        hb = hbufs[j // 8]
                        kb.ev("act", lambda e, o=hb.ap()[:, j % 8, 512 * tc2:512 * (tc2 + 1)],
                              i=bk.ap()[:, :], b=b1c.ap()[:, j:j + 1]:
                              e.activation(o, i, AF.Relu, bias=b),
                              reads=[bk, b1c], writes=[hb], tag="moe_hev")
            for tc2 in range(2):
                ts = slice(tb + 512 * tc2, tb + 512 * (tc2 + 1))
                hs = slice(512 * tc2, 512 * (tc2 + 1))
                bks = kb.banks(8)
                for jj in range(16):
                    wsl = w2sl[nw2 % NW2]
                    nw2 += 1
                    kb.dma("sp" if nw2 % 2 == 0 else "pool", wsl.ap()[:, :], c.moe_w2bf[ex, jj, :, :],
                           tag="moe_w2l", extra_reads=["moe_w2bf"])
                    for u in range(2):
                        k2 = 2 * jj + u
                        hb = hbufs[k2 // 8]
                        for fo2 in range(8):
                            kb.b.emit("pe", lambda e, o=bks[fo2].ap()[:, :],
                                      l=wsl.ap()[:, 1024 * u + 128 * fo2:1024 * u + 128 * (fo2 + 1)],
                                      r=hb.ap()[:, k2 % 8, hs], st=(k2 == 0), sp=(k2 == 31):
                                      e.matmul(o, l, r, start=st, stop=sp),
                                      reads=[wsl, hb], writes=[bks[fo2]], tag="moe_ymm")
                for fo2 in range(8):
                    bk = bks[fo2]
                    kb.ev("dve", lambda e, o=bk.ap()[:, :], i=bk.ap()[:, :], b=b2c.ap()[:, fo2:fo2 + 1],
                          w=wred.ap()[:, hs]:
                          e.scalar_tensor_tensor(o, i, b, w, ALU.add, ALU.mult),
                          reads=[bk, b2c, wred], writes=[bk], tag="moe_yev")
                    kb.ev("dve", lambda e, o=c.X[fo2].ap()[:, ts], i=bk.ap()[:, :]:
                          e.tensor_add(o, o, i),
                          reads=[bk, c.X[fo2]], writes=[c.X[fo2]], tag="moe_xadd")
    for t_ in c.NB:
        ar.free(t_)
    c.NB = None
    ar.free(wred, *w1sl, *w2sl, *hbufs, b1c, b2c, wT)
    c.moe_call += 1


def setup_attn_b0(c):
    kb = c.kb
    nc = kb.nc
    c.b0_win = kb.din("b0_win", [128, 8, 2048], F32)
    c.b0_wq = kb.din("b0_wq", [6, 128, 1024], F32)
    c.b0_wk = kb.din("b0_wk", [6, 128, 1024], F32)
    c.b0_wk_fi = kb.din("b0_wk_fi", [128, 8, 768], F32)
    c.b0_wv_fi = kb.din("b0_wv_fi", [128, 8, 1536], F32)
    c.b0_wg = kb.din("b0_wg", [12, 128, 1024], F32)
    c.b0_wgkb = kb.din("b0_wgkb", [128, 8, 8], F32)
    c.b0_wso = kb.din("b0_wso", [8, 128, 1536], F32)
    c.b0_wout = kb.din("b0_wout", [8, 128, 2048], F32)
    c.b0_convw = kb.din("b0_convw", [128, 8, 8], F32)
    c.b0_convb = kb.din("b0_convb", [128, 16], F32)
    c.b0_alog = kb.din("b0_alog", [4, 1], F32)
    c.b0_dtb = kb.din("b0_dtb", [4, 1], F32)
    c.b0_bb = kb.din("b0_bb", [4, 1], F32)
    c.b0_gnc = kb.din("b0_gnc", [128, 3], F32)
    c.c_sel4 = kb.din("c_sel4", [4, 512], F32)
    c.c_umask_s = kb.din("c_umask_s", [128, 128], F32)
    c.c_umask_i = kb.din("c_umask_i", [128, 128], F32)
    c.c_lmask_s = kb.din("c_lmask_s", [128, 128], F32)
    c.b0_wqb_b = nc.dram_tensor("b0_wqb", [6, 128, 1024], BF16, kind="Internal").ap()
    c.b0_wkb_b = nc.dram_tensor("b0_wkb", [6, 128, 1024], BF16, kind="Internal").ap()
    c.b0_wkfib_b = nc.dram_tensor("b0_wkfib", [128, 8, 768], BF16, kind="Internal").ap()
    c.b0_wvfib_b = nc.dram_tensor("b0_wvfib", [128, 8, 1536], BF16, kind="Internal").ap()
    c.b0_wgb_b = nc.dram_tensor("b0_wgb", [12, 128, 1024], BF16, kind="Internal").ap()
    c.b0_wsob_b = nc.dram_tensor("b0_wsob", [8, 128, 1536], BF16, kind="Internal").ap()
    c.b0_woutb_b = nc.dram_tensor("b0_woutb", [8, 128, 2048], BF16, kind="Internal").ap()
    c.zspill = nc.dram_tensor("zspill", [128, 8, 2048], BF16, kind="Internal").ap()


def emit_b0_prepass(c, idxs):
    kb = c.kb
    pairs = ((c.b0_wqb_b, c.b0_wq), (c.b0_wkb_b, c.b0_wk), (c.b0_wkfib_b, c.b0_wk_fi),
             (c.b0_wvfib_b, c.b0_wv_fi), (c.b0_wgb_b, c.b0_wg), (c.b0_wsob_b, c.b0_wso),
             (c.b0_woutb_b, c.b0_wout))
    for i in idxs:
        dst, src_ = pairs[i]
        kb.dma("pool", dst, src_, tag="b0_pp", dmaclass=6 + i % 2)


def emit_attn_b0(c):
    kb = c.kb
    ar = kb.ar
    H, DK, DV = 4, 192, 384
    SEG, NSEG, CH = 512, 4, 128

    for ci in range(NCHUNK):
        kb.dma("sp", c.xspill[128 * ci:128 * (ci + 1), :], c.X[ci].ap()[:, :], tag="b0_spill")
    for ci in range(NCHUNK):
        ar.free(c.X[ci])
    c.X = None
    winb = ar.alloc(kb.nid("b0_winb"), [128, 8, 2048], BF16)
    kb.dma("pool", winb.ap(), c.b0_win, tag="b0_winl")
    xcb = ar.alloc(kb.nid("b0_xcb"), [128, 8, T], BF16)
    zcb = ar.alloc(kb.nid("b0_zcb"), [128, 8, T], BF16)
    for half, dst in ((0, xcb), (1, zcb)):
        for f in range(8):
            fo = half * 8 + f
            for tc in range(4):
                ts = slice(512 * tc, 512 * (tc + 1))
                bk = kb.bank()
                for ci in range(NCHUNK):
                    kb.b.emit("pe", lambda e, o=bk.ap()[:, :], l=winb.ap()[:, ci, 128 * fo:128 * (fo + 1)],
                              r=c.NB[ci].ap()[:, ts], st=(ci == 0), sp=(ci == 7):
                              e.matmul(o, l, r, start=st, stop=sp),
                              reads=[winb, c.NB[ci]], writes=[bk], tag="b0_xzmm")
                kb.ev("act", lambda e, o=dst.ap()[:, f, ts], i=bk.ap()[:, :]:
                      e.activation(o, i, AF.Identity),
                      reads=[bk], writes=[dst], tag="b0_xzev")
    for t_ in c.NB:
        ar.free(t_)
    c.NB = None
    ar.free(winb)
    cvw = ar.alloc(kb.nid("b0_cw"), [128, 8, 8], F32)
    cvb = ar.alloc(kb.nid("b0_cb"), [128, 16], F32)
    kb.dma("sp", cvw.ap(), c.b0_convw, tag="b0_cwl")
    kb.dma("sp", cvb.ap(), c.b0_convb, tag="b0_cbl")
    xsb = ar.alloc(kb.nid("b0_xsb"), [128, 8, T], BF16)
    zsb = ar.alloc(kb.nid("b0_zsb"), [128, 8, T], BF16)
    acc = ar.alloc(kb.nid("b0_acc"), [128, T], F32)
    for half, srcb, dstb in ((0, xcb, xsb), (1, zcb, zsb)):
        for f in range(8):
            xa = srcb.ap()
            w0 = cvw.ap()[:, f, 4 * half + 0:4 * half + 1]
            w1 = cvw.ap()[:, f, 4 * half + 1:4 * half + 2]
            w2 = cvw.ap()[:, f, 4 * half + 2:4 * half + 3]
            w3 = cvw.ap()[:, f, 4 * half + 3:4 * half + 4]
            kb.ev("dve", lambda e, o=acc.ap()[:, :], i=xa[:, f, :], s=w1:
                  e.tensor_scalar(o, i, s, None, ALU.mult),
                  reads=[srcb, cvw], writes=[acc], tag="b0_cv1")
            kb.ev("dve", lambda e, o=acc.ap()[:, 1:T], i=xa[:, f, 0:T - 1], s=w0:
                  e.scalar_tensor_tensor(o, i, s, acc.ap()[:, 1:T], ALU.mult, ALU.add),
                  reads=[srcb, cvw, acc], writes=[acc], tag="b0_cv0")
            kb.ev("dve", lambda e, o=acc.ap()[:, 0:T - 1], i=xa[:, f, 1:T], s=w2:
                  e.scalar_tensor_tensor(o, i, s, acc.ap()[:, 0:T - 1], ALU.mult, ALU.add),
                  reads=[srcb, cvw, acc], writes=[acc], tag="b0_cv2")
            kb.ev("dve", lambda e, o=acc.ap()[:, 0:T - 2], i=xa[:, f, 2:T], s=w3:
                  e.scalar_tensor_tensor(o, i, s, acc.ap()[:, 0:T - 2], ALU.mult, ALU.add),
                  reads=[srcb, cvw, acc], writes=[acc], tag="b0_cv3")
            kb.ev("act", lambda e, o=dstb.ap()[:, f, :], i=acc.ap()[:, :],
                  b=cvb.ap()[:, 8 * half + f:8 * half + f + 1]:
                  e.activation(o, i, AF.Silu, bias=b),
                  reads=[acc, cvb], writes=[dstb], tag="b0_cvs")
    kb.dbg("xsb", xsb, [xsb])
    ar.free(xcb, zcb, acc, cvw, cvb)
    kb.dma("sp", c.zspill, zsb.ap(), tag="b0_zsp")
    ar.free(zsb)

    alog = ar.alloc(kb.nid("b0_al"), [4, 1], F32)
    dtb = ar.alloc(kb.nid("b0_dt"), [4, 1], F32)
    bbc = ar.alloc(kb.nid("b0_bbc"), [4, 1], F32)
    gnc = ar.alloc(kb.nid("b0_gn"), [128, 3], F32)
    onescol = ar.alloc(kb.nid("b0_onec"), [128, 1], F32)
    epscol = ar.alloc(kb.nid("b0_epsc"), [128, 1], F32)
    sel4 = ar.alloc(kb.nid("b0_sel4"), [4, 512], F32)
    umask_s = ar.alloc(kb.nid("b0_ums"), [128, 128], F32)
    umask_i = ar.alloc(kb.nid("b0_umi"), [128, 128], F32)
    lmask_s = ar.alloc(kb.nid("b0_lms"), [128, 128], F32)
    wgkb = ar.alloc(kb.nid("b0_wgkbs"), [128, 8, 8], BF16)
    for (t_, ap_) in ((alog, c.b0_alog), (dtb, c.b0_dtb), (bbc, c.b0_bb), (gnc, c.b0_gnc),
                      (sel4, c.c_sel4), (umask_s, c.c_umask_s), (umask_i, c.c_umask_i),
                      (lmask_s, c.c_lmask_s)):
        kb.dma("sp", t_.ap(), ap_, tag="b0c")
    kb.dma("pool", wgkb.ap(), c.b0_wgkb, tag="b0c2")
    kb.ev("dve", lambda e: e.memset(onescol.ap()[:, :], 1.0), reads=[], writes=[onescol], tag="b0_ones")
    kb.ev("dve", lambda e: e.memset(epscol.ap()[:, :], 1e-5), reads=[], writes=[epscol], tag="b0_eps")
    negA = ar.alloc(kb.nid("b0_negA"), [4, 1], F32)
    kb.ev("act", lambda e: e.activation(negA.ap()[:, :], alog.ap()[:, :], AF.Exp),
          reads=[alog], writes=[negA], tag="b0_expA")
    kb.ev("dve", lambda e: e.tensor_scalar_mul(negA.ap()[:, :], negA.ap()[:, :], -1.0),
          reads=[negA], writes=[negA], tag="b0_negA")

    Sa = ar.alloc(kb.nid("b0_Sa"), [128, 4, DV], F32)
    Sb = ar.alloc(kb.nid("b0_Sb"), [128, 2, DV], F32)
    Sab = ar.alloc(kb.nid("b0_Sab"), [128, 4, DV], BF16)
    Sbb = ar.alloc(kb.nid("b0_Sbb"), [128, 2, DV], BF16)
    for t_ in (Sa, Sb, Sab, Sbb):
        kb.ev("dve", lambda e, a=t_.ap()[:, :, :]: e.memset(a, 0.0), reads=[], writes=[t_], tag="b0_s0")

    fullc = [0, 2, 3, 5]
    halfc = [1, 1, 4, 4]
    halfb = [0, 64, 0, 64]

    qn = ar.alloc(kb.nid("b0_qn"), [128, 6, SEG], BF16)
    qg = ar.alloc(kb.nid("b0_qg"), [128, 6, SEG], BF16)
    kn = ar.alloc(kb.nid("b0_kn"), [128, 6, SEG], BF16)
    knr = ar.alloc(kb.nid("b0_knr"), [128, 6, SEG], BF16)
    ktm = ar.alloc(kb.nid("b0_ktm"), [128, 4, 768], BF16)
    vtm = ar.alloc(kb.nid("b0_vtm"), [128, 4, 1536], BF16)
    ofm = ar.alloc(kb.nid("b0_ofm"), [128, 12, SEG], BF16)
    gkb2 = ar.alloc(kb.nid("b0_gkb2"), [4, 2 * SEG], F32)
    cum = ar.alloc(kb.nid("b0_cum"), [4, SEG], F32)
    grow = ar.alloc(kb.nid("b0_grow"), [4, SEG], F32)
    ratr = ar.alloc(kb.nid("b0_ratr"), [4, SEG], F32)
    wqb = [ar.alloc(kb.nid("b0_wqb"), [128, 1024], BF16) for _ in range(2)]
    wfi = [ar.alloc(kb.nid("b0_wfi"), [128, 8, 512], BF16) for _ in range(2)]
    scl = ar.alloc(kb.nid("b0_scl"), [128, SEG], F32)
    sq = ar.alloc(kb.nid("b0_sq"), [128, SEG], BF16)
    nrm = ar.alloc(kb.nid("b0_nrm"), [128, SEG], F32)
    cumc = ar.alloc(kb.nid("b0_cumc"), [128, 1], F32)
    ratc = ar.alloc(kb.nid("b0_ratc"), [128, 1], F32)
    bnc = ar.alloc(kb.nid("b0_bnc"), [128, 1], F32)
    gcc = ar.alloc(kb.nid("b0_gcc"), [128, 1], F32)
    ncol = ar.alloc(kb.nid("b0_ncol"), [128, 1], F32)
    gcolx = ar.alloc(kb.nid("b0_gcolx"), [128, 1], F32)
    row0 = ar.alloc(kb.nid("b0_row0"), [1, 128], F32)
    row1 = ar.alloc(kb.nid("b0_row1"), [1, 128], F32)
    row2 = ar.alloc(kb.nid("b0_row2"), [1, 128], F32)
    gend = ar.alloc(kb.nid("b0_gend"), [1, 16], F32)
    em = ar.alloc(kb.nid("b0_em"), [128, CH], F32)
    emL = ar.alloc(kb.nid("b0_emL"), [128, CH], F32)
    emU = ar.alloc(kb.nid("b0_emU"), [128, CH], F32)
    emI = ar.alloc(kb.nid("b0_emI"), [128, CH], F32)
    NTm = ar.alloc(kb.nid("b0_NTm"), [128, CH], F32)
    Lm = ar.alloc(kb.nid("b0_Lm"), [128, CH], F32)
    NT2 = ar.alloc(kb.nid("b0_NT2"), [128, CH], F32)
    L2 = ar.alloc(kb.nid("b0_L2"), [128, CH], F32)
    MqT = ar.alloc(kb.nid("b0_MqT"), [128, CH], BF16)
    rhs = ar.alloc(kb.nid("b0_rhs"), [128, DV], F32)
    rv = ar.alloc(kb.nid("b0_rv"), [128, DV], F32)
    rvb = ar.alloc(kb.nid("b0_rvb"), [128, DV], BF16)
    kpc = ar.alloc(kb.nid("b0_kpc"), [128, DK], BF16)
    oscr = ar.alloc(kb.nid("b0_oscr"), [128, SEG], F32)
    ysob = ar.alloc(kb.nid("b0_ysob"), [128, 8, SEG], BF16)
    zss = ar.alloc(kb.nid("b0_zss"), [128, 8, SEG], BF16)
    wsob = [ar.alloc(kb.nid("b0_wsos"), [128, 1536], BF16) for _ in range(2)]
    woutb = [ar.alloc(kb.nid("b0_wouts"), [128, 2048], BF16) for _ in range(2)]
    wgb = [ar.alloc(kb.nid("b0_wgs"), [128, 1024], BF16) for _ in range(2)]
    gsil = ar.alloc(kb.nid("b0_gsil"), [128, SEG], BF16)
    xseg = [ar.alloc(kb.nid("b0_xseg"), [128, SEG], F32) for _ in range(2)]

    for s in range(NSEG):
        ss = slice(SEG * s, SEG * (s + 1))
        for (wt, dst) in ((c.b0_wqb_b, qn), (c.b0_wkb_b, kn)):
            for j in range(6):
                wb = wqb[j % 2]
                kb.dma("sp", wb.ap(), wt[j, :, :], tag="b0_wql")
                bk = kb.bank()
                for ci in range(NCHUNK):
                    kb.b.emit("pe", lambda e, o=bk.ap()[:, :], l=wb.ap()[:, 128 * ci:128 * (ci + 1)],
                              r=xsb.ap()[:, ci, ss], st=(ci == 0), sp=(ci == 7):
                              e.matmul(o, l, r, start=st, stop=sp),
                              reads=[wb, xsb], writes=[bk], tag="b0_qkmm")
                kb.ev("act", lambda e, o=dst.ap()[:, j, :], i=bk.ap()[:, :]:
                      e.activation(o, i, AF.Identity),
                      reads=[bk], writes=[dst], tag="b0_qkev")
        wix = 0
        for (wsrc, dst, nd) in ((c.b0_wkfib_b, ktm, 768), (c.b0_wvfib_b, vtm, 1536)):
            for np_ in range(ceil_div(nd, 512)):
                n0 = 512 * np_
                n1 = min(512 * (np_ + 1), nd)
                wb2 = wfi[wix % 2]
                wix += 1
                kb.dma("sp", wb2.ap()[:, :, 0:n1 - n0], wsrc[:, :, n0:n1], tag="b0_wfil")
                for tt in range(4):
                    tsl = slice(SEG * s + 128 * tt, SEG * s + 128 * (tt + 1))
                    bk = kb.bank()
                    for ci in range(NCHUNK):
                        kb.b.emit("pe", lambda e, o=bk.ap()[:, 0:n1 - n0], l=xsb.ap()[:, ci, tsl],
                                  r=wb2.ap()[:, ci, 0:n1 - n0], st=(ci == 0), sp=(ci == 7):
                                  e.matmul(o, l, r, start=st, stop=sp),
                                  reads=[xsb, wb2], writes=[bk], tag="b0_tmmm")
                    kb.ev("act", lambda e, o=dst.ap()[:, tt, n0:n1], i=bk.ap()[:, 0:n1 - n0]:
                          e.activation(o, i, AF.Identity),
                          reads=[bk], writes=[dst], tag="b0_tmev")
        bgk = kb.bank()
        bbt = kb.bank()
        for half in range(2):
            bx = bgk if half == 0 else bbt
            for ci in range(NCHUNK):
                kb.b.emit("pe", lambda e, o=bx.ap()[0:4, 0:SEG],
                          l=wgkb.ap()[:, ci, 4 * half:4 * half + 4],
                          r=xsb.ap()[:, ci, ss], st=(ci == 0), sp=(ci == 7):
                          e.matmul(o, l, r, start=st, stop=sp),
                          reads=[wgkb, xsb], writes=[bx], tag="b0_gkmm")
        kb.ev("act", lambda e, o=gkb2.ap()[:, 0:SEG], i=bgk.ap()[0:4, 0:SEG], b=dtb.ap()[:, :]:
              e.activation(o, i, AF.Identity, bias=b),
              reads=[bgk, dtb], writes=[gkb2], tag="b0_gk0")
        kb.ev("act", lambda e, o=gkb2.ap()[:, SEG:2 * SEG], i=bbt.ap()[0:4, 0:SEG], b=bbc.ap()[:, :]:
              e.activation(o, i, AF.Sigmoid, bias=b),
              reads=[bbt, bbc], writes=[gkb2], tag="b0_beta")
        kb.ev("act", lambda e, o=grow.ap()[:, :], i=gkb2.ap()[:, 0:SEG]:
              e.activation(o, i, AF.Abs), reads=[gkb2], writes=[grow], tag="b0_abs")
        kb.ev("act", lambda e, o=grow.ap()[:, :]: e.activation(o, grow.ap()[:, :], AF.Exp, scale=-1.0),
              reads=[grow], writes=[grow], tag="b0_expn")
        kb.ev("act", lambda e, o=grow.ap()[:, :], b=onescol.ap()[0:4, :]:
              e.activation(o, grow.ap()[:, :], AF.Ln, bias=b),
              reads=[grow, onescol], writes=[grow], tag="b0_ln1p")
        kb.ev("dve", lambda e, o=cum.ap()[:, :], i=gkb2.ap()[:, 0:SEG]: e.tensor_relu(o, i),
              reads=[gkb2], writes=[cum], tag="b0_relu")
        kb.ev("dve", lambda e, o=cum.ap()[:, :]: e.tensor_add(o, o, grow.ap()[:, :]),
              reads=[cum, grow], writes=[cum], tag="b0_spadd")
        kb.ev("dve", lambda e, o=cum.ap()[:, :], s=negA.ap()[:, :]:
              e.tensor_scalar(o, cum.ap()[:, :], s, None, ALU.mult),
              reads=[cum, negA], writes=[cum], tag="b0_gkmul")
        for cc in range(4):
            cs = slice(128 * cc, 128 * (cc + 1))
            kb.ev("dve", lambda e, o=cum.ap()[:, cs], i=cum.ap()[:, cs]:
                  e.tensor_tensor_scan(o, i, i, 0.0, ALU.add, ALU.bypass),
                  reads=[cum], writes=[cum], tag="b0_scan")
        kb.ev("act", lambda e, o=grow.ap()[:, :], i=cum.ap()[:, :]: e.activation(o, i, AF.Exp),
              reads=[cum], writes=[grow], tag="b0_gamma")
        for cc in range(4):
            cs = slice(128 * cc, 128 * (cc + 1))
            kb.ev("dve", lambda e, o=ratr.ap()[:, cs], i=cum.ap()[:, cs],
                  s=cum.ap()[:, 128 * cc + 127:128 * cc + 128]:
                  e.tensor_scalar(o, i, s, None, ALU.subtract),
                  reads=[cum], writes=[ratr], tag="b0_ratd")
        kb.ev("act", lambda e, o=ratr.ap()[:, :]: e.activation(o, ratr.ap()[:, :], AF.Exp, scale=-1.0),
              reads=[ratr], writes=[ratr], tag="b0_rate")
        for h in range(H):
            kb.dma("sp", gend.ap()[0:1, 4 * h:4 * (h + 1)], grow.ap()[h:h + 1, 127:512:128],
                   tag="b0_gend", slow_ok=True)
        if s == 0:
            kb.dbg("cum0", cum, [cum])
            kb.dbg("beta0", gkb2.ap()[:, SEG:2 * SEG], [gkb2])
        for h in range(H):
            fc, hc, hb = fullc[h], halfc[h], halfb[h]
            hsl = slice(hb, hb + 64)

            def pieces(buf):
                return ((buf.ap()[:, fc, :], 128, slice(0, 128)),
                        (buf.ap()[hsl, hc, :], 64, hsl))

            bkn = kb.bank()
            first = True
            for (ksrc, npart, psl) in pieces(kn):
                kb.ev("act", lambda e, o=sq.ap()[0:npart, :], i=ksrc: e.activation(o, i, AF.Square),
                      reads=[kn], writes=[sq], tag="b0_ksq")
                kb.b.emit("pe", lambda e, o=bkn.ap()[:, :], l=c.ones_km.ap()[0:npart, :],
                          r=sq.ap()[0:npart, :], st=first, sp=(not first):
                          e.matmul(o, l, r, start=st, stop=sp),
                          reads=[c.ones_km, sq], writes=[bkn], tag="b0_knmm")
                first = False
            kb.ev("dve", lambda e, o=nrm.ap()[:, :], i=bkn.ap()[:, :]: e.tensor_scalar_add(o, i, 1e-6),
                  reads=[bkn], writes=[nrm], tag="b0_keps")
            kb.ev("act", lambda e, o=nrm.ap()[:, :]: e.activation(o, nrm.ap()[:, :], AF.Sqrt),
                  reads=[nrm], writes=[nrm], tag="b0_ksqrt")
            kb.ev("dve", lambda e, o=nrm.ap()[:, :]: e.reciprocal(o, nrm.ap()[:, :]),
                  reads=[nrm], writes=[nrm], tag="b0_krec")
            bkr = kb.bank()
            kb.b.emit("pe", lambda e, o=bkr.ap()[:, :], l=sel4.ap()[:, 128 * h:128 * (h + 1)],
                      r=gkb2.ap()[:, SEG:2 * SEG]: e.matmul(o, l, r, start=True, stop=True),
                      reads=[sel4, gkb2], writes=[bkr], tag="b0_bred")
            kb.ev("dve", lambda e, o=scl.ap()[:, :], i=bkr.ap()[:, :]: e.tensor_mul(o, i, nrm.ap()[:, :]),
                  reads=[bkr, nrm], writes=[scl], tag="b0_bnr")
            for (ksrc, npart, psl) in pieces(kn):
                kdst = knr.ap()[:, fc, :] if npart == 128 else knr.ap()[hsl, hc, :]
                kb.ev("dve", lambda e, o=kdst, i=ksrc, m=scl.ap()[0:npart, :] if npart == 128 else scl.ap()[psl, :]:
                      e.tensor_mul(o, i, m), reads=[kn, scl], writes=[knr], tag="b0_knr")
            for (ksrc, npart, psl) in pieces(kn):
                kb.ev("dve", lambda e, o=ksrc, i=ksrc, m=nrm.ap()[0:npart, :] if npart == 128 else nrm.ap()[psl, :]:
                      e.tensor_mul(o, i, m), reads=[kn, nrm], writes=[kn], tag="b0_knn")
            for tt in range(4):
                btc = kb.bank()
                kb.b.emit("pe", lambda e, o=btc.ap()[0:128, 0:1], i=nrm.ap()[0:1, 128 * tt:128 * (tt + 1)]:
                          e.transpose(o, i, c.ident_f32.ap()[0:1, 0:1]),
                          reads=[nrm, c.ident_f32], writes=[btc], tag="b0_ntr")
                kb.ev("dve", lambda e, o=ncol.ap()[:, :], i=btc.ap()[0:128, 0:1]: e.tensor_copy(o, i),
                      reads=[btc], writes=[ncol], tag="b0_ntc")
                ksl = slice(192 * h, 192 * (h + 1))
                kb.ev("dve", lambda e, o=ktm.ap()[:, tt, ksl], s=ncol.ap()[:, 0:1]:
                      e.tensor_scalar(o, o, s, None, ALU.mult),
                      reads=[ktm, ncol], writes=[ktm], tag="b0_ktn")
            bq = kb.bank()
            first = True
            for (qsrc, npart, psl) in pieces(qn):
                kb.ev("act", lambda e, o=sq.ap()[0:npart, :], i=qsrc: e.activation(o, i, AF.Square),
                      reads=[qn], writes=[sq], tag="b0_qsq")
                kb.b.emit("pe", lambda e, o=bq.ap()[:, :], l=c.ones_km.ap()[0:npart, :],
                          r=sq.ap()[0:npart, :], st=first, sp=(not first):
                          e.matmul(o, l, r, start=st, stop=sp),
                          reads=[c.ones_km, sq], writes=[bq], tag="b0_qnmm")
                first = False
            kb.ev("dve", lambda e, o=nrm.ap()[:, :], i=bq.ap()[:, :]: e.tensor_scalar_add(o, i, 1e-6),
                  reads=[bq], writes=[nrm], tag="b0_qeps")
            kb.ev("act", lambda e, o=nrm.ap()[:, :]: e.activation(o, nrm.ap()[:, :], AF.Sqrt),
                  reads=[nrm], writes=[nrm], tag="b0_qsqrt")
            kb.ev("dve", lambda e, o=nrm.ap()[:, :]: e.reciprocal(o, nrm.ap()[:, :]),
                  reads=[nrm], writes=[nrm], tag="b0_qrec")
            bq2 = kb.bank()
            kb.b.emit("pe", lambda e, o=bq2.ap()[:, :], l=sel4.ap()[:, 128 * h:128 * (h + 1)],
                      r=grow.ap()[:, :]: e.matmul(o, l, r, start=True, stop=True),
                      reads=[sel4, grow], writes=[bq2], tag="b0_gred")
            kb.ev("dve", lambda e, o=scl.ap()[:, :], i=bq2.ap()[:, :]: e.tensor_mul(o, i, nrm.ap()[:, :]),
                  reads=[bq2, nrm], writes=[scl], tag="b0_gnr")
            for (qsrc, npart, psl) in pieces(qn):
                qdst = qg.ap()[:, fc, :] if npart == 128 else qg.ap()[hsl, hc, :]
                kb.ev("dve", lambda e, o=qdst, i=qsrc, m=scl.ap()[0:npart, :] if npart == 128 else scl.ap()[psl, :]:
                      e.tensor_mul(o, i, m), reads=[qn, scl], writes=[qg], tag="b0_qg")
            for (qsrc, npart, psl) in pieces(qn):
                kb.ev("dve", lambda e, o=qsrc, i=qsrc, m=nrm.ap()[0:npart, :] if npart == 128 else nrm.ap()[psl, :]:
                      e.tensor_mul(o, i, m), reads=[qn, nrm], writes=[qn], tag="b0_qn2")
        if s == 0:
            kb.dbg("kn0", kn, [kn])
            kb.dbg("qg0", qg, [qg])
            kb.dbg("ktm0", ktm, [ktm])
            kb.dbg("vtm0", vtm, [vtm])
        for ccc in range(4):
            csl = slice(128 * ccc, 128 * (ccc + 1))
            for h in range(H):
                fc, hc, hb = fullc[h], halfc[h], halfb[h]
                hsl = slice(hb, hb + 64)
                vsl = slice(384 * h, 384 * (h + 1))
                kb.dma("sp", row0.ap(), cum.ap()[h:h + 1, csl], tag="b0_r0")
                kb.dma("sp", row1.ap(), ratr.ap()[h:h + 1, csl], tag="b0_r1")
                kb.dma("sp", row2.ap(), gkb2.ap()[h:h + 1, SEG + 128 * ccc:SEG + 128 * (ccc + 1)], tag="b0_r2")
                bt = kb.bank()
                kb.b.emit("pe", lambda e, o=bt.ap()[0:128, 0:1], i=row0.ap()[0:1, :]:
                          e.transpose(o, i, c.ident_f32.ap()[0:1, 0:1]),
                          reads=[row0, c.ident_f32], writes=[bt], tag="b0_ctr")
                kb.ev("dve", lambda e, o=cumc.ap()[:, :], i=bt.ap()[0:128, 0:1]: e.tensor_copy(o, i),
                      reads=[bt], writes=[cumc], tag="b0_ctc")
                bt2 = kb.bank()
                kb.b.emit("pe", lambda e, o=bt2.ap()[0:128, 0:1], i=row1.ap()[0:1, :]:
                          e.transpose(o, i, c.ident_f32.ap()[0:1, 0:1]),
                          reads=[row1, c.ident_f32], writes=[bt2], tag="b0_rtr")
                kb.ev("dve", lambda e, o=ratc.ap()[:, :], i=bt2.ap()[0:128, 0:1]: e.tensor_copy(o, i),
                      reads=[bt2], writes=[ratc], tag="b0_rtc")
                bt3 = kb.bank()
                kb.b.emit("pe", lambda e, o=bt3.ap()[0:128, 0:1], i=row2.ap()[0:1, :]:
                          e.transpose(o, i, c.ident_f32.ap()[0:1, 0:1]),
                          reads=[row2, c.ident_f32], writes=[bt3], tag="b0_btr")
                kb.ev("dve", lambda e, o=bnc.ap()[:, :], i=bt3.ap()[0:128, 0:1]:
                      e.tensor_scalar_mul(o, i, -1.0),
                      reads=[bt3], writes=[bnc], tag="b0_btc")
                kb.ev("act", lambda e, o=gcolx.ap()[:, :], i=cumc.ap()[:, :]: e.activation(o, i, AF.Exp),
                      reads=[cumc], writes=[gcolx], tag="b0_gcol")
                brr = kb.bank()
                kb.b.emit("pe", lambda e, o=brr.ap()[:, 0:128], l=c.ones_row_f32.ap()[:, :], r=row0.ap()[0:1, :]:
                          e.matmul(o, l, r, start=True, stop=True),
                          reads=[c.ones_row_f32, row0], writes=[brr], tag="b0_crr")
                kb.ev("dve", lambda e, o=em.ap()[:, :], i=brr.ap()[:, 0:128]:
                      e.tensor_scalar(o, i, cumc.ap()[:, 0:1], 0.0, ALU.subtract, ALU.min),
                      reads=[brr, cumc], writes=[em], tag="b0_dmin")
                kb.ev("dve", lambda e, o=emL.ap()[:, :], i=brr.ap()[:, 0:128]:
                      e.tensor_scalar(o, i, cumc.ap()[:, 0:1], 0.0, ALU.subtract, ALU.max),
                      reads=[brr, cumc], writes=[emL], tag="b0_dmax")
                kb.ev("act", lambda e, o=em.ap()[:, :]: e.activation(o, em.ap()[:, :], AF.Exp),
                      reads=[em], writes=[em], tag="b0_eexp")
                kb.ev("act", lambda e, o=emL.ap()[:, :]: e.activation(o, emL.ap()[:, :], AF.Exp, scale=-1.0),
                      reads=[emL], writes=[emL], tag="b0_eexpL")
                kb.ev("dve", lambda e, o=emU.ap()[:, :]: e.tensor_mul(o, em.ap()[:, :], umask_s.ap()[:, :]),
                      reads=[em, umask_s], writes=[emU], tag="b0_emU")
                kb.ev("dve", lambda e, o=emI.ap()[:, :]: e.tensor_mul(o, em.ap()[:, :], umask_i.ap()[:, :]),
                      reads=[em, umask_i], writes=[emI], tag="b0_emI")
                kb.ev("dve", lambda e, o=emL.ap()[:, :]: e.tensor_mul(o, emL.ap()[:, :], lmask_s.ap()[:, :]),
                      reads=[emL, lmask_s], writes=[emL], tag="b0_emL")
                bkk = kb.bank()
                kb.b.emit("pe", lambda e, o=bkk.ap()[:, 0:128], l=kn.ap()[:, fc, csl], r=knr.ap()[:, fc, csl]:
                          e.matmul(o, l, r, start=True, stop=False),
                          reads=[kn, knr], writes=[bkk], tag="b0_kk1")
                kb.b.emit("pe", lambda e, o=bkk.ap()[:, 0:128], l=kn.ap()[hsl, hc, csl], r=knr.ap()[hsl, hc, csl]:
                          e.matmul(o, l, r, start=False, stop=True),
                          reads=[kn, knr], writes=[bkk], tag="b0_kk2")
                if KB.DEBUG and s == 0 and ccc == 0 and h == 0:
                    dbt = ar.alloc(kb.nid("b0_dbkk"), [128, 128], F32)
                    kb.ev("dve", lambda e, o=dbt.ap()[:, :], i=bkk.ap()[:, 0:128]: e.tensor_copy(o, i),
                          reads=[bkk], writes=[dbt], tag="dbg_bkkcp")
                    kb.dbg("bkk00", dbt, [dbt])
                kb.ev("dve", lambda e, o=NTm.ap()[:, :], i=bkk.ap()[:, 0:128]:
                      e.tensor_mul(o, i, emU.ap()[:, :]),
                      reads=[bkk, emU], writes=[NTm], tag="b0_NT")
                bkl = kb.bank()
                kb.b.emit("pe", lambda e, o=bkl.ap()[:, 0:128], l=knr.ap()[:, fc, csl], r=kn.ap()[:, fc, csl]:
                          e.matmul(o, l, r, start=True, stop=False),
                          reads=[kn, knr], writes=[bkl], tag="b0_kkl1")
                kb.b.emit("pe", lambda e, o=bkl.ap()[:, 0:128], l=knr.ap()[hsl, hc, csl], r=kn.ap()[hsl, hc, csl]:
                          e.matmul(o, l, r, start=False, stop=True),
                          reads=[kn, knr], writes=[bkl], tag="b0_kkl2")
                kb.ev("dve", lambda e, o=Lm.ap()[:, :], i=bkl.ap()[:, 0:128]:
                      e.tensor_mul(o, i, emL.ap()[:, :]),
                      reads=[bkl, emL], writes=[Lm], tag="b0_L")
                bqk = kb.bank()
                kb.b.emit("pe", lambda e, o=bqk.ap()[:, 0:128], l=kn.ap()[:, fc, csl], r=qn.ap()[:, fc, csl]:
                          e.matmul(o, l, r, start=True, stop=False),
                          reads=[kn, qn], writes=[bqk], tag="b0_qk1")
                kb.b.emit("pe", lambda e, o=bqk.ap()[:, 0:128], l=kn.ap()[hsl, hc, csl], r=qn.ap()[hsl, hc, csl]:
                          e.matmul(o, l, r, start=False, stop=True),
                          reads=[kn, qn], writes=[bqk], tag="b0_qk2")
                kb.ev("dve", lambda e, o=MqT.ap()[:, :], i=bqk.ap()[:, 0:128]:
                      e.tensor_mul(o, i, emI.ap()[:, :]),
                      reads=[bqk, emI], writes=[MqT], tag="b0_Mq")
                bks = kb.bank()
                kb.b.emit("pe", lambda e, o=bks.ap()[:, 0:DV], l=kn.ap()[:, fc, csl], r=Sab.ap()[:, h, :]:
                          e.matmul(o, l, r, start=True, stop=False),
                          reads=[kn, Sab], writes=[bks], tag="b0_ks1")
                kb.b.emit("pe", lambda e, o=bks.ap()[:, 0:DV], l=kn.ap()[hsl, hc, csl], r=Sbb.ap()[hsl, h // 2, :]:
                          e.matmul(o, l, r, start=False, stop=True),
                          reads=[kn, Sbb], writes=[bks], tag="b0_ks2")
                kb.ev("dve", lambda e, o=oscr.ap()[:, 0:DV], i=bks.ap()[:, 0:DV], s=gcolx.ap()[:, 0:1]:
                      e.tensor_scalar(o, i, s, None, ALU.mult),
                      reads=[bks, gcolx], writes=[oscr], tag="b0_rh1")
                kb.ev("dve", lambda e, o=oscr.ap()[:, 0:DV], m=vtm.ap()[:, ccc, vsl]:
                      e.tensor_sub(o, oscr.ap()[:, 0:DV], m),
                      reads=[oscr, vtm], writes=[oscr], tag="b0_rh2")
                kb.ev("dve", lambda e, o=rhs.ap()[:, :], i=oscr.ap()[:, 0:DV], s=bnc.ap()[:, 0:1]:
                      e.tensor_scalar(o, i, s, None, ALU.mult),
                      reads=[oscr, bnc], writes=[rhs], tag="b0_rh3")
                bsv = kb.bank()
                kb.b.emit("pe", lambda e, o=bsv.ap()[:, 0:DV], l=NTm.ap()[:, :], r=rhs.ap()[:, :]:
                          e.matmul(o, l, r, start=True, stop=True),
                          reads=[NTm, rhs], writes=[bsv], tag="b0_ap0")
                kb.ev("dve", lambda e, o=rv.ap()[:, :], i=rhs.ap()[:, :], bb=bsv.ap()[:, 0:DV]:
                      e.tensor_sub(o, i, bb),
                      reads=[rhs, bsv], writes=[rv], tag="b0_apv0")
                cN, cL, nN, nL = NTm, Lm, NT2, L2
                for lev in range(6):
                    bq1 = kb.bank()
                    kb.b.emit("pe", lambda e, o=bq1.ap()[:, 0:128], l=cL.ap()[:, :], r=cN.ap()[:, :]:
                              e.matmul(o, l, r, start=True, stop=True),
                              reads=[cL, cN], writes=[bq1], tag="b0_sqN")
                    bq2m = kb.bank()
                    kb.b.emit("pe", lambda e, o=bq2m.ap()[:, 0:128], l=cN.ap()[:, :], r=cL.ap()[:, :]:
                              e.matmul(o, l, r, start=True, stop=True),
                              reads=[cN, cL], writes=[bq2m], tag="b0_sqL")
                    kb.ev("dve", lambda e, o=nN.ap()[:, :], i=bq1.ap()[:, 0:128]: e.tensor_copy(o, i),
                          reads=[bq1], writes=[nN], tag="b0_sqNe")
                    kb.ev("dve", lambda e, o=nL.ap()[:, :], i=bq2m.ap()[:, 0:128]: e.tensor_copy(o, i),
                          reads=[bq2m], writes=[nL], tag="b0_sqLe")
                    bap = kb.bank()
                    kb.b.emit("pe", lambda e, o=bap.ap()[:, 0:DV], l=nN.ap()[:, :], r=rv.ap()[:, :]:
                              e.matmul(o, l, r, start=True, stop=True),
                              reads=[nN, rv], writes=[bap], tag="b0_ap")
                    kb.ev("dve", lambda e, o=rv.ap()[:, :], i=bap.ap()[:, 0:DV]:
                          e.tensor_add(o, rv.ap()[:, :], i),
                          reads=[rv, bap], writes=[rv], tag="b0_apv")
                    cN, cL, nN, nL = nN, nL, cN, cL
                kb.ev("dve", lambda e, o=rvb.ap()[:, :], i=rv.ap()[:, :]: e.tensor_copy(o, i),
                      reads=[rv], writes=[rvb], tag="b0_rvc")
                bo = kb.bank()
                for dvc in range(3):
                    dsl = slice(128 * dvc, 128 * (dvc + 1))
                    kb.b.emit("pe", lambda e, o=bo.ap()[:, dsl], l=Sab.ap()[:, h, dsl], r=qg.ap()[:, fc, csl],
                              st=(dvc == 0), sp=False: e.matmul(o, l, r, start=st, stop=sp),
                              reads=[Sab, qg], writes=[bo], tag="b0_oi1")
                    kb.b.emit("pe", lambda e, o=bo.ap()[:, dsl], l=Sbb.ap()[hsl, h // 2, dsl], r=qg.ap()[hsl, hc, csl]:
                              e.matmul(o, l, r, start=False, stop=False),
                              reads=[Sbb, qg], writes=[bo], tag="b0_oi2")
                    kb.b.emit("pe", lambda e, o=bo.ap()[:, dsl], l=rvb.ap()[:, dsl], r=MqT.ap()[:, :],
                              sp=(dvc == 2): e.matmul(o, l, r, start=False, stop=sp),
                              reads=[rvb, MqT], writes=[bo], tag="b0_oMq")
                for dvc in range(3):
                    kb.ev("dve", lambda e, o=ofm.ap()[:, 3 * h + dvc, csl], i=bo.ap()[:, 128 * dvc:128 * (dvc + 1)]:
                          e.tensor_copy(o, i),
                          reads=[bo], writes=[ofm], tag="b0_oev")
                kb.ev("dve", lambda e, o=kpc.ap()[:, :], i=ktm.ap()[:, ccc, 192 * h:192 * (h + 1)],
                      s=ratc.ap()[:, 0:1]: e.tensor_scalar(o, i, s, None, ALU.mult),
                      reads=[ktm, ratc], writes=[kpc], tag="b0_kpr")
                bgc = kb.bank()
                kb.b.emit("pe", lambda e, o=bgc.ap()[:, 0:1], l=c.ones_row_f32.ap()[:, :],
                          r=gend.ap()[0:1, 4 * h + ccc:4 * h + ccc + 1]:
                          e.matmul(o, l, r, start=True, stop=True),
                          reads=[c.ones_row_f32, gend], writes=[bgc], tag="b0_gcr")
                kb.ev("dve", lambda e, o=gcc.ap()[:, :], i=bgc.ap()[:, 0:1]: e.tensor_copy(o, i),
                      reads=[bgc], writes=[gcc], tag="b0_gcc")
                bst = kb.bank()
                fcoff = 0 if h % 2 == 0 else 64
                hcoff = 128 if h % 2 == 0 else 0
                kb.b.emit("pe", lambda e, o=bst.ap()[:, 0:DV], l=kpc.ap()[:, fcoff:fcoff + 128], r=rvb.ap()[:, :]:
                          e.matmul(o, l, r, start=True, stop=True),
                          reads=[kpc, rvb], writes=[bst], tag="b0_su1")
                kb.ev("dve", lambda e, o=Sa.ap()[:, h, :], i=Sa.ap()[:, h, :], s=gcc.ap()[:, 0:1],
                      bb=bst.ap()[:, 0:DV]:
                      e.scalar_tensor_tensor(o, i, s, bb, ALU.mult, ALU.add),
                      reads=[Sa, gcc, bst], writes=[Sa], tag="b0_sup1")
                bst2 = kb.bank()
                kb.b.emit("pe", lambda e, o=bst2.ap()[hsl, 0:DV], l=kpc.ap()[:, hcoff:hcoff + 64], r=rvb.ap()[:, :]:
                          e.matmul(o, l, r, start=True, stop=True),
                          reads=[kpc, rvb], writes=[bst2], tag="b0_su2")
                kb.ev("dve", lambda e, o=Sb.ap()[hsl, h // 2, :], i=Sb.ap()[hsl, h // 2, :],
                      s=gcc.ap()[hsl, 0:1], bb=bst2.ap()[hsl, 0:DV]:
                      e.scalar_tensor_tensor(o, i, s, bb, ALU.mult, ALU.add),
                      reads=[Sb, gcc, bst2], writes=[Sb], tag="b0_sup2")
                if s == 0 and ccc == 0 and h == 0:
                    kb.dbg("rv00", rv, [rv])
                    kb.dbg("NT00", NTm, [NTm])
                    kb.dbg("emU00", emU, [emU])
                    kb.dbg("rhs00", rhs, [rhs])
                    kb.dbg("Sa00", Sa.ap()[:, 0, :], [Sa])
                kb.ev("dve", lambda e, o=Sab.ap()[:, h, :], i=Sa.ap()[:, h, :]: e.tensor_copy(o, i),
                      reads=[Sa], writes=[Sab], tag="b0_scp1")
                kb.ev("dve", lambda e, o=Sbb.ap()[hsl, h // 2, :], i=Sb.ap()[hsl, h // 2, :]: e.tensor_copy(o, i),
                      reads=[Sb], writes=[Sbb], tag="b0_scp2")
        if s == 0:
            kb.dbg("ofm0", ofm, [ofm])
        rinvh = []
        for h in range(H):
            brm = kb.bank()
            for dvc in range(3):
                kb.ev("act", lambda e, o=sq.ap()[:, :], i=ofm.ap()[:, 3 * h + dvc, :]:
                      e.activation(o, i, AF.Square),
                      reads=[ofm], writes=[sq], tag="b0_osq")
                kb.b.emit("pe", lambda e, o=brm.ap()[:, :], l=c.ones_km.ap()[:, :], r=sq.ap()[:, :],
                          st=(dvc == 0), sp=(dvc == 2): e.matmul(o, l, r, start=st, stop=sp),
                          reads=[c.ones_km, sq], writes=[brm], tag="b0_omm")
            rh = ar.alloc(kb.nid("b0_rinv"), [128, SEG], BF16)
            rinvh.append(rh)
            kb.ev("act", lambda e, o=nrm.ap()[:, :], i=brm.ap()[:, :], b=epscol.ap()[:, :]:
                  e.activation(o, i, AF.Sqrt, bias=b, scale=1.0 / DV),
                  reads=[brm, epscol], writes=[nrm], tag="b0_orms")
            kb.ev("dve", lambda e, o=nrm.ap()[:, :]: e.reciprocal(o, o),
                  reads=[nrm], writes=[nrm], tag="b0_orec")
            kb.ev("dve", lambda e, o=rh.ap()[:, :], i=nrm.ap()[:, :]: e.tensor_copy(o, i),
                  reads=[nrm], writes=[rh], tag="b0_orcp")
        for j12 in range(12):
            wb = wgb[j12 % 2]
            kb.dma("sp", wb.ap(), c.b0_wgb_b[j12, :, :], tag="b0_wgl")
            bk = kb.bank()
            for ci in range(NCHUNK):
                kb.b.emit("pe", lambda e, o=bk.ap()[:, :], l=wb.ap()[:, 128 * ci:128 * (ci + 1)],
                          r=xsb.ap()[:, ci, ss], st=(ci == 0), sp=(ci == 7):
                          e.matmul(o, l, r, start=st, stop=sp),
                          reads=[wb, xsb], writes=[bk], tag="b0_gmm")
            kb.ev("act", lambda e, o=gsil.ap()[:, :], i=bk.ap()[:, :]:
                  e.activation(o, i, AF.Silu),
                  reads=[bk], writes=[gsil], tag="b0_gsil")
            h = j12 // 3
            dvc = j12 % 3
            kb.ev("dve", lambda e, o=oscr.ap()[:, :], i=ofm.ap()[:, j12, :], s=gnc.ap()[:, dvc:dvc + 1],
                  rr=rinvh[h].ap()[:, :]:
                  e.scalar_tensor_tensor(o, i, s, rr, ALU.mult, ALU.mult),
                  reads=[ofm, gnc, rinvh[h]], writes=[oscr], tag="b0_osc")
            kb.ev("dve", lambda e, o=ofm.ap()[:, j12, :], i=oscr.ap()[:, :]:
                  e.tensor_mul(o, i, gsil.ap()[:, :]),
                  reads=[oscr, gsil], writes=[ofm], tag="b0_ogate")
        for rh in rinvh:
            ar.free(rh)
        for j in range(8):
            wb = wsob[j % 2]
            kb.dma("sp", wb.ap(), c.b0_wsob_b[j, :, :], tag="b0_wsol")
            bk = kb.bank()
            for ci2 in range(12):
                kb.b.emit("pe", lambda e, o=bk.ap()[:, :], l=wb.ap()[:, 128 * ci2:128 * (ci2 + 1)],
                          r=ofm.ap()[:, ci2, :], st=(ci2 == 0), sp=(ci2 == 11):
                          e.matmul(o, l, r, start=st, stop=sp),
                          reads=[wb, ofm], writes=[bk], tag="b0_somm")
            kb.ev("act", lambda e, o=ysob.ap()[:, j, :], i=bk.ap()[:, :]:
                  e.activation(o, i, AF.Identity),
                  reads=[bk], writes=[ysob], tag="b0_soev")
        for f in range(8):
            kb.dma("sp", zss.ap()[:, f, :], c.zspill[:, f, ss], tag="b0_zsl")
        for fo in range(8):
            xg = xseg[fo % 2]
            kb.dma("sp", xg.ap()[:, :], c.xspill[128 * fo:128 * (fo + 1), ss], tag="b0_xld")
            wb = woutb[fo % 2]
            kb.dma("sp", wb.ap(), c.b0_woutb_b[fo, :, :], tag="b0_woutl")
            bk = kb.bank()
            for ci2 in range(16):
                rsrc = ysob.ap()[:, ci2, :] if ci2 < 8 else zss.ap()[:, ci2 - 8, :]
                rkey = ysob if ci2 < 8 else zss
                kb.b.emit("pe", lambda e, o=bk.ap()[:, :], l=wb.ap()[:, 128 * ci2:128 * (ci2 + 1)],
                          r=rsrc, st=(ci2 == 0), sp=(ci2 == 15):
                          e.matmul(o, l, r, start=st, stop=sp),
                          reads=[wb, rkey], writes=[bk], tag="b0_outmm")
            kb.ev("dve", lambda e, o=xg.ap()[:, :], i=bk.ap()[:, :]:
                  e.tensor_add(o, o, i),
                  reads=[bk, xg], writes=[xg], tag="b0_xadd")
            kb.dma("sp", c.xspill[128 * fo:128 * (fo + 1), ss], xg.ap()[:, :], tag="b0_xst")
    for t_ in (xsb, qn, qg, kn, knr, ktm, vtm, ofm, ysob, zss):
        ar.free(t_)
    c.X = []
    for fo in range(8):
        xt = ar.alloc(f"Xb{fo}_r", [128, T], F32, high=True)
        c.X.append(xt)
        kb.dma("sp", xt.ap()[:, :], c.xspill[128 * fo:128 * (fo + 1), :], tag="b0_xfin")
    ar.free(gkb2, cum, grow, ratr,
            wqb[0], wqb[1], wfi[0], wfi[1], scl, sq, nrm, cumc, ratc, bnc, gcc,
            ncol, gcolx, row0, row1, row2, gend, em, emL, emU, emI, NTm, Lm,
            NT2, L2, MqT, rhs, rv, rvb, kpc, oscr, wsob[0], wsob[1],
            woutb[0], woutb[1], wgb[0], wgb[1], gsil, xseg[0], xseg[1],
            alog, dtb, bbc, gnc, onescol, epscol, sel4, umask_s, umask_i,
            lmask_s, wgkb, negA, Sa, Sb, Sab, Sbb)



SEM_NAMES = ["pe", "act", "dve", "sp", "pool", "qsp", "qpool"]


def assemble(kb):
    kb.b.finalize()
    nc = kb.nc
    sems = {}
    for n in ("pe", "act", "dve", "pool", "sp"):
        sems[n] = nc.alloc_semaphore("sem_" + n)
    for qe in ("sp", "pool"):
        for j in range(NDMACLASS):
            sems[f"q{qe}{j}"] = nc.alloc_semaphore(f"sem_q{qe}{j}")
    kb.b.build(nc, sems)
    return nc


def shard_inputs(full, core_id):
    m = dict(host_constants(core_id))
    m["x_fm"] = np.ascontiguousarray(full["L0_attn_a_x"][core_id].T)
    w1 = np.asarray(full["L1_ffn_w1"])
    w2 = np.asarray(full["L1_ffn_w2"])
    m["l1f_w1t_t"] = tile_fo(np.ascontiguousarray(w1.T))
    m["l1f_w1n"] = np.ascontiguousarray(w1)
    m["l1f_w2t"] = np.ascontiguousarray(w2.T)
    m["l1f_w2n"] = np.ascontiguousarray(w2)
    m["l1f_b1c"] = np.ascontiguousarray(np.asarray(full["L1_ffn_b1"]).reshape(32, 128).T)
    m["l1f_b2c"] = np.ascontiguousarray(np.asarray(full["L1_ffn_b2"]).reshape(8, 128).T)
    m["a0_wqT_t"] = tile_fo(np.ascontiguousarray(np.asarray(full["L0_attn_a_wq"]).T), 512)
    m["a0_wkT_t"] = tile_fo(np.ascontiguousarray(np.asarray(full["L0_attn_a_wk"]).T), 512)
    m["a0_wvT_t"] = tile_fo(np.ascontiguousarray(np.asarray(full["L0_attn_a_wv"]).T), 512)
    m["aa_wq_t"] = tile_fo(np.asarray(full["L1_attn_a_Wq"]), 512)
    m["aa_wkv_t"] = tile_fo(np.asarray(full["L1_attn_a_Wkv"]), 512)
    m["aa_wo_t"] = tile_fo(np.asarray(full["L1_attn_a_Wo"]), 512)
    m["aa_bqc"] = np.ascontiguousarray(np.asarray(full["L1_attn_a_bq"]).reshape(8, 128).T)
    kvb = np.asarray(full["L1_attn_a_bkv"])
    m["aa_bkc"] = np.ascontiguousarray(kvb[:1024].reshape(8, 128).T)
    m["aa_bvr"] = np.ascontiguousarray(kvb[1024:].reshape(1, 1024))
    m["aa_boc"] = np.ascontiguousarray(np.asarray(full["L1_attn_a_bo"]).reshape(8, 128).T)
    wqc = np.asarray(full["L1_attn_b_w_q"]).transpose(1, 0, 2).reshape(1024, 1024)
    wkc = np.asarray(full["L1_attn_b_w_k"]).transpose(1, 0, 2).reshape(1024, 1024)
    wvc = np.asarray(full["L1_attn_b_w_v"]).transpose(1, 0, 2).reshape(1024, 1024)
    m["ab_wq_t"] = tile_fo(wqc, 512)
    m["ab_wk_t"] = tile_fo(wkc, 512)
    m["ab_wv_t"] = tile_fo(wvc, 512)
    m["ab_wo_t"] = tile_fo(np.asarray(full["L1_attn_b_w_o"]), 512)
    m["moe_gate_t"] = tile_fi(np.ascontiguousarray(np.asarray(full["L0_ffn_gate_w"]).T))
    m["moe_gateb"] = np.ascontiguousarray(np.asarray(full["L0_ffn_gate_b"]).reshape(1, 8))
    w1t = np.stack([tile_fo(np.asarray(full["L0_ffn_W1"])[e], 128).reshape(32, 128, 1024)
                    for e in range(8)])
    m["moe_w1"] = np.ascontiguousarray(
        w1t.reshape(8, 16, 2, 128, 1024).transpose(0, 1, 3, 2, 4).reshape(8, 16, 128, 2048))
    w2t = np.asarray(full["L0_ffn_W2"]).reshape(8, 32, 128, 1024)
    m["moe_w2"] = np.ascontiguousarray(
        w2t.reshape(8, 16, 2, 128, 1024).transpose(0, 1, 3, 2, 4).reshape(8, 16, 128, 2048))
    m["moe_b1c"] = np.ascontiguousarray(np.asarray(full["L0_ffn_b1"]).reshape(8, 32, 128).transpose(0, 2, 1))
    m["moe_b2c"] = np.ascontiguousarray(np.asarray(full["L0_ffn_b2"]).reshape(8, 8, 128).transpose(0, 2, 1))
    m["b0_win"] = tile_fi(np.asarray(full["L0_attn_b_W_in"]))
    m["b0_wq"] = tile_fo(np.asarray(full["L0_attn_b_W_q"]), 128)
    m["b0_wk"] = tile_fo(np.asarray(full["L0_attn_b_W_k"]), 128)
    m["b0_wk_fi"] = tile_fi(np.asarray(full["L0_attn_b_W_k"]))
    m["b0_wv_fi"] = tile_fi(np.asarray(full["L0_attn_b_W_v"]))
    m["b0_wg"] = tile_fo(np.asarray(full["L0_attn_b_W_g"]), 128)
    m["b0_wgkb"] = tile_fi(np.concatenate([np.asarray(full["L0_attn_b_W_gk"]),
                                           np.asarray(full["L0_attn_b_W_b"])], axis=1))
    m["b0_wso"] = tile_fo(np.asarray(full["L0_attn_b_W_so"]), 128)
    m["b0_wout"] = tile_fo(np.asarray(full["L0_attn_b_W_out"]), 128)
    cxw = np.asarray(full["L0_attn_b_conv_x_w"])[:, 0, :]
    czw = np.asarray(full["L0_attn_b_conv_z_w"])[:, 0, :]
    m["b0_convw"] = np.concatenate([tile_fi(cxw), tile_fi(czw)], axis=2)
    m["b0_convb"] = np.concatenate([
        np.asarray(full["L0_attn_b_conv_x_b"]).reshape(8, 128).T,
        np.asarray(full["L0_attn_b_conv_z_b"]).reshape(8, 128).T], axis=1)
    m["b0_convb"] = np.ascontiguousarray(m["b0_convb"])
    m["b0_alog"] = np.asarray(full["L0_attn_b_A_log"]).reshape(4, 1)
    m["b0_dtb"] = np.asarray(full["L0_attn_b_dt_bias"]).reshape(4, 1)
    m["b0_bb"] = np.asarray(full["L0_attn_b_b_b"]).reshape(4, 1)
    m["b0_gnc"] = np.ascontiguousarray(np.asarray(full["L0_attn_b_g_norm_w"]).reshape(3, 128).T)
    return m


def build_nc(blocks):
    nc = bass.Bass("TRN2", target_bir_lowering=False)
    kb = KB(nc)
    c = setup_core(kb)
    if "l1ffn" in blocks:
        setup_l1ffn(c)
    if "attn_a0" in blocks:
        setup_attn_a0(c)
    if "l1attn_a" in blocks:
        setup_l1attn_a(c)
    if "l1attn_b" in blocks:
        setup_l1attn_b(c)
    if "moe" in blocks:
        setup_moe(c)
    if "attn_b0" in blocks:
        setup_attn_b0(c)
    c.moe_spread = "moe" in blocks and blocks.index("moe") > 0
    c.b0_spread = ("attn_b0" in blocks and "moe" in blocks
                   and blocks.index("moe") < blocks.index("attn_b0"))
    if "moe" in blocks and not c.moe_spread:
        emit_moe_prepass(c, range(8))
    if "attn_b0" in blocks and not c.b0_spread:
        emit_b0_prepass(c, range(7))
    for bi, blk in enumerate(blocks):
        if bi == 1 and c.moe_spread:
            emit_moe_prepass(c, [0, 1])
        emit_ln(c)
        if blk == "l1ffn":
            emit_l1ffn(c)
        elif blk == "attn_a0":
            emit_attn_a0(c)
        elif blk == "l1attn_a":
            emit_l1attn(c, "a")
        elif blk == "l1attn_b":
            emit_l1attn(c, "b")
        elif blk == "moe":
            emit_moe(c)
        elif blk == "attn_b0":
            emit_attn_b0(c)
        else:
            raise ValueError(blk)
    store_out(c)
    return assemble(kb), c


FULL_CHAIN = ["attn_a0", "moe", "attn_b0", "moe", "l1attn_a", "l1ffn", "l1attn_b", "l1ffn"]


def kernel(**inputs):
    blocks = FULL_CHAIN
    nc, c = build_nc(blocks)
    declared = set(c.kb.inputs.keys())
    in_maps = []
    for i in range(8):
        m = shard_inputs(inputs, i)
        in_maps.append({k: v for k, v in m.items() if k in declared})
        missing = declared - set(in_maps[-1].keys())
        assert not missing, f"missing inputs: {missing}"
    res = run_bass_kernel_spmd(nc, in_maps, core_ids=list(range(8)))
    out = np.stack([res.results[i]["y_fm"].T for i in range(8)], axis=0)
    return np.ascontiguousarray(out.astype(np.float32))
